# version 14
# speedup vs baseline: 1.0446x; 1.0051x over previous
import numpy as np
from contextlib import ExitStack

import concourse.bass as bass
import concourse.mybir as mybir
from concourse.bass_utils import run_bass_kernel_spmd

F32 = mybir.dt.float32
BF16 = mybir.dt.bfloat16
AF = mybir.ActivationFunctionType
ALU = mybir.AluOpType
AX = mybir.AxisListType

NCORES = 8
B, T, D = 32, 2048, 1024
BL = B // NCORES
NTOK = BL * T
EPS = 1e-5
DCH = D // 128


class Ctx:
    def __init__(self, nc, es):
        self.nc = nc
        self.es = es
        self._sems = {}
        self.bufs = {}
        self.cp = {}
        self.dma_sems = set()

    def semh(self, name):
        if name not in self._sems:
            h = self.es.enter_context(self.nc.semaphore(name))
            self._sems[name] = [h, 0]
        return self._sems[name][0]

    def inc(self, name, inst, n=1):
        h = self.semh(name)
        self._sems[name][1] += n
        inst.then_inc(h, n)
        if n >= 16:
            self.dma_sems.add(name)
        return self._sems[name][1]

    def val(self, name):
        self.semh(name)
        return self._sems[name][1]

    def wait(self, eng, name, v=None):
        eng.wait_ge(self.semh(name), self.val(name) if v is None else v)

    def mark(self, key, sem):
        self.cp[key] = (sem, self.val(sem))

    def wait_cp(self, eng, key):
        if key in self.cp:
            sem, v = self.cp[key]
            if v > 0:
                eng.wait_ge(self.semh(sem), v)

    def sb(self, name, shape, dtype):
        if name not in self.bufs:
            self.bufs[name] = self.es.enter_context(
                self.nc.sbuf_tensor(name, list(shape), dtype))
        return self.bufs[name]

    def open_stage(self):
        self.stage_es = ExitStack()
        self.stage_bufs = {}
        self._scopes = []
        self._stage_id = getattr(self, "_stage_id", -1) + 1

    def ssb(self, name, shape, dtype):
        if name not in self.stage_bufs:
            es = self._scopes[-1][0] if self._scopes else self.stage_es
            uname = f"{name}_s{self._stage_id}"
            self.stage_bufs[name] = es.enter_context(
                self.nc.sbuf_tensor(uname, list(shape), dtype))
            if self._scopes:
                self._scopes[-1][1].append(name)
        return self.stage_bufs[name]

    def push_scope(self):
        self._scopes.append((ExitStack(), []))

    def pop_scope(self):
        es, names = self._scopes.pop()
        es.close()
        for n in names:
            del self.stage_bufs[n]

    def close_stage(self):
        self.stage_es.close()
        self.stage_bufs = {}

    def constv(self, val):
        name = f"cst_{abs(hash(('c', float(val)))) % 10**10}"
        if name not in self.bufs:
            buf = self.sb(name, (128, 1), F32)
            with self.nc.Block() as blk:
                def g(gp, buf=buf, val=val):
                    gp.memset(buf[:, :], float(val))
                blk.gpsimd(g)
        return self.bufs[name]

    def ps_pool(self):
        if "PSF0" not in self.bufs:
            for i in range(4):
                self.bufs[f"PSF{i}"] = self.es.enter_context(
                    self.nc.psum_tensor(f"PSF{i}", [128, 512], F32))
            self.bufs["PSW"] = self.es.enter_context(
                self.nc.psum_tensor("PSW", [128, 1024], F32))
            for i in range(2):
                self.bufs[f"PSB{i}"] = self.es.enter_context(
                    self.nc.psum_tensor(f"PSB{i}", [128, 1024], BF16))
        return [self.bufs[f"PSF{i}"] for i in range(4)], self.bufs["PSW"], \
               [self.bufs[f"PSB{i}"] for i in range(2)]


def fmv(buf, unit, c0, c1):
    return buf[:, :].rearrange("p (c u) -> p c u", c=DCH)[:, :, c0:c1]


def emit_idents(ctx):
    nc = ctx.nc
    ib = ctx.sb("ident_bf", (128, 128), BF16)
    if_ = ctx.sb("ident_f32", (128, 128), F32)
    cm = ctx.sb("cmask", (128, 128), F32)
    eps = ctx.sb("eps", (128, 1), F32)
    with nc.Block() as blk:
        def g(gp):
            ctx.inc("sv", gp.memset(eps[:, :], EPS))
            ctx.inc("sv", gp.memset(ib[:, :], 1.0))
            ctx.inc("sv", gp.memset(if_[:, :], 1.0))
            ctx.inc("sv", gp.memset(cm[:, :], 0.0))
            ctx.wait(gp, "sv")
            ctx.inc("sv", gp.affine_select(cm[:, :], cm[:, :], [[-1, 128]],
                                           ALU.is_ge, -1e30, base=0,
                                           channel_multiplier=1))
            ctx.wait(gp, "sv")
            ctx.inc("sv", gp.affine_select(ib[:, :], ib[:, :], [[1, 128]],
                                           ALU.is_equal, 0.0, base=0,
                                           channel_multiplier=-1))
            ctx.inc("sv", gp.affine_select(if_[:, :], if_[:, :], [[1, 128]],
                                           ALU.is_equal, 0.0, base=0,
                                           channel_multiplier=-1))
            ctx.wait(gp, "sv")
        blk.gpsimd(g)



def emit_ln_to_fm(ctx, blk, x_dram, row0, ntok, fm_buf, fm_unit, fm_col0, tag,
                  out_f32=False, sub_range=None):
    nsub = ntok // 128
    dt_out = F32 if out_f32 else BF16
    ident = ctx.bufs["ident_f32" if out_f32 else "ident_bf"]
    psf, psw, psb = ctx.ps_pool()

    xt = [ctx.ssb(f"lnx{p}", (128, D), F32) for p in range(2)]
    st_ = [ctx.ssb(f"lns{p}", (128, 12), F32) for p in range(2)]
    mv = [ctx.ssb(f"lnm{p}", (128, 2), F32) for p in range(2)]
    rs = [ctx.ssb(f"lnr{p}", (128, 1), F32) for p in range(2)]
    nm = [ctx.ssb(f"lnn{p}", (128, 1), F32) for p in range(2)]
    if out_f32:
        xn = [ctx.ssb(f"lnof{p}", (128, D), F32) for p in range(2)]
    else:
        xn = [ctx.ssb(f"lnob{p}", (128, D), BF16) for p in range(2)]

    dv, ac, pe, ev = "Ldv", "Lac", "Lpe", "Lev"

    for s in (range(nsub) if sub_range is None else range(*sub_range)):
        p = s % 2
        ld = f"Lld{p}"
        r0 = row0 + s * 128
        if out_f32:
            tp_tiles = [psf[2 * p][:, :], psf[2 * p + 1][:, :]]
        else:
            tp_tiles = [psb[p][:, :]]

        def dma_in(sync, s=s, p=p, r0=r0, ld=ld):
            ctx.wait_cp(sync, (tag, "xtfree", s - 2))
            ctx.inc(ld, sync.dma_start(out=xt[p][:, :],
                                       in_=x_dram[r0:r0 + 128, :]), 16)
            ctx.mark((tag, "xload", s), ld)
        blk.sync(dma_in)

        def dve(v, s=s, p=p):
            ctx.wait_cp(v, (tag, "xload", s))
            ctx.inc(dv, v.bn_stats(st_[p][:, 0:6], xt[p][:, 0:512]))
            ctx.inc(dv, v.bn_stats(st_[p][:, 6:12], xt[p][:, 512:1024]))
            ctx.wait(v, dv)
            ctx.inc(dv, v.bn_aggr(mv[p][:, :], st_[p][:, :]))
            ctx.mark((tag, "aggr", s), dv)
        blk.vector(dve)

        def act1(sc, s=s, p=p):
            ctx.wait_cp(sc, (tag, "aggr", s))
            ctx.inc(ac, sc.activation(rs[p][:, :], mv[p][:, 1:2], AF.Ln,
                                      bias=ctx.bufs["eps"][:, 0:1]))
            ctx.wait(sc, ac)
            ctx.inc(ac, sc.activation(rs[p][:, :], rs[p][:, :], AF.Exp, scale=-0.5))
            ctx.mark((tag, "rstd", s), ac)
        blk.scalar(act1)

        def dve2(v, s=s, p=p):
            ctx.wait_cp(v, (tag, "rstd", s))
            ctx.inc(dv, v.scalar_tensor_tensor(nm[p][:, :], mv[p][:, 0:1], -1.0,
                                               rs[p][:, :], ALU.mult, ALU.mult))
            ctx.mark((tag, "nm", s), dv)
        blk.vector(dve2)

        def act2(v, s=s, p=p):
            ctx.wait_cp(v, (tag, "nm", s))
            ctx.wait_cp(v, (tag, "xnfree", s - 2))
            ctx.inc(dv, v.tensor_scalar(xn[p][:, :], xt[p][:, :],
                                        rs[p][:, 0:1], nm[p][:, 0:1],
                                        ALU.mult, ALU.add))
            ctx.mark((tag, "norm", s), dv)
            ctx.mark((tag, "xtfree", s), dv)
        blk.vector(act2)

        def pe_t(te, s=s, p=p, tp_tiles=tp_tiles):
            ctx.wait_cp(te, (tag, "norm", s))
            ctx.wait_cp(te, (tag, "tpfree", s - 2))
            last = None
            for c in range(DCH):
                tpt = tp_tiles[c // 4] if out_f32 else tp_tiles[0]
                off = (c % 4) * 128 if out_f32 else c * 128
                last = te.transpose(tpt[:, off:off + 128],
                                    xn[p][:, c * 128:(c + 1) * 128], ident[:, :])
            ctx.inc(pe, last)
            ctx.mark((tag, "tdone", s), pe)
            ctx.mark((tag, "xnfree", s), pe)
        blk.tensor(pe_t)

        def evac(v, s=s, p=p, tp_tiles=tp_tiles):
            ctx.wait_cp(v, (tag, "tdone", s))
            if out_f32:
                for h in range(2):
                    dst = fmv(fm_buf, fm_unit, fm_col0 + s * 128,
                              fm_col0 + (s + 1) * 128)[:, 4 * h:4 * (h + 1), :]
                    src = tp_tiles[h].rearrange("p (c u) -> p c u", c=4)
                    ctx.inc(ev, v.tensor_copy(dst, src))
            else:
                dst = fmv(fm_buf, fm_unit, fm_col0 + s * 128,
                          fm_col0 + (s + 1) * 128)
                src = tp_tiles[0].rearrange("p (c u) -> p c u", c=DCH)
                ctx.inc(ev, v.tensor_copy(dst, src))
            ctx.mark((tag, "tpfree", s), ev)
        blk.vector(evac)

    if sub_range is None or sub_range[1] >= nsub:
        ctx.mark((tag, "done"), ev)



def emit_cmix_weights(ctx, wrT, wkT, wvT, mur, muk, tag):
    nc = ctx.nc
    wr = ctx.ssb(f"{tag}wr", (128, DCH * D), BF16)
    wk = ctx.ssb(f"{tag}wk", (128, DCH * D), BF16)
    wv = ctx.ssb(f"{tag}wv", (128, DCH * D), BF16)
    mr = ctx.ssb(f"{tag}mr", (128, DCH), F32)
    mk = ctx.ssb(f"{tag}mk", (128, DCH), F32)
    omr = ctx.ssb(f"{tag}omr", (128, DCH), F32)
    omk = ctx.ssb(f"{tag}omk", (128, DCH), F32)
    with nc.Block() as blk:
        def wload(gp):
            for (dst, src) in ((wr, wrT), (wk, wkT), (wv, wvT)):
                for c in range(DCH):
                    ctx.inc("wl", gp.dma_start(
                        out=dst[:, c * D:(c + 1) * D],
                        in_=src[c * 128:(c + 1) * 128, :]), 16)
            ctx.inc("wl", gp.dma_start(out=mr[:, :], in_=mur[:, :]), 16)
            ctx.inc("wl", gp.dma_start(out=mk[:, :], in_=muk[:, :]), 16)
            ctx.mark((tag, "wloaded"), "wl")
        blk.gpsimd(wload)

        def prep(v):
            ctx.wait_cp(v, (tag, "wloaded"))
            ctx.inc("sv", v.tensor_scalar(omr[:, :], mr[:, :], -1.0, 1.0,
                                          ALU.mult, ALU.add))
            ctx.inc("sv", v.tensor_scalar(omk[:, :], mk[:, :], -1.0, 1.0,
                                          ALU.mult, ALU.add))
        blk.vector(prep)


def emit_cmix(ctx, x_in, x_out, tag):
    nc = ctx.nc
    HB = 1024
    NU = NTOK // HB
    NSUB = HB // 128
    psf, psw, psb = ctx.ps_pool()

    wr, wk, wv = ctx.stage_bufs[f"{tag}wr"], ctx.stage_bufs[f"{tag}wk"], ctx.stage_bufs[f"{tag}wv"]
    mr, mk = ctx.stage_bufs[f"{tag}mr"], ctx.stage_bufs[f"{tag}mk"]
    omr, omk = ctx.stage_bufs[f"{tag}omr"], ctx.stage_bufs[f"{tag}omk"]

    xnf = ctx.ssb("cmx_xnf", (128, DCH * HB), F32)
    bnd = ctx.ssb("cmx_bnd", (128, DCH), F32)
    xrb = ctx.ssb("cmx_xrb", (128, DCH * HB), BF16)
    xkb = ctx.ssb("cmx_xkb", (128, DCH * HB), BF16)
    kab = ctx.ssb("cmx_kab", (128, DCH * HB), BF16)
    t1 = ctx.ssb("cmx_t1", (128, HB), F32)
    krs = [ctx.ssb(f"cmx_krs{p}", (128, 512), F32) for p in range(2)]
    rtm = [ctx.ssb(f"cmx_rtm{p}", (128, D), F32) for p in range(2)]
    xot = [ctx.ssb(f"cmx_xot{p}", (128, D), F32) for p in range(2)]
    xnt = [ctx.ssb(f"cmx_xnt{p}", (128, D), F32) for p in range(2)]

    pk, aK, pR, aR, evo, sto = "Cpk", "CaK", "CpR", "CaR", "Cev", "Cst"

    for u in range(NU):
        new_seq = (u % (T // HB) == 0)
        row0 = u * HB

        with nc.Block() as blk:
            emit_ln_to_fm(ctx, blk, x_in, row0, HB, xnf, HB, 0,
                          ("cmA", tag, u), out_f32=True)

        with nc.Block() as blk:
            def mix(v, u=u, new_seq=new_seq):
                for c in range(DCH):
                    xc = xnf[:, c * HB:(c + 1) * HB]
                    for (dst, m, om) in ((xrb, mr, omr), (xkb, mk, omk)):
                        dc = dst[:, c * HB:(c + 1) * HB]
                        ctx.inc("sv", v.tensor_scalar(
                            t1[:, 1:HB], xnf[:, c * HB:c * HB + HB - 1],
                            om[:, c:c + 1], None, ALU.mult))
                        if new_seq:
                            i0 = v.memset(t1[:, 0:1], 0.0)
                        else:
                            i0 = v.tensor_tensor(t1[:, 0:1], bnd[:, c:c + 1],
                                                 om[:, c:c + 1], ALU.mult)
                        ctx.inc("sv", i0)
                        ctx.wait(v, "sv")
                        ctx.inc("sv", v.scalar_tensor_tensor(
                            dc, xc, m[:, c:c + 1], t1[:, :], ALU.mult, ALU.add))
                        ctx.wait(v, "sv")
                for c in range(DCH):
                    ctx.inc("sv", v.tensor_copy(bnd[:, c:c + 1],
                                                xnf[:, (c + 1) * HB - 1:(c + 1) * HB]))
            blk.vector(mix)

        with nc.Block() as blk:
            for g in range(2 * DCH):
                do, tb = g // 2, g % 2
                gp_ = g % 2

                def c1_pe(te, do=do, tb=tb, g=g, gp_=gp_):
                    ctx.wait_cp(te, (tag, "kreluD", u, g - 2))
                    last = None
                    for c in range(DCH):
                        last = te.matmul(
                            psf[gp_][:, :],
                            wk[:, c * D + do * 128: c * D + (do + 1) * 128],
                            xkb[:, c * HB + tb * 512: c * HB + (tb + 1) * 512],
                            start=(c == 0), stop=(c == DCH - 1))
                    ctx.inc(pk, last)
                    ctx.mark((tag, "kmm", u, g), pk)
                blk.tensor(c1_pe)

                def c1_act(sc, do=do, tb=tb, g=g, gp_=gp_):
                    ctx.wait_cp(sc, (tag, "kmm", u, g))
                    ctx.inc(aK, sc.activation(krs[gp_][:, :], psf[gp_][:, :], AF.Relu))
                    ctx.mark((tag, "kreluD", u, g), aK)
                    ctx.wait(sc, aK)
                    ctx.inc(aK, sc.activation(
                        kab[:, do * HB + tb * 512: do * HB + (tb + 1) * 512],
                        krs[gp_][:, :], AF.Square))
                blk.scalar(c1_act)

        with nc.Block() as blk:
            for s in range(NSUB):
                p = s % 2

                def c2_dma(sync, s=s, p=p):
                    ctx.wait_cp(sync, (tag, "xofree", u, s - 2))
                    ctx.inc(f"Cldx{p}", sync.dma_start(
                        out=xot[p][:, :],
                        in_=x_in[row0 + s * 128: row0 + (s + 1) * 128, :]), 16)
                    ctx.mark((tag, "xold", u, s), f"Cldx{p}")
                blk.sync(c2_dma)

                def c2_pe(te, s=s, p=p):
                    ctx.wait_cp(te, (tag, "sig", u, s - 1))
                    last = None
                    for c in range(DCH):
                        for half in range(2):
                            last = te.matmul(
                                psw[:, half * 512:(half + 1) * 512],
                                xrb[:, c * HB + s * 128: c * HB + (s + 1) * 128],
                                wr[:, c * D + half * 512: c * D + (half + 1) * 512],
                                start=(c == 0), stop=(c == DCH - 1))
                    ctx.inc(pR, last)
                    ctx.mark((tag, "rmm", u, s), pR)
                    ctx.wait_cp(te, (tag, "omul", u, s - 1))
                    last = None
                    for c in range(DCH):
                        for half in range(2):
                            last = te.matmul(
                                psf[2 + half][:, :],
                                kab[:, c * HB + s * 128: c * HB + (s + 1) * 128],
                                wv[:, c * D + half * 512: c * D + (half + 1) * 512],
                                start=(c == 0), stop=(c == DCH - 1))
                    ctx.inc(pR, last)
                    ctx.mark((tag, "omm", u, s), pR)
                blk.tensor(c2_pe)

                def c2_act(sc, s=s, p=p):
                    ctx.wait_cp(sc, (tag, "rmm", u, s))
                    ctx.wait_cp(sc, (tag, "rtmfree", u, s - 2))
                    ctx.inc(aR, sc.activation(rtm[p][:, :], psw[:, :], AF.Sigmoid))
                    ctx.mark((tag, "sig", u, s), aR)
                blk.scalar(c2_act)

                def c2_ev(v, s=s, p=p, u=u):
                    ctx.wait_cp(v, (tag, "sig", u, s))
                    ctx.wait_cp(v, (tag, "omm", u, s))
                    ctx.wait_cp(v, (tag, "xold", u, s))
                    ctx.wait_cp(v, (tag, "xnfree", u * NSUB + s - 2))
                    for half in range(2):
                        ctx.inc(evo, v.tensor_tensor(
                            xnt[p][:, half * 512:(half + 1) * 512],
                            rtm[p][:, half * 512:(half + 1) * 512],
                            psf[2 + half][:, :], ALU.mult))
                    ctx.mark((tag, "omul", u, s), evo)
                    ctx.mark((tag, "rtmfree", u, s), evo)
                    ctx.wait(v, evo)
                    ctx.inc(evo, v.tensor_tensor(xnt[p][:, :], xnt[p][:, :],
                                                 xot[p][:, :], ALU.add))
                    ctx.mark((tag, "sum", u, s), evo)
                    ctx.mark((tag, "xofree", u, s), evo)
                blk.vector(c2_ev)

                def c2_st(gp, s=s, p=p, u=u):
                    ctx.wait_cp(gp, (tag, "sum", u, s))
                    i = gp.dma_start(
                        out=x_out[row0 + s * 128: row0 + (s + 1) * 128, :],
                        in_=xnt[p][:, :])
                    ctx.inc(f"Cst{p}", i, 16)
                    ctx.mark((tag, "xnfree", u * NSUB + s), f"Cst{p}")
                blk.gpsimd(c2_st)




NH, DH = 16, 64

def emit_mha(ctx, x_in, x_out, wq_d, wk_d, wv_d, wo_d, tag):
    nc = ctx.nc
    psf, psw, psb = ctx.ps_pool()
    ident = ctx.bufs["ident_bf"]
    scale = 1.0 / np.sqrt(DH)

    Kf = ctx.ssb("mhaKf", (128, DCH * T), BF16)
    Vt = ctx.ssb("mhaVt", (128, (T // 128) * D), BF16)
    xnf = ctx.ssb("mhaxnf", (128, DCH * T), BF16)
    wbuf = ctx.ssb("mhawb", (128, DCH * D), BF16)
    wo = ctx.ssb("mhawo", (128, DCH * D), BF16)
    qt = [ctx.ssb("mhaqt", (128, DCH * 128), BF16)] * 2
    srow = [ctx.ssb(f"mhasr{p}", (128, T), F32) for p in range(2)]
    pb = [ctx.ssb(f"mhapb{p}", (128, T), BF16) for p in range(4)]
    ptb = [ctx.ssb(f"mhapt{p}", (128, T), BF16) for p in range(2)]
    yfm = [ctx.ssb("mhayf", (128, DCH * 128), BF16)] * 2
    mx = [ctx.ssb(f"mhamx{p}", (128, 1), F32) for p in range(2)]
    ngm = [ctx.ssb(f"mhangm{p}", (128, 1), F32) for p in range(4)]
    sxp = [ctx.ssb(f"mhasxp{p}", (128, 1), F32) for p in range(4)]
    rcp = [ctx.ssb(f"mharcp{p}", (128, 1), F32) for p in range(4)]
    xnt = [ctx.ssb(f"mhaxnt{p}", (128, D), F32) for p in range(2)]
    xot = xnt

    wl, pq, evq, pvs, sev, msk, dvx, axp, pvm, yev, pop, res, sto = (
        f"M{n}" for n in ("wl", "pq", "evq", "pvs", "sev", "msk", "dvx",
                          "axp", "pvm", "yev", "pop", "res", "sto"))

    def load_w(dst, src_dram):
        with nc.Block() as blk:
            def g(gp):
                for c in range(DCH):
                    ctx.inc(wl, gp.dma_start(
                        out=dst[:, c * D:(c + 1) * D],
                        in_=src_dram[c * 128:(c + 1) * 128, :]), 16)
                ctx.wait(gp, wl)
            blk.gpsimd(g)

    for b in range(BL):
        row0 = b * T

        with nc.Block() as blk:
            emit_ln_to_fm(ctx, blk, x_in, row0, T, xnf, T, 0, (tag, "ln", b))

        load_w(wbuf, wk_d)
        with nc.Block() as blk:
            for tb in range(T // 512):
                for do in range(DCH):
                    gpar = (tb * DCH + do) % 2

                    def kmm(te, tb=tb, do=do, gpar=gpar):
                        ctx.wait_cp(te, (tag, "kev", b, tb * DCH + do - 2))
                        last = None
                        for c in range(DCH):
                            last = te.matmul(
                                psf[gpar][:, :],
                                wbuf[:, c * D + do * 128: c * D + (do + 1) * 128],
                                xnf[:, c * T + tb * 512: c * T + (tb + 1) * 512],
                                start=(c == 0), stop=(c == DCH - 1))
                        ctx.inc(pq, last)
                        ctx.mark((tag, "kmm", b, tb * DCH + do), pq)
                    blk.tensor(kmm)

                    def kev(v, tb=tb, do=do, gpar=gpar):
                        ctx.wait_cp(v, (tag, "kmm", b, tb * DCH + do))
                        ctx.inc(evq, v.tensor_copy(
                            Kf[:, do * T + tb * 512: do * T + (tb + 1) * 512],
                            psf[gpar][:, :]))
                        ctx.mark((tag, "kev", b, tb * DCH + do), evq)
                    blk.vector(kev)

        load_w(wbuf, wv_d)
        with nc.Block() as blk:
            for kt in range(T // 128):
                gpar = kt % 2

                def vmm(te, kt=kt, gpar=gpar):
                    ctx.wait_cp(te, (tag, "vev", b, kt - 2))
                    for half in range(2):
                        last = None
                        for c in range(DCH):
                            last = te.matmul(
                                psf[2 * gpar + half][:, :],
                                xnf[:, c * T + kt * 128: c * T + (kt + 1) * 128],
                                wbuf[:, c * D + half * 512: c * D + (half + 1) * 512],
                                start=(c == 0), stop=(c == DCH - 1))
                        ctx.inc(pq, last)
                    ctx.mark((tag, "vmm", b, kt), pq)
                blk.tensor(vmm)

                def vev(v, kt=kt, gpar=gpar):
                    ctx.wait_cp(v, (tag, "vmm", b, kt))
                    for half in range(2):
                        ctx.inc(evq, v.tensor_copy(
                            Vt[:, kt * D + half * 512: kt * D + (half + 1) * 512],
                            psf[2 * gpar + half][:, :]))
                    ctx.mark((tag, "vev", b, kt), evq)
                blk.vector(vev)

        load_w(wbuf, wq_d)
        load_w(wo, wo_d)
        with nc.Block() as blk:
            for qi in range(T // 128):
                qp = qi % 2
                klen = (qi + 1) * 128
                nkt = qi + 1
                nkb = (klen + 511) // 512
                q0 = row0 + qi * 128

                for rnd in range(2):
                    def qmm(te, qi=qi, rnd=rnd):
                        ctx.wait_cp(te, (tag, "qfree", b, qi - 1))
                        ctx.wait_cp(te, (tag, "qev", b, qi, rnd - 1))
                        if rnd == 0:
                            ctx.wait_cp(te, (tag, "qev", b, qi - 1, 1))
                        for do in range(4 * rnd, 4 * rnd + 4):
                            last = None
                            for c in range(DCH):
                                last = te.matmul(
                                    psf[2][:, (do % 4) * 128:(do % 4) * 128 + 128],
                                    wbuf[:, c * D + do * 128: c * D + (do + 1) * 128],
                                    xnf[:, c * T + qi * 128: c * T + (qi + 1) * 128],
                                    start=(c == 0), stop=(c == DCH - 1))
                            ctx.inc(pq, last)
                        ctx.mark((tag, "qmm", b, qi, rnd), pq)
                    blk.tensor(qmm)

                    def qev(v, qi=qi, qp=qp, rnd=rnd):
                        ctx.wait_cp(v, (tag, "qmm", b, qi, rnd))
                        ctx.inc(evq, v.tensor_scalar(
                            qt[qp][:, rnd * 512:(rnd + 1) * 512], psf[2][:, :],
                            scale, None, ALU.mult))
                        ctx.mark((tag, "qev", b, qi, rnd), evq)
                        if rnd == 1:
                            ctx.mark((tag, "qdone", b, qi), evq)
                    blk.vector(qev)

                def _front(h):
                    g = qi * NH + h
                    hp = h % 4
                    hps = h % 2
                    hp2 = h % 2
                    hp4 = h % 4
                    hr = 64 * (h % 2)
                    hc = (h // 2) * 128

                    for kb in range(nkb):
                        kw = min(512, klen - kb * 512)
                        sb_id = ctx.cp.get((tag, "sblkid", b), 0)
                        ctx.cp[(tag, "sblkid", b)] = sb_id + 1
                        par = sb_id % 2

                        def smm(te, qi=qi, h=h, kb=kb, kw=kw, sb_id=sb_id, par=par):
                            if kb == 0:
                                ctx.wait_cp(te, (tag, "qdone", b, qi))
                            ctx.wait_cp(te, (tag, "sfree", b, sb_id - 2))
                            last = te.matmul(
                                psf[par][:, 0:kw],
                                qt[qp][hr:hr + 64, hc:hc + 128],
                                Kf[hr:hr + 64,
                                   (h // 2) * T + kb * 512:
                                   (h // 2) * T + kb * 512 + kw],
                                start=True, stop=True)
                            ctx.inc(pvs, last)
                            ctx.mark((tag, "smm", b, sb_id), pvs)
                        blk.tensor(smm)

                        def sevac(v, qi=qi, h=h, g=g, kb=kb, kw=kw, sb_id=sb_id,
                                  par=par, nkb=nkb, hp=hp):
                            ctx.wait_cp(v, (tag, "smm", b, sb_id))
                            if kb == 0:
                                ctx.wait_cp(v, (tag, "exp", b, g - 2))
                            if kb == nkb - 1:
                                if kw > 128:
                                    ctx.inc(sev, v.tensor_copy(
                                        srow[hps][:, kb * 512: kb * 512 + kw - 128],
                                        psf[par][:, 0:kw - 128]))
                                ctx.inc(sev, v.tensor_tensor(
                                    srow[hps][:, kb * 512 + kw - 128: kb * 512 + kw],
                                    psf[par][:, kw - 128:kw],
                                    ctx.bufs["cmask"][:, :], ALU.add))
                            else:
                                ctx.inc(sev, v.tensor_copy(
                                    srow[hps][:, kb * 512: kb * 512 + kw],
                                    psf[par][:, 0:kw]))
                            ctx.mark((tag, "sfree", b, sb_id), sev)
                            if kb == nkb - 1:
                                ctx.mark((tag, "srow", b, g), sev)
                        blk.vector(sevac)

                    def dmax(v, g=g, klen=klen, hp=hp, hps=hps):
                        ctx.wait_cp(v, (tag, "srow", b, g))
                        ctx.wait_cp(v, (tag, "exp", b, g - 4))
                        ctx.inc(dvx, v.reduce_max(ngm[hp][:, :],
                                                  srow[hps][:, 0:klen], AX.X,
                                                  negate=True))
                        ctx.mark((tag, "max", b, g), dvx)
                    blk.vector(dmax)



                def _fexp(h0):
                    def aexp2(sc, h0=h0):
                        for h in (h0, h0 + 1):
                            g = qi * NH + h
                            hp = h % 4
                            hps = h % 2
                            hp4 = h % 4
                            ctx.wait_cp(sc, (tag, "max", b, g))
                            ctx.wait_cp(sc, (tag, "ptmm", b, g - 4))
                            ctx.inc(axp, sc.activation(pb[hp4][:, 0:klen],
                                                       srow[hps][:, 0:klen],
                                                       AF.Exp,
                                                       bias=ngm[hp][:, 0:1],
                                                       accum_out=sxp[hp4][:, 0:1]))
                            ctx.mark((tag, "exp", b, g), axp)
                    blk.scalar(aexp2)

                def _tail(h):
                    g = qi * NH + h
                    hp = h % 4
                    hps = h % 2
                    hp2 = h % 2
                    hp4 = h % 4
                    hr = 64 * (h % 2)
                    hc = (h // 2) * 128

                    def dscl(v, g=g, klen=klen, hp=hp, hp4=hp4):
                        ctx.wait_cp(v, (tag, "exp", b, g))
                        ctx.inc(dvx, v.reciprocal(rcp[hp][:, :], sxp[hp4][:, :]))
                        ctx.wait(v, dvx)
                        ctx.inc(dvx, v.tensor_scalar(pb[hp4][:, 0:klen],
                                                     pb[hp4][:, 0:klen],
                                                     rcp[hp][:, 0:1], None, ALU.mult))
                        ctx.mark((tag, "pscl", b, g), dvx)
                    blk.vector(dscl)

                    for r0 in range(0, nkt, 8):
                        r1 = min(nkt, r0 + 8)

                        def ptr(te, g=g, r0=r0, r1=r1, hp=hp, hp2=hp2, hp4=hp4):
                            ctx.wait_cp(te, (tag, "pscl", b, g))
                            if r0 == 0:
                                ctx.wait_cp(te, (tag, "ptev", b, g - 2, 0))
                                ctx.wait_cp(te, (tag, "ptev", b, g - 2, 8))
                            else:
                                ctx.wait_cp(te, (tag, "ptev", b, g, 0))
                            last = None
                            for kc in range(r0, r1):
                                last = te.transpose(
                                    psb[hp2][:, (kc % 8) * 128:(kc % 8) * 128 + 128],
                                    pb[hp4][:, kc * 128:(kc + 1) * 128], ident[:, :])
                            ctx.inc(pvm, last)
                            ctx.mark((tag, "ptmm", b, g), pvm)
                            ctx.mark((tag, "ptmm2", b, g, r0), pvm)
                        blk.tensor(ptr)

                        def ptev(v, g=g, r0=r0, r1=r1, hp=hp, hp2=hp2):
                            ctx.wait_cp(v, (tag, "ptmm2", b, g, r0))
                            if r0 == 0:
                                ctx.wait_cp(v, (tag, "pv", b, g - 2))
                            w = (r1 - r0) * 128
                            ctx.inc(sev, v.tensor_copy(
                                ptb[hp2][:, r0 * 128: r0 * 128 + w],
                                psb[hp2][:, 0:w]))
                            ctx.mark((tag, "ptev", b, g, r0), sev)
                            if r1 == nkt if False else True:
                                ctx.mark((tag, "ptevd", b, g), sev)
                        blk.vector(ptev)

                    def pvmm(te, g=g, h=h, nkt=nkt, hp=hp, hp2=hp2):
                        ctx.wait_cp(te, (tag, "ptevd", b, g))
                        if h % 2 == 0:
                            ctx.wait_cp(te, (tag, "yevd", b, g - 2))
                        last = None
                        for kc in range(nkt):
                            last = te.matmul(
                                psf[3][64 * (h % 2): 64 * (h % 2) + 64, 0:128],
                                Vt[:, kc * D + 64 * h: kc * D + 64 * h + 64],
                                ptb[hp2][:, kc * 128:(kc + 1) * 128],
                                start=(kc == 0), stop=(kc == nkt - 1))
                        ctx.inc(pvm, last)
                        ctx.mark((tag, "pv", b, g), pvm)
                    blk.tensor(pvmm)

                    if h % 2 == 1:
                        def yevac(v, g=g, qi=qi, h=h, qp=qp):
                            if h == 1:
                                ctx.wait_cp(v, (tag, "oproj", b, qi - 1))
                            ctx.wait_cp(v, (tag, "pv", b, g - 1))
                            ctx.wait_cp(v, (tag, "pv", b, g))
                            ctx.inc(sev, v.tensor_copy(
                                yfm[qp][:, (h // 2) * 128:(h // 2) * 128 + 128],
                                psf[3][:, 0:128]))
                            ctx.mark((tag, "yevd", b, g), sev)
                            ctx.mark((tag, "yevd", b, g - 1), sev)
                            if h == NH - 1:
                                ctx.mark((tag, "ydone", b, qi), sev)
                        blk.vector(yevac)


                for j in range(NH // 4 + 1):
                    if j >= 1:
                        for hh in range(4 * j - 4, 4 * j):
                            _tail(hh)
                    if j < NH // 4:
                        _front(4 * j)
                        _front(4 * j + 1)
                        _fexp(4 * j)
                        _front(4 * j + 2)
                        _front(4 * j + 3)
                        _fexp(4 * j + 2)

                def xold(sync, qi=qi, qp=qp, q0=q0, b=b):
                    ctx.wait_cp(sync, (tag, "xnfree", b * 16 + qi - 2))
                    ctx.inc(f"Mldx{qp}", sync.dma_start(out=xnt[qp][:, :],
                                                in_=x_in[q0:q0 + 128, :]), 16)
                    ctx.mark((tag, "xold", b, qi), f"Mldx{qp}")
                blk.sync(xold)

                def oproj(te, qi=qi, qp=qp):
                    ctx.wait_cp(te, (tag, "ydone", b, qi))
                    ctx.wait_cp(te, (tag, "ofree", b, qi - 1))
                    for half in range(2):
                        last = None
                        for c in range(DCH):
                            last = te.matmul(
                                psw[:, half * 512:(half + 1) * 512],
                                yfm[qp][:, c * 128:(c + 1) * 128],
                                wo[:, c * D + half * 512: c * D + (half + 1) * 512],
                                start=(c == 0), stop=(c == DCH - 1))
                        ctx.inc(pop, last)
                    ctx.mark((tag, "oproj", b, qi), pop)
                    ctx.mark((tag, "qfree", b, qi), pop)
                blk.tensor(oproj)

                def resid(v, qi=qi, qp=qp, b=b):
                    ctx.wait_cp(v, (tag, "oproj", b, qi))
                    ctx.wait_cp(v, (tag, "xold", b, qi))
                    ctx.wait_cp(v, (tag, "xnfree", b * 16 + qi - 2))
                    ctx.inc(res, v.tensor_tensor(xnt[qp][:, :], psw[:, :],
                                                 xnt[qp][:, :], ALU.add))
                    ctx.mark((tag, "resid", b, qi), res)
                    ctx.mark((tag, "ofree", b, qi), res)
                blk.vector(resid)

                def store(gp, qi=qi, qp=qp, q0=q0, b=b):
                    ctx.wait_cp(gp, (tag, "resid", b, qi))
                    ctx.inc(f"Mst{qp}", gp.dma_start(out=x_out[q0:q0 + 128, :],
                                              in_=xnt[qp][:, :]), 16)
                    ctx.mark((tag, "xnfree", b * 16 + qi), f"Mst{qp}")
                blk.gpsimd(store)




HMOE = 2048
HCH = HMOE // 128
MOE_DEBUG = None

def emit_moe(ctx, x_in, x_out, wg_d, w1_d, b1_d, w2_d, b2_d, tag):
    nc = ctx.nc
    psf, psw, psb = ctx.ps_pool()
    ST = 1024
    NST = NTOK // ST

    xnf = [ctx.ssb(f"moex{p}", (128, DCH * ST), BF16) for p in range(2)]
    yac = ctx.ssb("moey", (128, ST * 8), F32)
    hfm = [ctx.ssb(f"moeh{p}", (128, HCH * 512), BF16) for p in range(2)]
    w1b = ctx.ssb("moew1", (128, DCH * HMOE), BF16)
    w2b = ctx.ssb("moew2", (128, HCH * D), BF16)
    wgb = ctx.ssb("moewg", (128, DCH * 8), BF16)
    b1r = ctx.ssb("moeb1", (1, HMOE), BF16)
    b2r = ctx.ssb("moeb2", (1, D), BF16)
    ones = ctx.ssb("moeones", (1, 512), BF16)
    gates = [ctx.ssb(f"moeg{p}", (128, 64), F32) for p in range(2)]
    ge8 = ctx.ssb("moege", (128, 8), F32)
    gm8 = ctx.ssb("moegm", (128, 1), F32)
    gsum = ctx.ssb("moegs", (128, 1), F32)
    grc = ctx.ssb("moegrc", (128, 1), F32)
    gt1 = ctx.ssb("moegt1", (128, 1), F32)
    gt2 = ctx.ssb("moegt2", (128, 1), F32)
    gmk = ctx.ssb("moegmk", (128, 8), F32)
    ge2 = ctx.ssb("moege2", (128, 8), F32)

    wl, pmm, rl, yev, sto, gdv, gac = (f"O{n}" for n in
                                       ("wl", "pmm", "rl", "yev", "sto", "gdv", "gac"))

    with nc.Block() as blk:
        def g0(gp):
            ctx.inc("sv", gp.memset(ones[:, :], 1.0))
            for c in range(DCH):
                ctx.inc(wl, gp.dma_start(out=wgb[:, c * 8:(c + 1) * 8],
                                         in_=wg_d[c * 128:(c + 1) * 128, :]), 16)
            ctx.wait(gp, wl)
        blk.gpsimd(g0)

    def emit_ln_u(u, blk, sub_range=None):
        emit_ln_to_fm(ctx, blk, x_in, u * ST, ST, xnf[u % 2], ST, 0,
                      (tag, "ln", u), sub_range=sub_range)

    def emit_gate_u(u, blk):
        for s in range(8):
            def gmm(te, s=s, u=u):
                ctx.wait_cp(te, (tag, "gex", u, s - 2))
                ctx.wait_cp(te, ((tag, "ln", u), "done"))
                ctx.wait_cp(te, (tag, "relu", u - 1, 7, 1, 14 + (s % 2)))
                last = None
                for c in range(DCH):
                    last = te.matmul(
                        psf[s % 2][:, 0:8],
                        xnf[u % 2][:, c * ST + s * 128: c * ST + s * 128 + 128],
                        wgb[:, c * 8:(c + 1) * 8],
                        start=(c == 0), stop=(c == DCH - 1))
                ctx.inc(pmm, last)
                ctx.mark((tag, "gmm", u, s), pmm)
            blk.tensor(gmm)

            def gsm(v, s=s, u=u):
                ctx.wait_cp(v, (tag, "gmm", u, s))
                ctx.wait_cp(v, (tag, "gex", u, s - 1))
                sl = psf[s % 2][:, 0:8]
                ctx.inc(gdv, v.reduce_max(gm8[:, :], sl, AX.X, negate=True))
                ctx.mark((tag, "gng", u, s), gdv)
            blk.vector(gsm)

            def gex(sc, s=s, u=u):
                ctx.wait_cp(sc, (tag, "gng", u, s))
                ctx.wait_cp(sc, (tag, "gdone", u, s - 1))
                ctx.inc(gac, sc.activation(ge8[:, :], psf[s % 2][:, 0:8],
                                           AF.Exp, bias=gm8[:, 0:1],
                                           accum_out=gsum[:, 0:1]))
                ctx.mark((tag, "gex", u, s), gac)
            blk.scalar(gex)

            def gtop(v, s=s, u=u):
                ctx.wait_cp(v, (tag, "gex", u, s))
                ctx.inc(gdv, v.reciprocal(grc[:, :], gsum[:, :]))
                ctx.inc(gdv, v.reduce_max(gt1[:, :], ge8[:, :], AX.X))
                ctx.wait(v, gdv)
                ctx.inc(gdv, v.tensor_scalar(gmk[:, :], ge8[:, :], gt1[:, 0:1],
                                             None, ALU.is_ge))
                ctx.wait(v, gdv)
                ctx.inc(gdv, v.scalar_tensor_tensor(ge2[:, :], gmk[:, :], -1e30,
                                                    ge8[:, :], ALU.mult, ALU.add))
                ctx.wait(v, gdv)
                ctx.inc(gdv, v.reduce_max(gt2[:, :], ge2[:, :], AX.X))
                ctx.wait(v, gdv)
                ctx.inc(gdv, v.tensor_scalar(gmk[:, :], ge8[:, :], gt2[:, 0:1],
                                             None, ALU.is_ge))
                ctx.wait(v, gdv)
                ctx.inc(gdv, v.tensor_tensor(ge2[:, :], ge8[:, :], gmk[:, :],
                                             ALU.mult))
                ctx.wait(v, gdv)
                ctx.inc(gdv, v.tensor_scalar(gates[u % 2][:, s * 8:(s + 1) * 8],
                                             ge2[:, :], grc[:, 0:1],
                                             None, ALU.mult))
                ctx.mark((tag, "gdone", u, s), gdv)
            blk.vector(gtop)

    with nc.Block() as blk:
        emit_ln_u(0, blk)
    with nc.Block() as blk:
        emit_gate_u(0, blk)
        def w1pre(gp):
            for c in range(DCH):
                ctx.inc("Ow1", gp.dma_start(
                    out=w1b[:, c * HMOE:(c + 1) * HMOE],
                    in_=w1_d[0, c * 128:(c + 1) * 128, :]), 16)
            ctx.inc("Ow1", gp.dma_start(out=b1r[:, :], in_=b1_d[0:1, :]), 16)
            ctx.mark((tag, "w1L", 0, 0), "Ow1")
        blk.gpsimd(w1pre)

    for u in range(NST):
        row0 = u * ST
        with nc.Block() as blk:
            for s in range(8):
                def yinit(sync, s=s, u=u):
                    ctx.wait_cp(sync, (tag, "ystored", u - 1))
                    ctx.inc(sto, sync.dma_start(
                        out=yac[:, s * D:(s + 1) * D],
                        in_=x_in[row0 + s * 128: row0 + (s + 1) * 128, :]), 16)
                    if s == 7:
                        ctx.mark((tag, "yinit", u), sto)
                blk.sync(yinit)

            n_e = 0 if MOE_DEBUG in ("gate", "ln", "noln") else (1 if MOE_DEBUG == "ne1" else 8)
            def w1load_fn(gp, e, u, ue, uw):
                    ctx.wait_cp(gp, (tag, "w1free", uw, (e - 1) % 8))
                    for c in range(DCH):
                        ctx.inc("Ow1", gp.dma_start(
                            out=w1b[:, c * HMOE:(c + 1) * HMOE],
                            in_=w1_d[e, c * 128:(c + 1) * 128, :]), 16)
                    ctx.inc("Ow1", gp.dma_start(out=b1r[:, :], in_=b1_d[e:e + 1, :]), 16)
                    ctx.mark((tag, "w1L", ue, e), "Ow1")

            for e in range(n_e):
                if e > 0 or (u == 0 and n_e != 8):
                    blk.gpsimd(lambda gp, e=e, u=u: w1load_fn(gp, e, u, u, u))

                for half in range(2):
                    hp = half
                    for ho in range(HCH):
                        par = ho % 2

                        def w1mm(te, e=e, half=half, ho=ho, par=par, hp=hp, u=u):
                            ctx.wait_cp(te, (tag, "w1L", u, e))
                            if e == 0 and half == 0 and ho == 0:
                                ctx.wait_cp(te, ((tag, "ln", u), "done"))
                            if ho < 2:
                                pe_, ph_ = (e, half - 1) if half > 0 else (e - 1, 1)
                                ctx.wait_cp(te, (tag, "relu", u, pe_, ph_,
                                                 HCH - 2 + ho))
                            else:
                                ctx.wait_cp(te, (tag, "relu", u, e, half, ho - 2))
                            last = None
                            for c in range(DCH):
                                last = te.matmul(
                                    psf[par][:, :],
                                    w1b[:, c * HMOE + ho * 128: c * HMOE + (ho + 1) * 128],
                                    xnf[u % 2][:, c * ST + half * 512: c * ST + (half + 1) * 512],
                                    start=(c == 0), stop=False)
                            last = te.matmul(
                                psf[par][:, :],
                                b1r[0:1, ho * 128:(ho + 1) * 128],
                                ones[0:1, :],
                                start=False, stop=True)
                            ctx.inc(pmm, last)
                            ctx.mark((tag, "w1mm", u, e, half, ho), pmm)
                            if half == 1 and ho == HCH - 1:
                                ctx.mark((tag, "w1free", u, e), pmm)
                        blk.tensor(w1mm)

                        def relu(v, e=e, half=half, ho=ho, par=par, hp=hp, u=u):
                            ctx.wait_cp(v, (tag, "w1mm", u, e, half, ho))
                            ctx.inc(rl, v.tensor_scalar_max(
                                hfm[hp][:, ho * 512:(ho + 1) * 512],
                                psf[par][:, :], 0.0))
                            ctx.mark((tag, "relu", u, e, half, ho), rl)
                        blk.vector(relu)

                    if half == 0:
                        def w2load(gp, e=e, u=u):
                            ctx.wait_cp(gp, (tag, "w2free", u, e - 1))
                            for hc in range(HCH):
                                ctx.inc("Ow2", gp.dma_start(
                                    out=w2b[:, hc * D:(hc + 1) * D],
                                    in_=w2_d[e, hc * 128:(hc + 1) * 128, :]), 16)
                            ctx.inc("Ow2", gp.dma_start(out=b2r[:, :],
                                                     in_=b2_d[e:e + 1, :]), 16)
                            ctx.mark((tag, "w2L", u, e), "Ow2")
                        blk.gpsimd(w2load)

                    for tl in (range(0) if MOE_DEBUG == "w1" else range(4)):
                        s = half * 4 + tl

                        def w2mm(te, e=e, half=half, tl=tl, s=s, hp=hp, u=u):
                            ctx.wait_cp(te, (tag, "relu", u, e, half, HCH - 1))
                            ctx.wait_cp(te, (tag, "w2L", u, e))
                            ctx.wait_cp(te, (tag, "yev", u, e, s - 2))
                            wt = ([psw[:, 0:512], psw[:, 512:1024]] if s % 2 == 0
                                  else [psf[2][:, :], psf[3][:, :]])
                            for hc in range(HCH):
                                for dh in range(2):
                                    te.matmul(
                                        wt[dh],
                                        hfm[hp][:, hc * 512 + tl * 128:
                                                 hc * 512 + (tl + 1) * 128],
                                        w2b[:, hc * D + dh * 512: hc * D + (dh + 1) * 512],
                                        start=(hc == 0), stop=False)
                            last = None
                            for dh in range(2):
                                last = te.matmul(
                                    wt[dh],
                                    ones[0:1, 0:128],
                                    b2r[0:1, dh * 512:(dh + 1) * 512],
                                    start=False, stop=True)
                            ctx.inc(pmm, last)
                            ctx.mark((tag, "w2mm", u, e, s), pmm)
                            if s == 7:
                                ctx.mark((tag, "w2free", u, e), pmm)
                        blk.tensor(w2mm)

                        def w2ev(v, e=e, s=s, u=u):
                            ctx.wait_cp(v, (tag, "w2mm", u, e, s))
                            if e == 0:
                                ctx.wait_cp(v, (tag, "yinit", u))
                                ctx.wait_cp(v, (tag, "gdone", u, 7))
                            gcol = gates[u % 2][:, s * 8 + e: s * 8 + e + 1]
                            if s % 2 == 0:
                                ctx.inc(yev, v.scalar_tensor_tensor(
                                    yac[:, s * D:(s + 1) * D], psw[:, :], gcol,
                                    yac[:, s * D:(s + 1) * D], ALU.mult, ALU.add))
                            else:
                                for dh in range(2):
                                    c0 = s * D + dh * 512
                                    ctx.inc(yev, v.scalar_tensor_tensor(
                                        yac[:, c0:c0 + 512], psf[2 + dh][:, :],
                                        gcol, yac[:, c0:c0 + 512],
                                        ALU.mult, ALU.add))
                            ctx.mark((tag, "yev", u, e, s), yev)
                        blk.vector(w2ev)

                if u + 1 < NST and n_e == 8:
                    emit_ln_u(u + 1, blk, sub_range=(e, e + 1))
                if e == 7 and n_e == 8:
                    un = u + 1 if u + 1 < NST else None
                    if un is not None:
                        blk.gpsimd(lambda gp, u=u, un=un:
                                   w1load_fn(gp, 0, u, un, u))

            if u + 1 < NST and n_e != 8:
                emit_ln_u(u + 1, blk)

            if u + 1 < NST:
                emit_gate_u(u + 1, blk)

            for s in range(8):
                def yst(sync, s=s, u=u):
                    if n_e == 0:
                        ctx.wait_cp(sync, (tag, "yinit", u))
                    else:
                        ctx.wait_cp(sync, (tag, "yev", u, n_e - 1, s))
                    i = sync.dma_start(out=x_out[row0 + s * 128: row0 + (s + 1) * 128, :],
                                       in_=yac[:, s * D:(s + 1) * D])
                    ctx.inc("Yst", i, 16)
                    if s == 7:
                        ctx.mark((tag, "ystored", u), "Yst")
                blk.sync(yst)


def moe_host_weights(Wg, W1, b1, W2, b2, prefix):
    return {
        f"{prefix}wg": _np(Wg),
        f"{prefix}w1": _np(W1),
        f"{prefix}b1": _np(b1),
        f"{prefix}w2": _np(W2),
        f"{prefix}b2": _np(b2),
    }


def build_moe_test():
    es = ExitStack()
    nc = bass.Bass("TRN2", target_bir_lowering=False)
    x_in = nc.dram_tensor("x_in", [NTOK, D], F32, kind="ExternalInput").ap()
    wg = nc.dram_tensor("wg", [D, 8], F32, kind="ExternalInput").ap()
    w1 = nc.dram_tensor("w1", [8, D, HMOE], F32, kind="ExternalInput").ap()
    b1 = nc.dram_tensor("b1", [8, HMOE], F32, kind="ExternalInput").ap()
    w2 = nc.dram_tensor("w2", [8, HMOE, D], F32, kind="ExternalInput").ap()
    b2 = nc.dram_tensor("b2", [8, D], F32, kind="ExternalInput").ap()
    x_out = nc.dram_tensor("x_out", [NTOK, D], F32, kind="ExternalOutput").ap()
    ctx = Ctx(nc, es)
    ctx.ps_pool()
    emit_idents(ctx)
    ctx.open_stage()
    emit_moe(ctx, x_in, x_out, wg, w1, b1, w2, b2, "mo")
    ctx.close_stage()
    emit_drain_outputs(ctx)
    es.close()
    return nc



def emit_tmix(ctx, x_in, x_out, kwT_d, vwT_d, owT_d, tsg_d, lnw_d, lnb_d,
              td_d, tf_d, tag):
    nc = ctx.nc
    psf, psw, psb = ctx.ps_pool()
    ident = ctx.bufs["ident_bf"]
    identf = ctx.bufs["ident_f32"]
    N = 64

    G = ctx.ssb("txG", (128, D), F32)
    OMG = ctx.ssb("txOMG", (128, D), F32)
    kTf = ctx.ssb("txkTf", (128, DCH * N), F32)
    vTf = ctx.ssb("txvTf", (128, DCH * N), F32)
    kTs = ctx.ssb("txkTs", (128, DCH * N), BF16)
    vTs = ctx.ssb("txvTs", (128, DCH * N), BF16)
    kTb = ctx.ssb("txkTb", (128, DCH * N), BF16)
    vTb = ctx.ssb("txvTb", (128, DCH * N), BF16)
    lnw = ctx.ssb("txlnw", (128, DCH), F32)
    lnbb = ctx.ssb("txlnb", (128, DCH), BF16)
    oT = ctx.ssb("txoT", (64, D), BF16)
    tsg = ctx.ssb("txtsg", (1, D), F32)
    grow = ctx.ssb("txgrow", (1, D), F32)
    onesf = ctx.ssb("txonesf", (1, 128), F32)
    tdc = ctx.ssb("txtd", (64, 1), F32)
    tfc = ctx.ssb("txtf", (64, 1), F32)
    wcol = ctx.ssb("txw", (64, 1), F32)
    netf = ctx.ssb("txnetf", (64, 1), F32)
    kbias = ctx.ssb("txkb", (64, 1), F32)
    vbias = ctx.ssb("txvb", (64, 1), F32)
    kbr = ctx.ssb("txkbr", (1, 128), F32)
    xt = [ctx.ssb(f"txxt{p}", (128, D), F32) for p in range(2)]
    xr = [ctx.ssb(f"txxr{p}", (128, D), F32) for p in range(2)]
    aa = [ctx.ssb(f"txaa{p}", (128, D), F32) for p in range(2)]
    ar = [ctx.ssb(f"txar{p}", (128, D), F32) for p in range(2)]
    xs = [ctx.ssb(f"txxs{p}", (128, D), F32) for p in range(2)]
    sta = [ctx.ssb(f"txsa{p}", (128, 12), F32) for p in range(2)]
    str_ = [ctx.ssb(f"txsr{p}", (128, 12), F32) for p in range(2)]
    sts = [ctx.ssb(f"txss{p}", (128, 12), F32) for p in range(2)]
    mva = [ctx.ssb(f"txma{p}", (128, 2), F32) for p in range(2)]
    mvr = [ctx.ssb(f"txmr{p}", (128, 2), F32) for p in range(2)]
    mvs = [ctx.ssb(f"txms{p}", (128, 2), F32) for p in range(2)]
    rsa = [ctx.ssb(f"txra{p}", (128, 1), F32) for p in range(2)]
    rsr = [ctx.ssb(f"txrr{p}", (128, 1), F32) for p in range(2)]
    rss = [ctx.ssb(f"txrz{p}", (128, 1), F32) for p in range(2)]
    nma = [ctx.ssb(f"txna{p}", (128, 1), F32) for p in range(2)]
    nmr = [ctx.ssb(f"txnr{p}", (128, 1), F32) for p in range(2)]
    nms = [ctx.ssb(f"txnz{p}", (128, 1), F32) for p in range(2)]
    t1m = [ctx.ssb(f"txt1{p}", (128, D), F32) for p in range(2)]
    xn2 = [ctx.ssb(f"txn2{p}", (128, D), BF16) for p in range(2)]
    xn2f = ctx.ssb("txn2f", (128, DCH * 512), BF16)
    kbuf = ctx.ssb("txkv", (64, 512), F32)
    vbuf = ctx.ssb("txvv", (64, 512), F32)
    ekb = ctx.ssb("txek", (64, 512), F32)
    WKV = ctx.ssb("txwkv", (64, T), F32)
    states = ctx.ssb("txsts2", (64, T), F32)
    stb = ctx.ssb("txstb", (64, T), BF16)
    xot = [ctx.ssb(f"txxo{p}", (128, D), F32) for p in range(2)]
    xnt = [ctx.ssb(f"txxn{p}", (128, D), F32) for p in range(2)]

    dv, ac, pe, ev = "Tdv", "Tac", "Tpe", "Tev"

    with nc.Block() as blk:
        def wload(gp):
            for (dst, srcd) in ((kTf, kwT_d), (vTf, vwT_d)):
                for c in range(DCH):
                    ctx.inc("Twl", gp.dma_start(
                        out=dst[:, c * N:(c + 1) * N],
                        in_=srcd[c * 128:(c + 1) * 128, :]), 16)
            for (dst, srcd) in ((kTb, kwT_d), (vTb, vwT_d)):
                for c in range(DCH):
                    ctx.inc("Twl", gp.dma_start(
                        out=dst[:, c * N:(c + 1) * N],
                        in_=srcd[c * 128:(c + 1) * 128, :]), 16)
            ctx.inc("Twl", gp.dma_start(out=oT[:, :], in_=owT_d[:, :]), 16)
            ctx.inc("Twl", gp.dma_start(out=tsg[:, :], in_=tsg_d[:, :]), 16)
            ctx.inc("Twl", gp.dma_start(out=lnw[:, :], in_=lnw_d[:, :]), 16)
            ctx.inc("Twl", gp.dma_start(out=lnbb[:, :], in_=lnb_d[:, :]), 16)
            ctx.inc("Twl", gp.dma_start(out=tdc[:, :], in_=td_d[:, :]), 16)
            ctx.inc("Twl", gp.dma_start(out=tfc[:, :], in_=tf_d[:, :]), 16)
            ctx.wait(gp, "Twl")
            ctx.inc("sv", gp.memset(onesf[:, :], 1.0))
            ctx.wait(gp, "sv")
        blk.gpsimd(wload)

    with nc.Block() as blk:
        def setup_act(sc):
            ctx.inc(ac, sc.activation(grow[:, :], tsg[:, :], AF.Sigmoid))
            ctx.inc(ac, sc.activation(wcol[:, :], tdc[:, :], AF.Exp))
            ctx.inc(ac, sc.activation(netf[:, :], tfc[:, :], AF.Exp))
            ctx.mark((tag, "sact"), ac)
        blk.scalar(setup_act)

        def setup_dv(v):
            ctx.wait_cp(v, (tag, "sact"))
            ctx.inc(dv, v.tensor_scalar(netf[:, :], netf[:, :], -1.0, None,
                                        ALU.mult))
            for c in range(DCH):
                ctx.inc(dv, v.tensor_scalar(kTs[:, c * N:(c + 1) * N],
                                            kTf[:, c * N:(c + 1) * N],
                                            lnw[:, c:c + 1], None, ALU.mult))
                ctx.inc(dv, v.tensor_scalar(vTs[:, c * N:(c + 1) * N],
                                            vTf[:, c * N:(c + 1) * N],
                                            lnw[:, c:c + 1], None, ALU.mult))
            ctx.mark((tag, "sdv"), dv)
        blk.vector(setup_dv)

        def setup_pe(te):
            ctx.wait_cp(te, (tag, "sdv"))
            for half in range(2):
                last = te.matmul(psw[:, half * 512:(half + 1) * 512],
                                 onesf[0:1, :],
                                 grow[0:1, half * 512:(half + 1) * 512],
                                 start=True, stop=True)
                ctx.inc(pe, last)
            for c in range(DCH):
                te.matmul(psf[0][0:1, 0:N], lnbb[:, c:c + 1],
                          kTb[:, c * N:(c + 1) * N],
                          start=(c == 0), stop=(c == DCH - 1))
            for c in range(DCH):
                last = te.matmul(psf[0][0:1, N:2 * N], lnbb[:, c:c + 1],
                                 vTb[:, c * N:(c + 1) * N],
                                 start=(c == 0), stop=(c == DCH - 1))
            ctx.inc(pe, last)
            ctx.mark((tag, "spe"), pe)
        blk.tensor(setup_pe)

        def setup_ev(v):
            ctx.wait_cp(v, (tag, "spe"))
            ctx.inc(ev, v.tensor_copy(G[:, :], psw[:, :]))
            ctx.inc(ev, v.tensor_copy(kbr[:, :], psf[0][0:1, 0:128]))
            ctx.wait(v, ev)
            ctx.inc(ev, v.tensor_scalar(OMG[:, :], G[:, :], -1.0, 1.0,
                                        ALU.mult, ALU.add))
            ctx.mark((tag, "sev"), ev)
        blk.vector(setup_ev)

        def setup_pe2(te):
            ctx.wait_cp(te, (tag, "sev"))
            ctx.inc(pe, te.transpose(psf[1][:, 0:1], kbr[:, :], identf[0:1, 0:1]))
            ctx.mark((tag, "spe2"), pe)
        blk.tensor(setup_pe2)

        def setup_ev2(v):
            ctx.wait_cp(v, (tag, "spe2"))
            ctx.inc(ev, v.tensor_copy(kbias[:, :], psf[1][0:N, 0:1]))
            ctx.inc(ev, v.tensor_copy(vbias[:, :], psf[1][N:2 * N, 0:1]))
        blk.vector(setup_ev2)

    for b in range(BL):
        row0 = b * T
        with nc.Block() as blk:
            for grp in range(4):
                for sl in range(4):
                    s = grp * 4 + sl
                    p = s % 2

                    def dma_in(sync, s=s, p=p):
                        ctx.wait_cp(sync, (tag, "xtfree", b, s - 2))
                        r0 = row0 + s * 128
                        rr = row0 + ((s * 128 + 1024) % T)
                        ctx.inc(f"Tld{p}", sync.dma_start(
                            out=xt[p][:, :], in_=x_in[r0:r0 + 128, :]), 16)
                        ctx.inc(f"Tld{p}", sync.dma_start(
                            out=xr[p][:, :], in_=x_in[rr:rr + 128, :]), 16)
                        ctx.mark((tag, "xload", b, s), f"Tld{p}")
                    blk.sync(dma_in)

                    def bn_ar(v, s=s, p=p):
                        ctx.wait_cp(v, (tag, "xload", b, s))
                        for (src, stt, mvv, i) in ((xt[p], sta[p], mva[p], 0),
                                                   (xr[p], str_[p], mvr[p], 1)):
                            ctx.inc(dv, v.bn_stats(stt[:, 0:6], src[:, 0:512]))
                            ctx.inc(dv, v.bn_stats(stt[:, 6:12], src[:, 512:1024]))
                            ctx.wait(v, dv)
                            ctx.inc(dv, v.bn_aggr(mvv[:, :], stt[:, :]))
                            ctx.mark((tag, "agg", b, s, i), dv)
                    blk.vector(bn_ar)

                    def rstds(sc, s=s, p=p):
                        for (mvv, rsv, i) in ((mva[p], rsa[p], 0),
                                              (mvr[p], rsr[p], 1)):
                            ctx.wait_cp(sc, (tag, "agg", b, s, i))
                            ctx.inc(ac, sc.activation(rsv[:, :], mvv[:, 1:2],
                                                      AF.Ln, bias=ctx.bufs["eps"][:, 0:1]))
                            ctx.wait(sc, ac)
                            ctx.inc(ac, sc.activation(rsv[:, :], rsv[:, :],
                                                      AF.Exp, scale=-0.5))
                            ctx.mark((tag, "rstd", b, s, i), ac)
                    blk.scalar(rstds)

                    def nm_dv(v, s=s, p=p):
                        for (mvv, rsv, nmv, i) in ((mva[p], rsa[p], nma[p], 0),
                                                   (mvr[p], rsr[p], nmr[p], 1)):
                            ctx.wait_cp(v, (tag, "rstd", b, s, i))
                            ctx.inc(dv, v.scalar_tensor_tensor(
                                nmv[:, :], mvv[:, 0:1], -1.0, rsv[:, :],
                                ALU.mult, ALU.mult))
                        ctx.mark((tag, "nm", b, s), dv)
                    blk.vector(nm_dv)

                    def norms(sc, s=s, p=p):
                        ctx.wait_cp(sc, (tag, "nm", b, s))
                        ctx.wait_cp(sc, (tag, "aafree", b, s - 2))
                        ctx.inc(ac, sc.activation(aa[p][:, :], xt[p][:, :],
                                                  AF.Identity, bias=nma[p][:, 0:1],
                                                  scale=rsa[p][:, 0:1]))
                        ctx.inc(ac, sc.activation(ar[p][:, :], xr[p][:, :],
                                                  AF.Identity, bias=nmr[p][:, 0:1],
                                                  scale=rsr[p][:, 0:1]))
                        ctx.mark((tag, "norm", b, s), ac)
                        ctx.mark((tag, "xtfree", b, s), ac)
                    blk.scalar(norms)

                    def mix(v, s=s, p=p):
                        ctx.wait_cp(v, (tag, "norm", b, s))
                        ctx.wait_cp(v, (tag, "xsfree", b, s - 2))
                        ctx.inc(dv, v.tensor_tensor(t1m[p][:, :], ar[p][:, :],
                                                    G[:, :], ALU.mult))
                        ctx.inc(dv, v.tensor_tensor(xs[p][:, :], aa[p][:, :],
                                                    OMG[:, :], ALU.mult))
                        ctx.wait(v, dv)
                        ctx.inc(dv, v.tensor_tensor(xs[p][:, :], xs[p][:, :],
                                                    t1m[p][:, :], ALU.add))
                        ctx.mark((tag, "mix", b, s), dv)
                        ctx.mark((tag, "aafree", b, s), dv)
                        ctx.wait(v, dv)
                        ctx.inc(dv, v.bn_stats(sts[p][:, 0:6], xs[p][:, 0:512]))
                        ctx.inc(dv, v.bn_stats(sts[p][:, 6:12], xs[p][:, 512:1024]))
                        ctx.wait(v, dv)
                        ctx.inc(dv, v.bn_aggr(mvs[p][:, :], sts[p][:, :]))
                        ctx.mark((tag, "agg2", b, s), dv)
                    blk.vector(mix)

                    def rstd2(sc, s=s, p=p):
                        ctx.wait_cp(sc, (tag, "agg2", b, s))
                        ctx.inc(ac, sc.activation(rss[p][:, :], mvs[p][:, 1:2],
                                                  AF.Ln, bias=ctx.bufs["eps"][:, 0:1]))
                        ctx.wait(sc, ac)
                        ctx.inc(ac, sc.activation(rss[p][:, :], rss[p][:, :],
                                                  AF.Exp, scale=-0.5))
                        ctx.mark((tag, "rstd2", b, s), ac)
                    blk.scalar(rstd2)

                    def nm2_dv(v, s=s, p=p):
                        ctx.wait_cp(v, (tag, "rstd2", b, s))
                        ctx.inc(dv, v.scalar_tensor_tensor(
                            nms[p][:, :], mvs[p][:, 0:1], -1.0, rss[p][:, :],
                            ALU.mult, ALU.mult))
                        ctx.mark((tag, "nm2", b, s), dv)
                    blk.vector(nm2_dv)

                    def norm2(sc, s=s, p=p):
                        ctx.wait_cp(sc, (tag, "nm2", b, s))
                        ctx.wait_cp(sc, (tag, "xn2free", b, s - 2))
                        ctx.inc(ac, sc.activation(xn2[p][:, :], xs[p][:, :],
                                                  AF.Identity, bias=nms[p][:, 0:1],
                                                  scale=rss[p][:, 0:1]))
                        ctx.mark((tag, "norm2", b, s), ac)
                        ctx.mark((tag, "xsfree", b, s), ac)
                    blk.scalar(norm2)

                    def tpose(te, s=s, p=p):
                        ctx.wait_cp(te, (tag, "norm2", b, s))
                        ctx.wait_cp(te, (tag, "tpfree", b, s - 2))
                        last = None
                        for c in range(DCH):
                            last = te.transpose(psb[p][:, c * 128:(c + 1) * 128],
                                                xn2[p][:, c * 128:(c + 1) * 128],
                                                ident[:, :])
                        ctx.inc(pe, last)
                        ctx.mark((tag, "tp", b, s), pe)
                        ctx.mark((tag, "xn2free", b, s), pe)
                    blk.tensor(tpose)

                    def tevac(v, s=s, p=p, sl=sl, grp=grp):
                        ctx.wait_cp(v, (tag, "tp", b, s))
                        if sl == 0:
                            ctx.wait_cp(v, (tag, "kvmm", b, grp - 1))
                        dst = xn2f[:, :].rearrange("q (c u) -> q c u", c=DCH)[
                            :, :, sl * 128:(sl + 1) * 128]
                        src = psb[p][:, :].rearrange("q (c u) -> q c u", c=DCH)
                        ctx.inc(ev, v.tensor_copy(dst, src))
                        ctx.mark((tag, "tpfree", b, s), ev)
                        ctx.mark((tag, "xn2f", b, s), ev)
                    blk.vector(tevac)

                def kvmm(te, grp=grp, b=b):
                    ctx.wait_cp(te, (tag, "xn2f", b, grp * 4 + 3))
                    ctx.wait_cp(te, (tag, "kvev", b, grp - 1))
                    for (dstps, wT) in ((psf[0], kTs), (psf[1], vTs)):
                        last = None
                        for c in range(DCH):
                            last = te.matmul(
                                dstps[0:N, :], wT[:, c * N:(c + 1) * N],
                                xn2f[:, c * 512:(c + 1) * 512],
                                start=(c == 0), stop=(c == DCH - 1))
                        ctx.inc(pe, last)
                    ctx.mark((tag, "kvmm", b, grp), pe)
                blk.tensor(kvmm)

                def kvev(v, grp=grp, b=b):
                    ctx.wait_cp(v, (tag, "kvmm", b, grp))
                    ctx.inc(dv, v.tensor_scalar(kbuf[:, :], psf[0][0:N, :],
                                                kbias[:, 0:1], None, ALU.add))
                    ctx.inc(dv, v.tensor_scalar(vbuf[:, :], psf[1][0:N, :],
                                                vbias[:, 0:1], None, ALU.add))
                    ctx.mark((tag, "kvev", b, grp), dv)
                blk.vector(kvev)

                def ekact(sc, grp=grp, b=b):
                    ctx.wait_cp(sc, (tag, "kvev", b, grp))
                    ctx.wait_cp(sc, (tag, "wkvd", b, grp - 1))
                    ctx.inc(ac, sc.activation(ekb[:, :], kbuf[:, :], AF.Exp,
                                              scale=netf[:, 0:1]))
                    ctx.mark((tag, "ek", b, grp), ac)
                blk.scalar(ekact)

                def wkvmul(v, grp=grp, b=b):
                    ctx.wait_cp(v, (tag, "ek", b, grp))
                    ctx.inc(dv, v.tensor_tensor(WKV[:, grp * 512:(grp + 1) * 512],
                                                ekb[:, :], vbuf[:, :], ALU.mult))
                    ctx.mark((tag, "wkv", b, grp), dv)
                    ctx.mark((tag, "wkvd", b, grp), dv)
                blk.vector(wkvmul)

            def scan(v, b=b):
                ctx.wait_cp(v, (tag, "wkv", b, 3))
                w_b = bass.AP(wcol, 0, [[1, N], [0, T]])
                ctx.inc(dv, v.tensor_tensor_scan(states[:, :], w_b, WKV[:, :],
                                                 0.0, ALU.mult, ALU.add))
                ctx.wait(v, dv)
                ctx.inc(dv, v.tensor_copy(stb[:, :], states[:, :]))
                ctx.mark((tag, "scan", b), dv)
            blk.vector(scan)

            for s in range(16):
                p = s % 2

                def xold(sync, s=s, p=p):
                    ctx.wait_cp(sync, (tag, "xofree", b, s - 2))
                    ctx.inc(f"Tlo{p}", sync.dma_start(
                        out=xot[p][:, :],
                        in_=x_in[row0 + s * 128: row0 + (s + 1) * 128, :]), 16)
                    ctx.mark((tag, "xold", b, s), f"Tlo{p}")
                blk.sync(xold)

                def oproj(te, s=s, p=p):
                    ctx.wait_cp(te, (tag, "scan", b))
                    ctx.wait_cp(te, (tag, "resid", b, s - 1))
                    last = None
                    for half in range(2):
                        last = te.matmul(psw[:, half * 512:(half + 1) * 512],
                                         stb[:, s * 128:(s + 1) * 128],
                                         oT[:, half * 512:(half + 1) * 512],
                                         start=True, stop=True)
                        ctx.inc(pe, last)
                    ctx.mark((tag, "omm", b, s), pe)
                blk.tensor(oproj)

                def resid(v, s=s, p=p, b=b):
                    ctx.wait_cp(v, (tag, "omm", b, s))
                    ctx.wait_cp(v, (tag, "xold", b, s))
                    ctx.wait_cp(v, (tag, "xnfree", b * 16 + s - 2))
                    ctx.inc(dv, v.tensor_tensor(xnt[p][:, :], psw[:, :],
                                                xot[p][:, :], ALU.add))
                    ctx.mark((tag, "resid", b, s), dv)
                    ctx.mark((tag, "xofree", b, s), dv)
                blk.vector(resid)

                def store(gp, s=s, p=p, b=b):
                    ctx.wait_cp(gp, (tag, "resid", b, s))
                    ctx.inc(f"Tst{p}", gp.dma_start(
                        out=x_out[row0 + s * 128: row0 + (s + 1) * 128, :],
                        in_=xnt[p][:, :]), 16)
                    ctx.mark((tag, "xnfree", b * 16 + s), f"Tst{p}")
                blk.gpsimd(store)


def build_tmix_test():
    es = ExitStack()
    nc = bass.Bass("TRN2", target_bir_lowering=False)
    x_in = nc.dram_tensor("x_in", [NTOK, D], F32, kind="ExternalInput").ap()
    kwT = nc.dram_tensor("kwT", [D, 64], F32, kind="ExternalInput").ap()
    vwT = nc.dram_tensor("vwT", [D, 64], F32, kind="ExternalInput").ap()
    owT = nc.dram_tensor("owT", [64, D], F32, kind="ExternalInput").ap()
    tsg = nc.dram_tensor("tsg", [1, D], F32, kind="ExternalInput").ap()
    lnw = nc.dram_tensor("lnw", [128, DCH], F32, kind="ExternalInput").ap()
    lnb = nc.dram_tensor("lnb", [128, DCH], F32, kind="ExternalInput").ap()
    td = nc.dram_tensor("td", [64, 1], F32, kind="ExternalInput").ap()
    tf = nc.dram_tensor("tf", [64, 1], F32, kind="ExternalInput").ap()
    x_out = nc.dram_tensor("x_out", [NTOK, D], F32, kind="ExternalOutput").ap()
    ctx = Ctx(nc, es)
    ctx.ps_pool()
    emit_idents(ctx)
    ctx.open_stage()
    emit_tmix(ctx, x_in, x_out, kwT, vwT, owT, tsg, lnw, lnb, td, tf, "tx")
    ctx.close_stage()
    emit_drain_outputs(ctx)
    es.close()
    return nc



MSCALE = 1.0 / np.sqrt(96.0)

def emit_mla(ctx, x_in, x_out, wq_d, wqr_d, wkr_d, wkvd_d, wku_d, wvu_d, wo_d,
             lobo_d, rcos_d, rsin_d, tag):
    nc = ctx.nc
    psf, psw, psb = ctx.ps_pool()
    ident = ctx.bufs["ident_bf"]

    Kc = ctx.ssb("mlaKc", (128, DCH * T), BF16)
    Krf = ctx.ssb("mlaKr", (128, T), BF16)
    Vt = ctx.ssb("mlaVt", (128, (T // 128) * D), BF16)
    xnf = ctx.ssb("mlaxnf", (128, DCH * T), BF16)
    latf = ctx.ssb("mlalat", (128, 2 * 512), BF16)
    wbuf = ctx.ssb("mlawb", (128, DCH * D), BF16)
    wo2 = ctx.ssb("mlawo2", (128, DCH * D), BF16)
    wqrb = ctx.ssb("mlawqr", (128, DCH * 512), BF16)
    wkrb = ctx.ssb("mlawkr", (128, DCH * 32), BF16)
    wkvb = ctx.ssb("mlawkv", (128, DCH * 256), BF16)
    wkub = ctx.ssb("mlawku", (128, 2 * D), BF16)
    wvub = ctx.ssb("mlawvu", (128, 2 * D), BF16)
    rcos = ctx.ssb("mlacos", (128, 256), F32)
    rsin = ctx.ssb("mlasin", (128, 256), F32)
    lobo = ctx.ssb("mlalobo", (1, 16), F32)
    Cb = ctx.ssb("mlaCb", (128, 16), F32)
    onesf = ctx.ssb("mlaones", (1, 128), F32)
    crow = ctx.ssb("mlacrow", (1, 16), F32)
    qt = [ctx.ssb("mlaqt", (128, DCH * 128), BF16)] * 2
    qrf = [ctx.ssb("mlaqrf", (128, 512), BF16)] * 2
    qrt = ctx.ssb("mlaqrt", (128, 512), BF16)
    ta_ = ctx.ssb("mlata", (128, 256), F32)
    tb_ = ctx.ssb("mlatb", (128, 256), F32)
    krt = ctx.ssb("mlakrt", (128, 32), BF16)
    srow = [ctx.ssb("mlasr", (128, T), F32)] * 2
    pb = [ctx.ssb(f"mlapb{p}", (128, T), BF16) for p in range(2)]
    ptb = [ctx.ssb(f"mlapt{p}", (128, T), BF16) for p in range(2)]
    yfm = [ctx.ssb("mlayf", (128, DCH * 128), BF16)] * 2
    mx = [ctx.ssb(f"mlamx{p}", (128, 1), F32) for p in range(2)]
    ngm = [ctx.ssb(f"mlangm{p}", (128, 1), F32) for p in range(2)]
    sxp = [ctx.ssb(f"mlasxp{p}", (128, 1), F32) for p in range(2)]
    rcp = [ctx.ssb(f"mlarcp{p}", (128, 1), F32) for p in range(2)]
    xnt = [ctx.ssb("mlaxnt", (128, D), F32)] * 2
    xot = xnt

    wl, pq, evq, pvs, sev, dvx, axp, pvm, pop, res = (
        f"A{n}" for n in ("wl", "pq", "evq", "pvs", "sev", "dvx",
                          "axp", "pvm", "pop", "res"))

    with nc.Block() as blk:
        def g0(gp):
            for c in range(DCH):
                ctx.inc(wl, gp.dma_start(out=wqrb[:, c * 512:(c + 1) * 512],
                                         in_=wqr_d[c * 128:(c + 1) * 128, :]), 16)
                ctx.inc(wl, gp.dma_start(out=wkrb[:, c * 32:(c + 1) * 32],
                                         in_=wkr_d[c * 128:(c + 1) * 128, :]), 16)
                ctx.inc(wl, gp.dma_start(out=wkvb[:, c * 256:(c + 1) * 256],
                                         in_=wkvd_d[c * 128:(c + 1) * 128, :]), 16)
            for l2 in range(2):
                ctx.inc(wl, gp.dma_start(out=wkub[:, l2 * D:(l2 + 1) * D],
                                         in_=wku_d[l2 * 128:(l2 + 1) * 128, :]), 16)
                ctx.inc(wl, gp.dma_start(out=wvub[:, l2 * D:(l2 + 1) * D],
                                         in_=wvu_d[l2 * 128:(l2 + 1) * 128, :]), 16)
            ctx.inc(wl, gp.dma_start(out=rcos[:, :], in_=rcos_d[:, :]), 16)
            ctx.inc(wl, gp.dma_start(out=rsin[:, :], in_=rsin_d[:, :]), 16)
            ctx.inc(wl, gp.dma_start(out=lobo[:, :], in_=lobo_d[:, :]), 16)
            ctx.wait(gp, wl)
            ctx.inc("sv", gp.memset(onesf[:, :], 1.0))
            ctx.wait(gp, "sv")
        blk.gpsimd(g0)

    with nc.Block() as blk:
        def cexp(sc):
            ctx.inc(axp, sc.activation(crow[:, :], lobo[:, :], AF.Exp))
            ctx.mark((tag, "cexp"), axp)
        blk.scalar(cexp)

        def cbc(te):
            ctx.wait_cp(te, (tag, "cexp"))
            ctx.inc(pq, te.matmul(psf[0][:, 0:16], onesf[0:1, :], crow[0:1, :],
                                  start=True, stop=True))
            ctx.mark((tag, "cbc"), pq)
        blk.tensor(cbc)

        def cev(v):
            ctx.wait_cp(v, (tag, "cbc"))
            ctx.inc(sev, v.tensor_copy(Cb[:, :], psf[0][:, 0:16]))
        blk.vector(cev)

    def load_w(dst, src_dram, width):
        with nc.Block() as blk:
            def g(gp):
                for c in range(DCH):
                    ctx.inc(wl, gp.dma_start(
                        out=dst[:, c * width:(c + 1) * width],
                        in_=src_dram[c * 128:(c + 1) * 128, :]), 16)
                ctx.wait(gp, wl)
            blk.gpsimd(g)

    for b in range(BL):
        row0 = b * T

        with nc.Block() as blk:
            emit_ln_to_fm(ctx, blk, x_in, row0, T, xnf, T, 0, (tag, "ln", b))

        with nc.Block() as blk:
            for tb in range(T // 512):
                def latmm(te, tb=tb):
                    ctx.wait_cp(te, (tag, "latev", b, tb - 1))
                    ctx.wait_cp(te, (tag, "krrope", b, tb * 4 - 1))
                    for l2 in range(2):
                        last = None
                        for c in range(DCH):
                            last = te.matmul(
                                psf[l2][:, :],
                                wkvb[:, c * 256 + l2 * 128: c * 256 + (l2 + 1) * 128],
                                xnf[:, c * T + tb * 512: c * T + (tb + 1) * 512],
                                start=(c == 0), stop=(c == DCH - 1))
                        ctx.inc(pq, last)
                    ctx.mark((tag, "latmm", b, tb), pq)
                blk.tensor(latmm)

                def latev(v, tb=tb):
                    ctx.wait_cp(v, (tag, "latmm", b, tb))
                    ctx.wait_cp(v, (tag, "kvdone", b, tb - 1))
                    for l2 in range(2):
                        ctx.inc(evq, v.tensor_copy(latf[:, l2 * 512:(l2 + 1) * 512],
                                                   psf[l2][:, :]))
                    ctx.mark((tag, "latev", b, tb), evq)
                blk.vector(latev)

                for do in range(DCH):
                    def kcmm(te, tb=tb, do=do):
                        ctx.wait_cp(te, (tag, "latev", b, tb))
                        ctx.wait_cp(te, (tag, "kcev", b, tb, do - 2))
                        last = None
                        for l2 in range(2):
                            last = te.matmul(
                                psf[2 + do % 2][:, :],
                                wkub[:, l2 * D + do * 128: l2 * D + (do + 1) * 128],
                                latf[:, l2 * 512:(l2 + 1) * 512],
                                start=(l2 == 0), stop=(l2 == 1))
                        ctx.inc(pq, last)
                        ctx.mark((tag, "kcmm", b, tb, do), pq)
                    blk.tensor(kcmm)

                    def kcev(v, tb=tb, do=do):
                        ctx.wait_cp(v, (tag, "kcmm", b, tb, do))
                        ctx.inc(evq, v.tensor_copy(
                            Kc[:, do * T + tb * 512: do * T + (tb + 1) * 512],
                            psf[2 + do % 2][:, :]))
                        ctx.mark((tag, "kcev", b, tb, do), evq)
                    blk.vector(kcev)

                for tl in range(4):
                    kt = tb * 4 + tl

                    def vmm(te, tb=tb, tl=tl, kt=kt):
                        ctx.wait_cp(te, (tag, "kcev", b, tb, DCH - 1))
                        ctx.wait_cp(te, (tag, "vev", b, kt - 1))
                        for half in range(2):
                            last = None
                            for l2 in range(2):
                                last = te.matmul(
                                    psf[2 * (tl % 2) + half][:, :] if False else
                                    psf[2 + half][:, :],
                                    latf[:, l2 * 512 + tl * 128: l2 * 512 + (tl + 1) * 128],
                                    wvub[:, l2 * D + half * 512: l2 * D + (half + 1) * 512],
                                    start=(l2 == 0), stop=(l2 == 1))
                            ctx.inc(pq, last)
                        ctx.mark((tag, "vmm", b, kt), pq)
                    blk.tensor(vmm)

                    def vev(v, kt=kt):
                        ctx.wait_cp(v, (tag, "vmm", b, kt))
                        for half in range(2):
                            ctx.inc(evq, v.tensor_copy(
                                Vt[:, kt * D + half * 512: kt * D + (half + 1) * 512],
                                psf[2 + half][:, :]))
                        ctx.mark((tag, "vev", b, kt), evq)
                    blk.vector(vev)

                for tl in range(4):
                    kt = tb * 4 + tl

                    def krmm(te, tb=tb, tl=tl, kt=kt):
                        ctx.wait_cp(te, (tag, "krrope", b, kt - 1))
                        last = None
                        for c in range(DCH):
                            last = te.matmul(
                                psf[0][:, 0:32],
                                xnf[:, c * T + kt * 128: c * T + (kt + 1) * 128],
                                wkrb[:, c * 32:(c + 1) * 32],
                                start=(c == 0), stop=(c == DCH - 1))
                        ctx.inc(pq, last)
                        ctx.mark((tag, "krmm", b, kt), pq)
                    blk.tensor(krmm)

                    def krrope(v, kt=kt):
                        ctx.wait_cp(v, (tag, "krmm", b, kt))
                        ctx.wait_cp(v, (tag, "krtp", b, kt - 1))
                        cs = rcos[:, (kt % 16) * 16:(kt % 16) * 16 + 16]
                        sn = rsin[:, (kt % 16) * 16:(kt % 16) * 16 + 16]
                        q1 = psf[0][:, 0:16]
                        q2 = psf[0][:, 16:32]
                        ctx.inc(dvx, v.tensor_tensor(ta_[:, 0:16], q1, cs, ALU.mult))
                        ctx.inc(dvx, v.tensor_tensor(tb_[:, 0:16], q2, sn, ALU.mult))
                        ctx.wait(v, dvx)
                        ctx.inc(dvx, v.tensor_tensor(krt[:, 0:16], ta_[:, 0:16],
                                                     tb_[:, 0:16], ALU.subtract))
                        ctx.wait(v, dvx)
                        ctx.inc(dvx, v.tensor_tensor(ta_[:, 0:16], q2, cs, ALU.mult))
                        ctx.inc(dvx, v.tensor_tensor(tb_[:, 0:16], q1, sn, ALU.mult))
                        ctx.wait(v, dvx)
                        ctx.inc(dvx, v.tensor_tensor(krt[:, 16:32], ta_[:, 0:16],
                                                     tb_[:, 0:16], ALU.add))
                        ctx.mark((tag, "krrope", b, kt), dvx)
                    blk.vector(krrope)

                    def krtp(te, kt=kt):
                        ctx.wait_cp(te, (tag, "krrope", b, kt))
                        ctx.wait_cp(te, (tag, "krev", b, kt - 1))
                        ctx.inc(pvm, te.transpose(psb[0][:, 0:32] if False else
                                                  psb[0][0:32, 0:128],
                                                  krt[:, 0:32], ident[:, :]))
                        ctx.mark((tag, "krtp", b, kt), pvm)
                    blk.tensor(krtp)

                    def krev(v, kt=kt):
                        ctx.wait_cp(v, (tag, "krtp", b, kt))
                        for a in range(4):
                            ctx.inc(evq, v.tensor_copy(
                                Krf[32 * a:32 * a + 32, kt * 128:(kt + 1) * 128],
                                psb[0][0:32, 0:128]))
                        ctx.mark((tag, "krev", b, kt), evq)
                    blk.vector(krev)

            ctx.mark((tag, "kvdone", b, 100), evq)

        if b == 0:
            load_w(wbuf, wq_d, D)
            load_w(wo2, wo_d, D)
        with nc.Block() as blk:
            for qi in range(T // 128):
                qp = qi % 2
                klen = (qi + 1) * 128
                nkt = qi + 1
                nkb = (klen + 511) // 512
                q0 = row0 + qi * 128

                for rnd in range(2):
                    def qmm(te, qi=qi, rnd=rnd):
                        ctx.wait_cp(te, (tag, "qfree", b, qi - 1))
                        ctx.wait_cp(te, (tag, "qev", b, qi, rnd - 1))
                        if rnd == 0:
                            ctx.wait_cp(te, (tag, "qev", b, qi - 1, 1))
                        for do in range(4 * rnd, 4 * rnd + 4):
                            last = None
                            for c in range(DCH):
                                last = te.matmul(
                                    psf[2][:, (do % 4) * 128:(do % 4) * 128 + 128],
                                    wbuf[:, c * D + do * 128: c * D + (do + 1) * 128],
                                    xnf[:, c * T + qi * 128: c * T + (qi + 1) * 128],
                                    start=(c == 0), stop=(c == DCH - 1))
                            ctx.inc(pq, last)
                        ctx.mark((tag, "qmm", b, qi, rnd), pq)
                    blk.tensor(qmm)

                    def qev(v, qi=qi, qp=qp, rnd=rnd):
                        ctx.wait_cp(v, (tag, "qmm", b, qi, rnd))
                        ctx.inc(evq, v.tensor_scalar(
                            qt[qp][:, rnd * 512:(rnd + 1) * 512], psf[2][:, :],
                            MSCALE, None, ALU.mult))
                        ctx.mark((tag, "qev", b, qi, rnd), evq)
                        if rnd == 1:
                            ctx.mark((tag, "qdone", b, qi), evq)
                    blk.vector(qev)

                def qrmm(te, qi=qi):
                    ctx.wait_cp(te, (tag, "qrrope", b, qi - 1))
                    ctx.wait_cp(te, (tag, "yevd", b, qi * NH - 2))
                    last = None
                    for c in range(DCH):
                        last = te.matmul(
                            psf[3][:, :],
                            xnf[:, c * T + qi * 128: c * T + (qi + 1) * 128],
                            wqrb[:, c * 512:(c + 1) * 512],
                            start=(c == 0), stop=(c == DCH - 1))
                    ctx.inc(pq, last)
                    ctx.mark((tag, "qrmm", b, qi), pq)
                blk.tensor(qrmm)

                def qrrope(v, qi=qi):
                    ctx.wait_cp(v, (tag, "qrmm", b, qi))
                    ctx.wait_cp(v, (tag, "qrtp", b, qi - 1))
                    cs = bass.AP(rcos, (qi % 16) * 16, [[256, 128], [0, 16], [1, 16]])
                    sn = bass.AP(rsin, (qi % 16) * 16, [[256, 128], [0, 16], [1, 16]])
                    q1 = psf[3][:, :].rearrange("p (h i) -> p h i", i=32)[:, :, 0:16]
                    q2 = psf[3][:, :].rearrange("p (h i) -> p h i", i=32)[:, :, 16:32]
                    o1 = qrt[:, :].rearrange("p (h i) -> p h i", i=32)[:, :, 0:16]
                    o2 = qrt[:, :].rearrange("p (h i) -> p h i", i=32)[:, :, 16:32]
                    tav = ta_[:, :].rearrange("p (h i) -> p h i", i=16)
                    tbv = tb_[:, :].rearrange("p (h i) -> p h i", i=16)
                    ctx.inc(dvx, v.tensor_tensor(tav, q1, cs, ALU.mult))
                    ctx.inc(dvx, v.tensor_tensor(tbv, q2, sn, ALU.mult))
                    ctx.wait(v, dvx)
                    ctx.inc(dvx, v.tensor_tensor(o1, tav, tbv, ALU.subtract))
                    ctx.wait(v, dvx)
                    ctx.inc(dvx, v.tensor_tensor(tav, q2, cs, ALU.mult))
                    ctx.inc(dvx, v.tensor_tensor(tbv, q1, sn, ALU.mult))
                    ctx.wait(v, dvx)
                    ctx.inc(dvx, v.tensor_tensor(o2, tav, tbv, ALU.add))
                    ctx.mark((tag, "qrrope", b, qi), dvx)
                blk.vector(qrrope)

                def qrtp(te, qi=qi):
                    ctx.wait_cp(te, (tag, "qrrope", b, qi))
                    ctx.wait_cp(te, (tag, "qrev", b, qi - 1))
                    last = None
                    for j in range(4):
                        last = te.transpose(psb[0][:, j * 128:(j + 1) * 128],
                                            qrt[:, j * 128:(j + 1) * 128],
                                            ident[:, :])
                    ctx.inc(pvm, last)
                    ctx.mark((tag, "qrtp", b, qi), pvm)
                blk.tensor(qrtp)

                def qrev(v, qi=qi, qp=qp):
                    ctx.wait_cp(v, (tag, "qrtp", b, qi))
                    ctx.wait_cp(v, (tag, "qrffree", b, qi - 1))
                    ctx.inc(evq, v.tensor_scalar(qrf[qp][:, :], psb[0][:, 0:512],
                                                 MSCALE, None, ALU.mult))
                    ctx.mark((tag, "qrev", b, qi), evq)
                blk.vector(qrev)

                def _front(h):
                    g = qi * NH + h
                    hp = h % 2
                    hr = 64 * (h % 2)
                    hc = (h // 2) * 128
                    ja = h % 4

                    for kb in range(nkb):
                        kw = min(512, klen - kb * 512)
                        sb_id = ctx.cp.get((tag, "sblkid", b), 0)
                        ctx.cp[(tag, "sblkid", b)] = sb_id + 1
                        par = sb_id % 2

                        def smm(te, qi=qi, h=h, kb=kb, kw=kw, sb_id=sb_id,
                                par=par, hr=hr, hc=hc, ja=ja, qp=qp):
                            if kb == 0 and h == 0:
                                ctx.wait_cp(te, (tag, "qdone", b, qi))
                                ctx.wait_cp(te, (tag, "qrev", b, qi))
                            ctx.wait_cp(te, (tag, "sfree", b, sb_id - 2))
                            te.matmul(
                                psf[par][:, 0:kw],
                                qt[qp][hr:hr + 64, hc:hc + 128],
                                Kc[hr:hr + 64,
                                   (h // 2) * T + kb * 512:
                                   (h // 2) * T + kb * 512 + kw],
                                start=True, stop=False)
                            last = te.matmul(
                                psf[par][:, 0:kw],
                                qrf[qp][32 * ja:32 * ja + 32,
                                        (h // 4) * 128:(h // 4) * 128 + 128],
                                Krf[32 * ja:32 * ja + 32, kb * 512: kb * 512 + kw],
                                start=False, stop=True,
                                tile_position=(32 * ja, 0))
                            ctx.inc(pvs, last)
                            ctx.mark((tag, "smm", b, sb_id), pvs)
                        blk.tensor(smm)

                        def sevac(v, g=g, kb=kb, kw=kw, sb_id=sb_id, par=par,
                                  nkb=nkb, hp=hp):
                            ctx.wait_cp(v, (tag, "smm", b, sb_id))
                            if kb == 0:
                                ctx.wait_cp(v, (tag, "exp", b, g - 1))
                            if kb == nkb - 1:
                                if kw > 128:
                                    ctx.inc(sev, v.tensor_copy(
                                        srow[hp][:, kb * 512: kb * 512 + kw - 128],
                                        psf[par][:, 0:kw - 128]))
                                ctx.inc(sev, v.tensor_tensor(
                                    srow[hp][:, kb * 512 + kw - 128: kb * 512 + kw],
                                    psf[par][:, kw - 128:kw],
                                    ctx.bufs["cmask"][:, :], ALU.add))
                            else:
                                ctx.inc(sev, v.tensor_copy(
                                    srow[hp][:, kb * 512: kb * 512 + kw],
                                    psf[par][:, 0:kw]))
                            ctx.mark((tag, "sfree", b, sb_id), sev)
                            if kb == nkb - 1:
                                ctx.mark((tag, "srow", b, g), sev)
                        blk.vector(sevac)

                    def dmax(v, g=g, klen=klen, hp=hp):
                        ctx.wait_cp(v, (tag, "srow", b, g))
                        ctx.wait_cp(v, (tag, "exp", b, g - 2))
                        ctx.inc(dvx, v.reduce_max(ngm[hp][:, :],
                                                  srow[hp][:, 0:klen], AX.X,
                                                  negate=True))
                        ctx.mark((tag, "max", b, g), dvx)
                    blk.vector(dmax)

                    def aexp(sc, g=g, klen=klen, hp=hp):
                        ctx.wait_cp(sc, (tag, "max", b, g))
                        ctx.wait_cp(sc, (tag, "ptmm", b, g - 2))
                        ctx.inc(axp, sc.activation(pb[hp][:, 0:klen],
                                                   srow[hp][:, 0:klen],
                                                   AF.Exp, bias=ngm[hp][:, 0:1],
                                                   accum_out=sxp[hp][:, 0:1]))
                        ctx.mark((tag, "exp", b, g), axp)
                    blk.scalar(aexp)


                def _tail(h):
                    g = qi * NH + h
                    hp = h % 2
                    hr = 64 * (h % 2)
                    hc = (h // 2) * 128
                    ja = h % 4

                    def dscl(v, g=g, h=h, klen=klen, hp=hp):
                        ctx.wait_cp(v, (tag, "exp", b, g))
                        ctx.inc(dvx, v.tensor_scalar(rcp[hp][:, :], sxp[hp][:, :],
                                                     Cb[:, h:h + 1], None, ALU.add))
                        ctx.wait(v, dvx)
                        ctx.inc(dvx, v.reciprocal(rcp[hp][:, :], rcp[hp][:, :]))
                        ctx.wait(v, dvx)
                        ctx.inc(dvx, v.tensor_scalar(pb[hp][:, 0:klen],
                                                     pb[hp][:, 0:klen],
                                                     rcp[hp][:, 0:1], None, ALU.mult))
                        ctx.mark((tag, "pscl", b, g), dvx)
                    blk.vector(dscl)

                    for r0 in range(0, nkt, 8):
                        r1 = min(nkt, r0 + 8)

                        def ptr(te, g=g, r0=r0, r1=r1, hp=hp):
                            ctx.wait_cp(te, (tag, "pscl", b, g))
                            if r0 == 0:
                                ctx.wait_cp(te, (tag, "ptev", b, g - 2, 0))
                                ctx.wait_cp(te, (tag, "ptev", b, g - 2, 8))
                            else:
                                ctx.wait_cp(te, (tag, "ptev", b, g, 0))
                            last = None
                            for kc in range(r0, r1):
                                last = te.transpose(
                                    psb[hp][:, (kc % 8) * 128:(kc % 8) * 128 + 128],
                                    pb[hp][:, kc * 128:(kc + 1) * 128], ident[:, :])
                            ctx.inc(pvm, last)
                            ctx.mark((tag, "ptmm", b, g), pvm)
                            ctx.mark((tag, "ptmm2", b, g, r0), pvm)
                        blk.tensor(ptr)

                        def ptev(v, g=g, r0=r0, r1=r1, hp=hp, nkt=nkt):
                            ctx.wait_cp(v, (tag, "ptmm2", b, g, r0))
                            if r0 == 0:
                                ctx.wait_cp(v, (tag, "pv", b, g - 2))
                            w = (r1 - r0) * 128
                            ctx.inc(sev, v.tensor_copy(
                                ptb[hp][:, r0 * 128: r0 * 128 + w],
                                psb[hp][:, 0:w]))
                            ctx.mark((tag, "ptev", b, g, r0), sev)
                            ctx.mark((tag, "ptevd", b, g), sev)
                        blk.vector(ptev)

                    def pvmm(te, g=g, h=h, nkt=nkt, hp=hp):
                        ctx.wait_cp(te, (tag, "ptevd", b, g))
                        if h % 2 == 0:
                            ctx.wait_cp(te, (tag, "yevd", b, g - 2))
                        last = None
                        for kc in range(nkt):
                            last = te.matmul(
                                psf[3][64 * (h % 2): 64 * (h % 2) + 64, 0:128],
                                Vt[:, kc * D + 64 * h: kc * D + 64 * h + 64],
                                ptb[hp][:, kc * 128:(kc + 1) * 128],
                                start=(kc == 0), stop=(kc == nkt - 1))
                        ctx.inc(pvm, last)
                        ctx.mark((tag, "pv", b, g), pvm)
                    blk.tensor(pvmm)

                    if h % 2 == 1:
                        def yevac(v, g=g, qi=qi, h=h, qp=qp):
                            if h == 1:
                                ctx.wait_cp(v, (tag, "oproj", b, qi - 1))
                            ctx.wait_cp(v, (tag, "pv", b, g - 1))
                            ctx.wait_cp(v, (tag, "pv", b, g))
                            ctx.inc(sev, v.tensor_copy(
                                yfm[qp][:, (h // 2) * 128:(h // 2) * 128 + 128],
                                psf[3][:, 0:128]))
                            ctx.mark((tag, "yevd", b, g), sev)
                            ctx.mark((tag, "yevd", b, g - 1), sev)
                            if h == NH - 1:
                                ctx.mark((tag, "ydone", b, qi), sev)
                                ctx.mark((tag, "qrffree", b, qi), sev)
                        blk.vector(yevac)


                for h in range(NH + 1):
                    if h < NH:
                        _front(h)
                    if h >= 1:
                        _tail(h - 1)

                def xold(sync, qi=qi, qp=qp, q0=q0, b=b):
                    ctx.wait_cp(sync, (tag, "xnfree", b * 16 + qi - 1))
                    ctx.inc(f"Aldx{qp}", sync.dma_start(out=xnt[qp][:, :],
                                                        in_=x_in[q0:q0 + 128, :]), 16)
                    ctx.mark((tag, "xold", b, qi), f"Aldx{qp}")
                blk.sync(xold)

                def oproj(te, qi=qi, qp=qp):
                    ctx.wait_cp(te, (tag, "ydone", b, qi))
                    ctx.wait_cp(te, (tag, "ofree", b, qi - 1))
                    for half in range(2):
                        last = None
                        for c in range(DCH):
                            last = te.matmul(
                                psw[:, half * 512:(half + 1) * 512],
                                yfm[qp][:, c * 128:(c + 1) * 128],
                                wo2[:, c * D + half * 512:
                                        c * D + (half + 1) * 512],
                                start=(c == 0), stop=(c == DCH - 1))
                        ctx.inc(pop, last)
                    ctx.mark((tag, "oproj", b, qi), pop)
                    ctx.mark((tag, "qfree", b, qi), pop)
                blk.tensor(oproj)

                def resid(v, qi=qi, qp=qp, b=b):
                    ctx.wait_cp(v, (tag, "oproj", b, qi))
                    ctx.wait_cp(v, (tag, "xold", b, qi))
                    ctx.wait_cp(v, (tag, "xnfree", b * 16 + qi - 1))
                    ctx.inc(res, v.tensor_tensor(xnt[qp][:, :], psw[:, :],
                                                 xnt[qp][:, :], ALU.add))
                    ctx.mark((tag, "resid", b, qi), res)
                    ctx.mark((tag, "ofree", b, qi), res)
                blk.vector(resid)

                def store(gp, qi=qi, qp=qp, q0=q0, b=b):
                    ctx.wait_cp(gp, (tag, "resid", b, qi))
                    ctx.inc(f"Ast{qp}", gp.dma_start(out=x_out[q0:q0 + 128, :],
                                                     in_=xnt[qp][:, :]), 16)
                    ctx.mark((tag, "xnfree", b * 16 + qi), f"Ast{qp}")
                blk.gpsimd(store)


def build_mla_test():
    es = ExitStack()
    nc = bass.Bass("TRN2", target_bir_lowering=False)
    x_in = nc.dram_tensor("x_in", [NTOK, D], F32, kind="ExternalInput").ap()
    wq = nc.dram_tensor("wq", [D, D], F32, kind="ExternalInput").ap()
    wqr = nc.dram_tensor("wqr", [D, 512], F32, kind="ExternalInput").ap()
    wkr = nc.dram_tensor("wkr", [D, 32], F32, kind="ExternalInput").ap()
    wkvd = nc.dram_tensor("wkvd", [D, 256], F32, kind="ExternalInput").ap()
    wku = nc.dram_tensor("wku", [256, D], F32, kind="ExternalInput").ap()
    wvu = nc.dram_tensor("wvu", [256, D], F32, kind="ExternalInput").ap()
    wo = nc.dram_tensor("wo", [D, D], F32, kind="ExternalInput").ap()
    lobo = nc.dram_tensor("lobo", [1, 16], F32, kind="ExternalInput").ap()
    rcos = nc.dram_tensor("rcos", [128, 256], F32, kind="ExternalInput").ap()
    rsin = nc.dram_tensor("rsin", [128, 256], F32, kind="ExternalInput").ap()
    x_out = nc.dram_tensor("x_out", [NTOK, D], F32, kind="ExternalOutput").ap()
    ctx = Ctx(nc, es)
    ctx.ps_pool()
    emit_idents(ctx)
    ctx.open_stage()
    emit_mla(ctx, x_in, x_out, wq, wqr, wkr, wkvd, wku, wvu, wo,
             lobo, rcos, rsin, "ml")
    ctx.close_stage()
    emit_drain_outputs(ctx)
    es.close()
    return nc



NSTEP = 32
TSH = NTOK // NSTEP
H4 = 4096


def emit_lstm(ctx, x_in, hs_out, w_d, u_d, bias_d, xp_dram, tag, q_cb=None):
    nc = ctx.nc
    psf, psw, psb = ctx.ps_pool()
    ident = ctx.bufs["ident_bf"]

    ub = ctx.ssb("lsub", (128, DCH * H4), BF16)
    biasr = ctx.ssb("lsbias", (1, H4), BF16)
    onesb = ctx.ssb("lsones", (1, 128), BF16)

    dv, ac, pe, ev = "Ldv2", "Lac2", "Lpe2", "Lev2"

    with nc.Block() as blk:
        def uload(gp):
            for c in range(DCH):
                ctx.inc("Lwu", gp.dma_start(
                    out=ub[:, c * H4:(c + 1) * H4],
                    in_=u_d[c * 128:(c + 1) * 128, :]), 16)
            ctx.inc("Lwu", gp.dma_start(out=biasr[:, :], in_=bias_d[:, :]), 16)
            ctx.wait(gp, "Lwu")
            ctx.inc("sv", gp.memset(onesb[:, :], 1.0))
            ctx.wait(gp, "sv")
        blk.gpsimd(uload)

    ctx.push_scope()
    xnf = ctx.ssb("lsxnf", (128, DCH * 512), BF16)
    wch = ctx.ssb("lswc", (128, DCH * 512), BF16)
    xpev = [ctx.ssb(f"lsxe{p}", (128, 512), F32) for p in range(2)]

    for g in range(NTOK // 512):
        r0 = g * 512
        with nc.Block() as blk:
            emit_bf_to_fm(ctx, blk, x_in, r0, 512, xnf, 512, 0,
                          (tag, "ln", g))
        with nc.Block() as blk:
            for nb in range(8):
                def wload(gp, g=g, nb=nb):
                    ctx.wait_cp(gp, (tag, "xpmm", g, nb - 1, 3))
                    for c in range(DCH):
                        ctx.inc("Lww", gp.dma_start(
                            out=wch[:, c * 512:(c + 1) * 512],
                            in_=w_d[c * 128:(c + 1) * 128,
                                    nb * 512:(nb + 1) * 512]), 16)
                    ctx.mark((tag, "wld", g, nb), "Lww")
                blk.gpsimd(wload)

                for rb in range(4):
                    par = (nb * 4 + rb) % 2

                    def xpmm(te, g=g, nb=nb, rb=rb, par=par):
                        ctx.wait_cp(te, (tag, "wld", g, nb))
                        ctx.wait_cp(te, (tag, "xpev", g * 32 + nb * 4 + rb - 2))
                        for c in range(DCH):
                            te.matmul(
                                psf[par][:, :],
                                xnf[:, c * 512 + rb * 128: c * 512 + (rb + 1) * 128],
                                wch[:, c * 512:(c + 1) * 512],
                                start=(c == 0), stop=False)
                        last = te.matmul(
                            psf[par][:, :], onesb[0:1, 0:128],
                            biasr[0:1, nb * 512:(nb + 1) * 512],
                            start=False, stop=True)
                        ctx.inc(pe, last)
                        ctx.mark((tag, "xpmm", g, nb, rb), pe)
                    blk.tensor(xpmm)

                    def xpevac(v, g=g, nb=nb, rb=rb, par=par):
                        ctx.wait_cp(v, (tag, "xpmm", g, nb, rb))
                        ctx.wait_cp(v, (tag, "xpst", g * 32 + nb * 4 + rb - 2))
                        ctx.inc(ev, v.tensor_copy(xpev[par][:, :], psf[par][:, :]))
                        ctx.mark((tag, "xpev", g * 32 + nb * 4 + rb), ev)
                    blk.vector(xpevac)

                    def xpstore(gp, g=g, nb=nb, rb=rb, par=par):
                        ctx.wait_cp(gp, (tag, "xpev", g * 32 + nb * 4 + rb))
                        ctx.inc(f"Lxs{par}", gp.dma_start(
                            out=xp_dram[g * 512 + rb * 128:
                                        g * 512 + (rb + 1) * 128,
                                        nb * 512:(nb + 1) * 512],
                            in_=xpev[par][:, :]), 16)
                        ctx.mark((tag, "xpst", g * 32 + nb * 4 + rb), f"Lxs{par}")
                    blk.gpsimd(xpstore)
    ctx.pop_scope()

    gact = [ctx.ssb(f"lsga{t}", (128, H4), F32) for t in range(2)]
    c_ = [ctx.ssb(f"lsc{t}", (128, D), F32) for t in range(2)]
    hfm = ctx.ssb("lshfm", (128, DCH * TSH), BF16)
    ht = [ctx.ssb(f"lsht{t}", (128, D), F32) for t in range(2)]
    htb = [ctx.ssb(f"lshtb{t}", (128, D), BF16) for t in range(2)]
    ctmp = [ctx.ssb("lsct", (128, D), F32)] * 2
    ctmp2 = [ctx.ssb("lsc2", (128, D), F32)] * 2
    ttc = [ctx.ssb(f"lsttc{t}", (128, D), F32) for t in range(2)]
    gtmp = [ctx.ssb(f"lsgt{p}", (128, 512), F32) for p in range(8)]
    xpt = [ctx.ssb(f"lsxp{p}", (128, 512), F32) for p in range(8)]
    with nc.Block() as blk:
        def init2(gp):
            ctx.inc("sv", gp.memset(c_[0][:, :], 0.0))
            ctx.inc("sv", gp.memset(c_[1][:, :], 0.0))
            ctx.inc("sv", gp.memset(hfm[:, :], 0.0))
            ctx.wait(gp, "sv")
        blk.gpsimd(init2)

    with nc.Block() as blk:
        for i in range(NSTEP):
            for tt in range(2):
                r0 = i * TSH + tt * 128
                for nb in range(8):
                    par = nb % 2

                    def xpl(sync, i=i, tt=tt, nb=nb, par=par, r0=r0):
                        gi = (i * 2 + tt) * 8 + nb
                        if gi == 0:
                            ctx.wait(sync, "Lxs0")
                            ctx.wait(sync, "Lxs1")
                        pi, pt = (i, tt - 1) if tt == 1 else (i - 1, 1)
                        ctx.wait_cp(sync, (tag, "gac", pi, pt, nb))
                        ctx.inc(f"Lxl{nb}", sync.dma_start(
                            out=xpt[nb][:, :],
                            in_=xp_dram[r0:r0 + 128, nb * 512:(nb + 1) * 512]), 16)
                        ctx.mark((tag, "xpl", i, tt, nb), f"Lxl{nb}")
                    blk.sync(xpl)

                    if i > 0:
                        def gmm(te, i=i, tt=tt, nb=nb, par=par):
                            gi = (i * 2 + tt) * 8 + nb
                            ctx.wait_cp(te, (tag, "psfree", gi - 2))
                            ctx.wait_cp(te, (tag, "hfm", i - 1, tt, "done"))
                            last = None
                            for c in range(DCH):
                                last = te.matmul(
                                    psf[par][:, :],
                                    hfm[:, c * TSH + tt * 128: c * TSH + (tt + 1) * 128],
                                    ub[:, c * H4 + nb * 512: c * H4 + (nb + 1) * 512],
                                    start=(c == 0), stop=(c == DCH - 1))
                            ctx.inc(pe, last)
                            ctx.mark((tag, "gmm", i, tt, nb), pe)
                        blk.tensor(gmm)

                        def gadd(v, i=i, tt=tt, nb=nb, par=par):
                            gi = (i * 2 + tt) * 8 + nb
                            ctx.wait_cp(v, (tag, "gmm", i, tt, nb))
                            ctx.wait_cp(v, (tag, "xpl", i, tt, nb))
                            pi, pt = (i, tt - 1) if tt == 1 else (i - 1, 1)
                            ctx.wait_cp(v, (tag, "gac", pi, pt, nb))
                            ctx.inc(dv, v.tensor_tensor(gtmp[nb][:, :],
                                                        psf[par][:, :],
                                                        xpt[nb][:, :], ALU.add))
                            ctx.mark((tag, "gadd", i, tt, nb), dv)
                            ctx.mark((tag, "psfree", gi), dv)
                        blk.vector(gadd)

                def gactf_all(sc, i=i, tt=tt):
                    ctx.wait_cp(sc, (tag, "gactfree", i, tt))
                    for nb in range(8):
                        gi = (i * 2 + tt) * 8 + nb
                        par = nb % 2
                        fn = AF.Sigmoid if nb < 6 else AF.Tanh
                        if i > 0:
                            ctx.wait_cp(sc, (tag, "gadd", i, tt, nb))
                            srcap = gtmp[nb][:, :]
                        else:
                            ctx.wait_cp(sc, (tag, "xpl", i, tt, nb))
                            srcap = xpt[nb][:, :]
                        ctx.inc(ac, sc.activation(
                            gact[tt][:, nb * 512:(nb + 1) * 512], srcap, fn))
                        ctx.mark((tag, "gac", i, tt, nb), ac)
                        ctx.mark((tag, "gtfree", gi), ac)
                        if i == 0:
                            ctx.mark((tag, "xptfree", gi), ac)
                blk.scalar(gactf_all)

                def chm(v, i=i, tt=tt):
                    ctx.wait_cp(v, (tag, "gac", i, tt, 7))
                    ga = gact[tt]
                    ctx.inc(dv, v.tensor_tensor(ctmp[tt][:, :], ga[:, 0:1024],
                                                ga[:, 3072:4096], ALU.mult))
                    ctx.inc(dv, v.tensor_tensor(ctmp2[tt][:, :], ga[:, 1024:2048],
                                                c_[tt][:, :], ALU.mult))
                    ctx.wait(v, dv)
                    ctx.inc(dv, v.tensor_tensor(c_[tt][:, :], ctmp[tt][:, :],
                                                ctmp2[tt][:, :], ALU.add))
                    ctx.mark((tag, "cnew", i, tt), dv)
                blk.vector(chm)

                def ctanh(sc, i=i, tt=tt):
                    ctx.wait_cp(sc, (tag, "cnew", i, tt))
                    ctx.inc(ac, sc.activation(ttc[tt][:, :], c_[tt][:, :], AF.Tanh))
                    ctx.mark((tag, "tanh", i, tt), ac)
                blk.scalar(ctanh)

                def hmul(v, i=i, tt=tt):
                    ctx.wait_cp(v, (tag, "tanh", i, tt))
                    ctx.wait_cp(v, (tag, "hst", i - 1, tt))
                    ctx.inc(dv, v.tensor_tensor(ht[tt][:, :],
                                                gact[tt][:, 2048:3072],
                                                ttc[tt][:, :], ALU.mult))
                    ctx.wait(v, dv)
                    ctx.inc(dv, v.tensor_copy(htb[tt][:, :], ht[tt][:, :]))
                    ctx.mark((tag, "hnew", i, tt), dv)
                    ctx.mark((tag, "gactfree", i + 1, tt), dv)
                blk.vector(hmul)

                def hstore(gp, i=i, tt=tt, r0=r0):
                    ctx.wait_cp(gp, (tag, "hnew", i, tt))
                    ctx.inc(f"Lhs{tt}", gp.dma_start(
                        out=hs_out[r0:r0 + 128, :], in_=ht[tt][:, :]), 16)
                    ctx.mark((tag, "hst", i, tt), f"Lhs{tt}")
                blk.gpsimd(hstore)

                def htp(te, i=i, tt=tt):
                    ctx.wait_cp(te, (tag, "hnew", i, tt))
                    ctx.wait_cp(te, (tag, "hev", i - 1, tt))
                    last = None
                    for c in range(DCH):
                        last = te.transpose(psb[tt][:, c * 128:(c + 1) * 128],
                                            htb[tt][:, c * 128:(c + 1) * 128],
                                            ident[:, :])
                    ctx.inc(pe, last)
                    ctx.mark((tag, "htp", i, tt), pe)
                blk.tensor(htp)

                def hev(v, i=i, tt=tt):
                    ctx.wait_cp(v, (tag, "htp", i, tt))
                    dst = hfm[:, :].rearrange("p (c u) -> p c u", c=DCH)[
                        :, :, tt * 128:(tt + 1) * 128]
                    srcv = psb[tt][:, :].rearrange("p (c u) -> p c u", c=DCH)
                    ctx.inc(ev, v.tensor_copy(dst, srcv))
                    ctx.mark((tag, "hev", i, tt), ev)
                    ctx.mark((tag, "hfm", i, tt, "done"), ev)
                blk.vector(hev)

            if q_cb is not None and i % 8 == 7:
                q_cb(i // 8, blk)


def build_lstm_test():
    es = ExitStack()
    nc = bass.Bass("TRN2", target_bir_lowering=False)
    x_in = nc.dram_tensor("x_in", [NTOK, D], BF16, kind="ExternalInput").ap()
    w = nc.dram_tensor("w", [D, H4], F32, kind="ExternalInput").ap()
    u = nc.dram_tensor("u", [D, H4], F32, kind="ExternalInput").ap()
    bias = nc.dram_tensor("bias", [1, H4], F32, kind="ExternalInput").ap()
    hs = nc.dram_tensor("hs", [NTOK, D], F32, kind="ExternalOutput").ap()
    xp = nc.dram_tensor("xp", [NTOK, H4], F32, kind="Internal").ap()
    ctx = Ctx(nc, es)
    ctx.ps_pool()
    emit_idents(ctx)
    ctx.open_stage()
    emit_lstm(ctx, x_in, hs, w, u, bias, xp, "ls")
    ctx.close_stage()
    with nc.Block() as blk:
        def fin(gp):
            for name in ("Lhs0", "Lhs1"):
                if name in ctx._sems:
                    ctx.wait(gp, name)
        blk.gpsimd(fin)
    es.close()
    return nc


def build_mha_test():
    es = ExitStack()
    nc = bass.Bass("TRN2", target_bir_lowering=False)
    x_in = nc.dram_tensor("x_in", [NTOK, D], F32, kind="ExternalInput").ap()
    wq = nc.dram_tensor("wq", [D, D], F32, kind="ExternalInput").ap()
    wk = nc.dram_tensor("wk", [D, D], F32, kind="ExternalInput").ap()
    wv = nc.dram_tensor("wv", [D, D], F32, kind="ExternalInput").ap()
    wo = nc.dram_tensor("wo", [D, D], F32, kind="ExternalInput").ap()
    x_out = nc.dram_tensor("x_out", [NTOK, D], F32, kind="ExternalOutput").ap()
    ctx = Ctx(nc, es)
    ctx.ps_pool()
    emit_idents(ctx)
    ctx.open_stage()
    emit_mha(ctx, x_in, x_out, wq, wk, wv, wo, "mh")
    ctx.close_stage()
    emit_drain_outputs(ctx)
    es.close()
    return nc


def emit_drain_outputs(ctx):
    nc = ctx.nc
    with nc.Block() as blk:
        def fin(gp):
            for name in list(ctx._sems):
                if ("st" in name and name[0] in "CMOTA") or name.endswith("sto"):
                    ctx.wait(gp, name)
        blk.gpsimd(fin)



def _np(x):
    return np.ascontiguousarray(np.asarray(x, dtype=np.float32))


def cmix_host_weights(W_r, W_k, W_v, mu_r, mu_k):
    W_r, W_k, W_v = np.asarray(W_r), np.asarray(W_k), np.asarray(W_v)
    return dict(
        wrT=_np(W_r.T), wkT=_np(W_k.T), wvT=_np(W_v.T),
        mur=_np(np.asarray(mu_r).reshape(DCH, 128).T),
        muk=_np(np.asarray(mu_k).reshape(DCH, 128).T))


def build_cmix_test():
    es = ExitStack()
    nc = bass.Bass("TRN2", target_bir_lowering=False)
    x_in = nc.dram_tensor("x_in", [NTOK, D], F32, kind="ExternalInput").ap()
    wrT = nc.dram_tensor("wrT", [D, D], F32, kind="ExternalInput").ap()
    wkT = nc.dram_tensor("wkT", [D, D], F32, kind="ExternalInput").ap()
    wvT = nc.dram_tensor("wvT", [D, D], F32, kind="ExternalInput").ap()
    mur = nc.dram_tensor("mur", [128, DCH], F32, kind="ExternalInput").ap()
    muk = nc.dram_tensor("muk", [128, DCH], F32, kind="ExternalInput").ap()
    x_out = nc.dram_tensor("x_out", [NTOK, D], F32, kind="ExternalOutput").ap()
    ctx = Ctx(nc, es)
    ctx.ps_pool()
    emit_idents(ctx)
    ctx.open_stage()
    emit_cmix_weights(ctx, wrT, wkT, wvT, mur, muk, "cm")
    emit_cmix(ctx, x_in, x_out, "cm")
    ctx.close_stage()
    emit_drain_outputs(ctx)
    es.close()
    return nc





def build_add_test():
    es = ExitStack()
    nc = bass.Bass("TRN2", target_bir_lowering=False)
    a_in = nc.dram_tensor("a_in", [NTOK, D], F32, kind="ExternalInput").ap()
    b_in = nc.dram_tensor("b_in", [NTOK, D], F32, kind="ExternalInput").ap()
    x_out = nc.dram_tensor("x_out", [NTOK, D], F32, kind="ExternalOutput").ap()
    ctx = Ctx(nc, es)
    at = [es.enter_context(nc.sbuf_tensor(f"adda{p}", [128, D], F32)) for p in range(2)]
    bt = [es.enter_context(nc.sbuf_tensor(f"addb{p}", [128, D], F32)) for p in range(2)]
    ot = [es.enter_context(nc.sbuf_tensor(f"addo{p}", [128, D], F32)) for p in range(2)]
    with nc.Block() as blk:
        for s in range(NTOK // 128):
            p = s % 2

            def dma_in(sync, s=s, p=p):
                ctx.wait_cp(sync, ("add", "sum", s - 2))
                ctx.inc(f"Dld{p}", sync.dma_start(
                    out=at[p][:, :], in_=a_in[s * 128:(s + 1) * 128, :]), 16)
                ctx.inc(f"Dld{p}", sync.dma_start(
                    out=bt[p][:, :], in_=b_in[s * 128:(s + 1) * 128, :]), 16)
                ctx.mark(("add", "ld", s), f"Dld{p}")
            blk.sync(dma_in)

            def addv(v, s=s, p=p):
                ctx.wait_cp(v, ("add", "ld", s))
                ctx.wait_cp(v, ("add", "st", s - 2))
                ctx.inc("Ddv", v.tensor_tensor(ot[p][:, :], at[p][:, :],
                                               bt[p][:, :], ALU.add))
                ctx.mark(("add", "sum", s), "Ddv")
            blk.vector(addv)

            def store(gp, s=s, p=p):
                ctx.wait_cp(gp, ("add", "sum", s))
                ctx.inc(f"Dst{p}", gp.dma_start(
                    out=x_out[s * 128:(s + 1) * 128, :], in_=ot[p][:, :]), 16)
                ctx.mark(("add", "st", s), f"Dst{p}")
            blk.gpsimd(store)
    with nc.Block() as blk:
        def fin(gp):
            ctx.wait(gp, "Dst0")
            ctx.wait(gp, "Dst1")
        blk.gpsimd(fin)
    es.close()
    return nc



_PROGRAM_CACHE = {}


def _prog(name, builder):
    if name not in _PROGRAM_CACHE:
        _PROGRAM_CACHE[name] = builder()
    return _PROGRAM_CACHE[name]


def _run(name, builder, in_maps, out_key):
    nc = _prog(name, builder)
    res = run_bass_kernel_spmd(nc, in_maps, core_ids=list(range(NCORES)))
    return [r[out_key] for r in res.results]


def _rope_tables():
    half = 16
    inv = 1.0 / 10000.0 ** (np.arange(half) / half)
    ang = np.arange(T)[:, None] * inv[None, :]
    rcos = np.cos(ang).astype(np.float32).reshape(16, 128, 16).transpose(1, 0, 2).reshape(128, 256)
    rsin = np.sin(ang).astype(np.float32).reshape(16, 128, 16).transpose(1, 0, 2).reshape(128, 256)
    return np.ascontiguousarray(rcos), np.ascontiguousarray(rsin)


pass





def emit_bf_to_fm(ctx, blk, x_dram_bf, row0, ntok, fm_buf, fm_unit, fm_col0, tag):
    nsub = ntok // 128
    ident = ctx.bufs["ident_bf"]
    psf, psw, psb = ctx.ps_pool()
    xt = [ctx.ssb(f"bffx{p}", (128, D), BF16) for p in range(2)]
    for s in range(nsub):
        p = s % 2
        r0 = row0 + s * 128

        def dma_in(sync, s=s, p=p, r0=r0):
            ctx.wait_cp(sync, (tag, "xtfree", s - 2))
            ctx.inc(f"Fld{p}", sync.dma_start(out=xt[p][:, :],
                                              in_=x_dram_bf[r0:r0 + 128, :]), 16)
            ctx.mark((tag, "xload", s), f"Fld{p}")
        blk.sync(dma_in)

        def pe_t(te, s=s, p=p):
            ctx.wait_cp(te, (tag, "xload", s))
            ctx.wait_cp(te, (tag, "tpfree", s - 2))
            last = None
            for c in range(DCH):
                last = te.transpose(psb[p][:, c * 128:(c + 1) * 128],
                                    xt[p][:, c * 128:(c + 1) * 128], ident[:, :])
            ctx.inc("Fpe", last)
            ctx.mark((tag, "tdone", s), "Fpe")
            ctx.mark((tag, "xtfree", s), "Fpe")
        blk.tensor(pe_t)

        def evac(v, s=s, p=p):
            ctx.wait_cp(v, (tag, "tdone", s))
            dst = fmv(fm_buf, fm_unit, fm_col0 + s * 128, fm_col0 + (s + 1) * 128)
            srcv = psb[p][:, :].rearrange("q (c u) -> q c u", c=DCH)
            ctx.inc("Fev", v.tensor_copy(dst, srcv))
            ctx.mark((tag, "tpfree", s), "Fev")
        blk.vector(evac)
    ctx.mark((tag, "done"), "Fev")


def emit_lncast(ctx, x_in, xnb_out, tag):
    nc = ctx.nc
    xt = [ctx.ssb(f"lcx{p}", (128, D), F32) for p in range(2)]
    st_ = [ctx.ssb(f"lcs{p}", (128, 12), F32) for p in range(2)]
    mv = [ctx.ssb(f"lcm{p}", (128, 2), F32) for p in range(2)]
    rs = [ctx.ssb(f"lcr{p}", (128, 1), F32) for p in range(2)]
    nm = [ctx.ssb(f"lcn{p}", (128, 1), F32) for p in range(2)]
    xn = [ctx.ssb(f"lco{p}", (128, D), BF16) for p in range(2)]
    with nc.Block() as blk:
        for s in range(NTOK // 128):
            p = s % 2
            r0 = s * 128

            def dma_in(sync, s=s, p=p, r0=r0):
                ctx.wait_cp(sync, (tag, "normd", s - 2))
                ctx.inc(f"Gld{p}", sync.dma_start(out=xt[p][:, :],
                                                  in_=x_in[r0:r0 + 128, :]), 16)
                ctx.mark((tag, "xload", s), f"Gld{p}")
            blk.sync(dma_in)

            def dve(v, s=s, p=p):
                ctx.wait_cp(v, (tag, "xload", s))
                ctx.inc("Gdv", v.bn_stats(st_[p][:, 0:6], xt[p][:, 0:512]))
                ctx.inc("Gdv", v.bn_stats(st_[p][:, 6:12], xt[p][:, 512:1024]))
                ctx.wait(v, "Gdv")
                ctx.inc("Gdv", v.bn_aggr(mv[p][:, :], st_[p][:, :]))
                ctx.mark((tag, "aggr", s), "Gdv")
            blk.vector(dve)

            def act1(sc, s=s, p=p):
                ctx.wait_cp(sc, (tag, "aggr", s))
                ctx.inc("Gac", sc.activation(rs[p][:, :], mv[p][:, 1:2], AF.Ln,
                                             bias=ctx.bufs["eps"][:, 0:1]))
                ctx.wait(sc, "Gac")
                ctx.inc("Gac", sc.activation(rs[p][:, :], rs[p][:, :], AF.Exp,
                                             scale=-0.5))
                ctx.mark((tag, "rstd", s), "Gac")
            blk.scalar(act1)

            def dve2(v, s=s, p=p):
                ctx.wait_cp(v, (tag, "rstd", s))
                ctx.inc("Gdv", v.scalar_tensor_tensor(nm[p][:, :], mv[p][:, 0:1],
                                                      -1.0, rs[p][:, :],
                                                      ALU.mult, ALU.mult))
                ctx.wait(v, "Gdv")
                ctx.wait_cp(v, (tag, "stored", s - 2))
                ctx.inc("Gdv", v.tensor_scalar(xn[p][:, :], xt[p][:, :],
                                               rs[p][:, 0:1], nm[p][:, 0:1],
                                               ALU.mult, ALU.add))
                ctx.mark((tag, "normd", s), "Gdv")
            blk.vector(dve2)

            def store(gp, s=s, p=p, r0=r0):
                ctx.wait_cp(gp, (tag, "normd", s))
                ctx.inc(f"Gst{p}", gp.dma_start(
                    out=xnb_out[r0:r0 + 128, :], in_=xn[p][:, :]), 16)
                ctx.mark((tag, "stored", s), f"Gst{p}")
            blk.gpsimd(store)


def emit_dma_flush(ctx):
    nc = ctx.nc
    with nc.Block() as blk:
        def fl(sync):
            for name in sorted(ctx.dma_sems):
                ctx.wait(sync, name)
        blk.sync(fl)


def build_fused():
    es = ExitStack()
    nc = bass.Bass("TRN2", target_bir_lowering=False)
    DI = nc.dram_tensor
    x_in = DI("x_in", [NTOK, D], F32, kind="ExternalInput").ap()
    x_out = DI("x_out", [NTOK, D], F32, kind="ExternalOutput").ap()
    xb0 = DI("xb0", [NTOK, D], F32, kind="Internal").ap()
    xb1 = DI("xb1", [NTOK, D], F32, kind="Internal").ap()
    xp = DI("xp", [NTOK, H4], F32, kind="Internal").ap()
    hs = DI("hs", [NTOK, D], F32, kind="Internal").ap()

    wq = DI("wq", [D, D], F32, kind="ExternalInput").ap()
    wqr = DI("wqr", [D, 512], F32, kind="ExternalInput").ap()
    wkr = DI("wkr", [D, 32], F32, kind="ExternalInput").ap()
    wkvd = DI("wkvd", [D, 256], F32, kind="ExternalInput").ap()
    wku = DI("wku", [256, D], F32, kind="ExternalInput").ap()
    wvu = DI("wvu", [256, D], F32, kind="ExternalInput").ap()
    wo_mla = DI("wo_mla", [D, D], F32, kind="ExternalInput").ap()
    lobo = DI("lobo", [1, 16], F32, kind="ExternalInput").ap()
    rcos = DI("rcos", [128, 256], F32, kind="ExternalInput").ap()
    rsin = DI("rsin", [128, 256], F32, kind="ExternalInput").ap()
    wg = DI("wg", [D, 8], F32, kind="ExternalInput").ap()
    w1 = DI("w1", [8, D, HMOE], F32, kind="ExternalInput").ap()
    b1 = DI("b1", [8, HMOE], F32, kind="ExternalInput").ap()
    w2 = DI("w2", [8, HMOE, D], F32, kind="ExternalInput").ap()
    b2 = DI("b2", [8, D], F32, kind="ExternalInput").ap()
    kwT = DI("kwT", [D, 64], F32, kind="ExternalInput").ap()
    vwT = DI("vwT", [D, 64], F32, kind="ExternalInput").ap()
    owT = DI("owT", [64, D], F32, kind="ExternalInput").ap()
    tsg = DI("tsg", [1, D], F32, kind="ExternalInput").ap()
    lnw = DI("lnw", [128, DCH], F32, kind="ExternalInput").ap()
    lnb = DI("lnb", [128, DCH], F32, kind="ExternalInput").ap()
    td = DI("td", [64, 1], F32, kind="ExternalInput").ap()
    tf = DI("tf", [64, 1], F32, kind="ExternalInput").ap()
    wq1 = DI("wq1", [D, D], F32, kind="ExternalInput").ap()
    wk1 = DI("wk1", [D, D], F32, kind="ExternalInput").ap()
    wv1 = DI("wv1", [D, D], F32, kind="ExternalInput").ap()
    wo1 = DI("wo1", [D, D], F32, kind="ExternalInput").ap()
    wrT = DI("wrT", [D, D], F32, kind="ExternalInput").ap()
    wkT = DI("wkT", [D, D], F32, kind="ExternalInput").ap()
    wvT = DI("wvT", [D, D], F32, kind="ExternalInput").ap()
    mur = DI("mur", [128, DCH], F32, kind="ExternalInput").ap()
    muk = DI("muk", [128, DCH], F32, kind="ExternalInput").ap()
    wl_ = DI("wl", [D, H4], F32, kind="ExternalInput").ap()
    ul_ = DI("ul", [D, H4], F32, kind="ExternalInput").ap()
    biasl = DI("biasl", [1, H4], F32, kind="ExternalInput").ap()

    ctx = Ctx(nc, es)
    ctx.ps_pool()
    emit_idents(ctx)

    ctx.open_stage()
    emit_mla(ctx, x_in, xb0, wq, wqr, wkr, wkvd, wku, wvu, wo_mla,
             lobo, rcos, rsin, "ml")
    ctx.close_stage()
    emit_dma_flush(ctx)

    ctx.open_stage()
    emit_moe(ctx, xb0, xb1, wg, w1, b1, w2, b2, "mo1")
    ctx.close_stage()
    emit_dma_flush(ctx)

    ctx.open_stage()
    emit_tmix(ctx, xb1, xb0, kwT, vwT, owT, tsg, lnw, lnb, td, tf, "tx")
    ctx.close_stage()
    emit_dma_flush(ctx)

    ctx.open_stage()
    emit_moe(ctx, xb0, xb1, wg, w1, b1, w2, b2, "mo2")
    ctx.close_stage()
    emit_dma_flush(ctx)

    ctx.open_stage()
    emit_mha(ctx, xb1, xb0, wq1, wk1, wv1, wo1, "mh")
    ctx.close_stage()
    emit_dma_flush(ctx)

    ctx.open_stage()
    emit_cmix_weights(ctx, wrT, wkT, wvT, mur, muk, "cm1")
    emit_cmix(ctx, xb0, xb1, "cm1")
    ctx.close_stage()
    emit_dma_flush(ctx)

    xnb = DI("xnb", [NTOK, D], BF16, kind="Internal").ap()
    agbuf = DI("agbuf", [NCORES * NTOK, D], BF16, kind="Internal",
               addr_space="Shared").ap()
    lstm_x = DI("lstm_x", [NTOK, D], BF16, kind="Internal").ap()
    ag2q = DI("ag2q", [4 * NCORES * 2048, D], F32, kind="Internal",
              addr_space="Shared").ap()
    hsb = DI("hsb", [NTOK, D], F32, kind="Internal").ap()

    ctx.open_stage()
    emit_lncast(ctx, xb1, xnb, "lc")
    ctx.close_stage()
    emit_dma_flush(ctx)

    with nc.Block() as blk:
        def cc1(gp):
            i = gp.collective_compute(
                "AllGather", ALU.bypass,
                replica_groups=[list(range(NCORES))],
                ins=[xnb.opt()], outs=[agbuf.opt()])
            ctx.inc("ccs", i, 1)
            ctx.wait(gp, "ccs")
            with gp.register("agr") as r2:
                gp.reg_mov(r2, gp.partition_id())
                gp.reg_mul(r2, r2, 256 * D)
                ctx.inc("ccd", gp.dma_start(
                    out=lstm_x[:, :].rearrange("(k t) d -> k t d", t=256),
                    in_=bass.AP(agbuf.tensor, r2,
                                [[T * D, 32], [D, 256], [1, D]])), 16)
                ctx.wait(gp, "ccd")
        blk.gpsimd(cc1)

    def lstm_qcb(q, blk):
        def agq(gp, q=q):
            ctx.wait(gp, "Lhs0")
            ctx.wait(gp, "Lhs1")
            i = gp.collective_compute(
                "AllGather", ALU.bypass,
                replica_groups=[list(range(NCORES))],
                ins=[hs[q * 2048:(q + 1) * 2048, :].opt()],
                outs=[ag2q[q * NCORES * 2048:(q + 1) * NCORES * 2048, :].opt()])
            ctx.inc("ccs", i, 1)
        blk.gpsimd(agq)

    ctx.open_stage()
    emit_lstm(ctx, lstm_x, hs, wl_, ul_, biasl, xp, "ls", q_cb=lstm_qcb)
    ctx.close_stage()
    emit_dma_flush(ctx)

    with nc.Block() as blk:
        def cc2(gp):
            ctx.wait(gp, "ccs")
            with gp.register("agr3") as r2, gp.register("agr4") as r3:
                gp.reg_mov(r2, gp.partition_id())
                gp.reg_div(r2, r2, 2)
                gp.reg_mul(r2, r2, NCORES * 2048 * D)
                gp.reg_mov(r3, gp.partition_id())
                gp.reg_mod(r3, r3, 2)
                gp.reg_mul(r3, r3, 4 * 256 * D)
                gp.reg_add(r2, r2, r3)
                for sl in range(4):
                    ctx.inc("ccd", gp.dma_start(
                        out=hsb[sl * T:(sl + 1) * T, :].rearrange(
                            "(c t) d -> c t d", t=256),
                        in_=bass.AP(ag2q.tensor, r2,
                                    [[2048 * D, 8], [D, 256], [1, D]])), 16)
                    if sl < 3:
                        gp.reg_add(r2, r2, 256 * D)
                ctx.wait(gp, "ccd")
        blk.gpsimd(cc2)

    at = [ctx.sb(f"fadda{p}", (128, D), F32) for p in range(2)]
    bt = [ctx.sb(f"faddb{p}", (128, D), F32) for p in range(2)]
    ot = [ctx.sb(f"faddo{p}", (128, D), F32) for p in range(2)]
    with nc.Block() as blk:
        for s in range(NTOK // 128):
            p = s % 2

            def dma_in(sync, s=s, p=p):
                ctx.wait_cp(sync, ("add", "sum", s - 2))
                ctx.inc(f"Dld{p}", sync.dma_start(
                    out=at[p][:, :], in_=xb1[s * 128:(s + 1) * 128, :]), 16)
                ctx.inc(f"Dld{p}", sync.dma_start(
                    out=bt[p][:, :], in_=hsb[s * 128:(s + 1) * 128, :]), 16)
                ctx.mark(("add", "ld", s), f"Dld{p}")
            blk.sync(dma_in)

            def addv(v, s=s, p=p):
                ctx.wait_cp(v, ("add", "ld", s))
                ctx.wait_cp(v, ("add", "st", s - 2))
                ctx.inc("Ddv", v.tensor_tensor(ot[p][:, :], at[p][:, :],
                                               bt[p][:, :], ALU.add))
                ctx.mark(("add", "sum", s), "Ddv")
            blk.vector(addv)

            def store(gp, s=s, p=p):
                ctx.wait_cp(gp, ("add", "sum", s))
                ctx.inc(f"Dst{p}", gp.dma_start(
                    out=xb0[s * 128:(s + 1) * 128, :], in_=ot[p][:, :]), 16)
                ctx.mark(("add", "st", s), f"Dst{p}")
            blk.gpsimd(store)
    emit_dma_flush(ctx)

    ctx.open_stage()
    emit_cmix_weights(ctx, wrT, wkT, wvT, mur, muk, "cm2")
    emit_cmix(ctx, xb0, x_out, "cm2")
    ctx.close_stage()

    emit_dma_flush(ctx)
    es.close()
    return nc


_FUSED = None


def kernel(**kw):
    global _FUSED
    if _FUSED is None:
        _FUSED = build_fused()
    rcos, rsin = _rope_tables()
    common = {
        "wq": _np(kw["L0_attn_a_Wq"]), "wqr": _np(kw["L0_attn_a_Wqr"]),
        "wkr": _np(kw["L0_attn_a_Wkr"]), "wkvd": _np(kw["L0_attn_a_Wkvd"]),
        "wku": _np(kw["L0_attn_a_Wku"]), "wvu": _np(kw["L0_attn_a_Wvu"]),
        "wo_mla": _np(kw["L0_attn_a_Wo"]),
        "lobo": _np(kw["L0_attn_a_lobo_log"]).reshape(1, 16),
        "rcos": rcos, "rsin": rsin,
        "wg": _np(kw["L0_ffn_Wg"]), "w1": _np(kw["L0_ffn_W1"]),
        "b1": _np(kw["L0_ffn_b1"]), "w2": _np(kw["L0_ffn_W2"]),
        "b2": _np(kw["L0_ffn_b2"]),
        "kwT": _np(np.asarray(kw["L0_attn_b_key_w"]).T),
        "vwT": _np(np.asarray(kw["L0_attn_b_value_w"]).T),
        "owT": _np(np.asarray(kw["L0_attn_b_output_w"]).T),
        "tsg": _np(kw["L0_attn_b_time_shift_gates"]).reshape(1, D),
        "lnw": _np(np.asarray(kw["L0_attn_b_ln_w"]).reshape(DCH, 128).T),
        "lnb": _np(np.asarray(kw["L0_attn_b_ln_b"]).reshape(DCH, 128).T),
        "td": _np(kw["L0_attn_b_time_decay"]).reshape(64, 1),
        "tf": _np(kw["L0_attn_b_time_first"]).reshape(64, 1),
        "wq1": _np(kw["L1_attn_a_W_Q"]), "wk1": _np(kw["L1_attn_a_W_K"]),
        "wv1": _np(kw["L1_attn_a_W_V"]), "wo1": _np(kw["L1_attn_a_W_o"]),
        "wrT": _np(np.asarray(kw["L1_ffn_W_r"]).T),
        "wkT": _np(np.asarray(kw["L1_ffn_W_k"]).T),
        "wvT": _np(np.asarray(kw["L1_ffn_W_v"]).T),
        "mur": _np(np.asarray(kw["L1_ffn_mu_r"]).reshape(DCH, 128).T),
        "muk": _np(np.asarray(kw["L1_ffn_mu_k"]).reshape(DCH, 128).T),
        "wl": _np(kw["L1_attn_b_w"]), "ul": _np(kw["L1_attn_b_u"]),
        "biasl": _np(kw["L1_attn_b_bias"]).reshape(1, H4),
    }
    in_maps = []
    for c in range(NCORES):
        xc = np.ascontiguousarray(
            np.asarray(kw["L0_attn_a_x"])[4 * c:4 * c + 4].reshape(NTOK, D),
            dtype=np.float32)
        in_maps.append(dict(x_in=xc, **common))
    res = run_bass_kernel_spmd(_FUSED, in_maps, core_ids=list(range(NCORES)))
    out = np.concatenate([r["x_out"].reshape(4, T, D) for r in res.results],
                         axis=0)
    return out


# version 15
# speedup vs baseline: 1.0474x; 1.0026x over previous
import numpy as np
from contextlib import ExitStack

import concourse.bass as bass
import concourse.mybir as mybir
from concourse.bass_utils import run_bass_kernel_spmd

F32 = mybir.dt.float32
BF16 = mybir.dt.bfloat16
AF = mybir.ActivationFunctionType
ALU = mybir.AluOpType
AX = mybir.AxisListType

NCORES = 8
B, T, D = 32, 2048, 1024
BL = B // NCORES
NTOK = BL * T
EPS = 1e-5
DCH = D // 128


class Ctx:
    def __init__(self, nc, es):
        self.nc = nc
        self.es = es
        self._sems = {}
        self.bufs = {}
        self.cp = {}
        self.dma_sems = set()

    def semh(self, name):
        if name not in self._sems:
            h = self.es.enter_context(self.nc.semaphore(name))
            self._sems[name] = [h, 0]
        return self._sems[name][0]

    def inc(self, name, inst, n=1):
        h = self.semh(name)
        self._sems[name][1] += n
        inst.then_inc(h, n)
        if n >= 16:
            self.dma_sems.add(name)
        return self._sems[name][1]

    def val(self, name):
        self.semh(name)
        return self._sems[name][1]

    def wait(self, eng, name, v=None):
        eng.wait_ge(self.semh(name), self.val(name) if v is None else v)

    def mark(self, key, sem):
        self.cp[key] = (sem, self.val(sem))

    def wait_cp(self, eng, key):
        if key in self.cp:
            sem, v = self.cp[key]
            if v > 0:
                eng.wait_ge(self.semh(sem), v)

    def sb(self, name, shape, dtype):
        if name not in self.bufs:
            self.bufs[name] = self.es.enter_context(
                self.nc.sbuf_tensor(name, list(shape), dtype))
        return self.bufs[name]

    def open_stage(self):
        self.stage_es = ExitStack()
        self.stage_bufs = {}
        self._scopes = []
        self._stage_id = getattr(self, "_stage_id", -1) + 1

    def ssb(self, name, shape, dtype):
        if name not in self.stage_bufs:
            es = self._scopes[-1][0] if self._scopes else self.stage_es
            uname = f"{name}_s{self._stage_id}"
            self.stage_bufs[name] = es.enter_context(
                self.nc.sbuf_tensor(uname, list(shape), dtype))
            if self._scopes:
                self._scopes[-1][1].append(name)
        return self.stage_bufs[name]

    def push_scope(self):
        self._scopes.append((ExitStack(), []))

    def pop_scope(self):
        es, names = self._scopes.pop()
        es.close()
        for n in names:
            del self.stage_bufs[n]

    def close_stage(self):
        self.stage_es.close()
        self.stage_bufs = {}

    def constv(self, val):
        name = f"cst_{abs(hash(('c', float(val)))) % 10**10}"
        if name not in self.bufs:
            buf = self.sb(name, (128, 1), F32)
            with self.nc.Block() as blk:
                def g(gp, buf=buf, val=val):
                    gp.memset(buf[:, :], float(val))
                blk.gpsimd(g)
        return self.bufs[name]

    def ps_pool(self):
        if "PSF0" not in self.bufs:
            for i in range(4):
                self.bufs[f"PSF{i}"] = self.es.enter_context(
                    self.nc.psum_tensor(f"PSF{i}", [128, 512], F32))
            self.bufs["PSW"] = self.es.enter_context(
                self.nc.psum_tensor("PSW", [128, 1024], F32))
            for i in range(2):
                self.bufs[f"PSB{i}"] = self.es.enter_context(
                    self.nc.psum_tensor(f"PSB{i}", [128, 1024], BF16))
        return [self.bufs[f"PSF{i}"] for i in range(4)], self.bufs["PSW"], \
               [self.bufs[f"PSB{i}"] for i in range(2)]


def fmv(buf, unit, c0, c1):
    return buf[:, :].rearrange("p (c u) -> p c u", c=DCH)[:, :, c0:c1]


def emit_idents(ctx):
    nc = ctx.nc
    ib = ctx.sb("ident_bf", (128, 128), BF16)
    if_ = ctx.sb("ident_f32", (128, 128), F32)
    cm = ctx.sb("cmask", (128, 128), F32)
    eps = ctx.sb("eps", (128, 1), F32)
    with nc.Block() as blk:
        def g(gp):
            ctx.inc("sv", gp.memset(eps[:, :], EPS))
            ctx.inc("sv", gp.memset(ib[:, :], 1.0))
            ctx.inc("sv", gp.memset(if_[:, :], 1.0))
            ctx.inc("sv", gp.memset(cm[:, :], 0.0))
            ctx.wait(gp, "sv")
            ctx.inc("sv", gp.affine_select(cm[:, :], cm[:, :], [[-1, 128]],
                                           ALU.is_ge, -1e30, base=0,
                                           channel_multiplier=1))
            ctx.wait(gp, "sv")
            ctx.inc("sv", gp.affine_select(ib[:, :], ib[:, :], [[1, 128]],
                                           ALU.is_equal, 0.0, base=0,
                                           channel_multiplier=-1))
            ctx.inc("sv", gp.affine_select(if_[:, :], if_[:, :], [[1, 128]],
                                           ALU.is_equal, 0.0, base=0,
                                           channel_multiplier=-1))
            ctx.wait(gp, "sv")
        blk.gpsimd(g)



def emit_ln_to_fm(ctx, blk, x_dram, row0, ntok, fm_buf, fm_unit, fm_col0, tag,
                  out_f32=False, sub_range=None):
    nsub = ntok // 128
    dt_out = F32 if out_f32 else BF16
    ident = ctx.bufs["ident_f32" if out_f32 else "ident_bf"]
    psf, psw, psb = ctx.ps_pool()

    xt = [ctx.ssb(f"lnx{p}", (128, D), F32) for p in range(2)]
    st_ = [ctx.ssb(f"lns{p}", (128, 12), F32) for p in range(2)]
    mv = [ctx.ssb(f"lnm{p}", (128, 2), F32) for p in range(2)]
    rs = [ctx.ssb(f"lnr{p}", (128, 1), F32) for p in range(2)]
    nm = [ctx.ssb(f"lnn{p}", (128, 1), F32) for p in range(2)]
    if out_f32:
        xn = [ctx.ssb(f"lnof{p}", (128, D), F32) for p in range(2)]
    else:
        xn = [ctx.ssb(f"lnob{p}", (128, D), BF16) for p in range(2)]

    dv, ac, pe, ev = "Ldv", "Lac", "Lpe", "Lev"

    for s in (range(nsub) if sub_range is None else range(*sub_range)):
        p = s % 2
        ld = f"Lld{p}"
        r0 = row0 + s * 128
        if out_f32:
            tp_tiles = [psf[2 * p][:, :], psf[2 * p + 1][:, :]]
        else:
            tp_tiles = [psb[p][:, :]]

        def dma_in(sync, s=s, p=p, r0=r0, ld=ld):
            ctx.wait_cp(sync, (tag, "xtfree", s - 2))
            ctx.inc(ld, sync.dma_start(out=xt[p][:, :],
                                       in_=x_dram[r0:r0 + 128, :]), 16)
            ctx.mark((tag, "xload", s), ld)
        blk.sync(dma_in)

        def dve(v, s=s, p=p):
            ctx.wait_cp(v, (tag, "xload", s))
            ctx.inc(dv, v.bn_stats(st_[p][:, 0:6], xt[p][:, 0:512]))
            ctx.inc(dv, v.bn_stats(st_[p][:, 6:12], xt[p][:, 512:1024]))
            ctx.wait(v, dv)
            ctx.inc(dv, v.bn_aggr(mv[p][:, :], st_[p][:, :]))
            ctx.mark((tag, "aggr", s), dv)
        blk.vector(dve)

        def act1(sc, s=s, p=p):
            ctx.wait_cp(sc, (tag, "aggr", s))
            ctx.inc(ac, sc.activation(rs[p][:, :], mv[p][:, 1:2], AF.Ln,
                                      bias=ctx.bufs["eps"][:, 0:1]))
            ctx.wait(sc, ac)
            ctx.inc(ac, sc.activation(rs[p][:, :], rs[p][:, :], AF.Exp, scale=-0.5))
            ctx.mark((tag, "rstd", s), ac)
        blk.scalar(act1)

        def dve2(v, s=s, p=p):
            ctx.wait_cp(v, (tag, "rstd", s))
            ctx.inc(dv, v.scalar_tensor_tensor(nm[p][:, :], mv[p][:, 0:1], -1.0,
                                               rs[p][:, :], ALU.mult, ALU.mult))
            ctx.mark((tag, "nm", s), dv)
        blk.vector(dve2)

        def act2(v, s=s, p=p):
            ctx.wait_cp(v, (tag, "nm", s))
            ctx.wait_cp(v, (tag, "xnfree", s - 2))
            ctx.inc(dv, v.tensor_scalar(xn[p][:, :], xt[p][:, :],
                                        rs[p][:, 0:1], nm[p][:, 0:1],
                                        ALU.mult, ALU.add))
            ctx.mark((tag, "norm", s), dv)
            ctx.mark((tag, "xtfree", s), dv)
        blk.vector(act2)

        def pe_t(te, s=s, p=p, tp_tiles=tp_tiles):
            ctx.wait_cp(te, (tag, "norm", s))
            ctx.wait_cp(te, (tag, "tpfree", s - 2))
            last = None
            for c in range(DCH):
                tpt = tp_tiles[c // 4] if out_f32 else tp_tiles[0]
                off = (c % 4) * 128 if out_f32 else c * 128
                last = te.transpose(tpt[:, off:off + 128],
                                    xn[p][:, c * 128:(c + 1) * 128], ident[:, :])
            ctx.inc(pe, last)
            ctx.mark((tag, "tdone", s), pe)
            ctx.mark((tag, "xnfree", s), pe)
        blk.tensor(pe_t)

        def evac(v, s=s, p=p, tp_tiles=tp_tiles):
            ctx.wait_cp(v, (tag, "tdone", s))
            if out_f32:
                for h in range(2):
                    dst = fmv(fm_buf, fm_unit, fm_col0 + s * 128,
                              fm_col0 + (s + 1) * 128)[:, 4 * h:4 * (h + 1), :]
                    src = tp_tiles[h].rearrange("p (c u) -> p c u", c=4)
                    ctx.inc(ev, v.tensor_copy(dst, src))
            else:
                dst = fmv(fm_buf, fm_unit, fm_col0 + s * 128,
                          fm_col0 + (s + 1) * 128)
                src = tp_tiles[0].rearrange("p (c u) -> p c u", c=DCH)
                ctx.inc(ev, v.tensor_copy(dst, src))
            ctx.mark((tag, "tpfree", s), ev)
        blk.vector(evac)

    if sub_range is None or sub_range[1] >= nsub:
        ctx.mark((tag, "done"), ev)



def emit_cmix_weights(ctx, wrT, wkT, wvT, mur, muk, tag):
    nc = ctx.nc
    wr = ctx.ssb(f"{tag}wr", (128, DCH * D), BF16)
    wk = ctx.ssb(f"{tag}wk", (128, DCH * D), BF16)
    wv = ctx.ssb(f"{tag}wv", (128, DCH * D), BF16)
    mr = ctx.ssb(f"{tag}mr", (128, DCH), F32)
    mk = ctx.ssb(f"{tag}mk", (128, DCH), F32)
    omr = ctx.ssb(f"{tag}omr", (128, DCH), F32)
    omk = ctx.ssb(f"{tag}omk", (128, DCH), F32)
    with nc.Block() as blk:
        def wload(gp):
            for (dst, src) in ((wr, wrT), (wk, wkT), (wv, wvT)):
                for c in range(DCH):
                    ctx.inc("wl", gp.dma_start(
                        out=dst[:, c * D:(c + 1) * D],
                        in_=src[c * 128:(c + 1) * 128, :]), 16)
            ctx.inc("wl", gp.dma_start(out=mr[:, :], in_=mur[:, :]), 16)
            ctx.inc("wl", gp.dma_start(out=mk[:, :], in_=muk[:, :]), 16)
            ctx.mark((tag, "wloaded"), "wl")
        blk.gpsimd(wload)

        def prep(v):
            ctx.wait_cp(v, (tag, "wloaded"))
            ctx.inc("sv", v.tensor_scalar(omr[:, :], mr[:, :], -1.0, 1.0,
                                          ALU.mult, ALU.add))
            ctx.inc("sv", v.tensor_scalar(omk[:, :], mk[:, :], -1.0, 1.0,
                                          ALU.mult, ALU.add))
        blk.vector(prep)


def emit_cmix(ctx, x_in, x_out, tag):
    nc = ctx.nc
    HB = 1024
    NU = NTOK // HB
    NSUB = HB // 128
    psf, psw, psb = ctx.ps_pool()

    wr, wk, wv = ctx.stage_bufs[f"{tag}wr"], ctx.stage_bufs[f"{tag}wk"], ctx.stage_bufs[f"{tag}wv"]
    mr, mk = ctx.stage_bufs[f"{tag}mr"], ctx.stage_bufs[f"{tag}mk"]
    omr, omk = ctx.stage_bufs[f"{tag}omr"], ctx.stage_bufs[f"{tag}omk"]

    xnf = ctx.ssb("cmx_xnf", (128, DCH * HB), F32)
    bnd = ctx.ssb("cmx_bnd", (128, DCH), F32)
    xrb = ctx.ssb("cmx_xrb", (128, DCH * HB), BF16)
    xkb = ctx.ssb("cmx_xkb", (128, DCH * HB), BF16)
    kab = ctx.ssb("cmx_kab", (128, DCH * HB), BF16)
    t1 = ctx.ssb("cmx_t1", (128, HB), F32)
    krs = [ctx.ssb(f"cmx_krs{p}", (128, 512), F32) for p in range(2)]
    rtm = [ctx.ssb(f"cmx_rtm{p}", (128, D), F32) for p in range(2)]
    xot = [ctx.ssb(f"cmx_xot{p}", (128, D), F32) for p in range(2)]
    xnt = [ctx.ssb(f"cmx_xnt{p}", (128, D), F32) for p in range(2)]

    pk, aK, pR, aR, evo, sto = "Cpk", "CaK", "CpR", "CaR", "Cev", "Cst"

    for u in range(NU):
        new_seq = (u % (T // HB) == 0)
        row0 = u * HB

        with nc.Block() as blk:
            emit_ln_to_fm(ctx, blk, x_in, row0, HB, xnf, HB, 0,
                          ("cmA", tag, u), out_f32=True)

        with nc.Block() as blk:
            def mix(v, u=u, new_seq=new_seq):
                for c in range(DCH):
                    xc = xnf[:, c * HB:(c + 1) * HB]
                    for (dst, m, om) in ((xrb, mr, omr), (xkb, mk, omk)):
                        dc = dst[:, c * HB:(c + 1) * HB]
                        ctx.inc("sv", v.tensor_scalar(
                            t1[:, 1:HB], xnf[:, c * HB:c * HB + HB - 1],
                            om[:, c:c + 1], None, ALU.mult))
                        if new_seq:
                            i0 = v.memset(t1[:, 0:1], 0.0)
                        else:
                            i0 = v.tensor_tensor(t1[:, 0:1], bnd[:, c:c + 1],
                                                 om[:, c:c + 1], ALU.mult)
                        ctx.inc("sv", i0)
                        ctx.wait(v, "sv")
                        ctx.inc("sv", v.scalar_tensor_tensor(
                            dc, xc, m[:, c:c + 1], t1[:, :], ALU.mult, ALU.add))
                        ctx.wait(v, "sv")
                for c in range(DCH):
                    ctx.inc("sv", v.tensor_copy(bnd[:, c:c + 1],
                                                xnf[:, (c + 1) * HB - 1:(c + 1) * HB]))
            blk.vector(mix)

        with nc.Block() as blk:
            for g in range(2 * DCH):
                do, tb = g // 2, g % 2
                gp_ = g % 2

                def c1_pe(te, do=do, tb=tb, g=g, gp_=gp_):
                    ctx.wait_cp(te, (tag, "kreluD", u, g - 2))
                    last = None
                    for c in range(DCH):
                        last = te.matmul(
                            psf[gp_][:, :],
                            wk[:, c * D + do * 128: c * D + (do + 1) * 128],
                            xkb[:, c * HB + tb * 512: c * HB + (tb + 1) * 512],
                            start=(c == 0), stop=(c == DCH - 1))
                    ctx.inc(pk, last)
                    ctx.mark((tag, "kmm", u, g), pk)
                blk.tensor(c1_pe)

                def c1_act(sc, do=do, tb=tb, g=g, gp_=gp_):
                    ctx.wait_cp(sc, (tag, "kmm", u, g))
                    ctx.inc(aK, sc.activation(krs[gp_][:, :], psf[gp_][:, :], AF.Relu))
                    ctx.mark((tag, "kreluD", u, g), aK)
                    ctx.wait(sc, aK)
                    ctx.inc(aK, sc.activation(
                        kab[:, do * HB + tb * 512: do * HB + (tb + 1) * 512],
                        krs[gp_][:, :], AF.Square))
                blk.scalar(c1_act)

        with nc.Block() as blk:
            for s in range(NSUB):
                p = s % 2

                def c2_dma(sync, s=s, p=p):
                    ctx.wait_cp(sync, (tag, "xofree", u, s - 2))
                    ctx.inc(f"Cldx{p}", sync.dma_start(
                        out=xot[p][:, :],
                        in_=x_in[row0 + s * 128: row0 + (s + 1) * 128, :]), 16)
                    ctx.mark((tag, "xold", u, s), f"Cldx{p}")
                blk.sync(c2_dma)

                def c2_pe(te, s=s, p=p):
                    ctx.wait_cp(te, (tag, "sig", u, s - 1))
                    last = None
                    for c in range(DCH):
                        for half in range(2):
                            last = te.matmul(
                                psw[:, half * 512:(half + 1) * 512],
                                xrb[:, c * HB + s * 128: c * HB + (s + 1) * 128],
                                wr[:, c * D + half * 512: c * D + (half + 1) * 512],
                                start=(c == 0), stop=(c == DCH - 1))
                    ctx.inc(pR, last)
                    ctx.mark((tag, "rmm", u, s), pR)
                    ctx.wait_cp(te, (tag, "omul", u, s - 1))
                    last = None
                    for c in range(DCH):
                        for half in range(2):
                            last = te.matmul(
                                psf[2 + half][:, :],
                                kab[:, c * HB + s * 128: c * HB + (s + 1) * 128],
                                wv[:, c * D + half * 512: c * D + (half + 1) * 512],
                                start=(c == 0), stop=(c == DCH - 1))
                    ctx.inc(pR, last)
                    ctx.mark((tag, "omm", u, s), pR)
                blk.tensor(c2_pe)

                def c2_act(sc, s=s, p=p):
                    ctx.wait_cp(sc, (tag, "rmm", u, s))
                    ctx.wait_cp(sc, (tag, "rtmfree", u, s - 2))
                    ctx.inc(aR, sc.activation(rtm[p][:, :], psw[:, :], AF.Sigmoid))
                    ctx.mark((tag, "sig", u, s), aR)
                blk.scalar(c2_act)

                def c2_ev(v, s=s, p=p, u=u):
                    ctx.wait_cp(v, (tag, "sig", u, s))
                    ctx.wait_cp(v, (tag, "omm", u, s))
                    ctx.wait_cp(v, (tag, "xold", u, s))
                    ctx.wait_cp(v, (tag, "xnfree", u * NSUB + s - 2))
                    for half in range(2):
                        ctx.inc(evo, v.tensor_tensor(
                            xnt[p][:, half * 512:(half + 1) * 512],
                            rtm[p][:, half * 512:(half + 1) * 512],
                            psf[2 + half][:, :], ALU.mult))
                    ctx.mark((tag, "omul", u, s), evo)
                    ctx.mark((tag, "rtmfree", u, s), evo)
                    ctx.wait(v, evo)
                    ctx.inc(evo, v.tensor_tensor(xnt[p][:, :], xnt[p][:, :],
                                                 xot[p][:, :], ALU.add))
                    ctx.mark((tag, "sum", u, s), evo)
                    ctx.mark((tag, "xofree", u, s), evo)
                blk.vector(c2_ev)

                def c2_st(gp, s=s, p=p, u=u):
                    ctx.wait_cp(gp, (tag, "sum", u, s))
                    i = gp.dma_start(
                        out=x_out[row0 + s * 128: row0 + (s + 1) * 128, :],
                        in_=xnt[p][:, :])
                    ctx.inc(f"Cst{p}", i, 16)
                    ctx.mark((tag, "xnfree", u * NSUB + s), f"Cst{p}")
                blk.gpsimd(c2_st)




NH, DH = 16, 64

def emit_mha(ctx, x_in, x_out, wq_d, wk_d, wv_d, wo_d, tag):
    nc = ctx.nc
    psf, psw, psb = ctx.ps_pool()
    ident = ctx.bufs["ident_bf"]
    scale = 1.0 / np.sqrt(DH)

    Kf = ctx.ssb("mhaKf", (128, DCH * T), BF16)
    Vt = ctx.ssb("mhaVt", (128, (T // 128) * D), BF16)
    xnf = ctx.ssb("mhaxnf", (128, DCH * T), BF16)
    wbuf = ctx.ssb("mhawb", (128, DCH * D), BF16)
    wo = ctx.ssb("mhawo", (128, DCH * D), BF16)
    qt = [ctx.ssb("mhaqt", (128, DCH * 128), BF16)] * 2
    srow = [ctx.ssb(f"mhasr{p}", (128, T), F32) for p in range(2)]
    pb = [ctx.ssb(f"mhapb{p}", (128, T), BF16) for p in range(4)]
    ptb = [ctx.ssb(f"mhapt{p}", (128, T), BF16) for p in range(2)]
    yfm = [ctx.ssb("mhayf", (128, DCH * 128), BF16)] * 2
    mx = [ctx.ssb(f"mhamx{p}", (128, 1), F32) for p in range(2)]
    ngm = [ctx.ssb(f"mhangm{p}", (128, 1), F32) for p in range(4)]
    sxp = [ctx.ssb(f"mhasxp{p}", (128, 1), F32) for p in range(4)]
    rcp = [ctx.ssb(f"mharcp{p}", (128, 1), F32) for p in range(4)]
    xnt = [ctx.ssb(f"mhaxnt{p}", (128, D), F32) for p in range(2)]
    xot = xnt

    wl, pq, evq, pvs, sev, msk, dvx, axp, pvm, yev, pop, res, sto = (
        f"M{n}" for n in ("wl", "pq", "evq", "pvs", "sev", "msk", "dvx",
                          "axp", "pvm", "yev", "pop", "res", "sto"))

    def load_w(dst, src_dram):
        with nc.Block() as blk:
            def g(gp):
                for c in range(DCH):
                    ctx.inc(wl, gp.dma_start(
                        out=dst[:, c * D:(c + 1) * D],
                        in_=src_dram[c * 128:(c + 1) * 128, :]), 16)
                ctx.wait(gp, wl)
            blk.gpsimd(g)

    for b in range(BL):
        row0 = b * T

        with nc.Block() as blk:
            emit_ln_to_fm(ctx, blk, x_in, row0, T, xnf, T, 0, (tag, "ln", b))

        load_w(wbuf, wk_d)
        with nc.Block() as blk:
            for tb in range(T // 512):
                for do in range(DCH):
                    gpar = (tb * DCH + do) % 2

                    def kmm(te, tb=tb, do=do, gpar=gpar):
                        ctx.wait_cp(te, (tag, "kev", b, tb * DCH + do - 2))
                        last = None
                        for c in range(DCH):
                            last = te.matmul(
                                psf[gpar][:, :],
                                wbuf[:, c * D + do * 128: c * D + (do + 1) * 128],
                                xnf[:, c * T + tb * 512: c * T + (tb + 1) * 512],
                                start=(c == 0), stop=(c == DCH - 1))
                        ctx.inc(pq, last)
                        ctx.mark((tag, "kmm", b, tb * DCH + do), pq)
                    blk.tensor(kmm)

                    def kev(v, tb=tb, do=do, gpar=gpar):
                        ctx.wait_cp(v, (tag, "kmm", b, tb * DCH + do))
                        ctx.inc(evq, v.tensor_copy(
                            Kf[:, do * T + tb * 512: do * T + (tb + 1) * 512],
                            psf[gpar][:, :]))
                        ctx.mark((tag, "kev", b, tb * DCH + do), evq)
                    blk.vector(kev)

        load_w(wbuf, wv_d)
        with nc.Block() as blk:
            for kt in range(T // 128):
                gpar = kt % 2

                def vmm(te, kt=kt, gpar=gpar):
                    ctx.wait_cp(te, (tag, "vev", b, kt - 2))
                    for half in range(2):
                        last = None
                        for c in range(DCH):
                            last = te.matmul(
                                psf[2 * gpar + half][:, :],
                                xnf[:, c * T + kt * 128: c * T + (kt + 1) * 128],
                                wbuf[:, c * D + half * 512: c * D + (half + 1) * 512],
                                start=(c == 0), stop=(c == DCH - 1))
                        ctx.inc(pq, last)
                    ctx.mark((tag, "vmm", b, kt), pq)
                blk.tensor(vmm)

                def vev(v, kt=kt, gpar=gpar):
                    ctx.wait_cp(v, (tag, "vmm", b, kt))
                    for half in range(2):
                        ctx.inc(evq, v.tensor_copy(
                            Vt[:, kt * D + half * 512: kt * D + (half + 1) * 512],
                            psf[2 * gpar + half][:, :]))
                    ctx.mark((tag, "vev", b, kt), evq)
                blk.vector(vev)

        load_w(wbuf, wq_d)
        load_w(wo, wo_d)
        with nc.Block() as blk:
            for qi in range(T // 128):
                qp = qi % 2
                klen = (qi + 1) * 128
                nkt = qi + 1
                nkb = (klen + 511) // 512
                q0 = row0 + qi * 128

                for rnd in range(2):
                    def qmm(te, qi=qi, rnd=rnd):
                        ctx.wait_cp(te, (tag, "qfree", b, qi - 1))
                        ctx.wait_cp(te, (tag, "qev", b, qi, rnd - 1))
                        if rnd == 0:
                            ctx.wait_cp(te, (tag, "qev", b, qi - 1, 1))
                        for do in range(4 * rnd, 4 * rnd + 4):
                            last = None
                            for c in range(DCH):
                                last = te.matmul(
                                    psf[2][:, (do % 4) * 128:(do % 4) * 128 + 128],
                                    wbuf[:, c * D + do * 128: c * D + (do + 1) * 128],
                                    xnf[:, c * T + qi * 128: c * T + (qi + 1) * 128],
                                    start=(c == 0), stop=(c == DCH - 1))
                            ctx.inc(pq, last)
                        ctx.mark((tag, "qmm", b, qi, rnd), pq)
                    blk.tensor(qmm)

                    def qev(v, qi=qi, qp=qp, rnd=rnd):
                        ctx.wait_cp(v, (tag, "qmm", b, qi, rnd))
                        ctx.inc(evq, v.tensor_scalar(
                            qt[qp][:, rnd * 512:(rnd + 1) * 512], psf[2][:, :],
                            scale, None, ALU.mult))
                        ctx.mark((tag, "qev", b, qi, rnd), evq)
                        if rnd == 1:
                            ctx.mark((tag, "qdone", b, qi), evq)
                    blk.vector(qev)

                def _front(h):
                    g = qi * NH + h
                    hp = h % 4
                    hps = h % 2
                    hp2 = h % 2
                    hp4 = h % 4
                    hr = 64 * (h % 2)
                    hc = (h // 2) * 128

                    for kb in range(nkb):
                        kw = min(512, klen - kb * 512)
                        sb_id = ctx.cp.get((tag, "sblkid", b), 0)
                        ctx.cp[(tag, "sblkid", b)] = sb_id + 1
                        par = sb_id % 2

                        def smm(te, qi=qi, h=h, kb=kb, kw=kw, sb_id=sb_id, par=par):
                            if kb == 0:
                                ctx.wait_cp(te, (tag, "qdone", b, qi))
                            ctx.wait_cp(te, (tag, "sfree", b, sb_id - 2))
                            last = te.matmul(
                                psf[par][:, 0:kw],
                                qt[qp][hr:hr + 64, hc:hc + 128],
                                Kf[hr:hr + 64,
                                   (h // 2) * T + kb * 512:
                                   (h // 2) * T + kb * 512 + kw],
                                start=True, stop=True)
                            ctx.inc(pvs, last)
                            ctx.mark((tag, "smm", b, sb_id), pvs)
                        blk.tensor(smm)

                        def sevac(v, qi=qi, h=h, g=g, kb=kb, kw=kw, sb_id=sb_id,
                                  par=par, nkb=nkb, hp=hp):
                            ctx.wait_cp(v, (tag, "smm", b, sb_id))
                            if kb == 0:
                                ctx.wait_cp(v, (tag, "exp", b, g - 2))
                            if kb == nkb - 1:
                                if kw > 128:
                                    ctx.inc(sev, v.tensor_copy(
                                        srow[hps][:, kb * 512: kb * 512 + kw - 128],
                                        psf[par][:, 0:kw - 128]))
                                ctx.inc(sev, v.tensor_tensor(
                                    srow[hps][:, kb * 512 + kw - 128: kb * 512 + kw],
                                    psf[par][:, kw - 128:kw],
                                    ctx.bufs["cmask"][:, :], ALU.add))
                            else:
                                ctx.inc(sev, v.tensor_copy(
                                    srow[hps][:, kb * 512: kb * 512 + kw],
                                    psf[par][:, 0:kw]))
                            ctx.mark((tag, "sfree", b, sb_id), sev)
                            if kb == nkb - 1:
                                ctx.mark((tag, "srow", b, g), sev)
                        blk.vector(sevac)

                    def dmax(v, g=g, klen=klen, hp=hp, hps=hps):
                        ctx.wait_cp(v, (tag, "srow", b, g))
                        ctx.wait_cp(v, (tag, "exp", b, g - 4))
                        ctx.inc(dvx, v.reduce_max(ngm[hp][:, :],
                                                  srow[hps][:, 0:klen], AX.X,
                                                  negate=True))
                        ctx.mark((tag, "max", b, g), dvx)
                    blk.vector(dmax)



                def _fexp(h0):
                    def aexp2(sc, h0=h0):
                        for h in (h0, h0 + 1):
                            g = qi * NH + h
                            hp = h % 4
                            hps = h % 2
                            hp4 = h % 4
                            ctx.wait_cp(sc, (tag, "max", b, g))
                            ctx.wait_cp(sc, (tag, "ptmm", b, g - 4))
                            ctx.inc(axp, sc.activation(pb[hp4][:, 0:klen],
                                                       srow[hps][:, 0:klen],
                                                       AF.Exp,
                                                       bias=ngm[hp][:, 0:1],
                                                       accum_out=sxp[hp4][:, 0:1]))
                            ctx.mark((tag, "exp", b, g), axp)
                    blk.scalar(aexp2)

                def _tail(h):
                    g = qi * NH + h
                    hp = h % 4
                    hps = h % 2
                    hp2 = h % 2
                    hp4 = h % 4
                    hr = 64 * (h % 2)
                    hc = (h // 2) * 128

                    def dscl(v, g=g, klen=klen, hp=hp, hp4=hp4):
                        ctx.wait_cp(v, (tag, "exp", b, g))
                        ctx.inc(dvx, v.reciprocal(rcp[hp][:, :], sxp[hp4][:, :]))
                        ctx.wait(v, dvx)
                        ctx.inc(dvx, v.tensor_scalar(pb[hp4][:, 0:klen],
                                                     pb[hp4][:, 0:klen],
                                                     rcp[hp][:, 0:1], None, ALU.mult))
                        ctx.mark((tag, "pscl", b, g), dvx)
                    blk.vector(dscl)

                    for r0 in range(0, nkt, 8):
                        r1 = min(nkt, r0 + 8)

                        def ptr(te, g=g, r0=r0, r1=r1, hp=hp, hp2=hp2, hp4=hp4):
                            ctx.wait_cp(te, (tag, "pscl", b, g))
                            if r0 == 0:
                                ctx.wait_cp(te, (tag, "ptev", b, g - 2, 0))
                                ctx.wait_cp(te, (tag, "ptev", b, g - 2, 8))
                            else:
                                ctx.wait_cp(te, (tag, "ptev", b, g, 0))
                            last = None
                            for kc in range(r0, r1):
                                last = te.transpose(
                                    psb[hp2][:, (kc % 8) * 128:(kc % 8) * 128 + 128],
                                    pb[hp4][:, kc * 128:(kc + 1) * 128], ident[:, :])
                            ctx.inc(pvm, last)
                            ctx.mark((tag, "ptmm", b, g), pvm)
                            ctx.mark((tag, "ptmm2", b, g, r0), pvm)
                        blk.tensor(ptr)

                        def ptev(v, g=g, r0=r0, r1=r1, hp=hp, hp2=hp2):
                            ctx.wait_cp(v, (tag, "ptmm2", b, g, r0))
                            if r0 == 0:
                                ctx.wait_cp(v, (tag, "pv", b, g - 2))
                            w = (r1 - r0) * 128
                            ctx.inc(sev, v.tensor_copy(
                                ptb[hp2][:, r0 * 128: r0 * 128 + w],
                                psb[hp2][:, 0:w]))
                            ctx.mark((tag, "ptev", b, g, r0), sev)
                            if r1 == nkt if False else True:
                                ctx.mark((tag, "ptevd", b, g), sev)
                        blk.vector(ptev)

                    def pvmm(te, g=g, h=h, nkt=nkt, hp=hp, hp2=hp2):
                        ctx.wait_cp(te, (tag, "ptevd", b, g))
                        if h % 2 == 0:
                            ctx.wait_cp(te, (tag, "yevd", b, g - 2))
                        last = None
                        for kc in range(nkt):
                            last = te.matmul(
                                psf[3][64 * (h % 2): 64 * (h % 2) + 64, 0:128],
                                Vt[:, kc * D + 64 * h: kc * D + 64 * h + 64],
                                ptb[hp2][:, kc * 128:(kc + 1) * 128],
                                start=(kc == 0), stop=(kc == nkt - 1))
                        ctx.inc(pvm, last)
                        ctx.mark((tag, "pv", b, g), pvm)
                    blk.tensor(pvmm)

                    if h % 2 == 1:
                        def yevac(v, g=g, qi=qi, h=h, qp=qp):
                            if h == 1:
                                ctx.wait_cp(v, (tag, "oproj", b, qi - 1))
                            ctx.wait_cp(v, (tag, "pv", b, g - 1))
                            ctx.wait_cp(v, (tag, "pv", b, g))
                            ctx.inc(sev, v.tensor_copy(
                                yfm[qp][:, (h // 2) * 128:(h // 2) * 128 + 128],
                                psf[3][:, 0:128]))
                            ctx.mark((tag, "yevd", b, g), sev)
                            ctx.mark((tag, "yevd", b, g - 1), sev)
                            if h == NH - 1:
                                ctx.mark((tag, "ydone", b, qi), sev)
                        blk.vector(yevac)


                for j in range(NH // 4 + 1):
                    if j >= 1:
                        for hh in range(4 * j - 4, 4 * j):
                            _tail(hh)
                    if j < NH // 4:
                        _front(4 * j)
                        _front(4 * j + 1)
                        _fexp(4 * j)
                        _front(4 * j + 2)
                        _front(4 * j + 3)
                        _fexp(4 * j + 2)

                def xold(sync, qi=qi, qp=qp, q0=q0, b=b):
                    ctx.wait_cp(sync, (tag, "xnfree", b * 16 + qi - 2))
                    ctx.inc(f"Mldx{qp}", sync.dma_start(out=xnt[qp][:, :],
                                                in_=x_in[q0:q0 + 128, :]), 16)
                    ctx.mark((tag, "xold", b, qi), f"Mldx{qp}")
                blk.sync(xold)

                def oproj(te, qi=qi, qp=qp):
                    ctx.wait_cp(te, (tag, "ydone", b, qi))
                    ctx.wait_cp(te, (tag, "ofree", b, qi - 1))
                    for half in range(2):
                        last = None
                        for c in range(DCH):
                            last = te.matmul(
                                psw[:, half * 512:(half + 1) * 512],
                                yfm[qp][:, c * 128:(c + 1) * 128],
                                wo[:, c * D + half * 512: c * D + (half + 1) * 512],
                                start=(c == 0), stop=(c == DCH - 1))
                        ctx.inc(pop, last)
                    ctx.mark((tag, "oproj", b, qi), pop)
                    ctx.mark((tag, "qfree", b, qi), pop)
                blk.tensor(oproj)

                def resid(v, qi=qi, qp=qp, b=b):
                    ctx.wait_cp(v, (tag, "oproj", b, qi))
                    ctx.wait_cp(v, (tag, "xold", b, qi))
                    ctx.wait_cp(v, (tag, "xnfree", b * 16 + qi - 2))
                    ctx.inc(res, v.tensor_tensor(xnt[qp][:, :], psw[:, :],
                                                 xnt[qp][:, :], ALU.add))
                    ctx.mark((tag, "resid", b, qi), res)
                    ctx.mark((tag, "ofree", b, qi), res)
                blk.vector(resid)

                def store(gp, qi=qi, qp=qp, q0=q0, b=b):
                    ctx.wait_cp(gp, (tag, "resid", b, qi))
                    ctx.inc(f"Mst{qp}", gp.dma_start(out=x_out[q0:q0 + 128, :],
                                              in_=xnt[qp][:, :]), 16)
                    ctx.mark((tag, "xnfree", b * 16 + qi), f"Mst{qp}")
                blk.gpsimd(store)




HMOE = 2048
HCH = HMOE // 128
MOE_DEBUG = None

def emit_moe(ctx, x_in, x_out, wg_d, w1_d, b1_d, w2_d, b2_d, tag):
    nc = ctx.nc
    psf, psw, psb = ctx.ps_pool()
    ST = 1024
    NST = NTOK // ST

    xnf = [ctx.ssb(f"moex{p}", (128, DCH * ST), BF16) for p in range(2)]
    yac = ctx.ssb("moey", (128, ST * 8), F32)
    hfm = [ctx.ssb(f"moeh{p}", (128, HCH * 512), BF16) for p in range(2)]
    w1b = ctx.ssb("moew1", (128, DCH * HMOE), BF16)
    w2b = ctx.ssb("moew2", (128, HCH * D), BF16)
    wgb = ctx.ssb("moewg", (128, DCH * 8), BF16)
    b1r = ctx.ssb("moeb1", (1, HMOE), BF16)
    b2r = ctx.ssb("moeb2", (1, D), BF16)
    ones = ctx.ssb("moeones", (1, 512), BF16)
    gates = [ctx.ssb(f"moeg{p}", (128, 64), F32) for p in range(2)]
    ge8 = ctx.ssb("moege", (128, 8), F32)
    gm8 = ctx.ssb("moegm", (128, 1), F32)
    gsum = ctx.ssb("moegs", (128, 1), F32)
    grc = ctx.ssb("moegrc", (128, 1), F32)
    gt1 = ctx.ssb("moegt1", (128, 1), F32)
    gt2 = ctx.ssb("moegt2", (128, 1), F32)
    gmk = ctx.ssb("moegmk", (128, 8), F32)
    ge2 = ctx.ssb("moege2", (128, 8), F32)

    wl, pmm, rl, yev, sto, gdv, gac = (f"O{n}" for n in
                                       ("wl", "pmm", "rl", "yev", "sto", "gdv", "gac"))

    with nc.Block() as blk:
        def g0(gp):
            ctx.inc("sv", gp.memset(ones[:, :], 1.0))
            for c in range(DCH):
                ctx.inc(wl, gp.dma_start(out=wgb[:, c * 8:(c + 1) * 8],
                                         in_=wg_d[c * 128:(c + 1) * 128, :]), 16)
            ctx.wait(gp, wl)
        blk.gpsimd(g0)

    def emit_ln_u(u, blk, sub_range=None):
        emit_ln_to_fm(ctx, blk, x_in, u * ST, ST, xnf[u % 2], ST, 0,
                      (tag, "ln", u), sub_range=sub_range)

    def emit_gate_u(u, blk):
        for s in range(8):
            def gmm(te, s=s, u=u):
                ctx.wait_cp(te, (tag, "gex", u, s - 2))
                ctx.wait_cp(te, ((tag, "ln", u), "done"))
                ctx.wait_cp(te, (tag, "relu", u - 1, 7, 1, 14 + (s % 2)))
                last = None
                for c in range(DCH):
                    last = te.matmul(
                        psf[s % 2][:, 0:8],
                        xnf[u % 2][:, c * ST + s * 128: c * ST + s * 128 + 128],
                        wgb[:, c * 8:(c + 1) * 8],
                        start=(c == 0), stop=(c == DCH - 1))
                ctx.inc(pmm, last)
                ctx.mark((tag, "gmm", u, s), pmm)
            blk.tensor(gmm)

            def gsm(v, s=s, u=u):
                ctx.wait_cp(v, (tag, "gmm", u, s))
                ctx.wait_cp(v, (tag, "gex", u, s - 1))
                sl = psf[s % 2][:, 0:8]
                ctx.inc(gdv, v.reduce_max(gm8[:, :], sl, AX.X, negate=True))
                ctx.mark((tag, "gng", u, s), gdv)
            blk.vector(gsm)

            def gex(sc, s=s, u=u):
                ctx.wait_cp(sc, (tag, "gng", u, s))
                ctx.wait_cp(sc, (tag, "gdone", u, s - 1))
                ctx.inc(gac, sc.activation(ge8[:, :], psf[s % 2][:, 0:8],
                                           AF.Exp, bias=gm8[:, 0:1],
                                           accum_out=gsum[:, 0:1]))
                ctx.mark((tag, "gex", u, s), gac)
            blk.scalar(gex)

            def gtop(v, s=s, u=u):
                ctx.wait_cp(v, (tag, "gex", u, s))
                ctx.inc(gdv, v.reciprocal(grc[:, :], gsum[:, :]))
                ctx.inc(gdv, v.reduce_max(gt1[:, :], ge8[:, :], AX.X))
                ctx.wait(v, gdv)
                ctx.inc(gdv, v.tensor_scalar(gmk[:, :], ge8[:, :], gt1[:, 0:1],
                                             None, ALU.is_ge))
                ctx.wait(v, gdv)
                ctx.inc(gdv, v.scalar_tensor_tensor(ge2[:, :], gmk[:, :], -1e30,
                                                    ge8[:, :], ALU.mult, ALU.add))
                ctx.wait(v, gdv)
                ctx.inc(gdv, v.reduce_max(gt2[:, :], ge2[:, :], AX.X))
                ctx.wait(v, gdv)
                ctx.inc(gdv, v.tensor_scalar(gmk[:, :], ge8[:, :], gt2[:, 0:1],
                                             None, ALU.is_ge))
                ctx.wait(v, gdv)
                ctx.inc(gdv, v.tensor_tensor(ge2[:, :], ge8[:, :], gmk[:, :],
                                             ALU.mult))
                ctx.wait(v, gdv)
                ctx.inc(gdv, v.tensor_scalar(gates[u % 2][:, s * 8:(s + 1) * 8],
                                             ge2[:, :], grc[:, 0:1],
                                             None, ALU.mult))
                ctx.mark((tag, "gdone", u, s), gdv)
            blk.vector(gtop)

    with nc.Block() as blk:
        emit_ln_u(0, blk)
    with nc.Block() as blk:
        emit_gate_u(0, blk)
        def w1pre(gp):
            for c in range(DCH):
                ctx.inc("Ow1", gp.dma_start(
                    out=w1b[:, c * HMOE:(c + 1) * HMOE],
                    in_=w1_d[0, c * 128:(c + 1) * 128, :]), 16)
            ctx.inc("Ow1", gp.dma_start(out=b1r[:, :], in_=b1_d[0:1, :]), 16)
            ctx.mark((tag, "w1L", 0, 0), "Ow1")
        blk.gpsimd(w1pre)

    for u in range(NST):
        row0 = u * ST
        with nc.Block() as blk:
            for s in range(8):
                def yinit(sync, s=s, u=u):
                    ctx.wait_cp(sync, (tag, "ystored", u - 1))
                    ctx.inc(sto, sync.dma_start(
                        out=yac[:, s * D:(s + 1) * D],
                        in_=x_in[row0 + s * 128: row0 + (s + 1) * 128, :]), 16)
                    if s == 7:
                        ctx.mark((tag, "yinit", u), sto)
                blk.sync(yinit)

            n_e = 0 if MOE_DEBUG in ("gate", "ln", "noln") else (1 if MOE_DEBUG == "ne1" else 8)
            def w1load_fn(gp, e, u, ue, uw):
                    ctx.wait_cp(gp, (tag, "w1free", uw, (e - 1) % 8))
                    for c in range(DCH):
                        ctx.inc("Ow1", gp.dma_start(
                            out=w1b[:, c * HMOE:(c + 1) * HMOE],
                            in_=w1_d[e, c * 128:(c + 1) * 128, :]), 16)
                    ctx.inc("Ow1", gp.dma_start(out=b1r[:, :], in_=b1_d[e:e + 1, :]), 16)
                    ctx.mark((tag, "w1L", ue, e), "Ow1")

            for e in range(n_e):
                if e > 0 or (u == 0 and n_e != 8):
                    blk.gpsimd(lambda gp, e=e, u=u: w1load_fn(gp, e, u, u, u))

                for half in range(2):
                    hp = half
                    for ho in range(HCH):
                        par = ho % 2

                        def w1mm(te, e=e, half=half, ho=ho, par=par, hp=hp, u=u):
                            ctx.wait_cp(te, (tag, "w1L", u, e))
                            if e == 0 and half == 0 and ho == 0:
                                ctx.wait_cp(te, ((tag, "ln", u), "done"))
                            if ho < 2:
                                pe_, ph_ = (e, half - 1) if half > 0 else (e - 1, 1)
                                ctx.wait_cp(te, (tag, "relu", u, pe_, ph_,
                                                 HCH - 2 + ho))
                            else:
                                ctx.wait_cp(te, (tag, "relu", u, e, half, ho - 2))
                            last = None
                            for c in range(DCH):
                                last = te.matmul(
                                    psf[par][:, :],
                                    w1b[:, c * HMOE + ho * 128: c * HMOE + (ho + 1) * 128],
                                    xnf[u % 2][:, c * ST + half * 512: c * ST + (half + 1) * 512],
                                    start=(c == 0), stop=False)
                            last = te.matmul(
                                psf[par][:, :],
                                b1r[0:1, ho * 128:(ho + 1) * 128],
                                ones[0:1, :],
                                start=False, stop=True)
                            ctx.inc(pmm, last)
                            ctx.mark((tag, "w1mm", u, e, half, ho), pmm)
                            if half == 1 and ho == HCH - 1:
                                ctx.mark((tag, "w1free", u, e), pmm)
                        blk.tensor(w1mm)

                        def relu(v, e=e, half=half, ho=ho, par=par, hp=hp, u=u):
                            ctx.wait_cp(v, (tag, "w1mm", u, e, half, ho))
                            ctx.inc(rl, v.tensor_scalar_max(
                                hfm[hp][:, ho * 512:(ho + 1) * 512],
                                psf[par][:, :], 0.0))
                            ctx.mark((tag, "relu", u, e, half, ho), rl)
                        blk.vector(relu)

                    if half == 0:
                        def w2load(gp, e=e, u=u):
                            ctx.wait_cp(gp, (tag, "w2free", u, e - 1))
                            for hc in range(HCH):
                                ctx.inc("Ow2", gp.dma_start(
                                    out=w2b[:, hc * D:(hc + 1) * D],
                                    in_=w2_d[e, hc * 128:(hc + 1) * 128, :]), 16)
                            ctx.inc("Ow2", gp.dma_start(out=b2r[:, :],
                                                     in_=b2_d[e:e + 1, :]), 16)
                            ctx.mark((tag, "w2L", u, e), "Ow2")
                        blk.gpsimd(w2load)

                    for tl in (range(0) if MOE_DEBUG == "w1" else range(4)):
                        s = half * 4 + tl

                        def w2mm(te, e=e, half=half, tl=tl, s=s, hp=hp, u=u):
                            ctx.wait_cp(te, (tag, "relu", u, e, half, HCH - 1))
                            ctx.wait_cp(te, (tag, "w2L", u, e))
                            ctx.wait_cp(te, (tag, "yev", u, e, s - 2))
                            wt = ([psw[:, 0:512], psw[:, 512:1024]] if s % 2 == 0
                                  else [psf[2][:, :], psf[3][:, :]])
                            for hc in range(HCH):
                                for dh in range(2):
                                    te.matmul(
                                        wt[dh],
                                        hfm[hp][:, hc * 512 + tl * 128:
                                                 hc * 512 + (tl + 1) * 128],
                                        w2b[:, hc * D + dh * 512: hc * D + (dh + 1) * 512],
                                        start=(hc == 0), stop=False)
                            last = None
                            for dh in range(2):
                                last = te.matmul(
                                    wt[dh],
                                    ones[0:1, 0:128],
                                    b2r[0:1, dh * 512:(dh + 1) * 512],
                                    start=False, stop=True)
                            ctx.inc(pmm, last)
                            ctx.mark((tag, "w2mm", u, e, s), pmm)
                            if s == 7:
                                ctx.mark((tag, "w2free", u, e), pmm)
                        blk.tensor(w2mm)

                        def w2ev(v, e=e, s=s, u=u):
                            ctx.wait_cp(v, (tag, "w2mm", u, e, s))
                            if e == 0:
                                ctx.wait_cp(v, (tag, "yinit", u))
                                ctx.wait_cp(v, (tag, "gdone", u, 7))
                            gcol = gates[u % 2][:, s * 8 + e: s * 8 + e + 1]
                            if s % 2 == 0:
                                ctx.inc(yev, v.scalar_tensor_tensor(
                                    yac[:, s * D:(s + 1) * D], psw[:, :], gcol,
                                    yac[:, s * D:(s + 1) * D], ALU.mult, ALU.add))
                            else:
                                for dh in range(2):
                                    c0 = s * D + dh * 512
                                    ctx.inc(yev, v.scalar_tensor_tensor(
                                        yac[:, c0:c0 + 512], psf[2 + dh][:, :],
                                        gcol, yac[:, c0:c0 + 512],
                                        ALU.mult, ALU.add))
                            ctx.mark((tag, "yev", u, e, s), yev)
                        blk.vector(w2ev)

                if u + 1 < NST and n_e == 8:
                    emit_ln_u(u + 1, blk, sub_range=(e, e + 1))
                if e == 7 and n_e == 8:
                    un = u + 1 if u + 1 < NST else None
                    if un is not None:
                        blk.gpsimd(lambda gp, u=u, un=un:
                                   w1load_fn(gp, 0, u, un, u))

            if u + 1 < NST and n_e != 8:
                emit_ln_u(u + 1, blk)

            if u + 1 < NST:
                emit_gate_u(u + 1, blk)

            for s in range(8):
                def yst(sync, s=s, u=u):
                    if n_e == 0:
                        ctx.wait_cp(sync, (tag, "yinit", u))
                    else:
                        ctx.wait_cp(sync, (tag, "yev", u, n_e - 1, s))
                    i = sync.dma_start(out=x_out[row0 + s * 128: row0 + (s + 1) * 128, :],
                                       in_=yac[:, s * D:(s + 1) * D])
                    ctx.inc("Yst", i, 16)
                    if s == 7:
                        ctx.mark((tag, "ystored", u), "Yst")
                blk.sync(yst)


def moe_host_weights(Wg, W1, b1, W2, b2, prefix):
    return {
        f"{prefix}wg": _np(Wg),
        f"{prefix}w1": _np(W1),
        f"{prefix}b1": _np(b1),
        f"{prefix}w2": _np(W2),
        f"{prefix}b2": _np(b2),
    }


def build_moe_test():
    es = ExitStack()
    nc = bass.Bass("TRN2", target_bir_lowering=False)
    x_in = nc.dram_tensor("x_in", [NTOK, D], F32, kind="ExternalInput").ap()
    wg = nc.dram_tensor("wg", [D, 8], F32, kind="ExternalInput").ap()
    w1 = nc.dram_tensor("w1", [8, D, HMOE], F32, kind="ExternalInput").ap()
    b1 = nc.dram_tensor("b1", [8, HMOE], F32, kind="ExternalInput").ap()
    w2 = nc.dram_tensor("w2", [8, HMOE, D], F32, kind="ExternalInput").ap()
    b2 = nc.dram_tensor("b2", [8, D], F32, kind="ExternalInput").ap()
    x_out = nc.dram_tensor("x_out", [NTOK, D], F32, kind="ExternalOutput").ap()
    ctx = Ctx(nc, es)
    ctx.ps_pool()
    emit_idents(ctx)
    ctx.open_stage()
    emit_moe(ctx, x_in, x_out, wg, w1, b1, w2, b2, "mo")
    ctx.close_stage()
    emit_drain_outputs(ctx)
    es.close()
    return nc



def emit_tmix(ctx, x_in, x_out, kwT_d, vwT_d, owT_d, tsg_d, lnw_d, lnb_d,
              td_d, tf_d, tag):
    nc = ctx.nc
    psf, psw, psb = ctx.ps_pool()
    ident = ctx.bufs["ident_bf"]
    identf = ctx.bufs["ident_f32"]
    N = 64

    G = ctx.ssb("txG", (128, D), F32)
    OMG = ctx.ssb("txOMG", (128, D), F32)
    kTf = ctx.ssb("txkTf", (128, DCH * N), F32)
    vTf = ctx.ssb("txvTf", (128, DCH * N), F32)
    kTs = ctx.ssb("txkTs", (128, DCH * N), BF16)
    vTs = ctx.ssb("txvTs", (128, DCH * N), BF16)
    kTb = ctx.ssb("txkTb", (128, DCH * N), BF16)
    vTb = ctx.ssb("txvTb", (128, DCH * N), BF16)
    lnw = ctx.ssb("txlnw", (128, DCH), F32)
    lnbb = ctx.ssb("txlnb", (128, DCH), BF16)
    oT = ctx.ssb("txoT", (64, D), BF16)
    tsg = ctx.ssb("txtsg", (1, D), F32)
    grow = ctx.ssb("txgrow", (1, D), F32)
    onesf = ctx.ssb("txonesf", (1, 128), F32)
    tdc = ctx.ssb("txtd", (64, 1), F32)
    tfc = ctx.ssb("txtf", (64, 1), F32)
    wcol = ctx.ssb("txw", (64, 1), F32)
    netf = ctx.ssb("txnetf", (64, 1), F32)
    kbias = ctx.ssb("txkb", (64, 1), F32)
    vbias = ctx.ssb("txvb", (64, 1), F32)
    kbr = ctx.ssb("txkbr", (1, 128), F32)
    xt = [ctx.ssb(f"txxt{p}", (128, D), F32) for p in range(2)]
    xr = [ctx.ssb(f"txxr{p}", (128, D), F32) for p in range(2)]
    aa = [ctx.ssb(f"txaa{p}", (128, D), F32) for p in range(2)]
    ar = [ctx.ssb(f"txar{p}", (128, D), F32) for p in range(2)]
    xs = [ctx.ssb(f"txxs{p}", (128, D), F32) for p in range(2)]
    sta = [ctx.ssb(f"txsa{p}", (128, 12), F32) for p in range(2)]
    str_ = [ctx.ssb(f"txsr{p}", (128, 12), F32) for p in range(2)]
    sts = [ctx.ssb(f"txss{p}", (128, 12), F32) for p in range(2)]
    mva = [ctx.ssb(f"txma{p}", (128, 2), F32) for p in range(2)]
    mvr = [ctx.ssb(f"txmr{p}", (128, 2), F32) for p in range(2)]
    mvs = [ctx.ssb(f"txms{p}", (128, 2), F32) for p in range(2)]
    rsa = [ctx.ssb(f"txra{p}", (128, 1), F32) for p in range(2)]
    rsr = [ctx.ssb(f"txrr{p}", (128, 1), F32) for p in range(2)]
    rss = [ctx.ssb(f"txrz{p}", (128, 1), F32) for p in range(2)]
    nma = [ctx.ssb(f"txna{p}", (128, 1), F32) for p in range(2)]
    nmr = [ctx.ssb(f"txnr{p}", (128, 1), F32) for p in range(2)]
    nms = [ctx.ssb(f"txnz{p}", (128, 1), F32) for p in range(2)]
    t1m = [ctx.ssb(f"txt1{p}", (128, D), F32) for p in range(2)]
    xn2 = [ctx.ssb(f"txn2{p}", (128, D), BF16) for p in range(2)]
    xn2f = ctx.ssb("txn2f", (128, DCH * 512), BF16)
    kbuf = ctx.ssb("txkv", (64, 512), F32)
    vbuf = ctx.ssb("txvv", (64, 512), F32)
    ekb = ctx.ssb("txek", (64, 512), F32)
    WKV = ctx.ssb("txwkv", (64, T), F32)
    states = ctx.ssb("txsts2", (64, T), F32)
    stb = ctx.ssb("txstb", (64, T), BF16)
    xot = [ctx.ssb(f"txxo{p}", (128, D), F32) for p in range(2)]
    xnt = [ctx.ssb(f"txxn{p}", (128, D), F32) for p in range(2)]

    dv, ac, pe, ev = "Tdv", "Tac", "Tpe", "Tev"

    with nc.Block() as blk:
        def wload(gp):
            for (dst, srcd) in ((kTf, kwT_d), (vTf, vwT_d)):
                for c in range(DCH):
                    ctx.inc("Twl", gp.dma_start(
                        out=dst[:, c * N:(c + 1) * N],
                        in_=srcd[c * 128:(c + 1) * 128, :]), 16)
            for (dst, srcd) in ((kTb, kwT_d), (vTb, vwT_d)):
                for c in range(DCH):
                    ctx.inc("Twl", gp.dma_start(
                        out=dst[:, c * N:(c + 1) * N],
                        in_=srcd[c * 128:(c + 1) * 128, :]), 16)
            ctx.inc("Twl", gp.dma_start(out=oT[:, :], in_=owT_d[:, :]), 16)
            ctx.inc("Twl", gp.dma_start(out=tsg[:, :], in_=tsg_d[:, :]), 16)
            ctx.inc("Twl", gp.dma_start(out=lnw[:, :], in_=lnw_d[:, :]), 16)
            ctx.inc("Twl", gp.dma_start(out=lnbb[:, :], in_=lnb_d[:, :]), 16)
            ctx.inc("Twl", gp.dma_start(out=tdc[:, :], in_=td_d[:, :]), 16)
            ctx.inc("Twl", gp.dma_start(out=tfc[:, :], in_=tf_d[:, :]), 16)
            ctx.wait(gp, "Twl")
            ctx.inc("sv", gp.memset(onesf[:, :], 1.0))
            ctx.wait(gp, "sv")
        blk.gpsimd(wload)

    with nc.Block() as blk:
        def setup_act(sc):
            ctx.inc(ac, sc.activation(grow[:, :], tsg[:, :], AF.Sigmoid))
            ctx.inc(ac, sc.activation(wcol[:, :], tdc[:, :], AF.Exp))
            ctx.inc(ac, sc.activation(netf[:, :], tfc[:, :], AF.Exp))
            ctx.mark((tag, "sact"), ac)
        blk.scalar(setup_act)

        def setup_dv(v):
            ctx.wait_cp(v, (tag, "sact"))
            ctx.inc(dv, v.tensor_scalar(netf[:, :], netf[:, :], -1.0, None,
                                        ALU.mult))
            for c in range(DCH):
                ctx.inc(dv, v.tensor_scalar(kTs[:, c * N:(c + 1) * N],
                                            kTf[:, c * N:(c + 1) * N],
                                            lnw[:, c:c + 1], None, ALU.mult))
                ctx.inc(dv, v.tensor_scalar(vTs[:, c * N:(c + 1) * N],
                                            vTf[:, c * N:(c + 1) * N],
                                            lnw[:, c:c + 1], None, ALU.mult))
            ctx.mark((tag, "sdv"), dv)
        blk.vector(setup_dv)

        def setup_pe(te):
            ctx.wait_cp(te, (tag, "sdv"))
            for half in range(2):
                last = te.matmul(psw[:, half * 512:(half + 1) * 512],
                                 onesf[0:1, :],
                                 grow[0:1, half * 512:(half + 1) * 512],
                                 start=True, stop=True)
                ctx.inc(pe, last)
            for c in range(DCH):
                te.matmul(psf[0][0:1, 0:N], lnbb[:, c:c + 1],
                          kTb[:, c * N:(c + 1) * N],
                          start=(c == 0), stop=(c == DCH - 1))
            for c in range(DCH):
                last = te.matmul(psf[0][0:1, N:2 * N], lnbb[:, c:c + 1],
                                 vTb[:, c * N:(c + 1) * N],
                                 start=(c == 0), stop=(c == DCH - 1))
            ctx.inc(pe, last)
            ctx.mark((tag, "spe"), pe)
        blk.tensor(setup_pe)

        def setup_ev(v):
            ctx.wait_cp(v, (tag, "spe"))
            ctx.inc(ev, v.tensor_copy(G[:, :], psw[:, :]))
            ctx.inc(ev, v.tensor_copy(kbr[:, :], psf[0][0:1, 0:128]))
            ctx.wait(v, ev)
            ctx.inc(ev, v.tensor_scalar(OMG[:, :], G[:, :], -1.0, 1.0,
                                        ALU.mult, ALU.add))
            ctx.mark((tag, "sev"), ev)
        blk.vector(setup_ev)

        def setup_pe2(te):
            ctx.wait_cp(te, (tag, "sev"))
            ctx.inc(pe, te.transpose(psf[1][:, 0:1], kbr[:, :], identf[0:1, 0:1]))
            ctx.mark((tag, "spe2"), pe)
        blk.tensor(setup_pe2)

        def setup_ev2(v):
            ctx.wait_cp(v, (tag, "spe2"))
            ctx.inc(ev, v.tensor_copy(kbias[:, :], psf[1][0:N, 0:1]))
            ctx.inc(ev, v.tensor_copy(vbias[:, :], psf[1][N:2 * N, 0:1]))
        blk.vector(setup_ev2)

    for b in range(BL):
        row0 = b * T
        with nc.Block() as blk:
            for grp in range(4):
                for sl in range(4):
                    s = grp * 4 + sl
                    p = s % 2

                    def dma_in(sync, s=s, p=p):
                        ctx.wait_cp(sync, (tag, "xtfree", b, s - 2))
                        r0 = row0 + s * 128
                        rr = row0 + ((s * 128 + 1024) % T)
                        ctx.inc(f"Tld{p}", sync.dma_start(
                            out=xt[p][:, :], in_=x_in[r0:r0 + 128, :]), 16)
                        ctx.inc(f"Tld{p}", sync.dma_start(
                            out=xr[p][:, :], in_=x_in[rr:rr + 128, :]), 16)
                        ctx.mark((tag, "xload", b, s), f"Tld{p}")
                    blk.sync(dma_in)

                    def bn_ar(v, s=s, p=p):
                        ctx.wait_cp(v, (tag, "xload", b, s))
                        for (src, stt, mvv, i) in ((xt[p], sta[p], mva[p], 0),
                                                   (xr[p], str_[p], mvr[p], 1)):
                            ctx.inc(dv, v.bn_stats(stt[:, 0:6], src[:, 0:512]))
                            ctx.inc(dv, v.bn_stats(stt[:, 6:12], src[:, 512:1024]))
                            ctx.wait(v, dv)
                            ctx.inc(dv, v.bn_aggr(mvv[:, :], stt[:, :]))
                            ctx.mark((tag, "agg", b, s, i), dv)
                    blk.vector(bn_ar)

                    def rstds(sc, s=s, p=p):
                        for (mvv, rsv, i) in ((mva[p], rsa[p], 0),
                                              (mvr[p], rsr[p], 1)):
                            ctx.wait_cp(sc, (tag, "agg", b, s, i))
                            ctx.inc(ac, sc.activation(rsv[:, :], mvv[:, 1:2],
                                                      AF.Ln, bias=ctx.bufs["eps"][:, 0:1]))
                            ctx.wait(sc, ac)
                            ctx.inc(ac, sc.activation(rsv[:, :], rsv[:, :],
                                                      AF.Exp, scale=-0.5))
                            ctx.mark((tag, "rstd", b, s, i), ac)
                    blk.scalar(rstds)

                    def nm_dv(v, s=s, p=p):
                        for (mvv, rsv, nmv, i) in ((mva[p], rsa[p], nma[p], 0),
                                                   (mvr[p], rsr[p], nmr[p], 1)):
                            ctx.wait_cp(v, (tag, "rstd", b, s, i))
                            ctx.inc(dv, v.scalar_tensor_tensor(
                                nmv[:, :], mvv[:, 0:1], -1.0, rsv[:, :],
                                ALU.mult, ALU.mult))
                        ctx.mark((tag, "nm", b, s), dv)
                    blk.vector(nm_dv)

                    def norms(sc, s=s, p=p):
                        ctx.wait_cp(sc, (tag, "nm", b, s))
                        ctx.wait_cp(sc, (tag, "aafree", b, s - 2))
                        ctx.inc(ac, sc.activation(aa[p][:, :], xt[p][:, :],
                                                  AF.Identity, bias=nma[p][:, 0:1],
                                                  scale=rsa[p][:, 0:1]))
                        ctx.inc(ac, sc.activation(ar[p][:, :], xr[p][:, :],
                                                  AF.Identity, bias=nmr[p][:, 0:1],
                                                  scale=rsr[p][:, 0:1]))
                        ctx.mark((tag, "norm", b, s), ac)
                        ctx.mark((tag, "xtfree", b, s), ac)
                    blk.scalar(norms)

                    def mix(v, s=s, p=p):
                        ctx.wait_cp(v, (tag, "norm", b, s))
                        ctx.wait_cp(v, (tag, "xsfree", b, s - 2))
                        ctx.inc(dv, v.tensor_tensor(t1m[p][:, :], ar[p][:, :],
                                                    G[:, :], ALU.mult))
                        ctx.inc(dv, v.tensor_tensor(xs[p][:, :], aa[p][:, :],
                                                    OMG[:, :], ALU.mult))
                        ctx.wait(v, dv)
                        ctx.inc(dv, v.tensor_tensor(xs[p][:, :], xs[p][:, :],
                                                    t1m[p][:, :], ALU.add))
                        ctx.mark((tag, "mix", b, s), dv)
                        ctx.mark((tag, "aafree", b, s), dv)
                        ctx.wait(v, dv)
                        ctx.inc(dv, v.bn_stats(sts[p][:, 0:6], xs[p][:, 0:512]))
                        ctx.inc(dv, v.bn_stats(sts[p][:, 6:12], xs[p][:, 512:1024]))
                        ctx.wait(v, dv)
                        ctx.inc(dv, v.bn_aggr(mvs[p][:, :], sts[p][:, :]))
                        ctx.mark((tag, "agg2", b, s), dv)
                    blk.vector(mix)

                    def rstd2(sc, s=s, p=p):
                        ctx.wait_cp(sc, (tag, "agg2", b, s))
                        ctx.inc(ac, sc.activation(rss[p][:, :], mvs[p][:, 1:2],
                                                  AF.Ln, bias=ctx.bufs["eps"][:, 0:1]))
                        ctx.wait(sc, ac)
                        ctx.inc(ac, sc.activation(rss[p][:, :], rss[p][:, :],
                                                  AF.Exp, scale=-0.5))
                        ctx.mark((tag, "rstd2", b, s), ac)
                    blk.scalar(rstd2)

                    def nm2_dv(v, s=s, p=p):
                        ctx.wait_cp(v, (tag, "rstd2", b, s))
                        ctx.inc(dv, v.scalar_tensor_tensor(
                            nms[p][:, :], mvs[p][:, 0:1], -1.0, rss[p][:, :],
                            ALU.mult, ALU.mult))
                        ctx.mark((tag, "nm2", b, s), dv)
                    blk.vector(nm2_dv)

                    def norm2(sc, s=s, p=p):
                        ctx.wait_cp(sc, (tag, "nm2", b, s))
                        ctx.wait_cp(sc, (tag, "xn2free", b, s - 2))
                        ctx.inc(ac, sc.activation(xn2[p][:, :], xs[p][:, :],
                                                  AF.Identity, bias=nms[p][:, 0:1],
                                                  scale=rss[p][:, 0:1]))
                        ctx.mark((tag, "norm2", b, s), ac)
                        ctx.mark((tag, "xsfree", b, s), ac)
                    blk.scalar(norm2)

                    def tpose(te, s=s, p=p):
                        ctx.wait_cp(te, (tag, "norm2", b, s))
                        ctx.wait_cp(te, (tag, "tpfree", b, s - 2))
                        last = None
                        for c in range(DCH):
                            last = te.transpose(psb[p][:, c * 128:(c + 1) * 128],
                                                xn2[p][:, c * 128:(c + 1) * 128],
                                                ident[:, :])
                        ctx.inc(pe, last)
                        ctx.mark((tag, "tp", b, s), pe)
                        ctx.mark((tag, "xn2free", b, s), pe)
                    blk.tensor(tpose)

                    def tevac(v, s=s, p=p, sl=sl, grp=grp):
                        ctx.wait_cp(v, (tag, "tp", b, s))
                        if sl == 0:
                            ctx.wait_cp(v, (tag, "kvmm", b, grp - 1))
                        dst = xn2f[:, :].rearrange("q (c u) -> q c u", c=DCH)[
                            :, :, sl * 128:(sl + 1) * 128]
                        src = psb[p][:, :].rearrange("q (c u) -> q c u", c=DCH)
                        ctx.inc(ev, v.tensor_copy(dst, src))
                        ctx.mark((tag, "tpfree", b, s), ev)
                        ctx.mark((tag, "xn2f", b, s), ev)
                    blk.vector(tevac)

                def kvmm(te, grp=grp, b=b):
                    ctx.wait_cp(te, (tag, "xn2f", b, grp * 4 + 3))
                    ctx.wait_cp(te, (tag, "kvev", b, grp - 1))
                    for (dstps, wT) in ((psf[0], kTs), (psf[1], vTs)):
                        last = None
                        for c in range(DCH):
                            last = te.matmul(
                                dstps[0:N, :], wT[:, c * N:(c + 1) * N],
                                xn2f[:, c * 512:(c + 1) * 512],
                                start=(c == 0), stop=(c == DCH - 1))
                        ctx.inc(pe, last)
                    ctx.mark((tag, "kvmm", b, grp), pe)
                blk.tensor(kvmm)

                def kvev(v, grp=grp, b=b):
                    ctx.wait_cp(v, (tag, "kvmm", b, grp))
                    ctx.inc(dv, v.tensor_scalar(kbuf[:, :], psf[0][0:N, :],
                                                kbias[:, 0:1], None, ALU.add))
                    ctx.inc(dv, v.tensor_scalar(vbuf[:, :], psf[1][0:N, :],
                                                vbias[:, 0:1], None, ALU.add))
                    ctx.mark((tag, "kvev", b, grp), dv)
                blk.vector(kvev)

                def ekact(sc, grp=grp, b=b):
                    ctx.wait_cp(sc, (tag, "kvev", b, grp))
                    ctx.wait_cp(sc, (tag, "wkvd", b, grp - 1))
                    ctx.inc(ac, sc.activation(ekb[:, :], kbuf[:, :], AF.Exp,
                                              scale=netf[:, 0:1]))
                    ctx.mark((tag, "ek", b, grp), ac)
                blk.scalar(ekact)

                def wkvmul(v, grp=grp, b=b):
                    ctx.wait_cp(v, (tag, "ek", b, grp))
                    ctx.inc(dv, v.tensor_tensor(WKV[:, grp * 512:(grp + 1) * 512],
                                                ekb[:, :], vbuf[:, :], ALU.mult))
                    ctx.mark((tag, "wkv", b, grp), dv)
                    ctx.mark((tag, "wkvd", b, grp), dv)
                blk.vector(wkvmul)

            def scan(v, b=b):
                ctx.wait_cp(v, (tag, "wkv", b, 3))
                w_b = bass.AP(wcol, 0, [[1, N], [0, T]])
                ctx.inc(dv, v.tensor_tensor_scan(states[:, :], w_b, WKV[:, :],
                                                 0.0, ALU.mult, ALU.add))
                ctx.wait(v, dv)
                ctx.inc(dv, v.tensor_copy(stb[:, :], states[:, :]))
                ctx.mark((tag, "scan", b), dv)
            blk.vector(scan)

            for s in range(16):
                p = s % 2

                def xold(sync, s=s, p=p):
                    ctx.wait_cp(sync, (tag, "xofree", b, s - 2))
                    ctx.inc(f"Tlo{p}", sync.dma_start(
                        out=xot[p][:, :],
                        in_=x_in[row0 + s * 128: row0 + (s + 1) * 128, :]), 16)
                    ctx.mark((tag, "xold", b, s), f"Tlo{p}")
                blk.sync(xold)

                def oproj(te, s=s, p=p):
                    ctx.wait_cp(te, (tag, "scan", b))
                    ctx.wait_cp(te, (tag, "resid", b, s - 1))
                    last = None
                    for half in range(2):
                        last = te.matmul(psw[:, half * 512:(half + 1) * 512],
                                         stb[:, s * 128:(s + 1) * 128],
                                         oT[:, half * 512:(half + 1) * 512],
                                         start=True, stop=True)
                        ctx.inc(pe, last)
                    ctx.mark((tag, "omm", b, s), pe)
                blk.tensor(oproj)

                def resid(v, s=s, p=p, b=b):
                    ctx.wait_cp(v, (tag, "omm", b, s))
                    ctx.wait_cp(v, (tag, "xold", b, s))
                    ctx.wait_cp(v, (tag, "xnfree", b * 16 + s - 2))
                    ctx.inc(dv, v.tensor_tensor(xnt[p][:, :], psw[:, :],
                                                xot[p][:, :], ALU.add))
                    ctx.mark((tag, "resid", b, s), dv)
                    ctx.mark((tag, "xofree", b, s), dv)
                blk.vector(resid)

                def store(gp, s=s, p=p, b=b):
                    ctx.wait_cp(gp, (tag, "resid", b, s))
                    ctx.inc(f"Tst{p}", gp.dma_start(
                        out=x_out[row0 + s * 128: row0 + (s + 1) * 128, :],
                        in_=xnt[p][:, :]), 16)
                    ctx.mark((tag, "xnfree", b * 16 + s), f"Tst{p}")
                blk.gpsimd(store)


def build_tmix_test():
    es = ExitStack()
    nc = bass.Bass("TRN2", target_bir_lowering=False)
    x_in = nc.dram_tensor("x_in", [NTOK, D], F32, kind="ExternalInput").ap()
    kwT = nc.dram_tensor("kwT", [D, 64], F32, kind="ExternalInput").ap()
    vwT = nc.dram_tensor("vwT", [D, 64], F32, kind="ExternalInput").ap()
    owT = nc.dram_tensor("owT", [64, D], F32, kind="ExternalInput").ap()
    tsg = nc.dram_tensor("tsg", [1, D], F32, kind="ExternalInput").ap()
    lnw = nc.dram_tensor("lnw", [128, DCH], F32, kind="ExternalInput").ap()
    lnb = nc.dram_tensor("lnb", [128, DCH], F32, kind="ExternalInput").ap()
    td = nc.dram_tensor("td", [64, 1], F32, kind="ExternalInput").ap()
    tf = nc.dram_tensor("tf", [64, 1], F32, kind="ExternalInput").ap()
    x_out = nc.dram_tensor("x_out", [NTOK, D], F32, kind="ExternalOutput").ap()
    ctx = Ctx(nc, es)
    ctx.ps_pool()
    emit_idents(ctx)
    ctx.open_stage()
    emit_tmix(ctx, x_in, x_out, kwT, vwT, owT, tsg, lnw, lnb, td, tf, "tx")
    ctx.close_stage()
    emit_drain_outputs(ctx)
    es.close()
    return nc



MSCALE = 1.0 / np.sqrt(96.0)

def emit_mla(ctx, x_in, x_out, wq_d, wqr_d, wkr_d, wkvd_d, wku_d, wvu_d, wo_d,
             lobo_d, rcos_d, rsin_d, tag):
    nc = ctx.nc
    psf, psw, psb = ctx.ps_pool()
    ident = ctx.bufs["ident_bf"]

    Kc = ctx.ssb("mlaKc", (128, DCH * T), BF16)
    Krf = ctx.ssb("mlaKr", (128, T), BF16)
    Vt = ctx.ssb("mlaVt", (128, (T // 128) * D), BF16)
    xnf = ctx.ssb("mlaxnf", (128, DCH * T), BF16)
    latf = ctx.ssb("mlalat", (128, 2 * 512), BF16)
    wbuf = ctx.ssb("mlawb", (128, DCH * D), BF16)
    wo2 = ctx.ssb("mlawo2", (128, DCH * D), BF16)
    wqrb = ctx.ssb("mlawqr", (128, DCH * 512), BF16)
    wkrb = ctx.ssb("mlawkr", (128, DCH * 32), BF16)
    wkvb = ctx.ssb("mlawkv", (128, DCH * 256), BF16)
    wkub = ctx.ssb("mlawku", (128, 2 * D), BF16)
    wvub = ctx.ssb("mlawvu", (128, 2 * D), BF16)
    rcos = ctx.ssb("mlacos", (128, 256), F32)
    rsin = ctx.ssb("mlasin", (128, 256), F32)
    lobo = ctx.ssb("mlalobo", (1, 16), F32)
    Cb = ctx.ssb("mlaCb", (128, 16), F32)
    onesf = ctx.ssb("mlaones", (1, 128), F32)
    crow = ctx.ssb("mlacrow", (1, 16), F32)
    qt = [ctx.ssb("mlaqt", (128, DCH * 128), BF16)] * 2
    qrf = [ctx.ssb("mlaqrf", (128, 512), BF16)] * 2
    qrt = ctx.ssb("mlaqrt", (128, 512), BF16)
    ta_ = ctx.ssb("mlata", (128, 256), F32)
    tb_ = ctx.ssb("mlatb", (128, 256), F32)
    krt = ctx.ssb("mlakrt", (128, 32), BF16)
    srow = [ctx.ssb("mlasr", (128, T), F32)] * 2
    pb = [ctx.ssb(f"mlapb{p}", (128, T), BF16) for p in range(2)]
    ptb = [ctx.ssb(f"mlapt{p}", (128, T), BF16) for p in range(2)]
    yfm = [ctx.ssb("mlayf", (128, DCH * 128), BF16)] * 2
    mx = [ctx.ssb(f"mlamx{p}", (128, 1), F32) for p in range(2)]
    ngm = [ctx.ssb(f"mlangm{p}", (128, 1), F32) for p in range(2)]
    sxp = [ctx.ssb(f"mlasxp{p}", (128, 1), F32) for p in range(2)]
    rcp = [ctx.ssb(f"mlarcp{p}", (128, 1), F32) for p in range(2)]
    xnt = [ctx.ssb("mlaxnt", (128, D), F32)] * 2
    xot = xnt

    wl, pq, evq, pvs, sev, dvx, axp, pvm, pop, res = (
        f"A{n}" for n in ("wl", "pq", "evq", "pvs", "sev", "dvx",
                          "axp", "pvm", "pop", "res"))

    with nc.Block() as blk:
        def g0(gp):
            for c in range(DCH):
                ctx.inc(wl, gp.dma_start(out=wqrb[:, c * 512:(c + 1) * 512],
                                         in_=wqr_d[c * 128:(c + 1) * 128, :]), 16)
                ctx.inc(wl, gp.dma_start(out=wkrb[:, c * 32:(c + 1) * 32],
                                         in_=wkr_d[c * 128:(c + 1) * 128, :]), 16)
                ctx.inc(wl, gp.dma_start(out=wkvb[:, c * 256:(c + 1) * 256],
                                         in_=wkvd_d[c * 128:(c + 1) * 128, :]), 16)
            for l2 in range(2):
                ctx.inc(wl, gp.dma_start(out=wkub[:, l2 * D:(l2 + 1) * D],
                                         in_=wku_d[l2 * 128:(l2 + 1) * 128, :]), 16)
                ctx.inc(wl, gp.dma_start(out=wvub[:, l2 * D:(l2 + 1) * D],
                                         in_=wvu_d[l2 * 128:(l2 + 1) * 128, :]), 16)
            ctx.inc(wl, gp.dma_start(out=rcos[:, :], in_=rcos_d[:, :]), 16)
            ctx.inc(wl, gp.dma_start(out=rsin[:, :], in_=rsin_d[:, :]), 16)
            ctx.inc(wl, gp.dma_start(out=lobo[:, :], in_=lobo_d[:, :]), 16)
            ctx.wait(gp, wl)
            ctx.inc("sv", gp.memset(onesf[:, :], 1.0))
            ctx.wait(gp, "sv")
        blk.gpsimd(g0)

    with nc.Block() as blk:
        def cexp(sc):
            ctx.inc(axp, sc.activation(crow[:, :], lobo[:, :], AF.Exp))
            ctx.mark((tag, "cexp"), axp)
        blk.scalar(cexp)

        def cbc(te):
            ctx.wait_cp(te, (tag, "cexp"))
            ctx.inc(pq, te.matmul(psf[0][:, 0:16], onesf[0:1, :], crow[0:1, :],
                                  start=True, stop=True))
            ctx.mark((tag, "cbc"), pq)
        blk.tensor(cbc)

        def cev(v):
            ctx.wait_cp(v, (tag, "cbc"))
            ctx.inc(sev, v.tensor_copy(Cb[:, :], psf[0][:, 0:16]))
        blk.vector(cev)

    def load_w(dst, src_dram, width):
        with nc.Block() as blk:
            def g(gp):
                for c in range(DCH):
                    ctx.inc(wl, gp.dma_start(
                        out=dst[:, c * width:(c + 1) * width],
                        in_=src_dram[c * 128:(c + 1) * 128, :]), 16)
                ctx.wait(gp, wl)
            blk.gpsimd(g)

    for b in range(BL):
        row0 = b * T

        with nc.Block() as blk:
            emit_ln_to_fm(ctx, blk, x_in, row0, T, xnf, T, 0, (tag, "ln", b))

        with nc.Block() as blk:
            for tb in range(T // 512):
                def latmm(te, tb=tb):
                    ctx.wait_cp(te, (tag, "latev", b, tb - 1))
                    ctx.wait_cp(te, (tag, "krrope", b, tb * 4 - 1))
                    for l2 in range(2):
                        last = None
                        for c in range(DCH):
                            last = te.matmul(
                                psf[l2][:, :],
                                wkvb[:, c * 256 + l2 * 128: c * 256 + (l2 + 1) * 128],
                                xnf[:, c * T + tb * 512: c * T + (tb + 1) * 512],
                                start=(c == 0), stop=(c == DCH - 1))
                        ctx.inc(pq, last)
                    ctx.mark((tag, "latmm", b, tb), pq)
                blk.tensor(latmm)

                def latev(v, tb=tb):
                    ctx.wait_cp(v, (tag, "latmm", b, tb))
                    ctx.wait_cp(v, (tag, "kvdone", b, tb - 1))
                    for l2 in range(2):
                        ctx.inc(evq, v.tensor_copy(latf[:, l2 * 512:(l2 + 1) * 512],
                                                   psf[l2][:, :]))
                    ctx.mark((tag, "latev", b, tb), evq)
                blk.vector(latev)

                for do in range(DCH):
                    def kcmm(te, tb=tb, do=do):
                        ctx.wait_cp(te, (tag, "latev", b, tb))
                        ctx.wait_cp(te, (tag, "kcev", b, tb, do - 2))
                        last = None
                        for l2 in range(2):
                            last = te.matmul(
                                psf[2 + do % 2][:, :],
                                wkub[:, l2 * D + do * 128: l2 * D + (do + 1) * 128],
                                latf[:, l2 * 512:(l2 + 1) * 512],
                                start=(l2 == 0), stop=(l2 == 1))
                        ctx.inc(pq, last)
                        ctx.mark((tag, "kcmm", b, tb, do), pq)
                    blk.tensor(kcmm)

                    def kcev(v, tb=tb, do=do):
                        ctx.wait_cp(v, (tag, "kcmm", b, tb, do))
                        ctx.inc(evq, v.tensor_copy(
                            Kc[:, do * T + tb * 512: do * T + (tb + 1) * 512],
                            psf[2 + do % 2][:, :]))
                        ctx.mark((tag, "kcev", b, tb, do), evq)
                    blk.vector(kcev)

                for tl in range(4):
                    kt = tb * 4 + tl

                    def vmm(te, tb=tb, tl=tl, kt=kt):
                        ctx.wait_cp(te, (tag, "kcev", b, tb, DCH - 1))
                        ctx.wait_cp(te, (tag, "vev", b, kt - 1))
                        for half in range(2):
                            last = None
                            for l2 in range(2):
                                last = te.matmul(
                                    psf[2 * (tl % 2) + half][:, :] if False else
                                    psf[2 + half][:, :],
                                    latf[:, l2 * 512 + tl * 128: l2 * 512 + (tl + 1) * 128],
                                    wvub[:, l2 * D + half * 512: l2 * D + (half + 1) * 512],
                                    start=(l2 == 0), stop=(l2 == 1))
                            ctx.inc(pq, last)
                        ctx.mark((tag, "vmm", b, kt), pq)
                    blk.tensor(vmm)

                    def vev(v, kt=kt):
                        ctx.wait_cp(v, (tag, "vmm", b, kt))
                        for half in range(2):
                            ctx.inc(evq, v.tensor_copy(
                                Vt[:, kt * D + half * 512: kt * D + (half + 1) * 512],
                                psf[2 + half][:, :]))
                        ctx.mark((tag, "vev", b, kt), evq)
                    blk.vector(vev)

                for tl in range(4):
                    kt = tb * 4 + tl

                    def krmm(te, tb=tb, tl=tl, kt=kt):
                        ctx.wait_cp(te, (tag, "krrope", b, kt - 1))
                        last = None
                        for c in range(DCH):
                            last = te.matmul(
                                psf[0][:, 0:32],
                                xnf[:, c * T + kt * 128: c * T + (kt + 1) * 128],
                                wkrb[:, c * 32:(c + 1) * 32],
                                start=(c == 0), stop=(c == DCH - 1))
                        ctx.inc(pq, last)
                        ctx.mark((tag, "krmm", b, kt), pq)
                    blk.tensor(krmm)

                    def krrope(v, kt=kt):
                        ctx.wait_cp(v, (tag, "krmm", b, kt))
                        ctx.wait_cp(v, (tag, "krtp", b, kt - 1))
                        cs = rcos[:, (kt % 16) * 16:(kt % 16) * 16 + 16]
                        sn = rsin[:, (kt % 16) * 16:(kt % 16) * 16 + 16]
                        q1 = psf[0][:, 0:16]
                        q2 = psf[0][:, 16:32]
                        ctx.inc(dvx, v.tensor_tensor(ta_[:, 0:16], q1, cs, ALU.mult))
                        ctx.inc(dvx, v.tensor_tensor(tb_[:, 0:16], q2, sn, ALU.mult))
                        ctx.wait(v, dvx)
                        ctx.inc(dvx, v.tensor_tensor(krt[:, 0:16], ta_[:, 0:16],
                                                     tb_[:, 0:16], ALU.subtract))
                        ctx.wait(v, dvx)
                        ctx.inc(dvx, v.tensor_tensor(ta_[:, 0:16], q2, cs, ALU.mult))
                        ctx.inc(dvx, v.tensor_tensor(tb_[:, 0:16], q1, sn, ALU.mult))
                        ctx.wait(v, dvx)
                        ctx.inc(dvx, v.tensor_tensor(krt[:, 16:32], ta_[:, 0:16],
                                                     tb_[:, 0:16], ALU.add))
                        ctx.mark((tag, "krrope", b, kt), dvx)
                    blk.vector(krrope)

                    def krtp(te, kt=kt):
                        ctx.wait_cp(te, (tag, "krrope", b, kt))
                        ctx.wait_cp(te, (tag, "krev", b, kt - 1))
                        ctx.inc(pvm, te.transpose(psb[0][:, 0:32] if False else
                                                  psb[0][0:32, 0:128],
                                                  krt[:, 0:32], ident[:, :]))
                        ctx.mark((tag, "krtp", b, kt), pvm)
                    blk.tensor(krtp)

                    def krev(v, kt=kt):
                        ctx.wait_cp(v, (tag, "krtp", b, kt))
                        for a in range(4):
                            ctx.inc(evq, v.tensor_copy(
                                Krf[32 * a:32 * a + 32, kt * 128:(kt + 1) * 128],
                                psb[0][0:32, 0:128]))
                        ctx.mark((tag, "krev", b, kt), evq)
                    blk.vector(krev)

            ctx.mark((tag, "kvdone", b, 100), evq)

        if b == 0:
            load_w(wbuf, wq_d, D)
            load_w(wo2, wo_d, D)
        with nc.Block() as blk:
            for qi in range(T // 128):
                qp = qi % 2
                klen = (qi + 1) * 128
                nkt = qi + 1
                nkb = (klen + 511) // 512
                q0 = row0 + qi * 128

                for rnd in range(2):
                    def qmm(te, qi=qi, rnd=rnd):
                        ctx.wait_cp(te, (tag, "qfree", b, qi - 1))
                        ctx.wait_cp(te, (tag, "qev", b, qi, rnd - 1))
                        if rnd == 0:
                            ctx.wait_cp(te, (tag, "qev", b, qi - 1, 1))
                        for do in range(4 * rnd, 4 * rnd + 4):
                            last = None
                            for c in range(DCH):
                                last = te.matmul(
                                    psf[2][:, (do % 4) * 128:(do % 4) * 128 + 128],
                                    wbuf[:, c * D + do * 128: c * D + (do + 1) * 128],
                                    xnf[:, c * T + qi * 128: c * T + (qi + 1) * 128],
                                    start=(c == 0), stop=(c == DCH - 1))
                            ctx.inc(pq, last)
                        ctx.mark((tag, "qmm", b, qi, rnd), pq)
                    blk.tensor(qmm)

                    def qev(v, qi=qi, qp=qp, rnd=rnd):
                        ctx.wait_cp(v, (tag, "qmm", b, qi, rnd))
                        ctx.inc(evq, v.tensor_scalar(
                            qt[qp][:, rnd * 512:(rnd + 1) * 512], psf[2][:, :],
                            MSCALE, None, ALU.mult))
                        ctx.mark((tag, "qev", b, qi, rnd), evq)
                        if rnd == 1:
                            ctx.mark((tag, "qdone", b, qi), evq)
                    blk.vector(qev)

                def qrmm(te, qi=qi):
                    ctx.wait_cp(te, (tag, "qrrope", b, qi - 1))
                    ctx.wait_cp(te, (tag, "yevd", b, qi * NH - 2))
                    last = None
                    for c in range(DCH):
                        last = te.matmul(
                            psf[3][:, :],
                            xnf[:, c * T + qi * 128: c * T + (qi + 1) * 128],
                            wqrb[:, c * 512:(c + 1) * 512],
                            start=(c == 0), stop=(c == DCH - 1))
                    ctx.inc(pq, last)
                    ctx.mark((tag, "qrmm", b, qi), pq)
                blk.tensor(qrmm)

                def qrrope(v, qi=qi):
                    ctx.wait_cp(v, (tag, "qrmm", b, qi))
                    ctx.wait_cp(v, (tag, "qrtp", b, qi - 1))
                    cs = bass.AP(rcos, (qi % 16) * 16, [[256, 128], [0, 16], [1, 16]])
                    sn = bass.AP(rsin, (qi % 16) * 16, [[256, 128], [0, 16], [1, 16]])
                    q1 = psf[3][:, :].rearrange("p (h i) -> p h i", i=32)[:, :, 0:16]
                    q2 = psf[3][:, :].rearrange("p (h i) -> p h i", i=32)[:, :, 16:32]
                    o1 = qrt[:, :].rearrange("p (h i) -> p h i", i=32)[:, :, 0:16]
                    o2 = qrt[:, :].rearrange("p (h i) -> p h i", i=32)[:, :, 16:32]
                    tav = ta_[:, :].rearrange("p (h i) -> p h i", i=16)
                    tbv = tb_[:, :].rearrange("p (h i) -> p h i", i=16)
                    ctx.inc(dvx, v.tensor_tensor(tav, q1, cs, ALU.mult))
                    ctx.inc(dvx, v.tensor_tensor(tbv, q2, sn, ALU.mult))
                    ctx.wait(v, dvx)
                    ctx.inc(dvx, v.tensor_tensor(o1, tav, tbv, ALU.subtract))
                    ctx.wait(v, dvx)
                    ctx.inc(dvx, v.tensor_tensor(tav, q2, cs, ALU.mult))
                    ctx.inc(dvx, v.tensor_tensor(tbv, q1, sn, ALU.mult))
                    ctx.wait(v, dvx)
                    ctx.inc(dvx, v.tensor_tensor(o2, tav, tbv, ALU.add))
                    ctx.mark((tag, "qrrope", b, qi), dvx)
                blk.vector(qrrope)

                def qrtp(te, qi=qi):
                    ctx.wait_cp(te, (tag, "qrrope", b, qi))
                    ctx.wait_cp(te, (tag, "qrev", b, qi - 1))
                    last = None
                    for j in range(4):
                        last = te.transpose(psb[0][:, j * 128:(j + 1) * 128],
                                            qrt[:, j * 128:(j + 1) * 128],
                                            ident[:, :])
                    ctx.inc(pvm, last)
                    ctx.mark((tag, "qrtp", b, qi), pvm)
                blk.tensor(qrtp)

                def qrev(v, qi=qi, qp=qp):
                    ctx.wait_cp(v, (tag, "qrtp", b, qi))
                    ctx.wait_cp(v, (tag, "qrffree", b, qi - 1))
                    ctx.inc(evq, v.tensor_scalar(qrf[qp][:, :], psb[0][:, 0:512],
                                                 MSCALE, None, ALU.mult))
                    ctx.mark((tag, "qrev", b, qi), evq)
                blk.vector(qrev)

                def _front(h):
                    g = qi * NH + h
                    hp = h % 2
                    hr = 64 * (h % 2)
                    hc = (h // 2) * 128
                    ja = h % 4

                    for kb in range(nkb):
                        kw = min(512, klen - kb * 512)
                        sb_id = ctx.cp.get((tag, "sblkid", b), 0)
                        ctx.cp[(tag, "sblkid", b)] = sb_id + 1
                        par = sb_id % 2

                        def smm(te, qi=qi, h=h, kb=kb, kw=kw, sb_id=sb_id,
                                par=par, hr=hr, hc=hc, ja=ja, qp=qp):
                            if kb == 0 and h == 0:
                                ctx.wait_cp(te, (tag, "qdone", b, qi))
                                ctx.wait_cp(te, (tag, "qrev", b, qi))
                            ctx.wait_cp(te, (tag, "sfree", b, sb_id - 2))
                            te.matmul(
                                psf[par][:, 0:kw],
                                qt[qp][hr:hr + 64, hc:hc + 128],
                                Kc[hr:hr + 64,
                                   (h // 2) * T + kb * 512:
                                   (h // 2) * T + kb * 512 + kw],
                                start=True, stop=False)
                            last = te.matmul(
                                psf[par][:, 0:kw],
                                qrf[qp][32 * ja:32 * ja + 32,
                                        (h // 4) * 128:(h // 4) * 128 + 128],
                                Krf[32 * ja:32 * ja + 32, kb * 512: kb * 512 + kw],
                                start=False, stop=True,
                                tile_position=(32 * ja, 0))
                            ctx.inc(pvs, last)
                            ctx.mark((tag, "smm", b, sb_id), pvs)
                        blk.tensor(smm)

                        def sevac(v, g=g, kb=kb, kw=kw, sb_id=sb_id, par=par,
                                  nkb=nkb, hp=hp):
                            ctx.wait_cp(v, (tag, "smm", b, sb_id))
                            if kb == 0:
                                ctx.wait_cp(v, (tag, "exp", b, g - 1))
                            if kb == nkb - 1:
                                if kw > 128:
                                    ctx.inc(sev, v.tensor_copy(
                                        srow[hp][:, kb * 512: kb * 512 + kw - 128],
                                        psf[par][:, 0:kw - 128]))
                                ctx.inc(sev, v.tensor_tensor(
                                    srow[hp][:, kb * 512 + kw - 128: kb * 512 + kw],
                                    psf[par][:, kw - 128:kw],
                                    ctx.bufs["cmask"][:, :], ALU.add))
                            else:
                                ctx.inc(sev, v.tensor_copy(
                                    srow[hp][:, kb * 512: kb * 512 + kw],
                                    psf[par][:, 0:kw]))
                            ctx.mark((tag, "sfree", b, sb_id), sev)
                            if kb == nkb - 1:
                                ctx.mark((tag, "srow", b, g), sev)
                        blk.vector(sevac)

                    def dmax(v, g=g, klen=klen, hp=hp):
                        ctx.wait_cp(v, (tag, "srow", b, g))
                        ctx.wait_cp(v, (tag, "exp", b, g - 2))
                        ctx.inc(dvx, v.reduce_max(ngm[hp][:, :],
                                                  srow[hp][:, 0:klen], AX.X,
                                                  negate=True))
                        ctx.mark((tag, "max", b, g), dvx)
                    blk.vector(dmax)

                    def aexp(sc, g=g, klen=klen, hp=hp):
                        ctx.wait_cp(sc, (tag, "max", b, g))
                        ctx.wait_cp(sc, (tag, "ptmm", b, g - 2))
                        ctx.inc(axp, sc.activation(pb[hp][:, 0:klen],
                                                   srow[hp][:, 0:klen],
                                                   AF.Exp, bias=ngm[hp][:, 0:1],
                                                   accum_out=sxp[hp][:, 0:1]))
                        ctx.mark((tag, "exp", b, g), axp)
                    blk.scalar(aexp)


                def _tail(h):
                    g = qi * NH + h
                    hp = h % 2
                    hr = 64 * (h % 2)
                    hc = (h // 2) * 128
                    ja = h % 4

                    def dscl(v, g=g, h=h, klen=klen, hp=hp):
                        ctx.wait_cp(v, (tag, "exp", b, g))
                        ctx.inc(dvx, v.tensor_scalar(rcp[hp][:, :], sxp[hp][:, :],
                                                     Cb[:, h:h + 1], None, ALU.add))
                        ctx.wait(v, dvx)
                        ctx.inc(dvx, v.reciprocal(rcp[hp][:, :], rcp[hp][:, :]))
                        ctx.wait(v, dvx)
                        ctx.inc(dvx, v.tensor_scalar(pb[hp][:, 0:klen],
                                                     pb[hp][:, 0:klen],
                                                     rcp[hp][:, 0:1], None, ALU.mult))
                        ctx.mark((tag, "pscl", b, g), dvx)
                    blk.vector(dscl)

                    for r0 in range(0, nkt, 8):
                        r1 = min(nkt, r0 + 8)

                        def ptr(te, g=g, r0=r0, r1=r1, hp=hp):
                            ctx.wait_cp(te, (tag, "pscl", b, g))
                            if r0 == 0:
                                ctx.wait_cp(te, (tag, "ptev", b, g - 2, 0))
                                ctx.wait_cp(te, (tag, "ptev", b, g - 2, 8))
                            else:
                                ctx.wait_cp(te, (tag, "ptev", b, g, 0))
                            last = None
                            for kc in range(r0, r1):
                                last = te.transpose(
                                    psb[hp][:, (kc % 8) * 128:(kc % 8) * 128 + 128],
                                    pb[hp][:, kc * 128:(kc + 1) * 128], ident[:, :])
                            ctx.inc(pvm, last)
                            ctx.mark((tag, "ptmm", b, g), pvm)
                            ctx.mark((tag, "ptmm2", b, g, r0), pvm)
                        blk.tensor(ptr)

                        def ptev(v, g=g, r0=r0, r1=r1, hp=hp, nkt=nkt):
                            ctx.wait_cp(v, (tag, "ptmm2", b, g, r0))
                            if r0 == 0:
                                ctx.wait_cp(v, (tag, "pv", b, g - 2))
                            w = (r1 - r0) * 128
                            ctx.inc(sev, v.tensor_copy(
                                ptb[hp][:, r0 * 128: r0 * 128 + w],
                                psb[hp][:, 0:w]))
                            ctx.mark((tag, "ptev", b, g, r0), sev)
                            ctx.mark((tag, "ptevd", b, g), sev)
                        blk.vector(ptev)

                    def pvmm(te, g=g, h=h, nkt=nkt, hp=hp):
                        ctx.wait_cp(te, (tag, "ptevd", b, g))
                        if h % 2 == 0:
                            ctx.wait_cp(te, (tag, "yevd", b, g - 2))
                        last = None
                        for kc in range(nkt):
                            last = te.matmul(
                                psf[3][64 * (h % 2): 64 * (h % 2) + 64, 0:128],
                                Vt[:, kc * D + 64 * h: kc * D + 64 * h + 64],
                                ptb[hp][:, kc * 128:(kc + 1) * 128],
                                start=(kc == 0), stop=(kc == nkt - 1))
                        ctx.inc(pvm, last)
                        ctx.mark((tag, "pv", b, g), pvm)
                    blk.tensor(pvmm)

                    if h % 2 == 1:
                        def yevac(v, g=g, qi=qi, h=h, qp=qp):
                            if h == 1:
                                ctx.wait_cp(v, (tag, "oproj", b, qi - 1))
                            ctx.wait_cp(v, (tag, "pv", b, g - 1))
                            ctx.wait_cp(v, (tag, "pv", b, g))
                            ctx.inc(sev, v.tensor_copy(
                                yfm[qp][:, (h // 2) * 128:(h // 2) * 128 + 128],
                                psf[3][:, 0:128]))
                            ctx.mark((tag, "yevd", b, g), sev)
                            ctx.mark((tag, "yevd", b, g - 1), sev)
                            if h == NH - 1:
                                ctx.mark((tag, "ydone", b, qi), sev)
                                ctx.mark((tag, "qrffree", b, qi), sev)
                        blk.vector(yevac)


                for h in range(NH + 1):
                    if h < NH:
                        _front(h)
                    if h >= 1:
                        _tail(h - 1)

                def xold(sync, qi=qi, qp=qp, q0=q0, b=b):
                    ctx.wait_cp(sync, (tag, "xnfree", b * 16 + qi - 1))
                    ctx.inc(f"Aldx{qp}", sync.dma_start(out=xnt[qp][:, :],
                                                        in_=x_in[q0:q0 + 128, :]), 16)
                    ctx.mark((tag, "xold", b, qi), f"Aldx{qp}")
                blk.sync(xold)

                def oproj(te, qi=qi, qp=qp):
                    ctx.wait_cp(te, (tag, "ydone", b, qi))
                    ctx.wait_cp(te, (tag, "ofree", b, qi - 1))
                    for half in range(2):
                        last = None
                        for c in range(DCH):
                            last = te.matmul(
                                psw[:, half * 512:(half + 1) * 512],
                                yfm[qp][:, c * 128:(c + 1) * 128],
                                wo2[:, c * D + half * 512:
                                        c * D + (half + 1) * 512],
                                start=(c == 0), stop=(c == DCH - 1))
                        ctx.inc(pop, last)
                    ctx.mark((tag, "oproj", b, qi), pop)
                    ctx.mark((tag, "qfree", b, qi), pop)
                blk.tensor(oproj)

                def resid(v, qi=qi, qp=qp, b=b):
                    ctx.wait_cp(v, (tag, "oproj", b, qi))
                    ctx.wait_cp(v, (tag, "xold", b, qi))
                    ctx.wait_cp(v, (tag, "xnfree", b * 16 + qi - 1))
                    ctx.inc(res, v.tensor_tensor(xnt[qp][:, :], psw[:, :],
                                                 xnt[qp][:, :], ALU.add))
                    ctx.mark((tag, "resid", b, qi), res)
                    ctx.mark((tag, "ofree", b, qi), res)
                blk.vector(resid)

                def store(gp, qi=qi, qp=qp, q0=q0, b=b):
                    ctx.wait_cp(gp, (tag, "resid", b, qi))
                    ctx.inc(f"Ast{qp}", gp.dma_start(out=x_out[q0:q0 + 128, :],
                                                     in_=xnt[qp][:, :]), 16)
                    ctx.mark((tag, "xnfree", b * 16 + qi), f"Ast{qp}")
                blk.gpsimd(store)


def build_mla_test():
    es = ExitStack()
    nc = bass.Bass("TRN2", target_bir_lowering=False)
    x_in = nc.dram_tensor("x_in", [NTOK, D], F32, kind="ExternalInput").ap()
    wq = nc.dram_tensor("wq", [D, D], F32, kind="ExternalInput").ap()
    wqr = nc.dram_tensor("wqr", [D, 512], F32, kind="ExternalInput").ap()
    wkr = nc.dram_tensor("wkr", [D, 32], F32, kind="ExternalInput").ap()
    wkvd = nc.dram_tensor("wkvd", [D, 256], F32, kind="ExternalInput").ap()
    wku = nc.dram_tensor("wku", [256, D], F32, kind="ExternalInput").ap()
    wvu = nc.dram_tensor("wvu", [256, D], F32, kind="ExternalInput").ap()
    wo = nc.dram_tensor("wo", [D, D], F32, kind="ExternalInput").ap()
    lobo = nc.dram_tensor("lobo", [1, 16], F32, kind="ExternalInput").ap()
    rcos = nc.dram_tensor("rcos", [128, 256], F32, kind="ExternalInput").ap()
    rsin = nc.dram_tensor("rsin", [128, 256], F32, kind="ExternalInput").ap()
    x_out = nc.dram_tensor("x_out", [NTOK, D], F32, kind="ExternalOutput").ap()
    ctx = Ctx(nc, es)
    ctx.ps_pool()
    emit_idents(ctx)
    ctx.open_stage()
    emit_mla(ctx, x_in, x_out, wq, wqr, wkr, wkvd, wku, wvu, wo,
             lobo, rcos, rsin, "ml")
    ctx.close_stage()
    emit_drain_outputs(ctx)
    es.close()
    return nc



NSTEP = 32
TSH = NTOK // NSTEP
H4 = 4096


def emit_lstm(ctx, x_in, hs_out, w_d, u_d, bias_d, xp_dram, tag, q_cb=None):
    nc = ctx.nc
    psf, psw, psb = ctx.ps_pool()
    ident = ctx.bufs["ident_bf"]

    ub = ctx.ssb("lsub", (128, DCH * H4), BF16)
    biasr = ctx.ssb("lsbias", (1, H4), BF16)
    onesb = ctx.ssb("lsones", (1, 128), BF16)

    dv, ac, pe, ev = "Ldv2", "Lac2", "Lpe2", "Lev2"

    with nc.Block() as blk:
        def uload(gp):
            for c in range(DCH):
                ctx.inc("Lwu", gp.dma_start(
                    out=ub[:, c * H4:(c + 1) * H4],
                    in_=u_d[c * 128:(c + 1) * 128, :]), 16)
            ctx.inc("Lwu", gp.dma_start(out=biasr[:, :], in_=bias_d[:, :]), 16)
            ctx.wait(gp, "Lwu")
            ctx.inc("sv", gp.memset(onesb[:, :], 1.0))
            ctx.wait(gp, "sv")
        blk.gpsimd(uload)

    ctx.push_scope()
    xnf = ctx.ssb("lsxnf", (128, DCH * 512), BF16)
    wch = ctx.ssb("lswc", (128, DCH * 512), BF16)
    xpev = [ctx.ssb(f"lsxe{p}", (128, 512), F32) for p in range(2)]

    for g in range(NTOK // 512):
        r0 = g * 512
        with nc.Block() as blk:
            emit_bf_to_fm(ctx, blk, x_in, r0, 512, xnf, 512, 0,
                          (tag, "ln", g))
        with nc.Block() as blk:
            for nb in range(8):
                def wload(gp, g=g, nb=nb):
                    ctx.wait_cp(gp, (tag, "xpmm", g, nb - 1, 3))
                    for c in range(DCH):
                        ctx.inc("Lww", gp.dma_start(
                            out=wch[:, c * 512:(c + 1) * 512],
                            in_=w_d[c * 128:(c + 1) * 128,
                                    nb * 512:(nb + 1) * 512]), 16)
                    ctx.mark((tag, "wld", g, nb), "Lww")
                blk.gpsimd(wload)

                for rb in range(4):
                    par = (nb * 4 + rb) % 2

                    def xpmm(te, g=g, nb=nb, rb=rb, par=par):
                        ctx.wait_cp(te, (tag, "wld", g, nb))
                        ctx.wait_cp(te, (tag, "xpev", g * 32 + nb * 4 + rb - 2))
                        for c in range(DCH):
                            te.matmul(
                                psf[par][:, :],
                                xnf[:, c * 512 + rb * 128: c * 512 + (rb + 1) * 128],
                                wch[:, c * 512:(c + 1) * 512],
                                start=(c == 0), stop=False)
                        last = te.matmul(
                            psf[par][:, :], onesb[0:1, 0:128],
                            biasr[0:1, nb * 512:(nb + 1) * 512],
                            start=False, stop=True)
                        ctx.inc(pe, last)
                        ctx.mark((tag, "xpmm", g, nb, rb), pe)
                    blk.tensor(xpmm)

                    def xpevac(v, g=g, nb=nb, rb=rb, par=par):
                        ctx.wait_cp(v, (tag, "xpmm", g, nb, rb))
                        ctx.wait_cp(v, (tag, "xpst", g * 32 + nb * 4 + rb - 2))
                        ctx.inc(ev, v.tensor_copy(xpev[par][:, :], psf[par][:, :]))
                        ctx.mark((tag, "xpev", g * 32 + nb * 4 + rb), ev)
                    blk.vector(xpevac)

                    def xpstore(gp, g=g, nb=nb, rb=rb, par=par):
                        ctx.wait_cp(gp, (tag, "xpev", g * 32 + nb * 4 + rb))
                        ctx.inc(f"Lxs{par}", gp.dma_start(
                            out=xp_dram[g * 512 + rb * 128:
                                        g * 512 + (rb + 1) * 128,
                                        nb * 512:(nb + 1) * 512],
                            in_=xpev[par][:, :]), 16)
                        ctx.mark((tag, "xpst", g * 32 + nb * 4 + rb), f"Lxs{par}")
                    blk.gpsimd(xpstore)
    ctx.pop_scope()

    gact = [ctx.ssb(f"lsga{t}", (128, H4), F32) for t in range(2)]
    c_ = [ctx.ssb(f"lsc{t}", (128, D), F32) for t in range(2)]
    hfm = ctx.ssb("lshfm", (128, DCH * TSH), BF16)
    ht = [ctx.ssb(f"lsht{t}", (128, D), F32) for t in range(2)]
    htb = [ctx.ssb(f"lshtb{t}", (128, D), BF16) for t in range(2)]
    ctmp = [ctx.ssb("lsct", (128, D), F32)] * 2
    ctmp2 = [ctx.ssb("lsc2", (128, D), F32)] * 2
    ttc = [ctx.ssb(f"lsttc{t}", (128, D), F32) for t in range(2)]
    gtmp = [ctx.ssb(f"lsgt{p}", (128, 512), F32) for p in range(8)]
    xpt = [ctx.ssb(f"lsxp{p}", (128, 512), F32) for p in range(8)]
    with nc.Block() as blk:
        def init2(gp):
            ctx.inc("sv", gp.memset(c_[0][:, :], 0.0))
            ctx.inc("sv", gp.memset(c_[1][:, :], 0.0))
            ctx.inc("sv", gp.memset(hfm[:, :], 0.0))
            ctx.wait(gp, "sv")
        blk.gpsimd(init2)

    with nc.Block() as blk:
        for i in range(NSTEP):
            for tt in range(2):
                r0 = i * TSH + tt * 128
                for nb in range(8):
                    par = nb % 2

                    def xpl(sync, i=i, tt=tt, nb=nb, par=par, r0=r0):
                        gi = (i * 2 + tt) * 8 + nb
                        if gi == 0:
                            ctx.wait(sync, "Lxs0")
                            ctx.wait(sync, "Lxs1")
                        pi, pt = (i, tt - 1) if tt == 1 else (i - 1, 1)
                        ctx.wait_cp(sync, (tag, "gac", pi, pt, nb))
                        ctx.inc(f"Lxl{nb}", sync.dma_start(
                            out=xpt[nb][:, :],
                            in_=xp_dram[r0:r0 + 128, nb * 512:(nb + 1) * 512]), 16)
                        ctx.mark((tag, "xpl", i, tt, nb), f"Lxl{nb}")
                    blk.sync(xpl)

                    if i > 0:
                        def gmm(te, i=i, tt=tt, nb=nb, par=par):
                            gi = (i * 2 + tt) * 8 + nb
                            ctx.wait_cp(te, (tag, "psfree", gi - 2))
                            ctx.wait_cp(te, (tag, "hfm", i - 1, tt, "done"))
                            last = None
                            for c in range(DCH):
                                last = te.matmul(
                                    psf[par][:, :],
                                    hfm[:, c * TSH + tt * 128: c * TSH + (tt + 1) * 128],
                                    ub[:, c * H4 + nb * 512: c * H4 + (nb + 1) * 512],
                                    start=(c == 0), stop=(c == DCH - 1))
                            ctx.inc(pe, last)
                            ctx.mark((tag, "gmm", i, tt, nb), pe)
                        blk.tensor(gmm)

                        def gadd(v, i=i, tt=tt, nb=nb, par=par):
                            gi = (i * 2 + tt) * 8 + nb
                            ctx.wait_cp(v, (tag, "gmm", i, tt, nb))
                            ctx.wait_cp(v, (tag, "xpl", i, tt, nb))
                            pi, pt = (i, tt - 1) if tt == 1 else (i - 1, 1)
                            ctx.wait_cp(v, (tag, "gac", pi, pt, nb))
                            ctx.inc(dv, v.tensor_tensor(gtmp[nb][:, :],
                                                        psf[par][:, :],
                                                        xpt[nb][:, :], ALU.add))
                            ctx.mark((tag, "gadd", i, tt, nb), dv)
                            ctx.mark((tag, "psfree", gi), dv)
                        blk.vector(gadd)

                def gactf_all(sc, i=i, tt=tt):
                    ctx.wait_cp(sc, (tag, "gactfree", i, tt))
                    for nb in range(8):
                        gi = (i * 2 + tt) * 8 + nb
                        par = nb % 2
                        fn = AF.Sigmoid if nb < 6 else AF.Tanh
                        if i > 0:
                            ctx.wait_cp(sc, (tag, "gadd", i, tt, nb))
                            srcap = gtmp[nb][:, :]
                        else:
                            ctx.wait_cp(sc, (tag, "xpl", i, tt, nb))
                            srcap = xpt[nb][:, :]
                        ctx.inc(ac, sc.activation(
                            gact[tt][:, nb * 512:(nb + 1) * 512], srcap, fn))
                        ctx.mark((tag, "gac", i, tt, nb), ac)
                        ctx.mark((tag, "gtfree", gi), ac)
                        if i == 0:
                            ctx.mark((tag, "xptfree", gi), ac)
                blk.scalar(gactf_all)

                def chm(v, i=i, tt=tt):
                    ctx.wait_cp(v, (tag, "gac", i, tt, 7))
                    ga = gact[tt]
                    ctx.inc(dv, v.tensor_tensor(ctmp[tt][:, :], ga[:, 0:1024],
                                                ga[:, 3072:4096], ALU.mult))
                    ctx.inc(dv, v.tensor_tensor(ctmp2[tt][:, :], ga[:, 1024:2048],
                                                c_[tt][:, :], ALU.mult))
                    ctx.wait(v, dv)
                    ctx.inc(dv, v.tensor_tensor(c_[tt][:, :], ctmp[tt][:, :],
                                                ctmp2[tt][:, :], ALU.add))
                    ctx.mark((tag, "cnew", i, tt), dv)
                blk.vector(chm)

                def ctanh(sc, i=i, tt=tt):
                    ctx.wait_cp(sc, (tag, "cnew", i, tt))
                    ctx.inc(ac, sc.activation(ttc[tt][:, :], c_[tt][:, :], AF.Tanh))
                    ctx.mark((tag, "tanh", i, tt), ac)
                blk.scalar(ctanh)

                def hmul(v, i=i, tt=tt):
                    ctx.wait_cp(v, (tag, "tanh", i, tt))
                    ctx.wait_cp(v, (tag, "hst", i - 1, tt))
                    ctx.inc(dv, v.tensor_tensor(ht[tt][:, :],
                                                gact[tt][:, 2048:3072],
                                                ttc[tt][:, :], ALU.mult))
                    ctx.wait(v, dv)
                    ctx.inc(dv, v.tensor_copy(htb[tt][:, :], ht[tt][:, :]))
                    ctx.mark((tag, "hnew", i, tt), dv)
                    ctx.mark((tag, "gactfree", i + 1, tt), dv)
                blk.vector(hmul)

                def hstore(gp, i=i, tt=tt, r0=r0):
                    ctx.wait_cp(gp, (tag, "hnew", i, tt))
                    ctx.inc(f"Lhs{tt}", gp.dma_start(
                        out=hs_out[r0:r0 + 128, :], in_=ht[tt][:, :]), 16)
                    ctx.mark((tag, "hst", i, tt), f"Lhs{tt}")
                blk.gpsimd(hstore)

                def htp(te, i=i, tt=tt):
                    ctx.wait_cp(te, (tag, "hnew", i, tt))
                    ctx.wait_cp(te, (tag, "hev", i - 1, tt))
                    last = None
                    for c in range(DCH):
                        last = te.transpose(psb[tt][:, c * 128:(c + 1) * 128],
                                            htb[tt][:, c * 128:(c + 1) * 128],
                                            ident[:, :])
                    ctx.inc(pe, last)
                    ctx.mark((tag, "htp", i, tt), pe)
                blk.tensor(htp)

                def hev(v, i=i, tt=tt):
                    ctx.wait_cp(v, (tag, "htp", i, tt))
                    dst = hfm[:, :].rearrange("p (c u) -> p c u", c=DCH)[
                        :, :, tt * 128:(tt + 1) * 128]
                    srcv = psb[tt][:, :].rearrange("p (c u) -> p c u", c=DCH)
                    ctx.inc(ev, v.tensor_copy(dst, srcv))
                    ctx.mark((tag, "hev", i, tt), ev)
                    ctx.mark((tag, "hfm", i, tt, "done"), ev)
                blk.vector(hev)

            if q_cb is not None and i % 8 == 7:
                q_cb(i // 8, blk)


def build_lstm_test():
    es = ExitStack()
    nc = bass.Bass("TRN2", target_bir_lowering=False)
    x_in = nc.dram_tensor("x_in", [NTOK, D], BF16, kind="ExternalInput").ap()
    w = nc.dram_tensor("w", [D, H4], F32, kind="ExternalInput").ap()
    u = nc.dram_tensor("u", [D, H4], F32, kind="ExternalInput").ap()
    bias = nc.dram_tensor("bias", [1, H4], F32, kind="ExternalInput").ap()
    hs = nc.dram_tensor("hs", [NTOK, D], F32, kind="ExternalOutput").ap()
    xp = nc.dram_tensor("xp", [NTOK, H4], F32, kind="Internal").ap()
    ctx = Ctx(nc, es)
    ctx.ps_pool()
    emit_idents(ctx)
    ctx.open_stage()
    emit_lstm(ctx, x_in, hs, w, u, bias, xp, "ls")
    ctx.close_stage()
    with nc.Block() as blk:
        def fin(gp):
            for name in ("Lhs0", "Lhs1"):
                if name in ctx._sems:
                    ctx.wait(gp, name)
        blk.gpsimd(fin)
    es.close()
    return nc


def build_mha_test():
    es = ExitStack()
    nc = bass.Bass("TRN2", target_bir_lowering=False)
    x_in = nc.dram_tensor("x_in", [NTOK, D], F32, kind="ExternalInput").ap()
    wq = nc.dram_tensor("wq", [D, D], F32, kind="ExternalInput").ap()
    wk = nc.dram_tensor("wk", [D, D], F32, kind="ExternalInput").ap()
    wv = nc.dram_tensor("wv", [D, D], F32, kind="ExternalInput").ap()
    wo = nc.dram_tensor("wo", [D, D], F32, kind="ExternalInput").ap()
    x_out = nc.dram_tensor("x_out", [NTOK, D], F32, kind="ExternalOutput").ap()
    ctx = Ctx(nc, es)
    ctx.ps_pool()
    emit_idents(ctx)
    ctx.open_stage()
    emit_mha(ctx, x_in, x_out, wq, wk, wv, wo, "mh")
    ctx.close_stage()
    emit_drain_outputs(ctx)
    es.close()
    return nc


def emit_drain_outputs(ctx):
    nc = ctx.nc
    with nc.Block() as blk:
        def fin(gp):
            for name in list(ctx._sems):
                if ("st" in name and name[0] in "CMOTA") or name.endswith("sto"):
                    ctx.wait(gp, name)
        blk.gpsimd(fin)



def _np(x):
    return np.ascontiguousarray(np.asarray(x, dtype=np.float32))


def cmix_host_weights(W_r, W_k, W_v, mu_r, mu_k):
    W_r, W_k, W_v = np.asarray(W_r), np.asarray(W_k), np.asarray(W_v)
    return dict(
        wrT=_np(W_r.T), wkT=_np(W_k.T), wvT=_np(W_v.T),
        mur=_np(np.asarray(mu_r).reshape(DCH, 128).T),
        muk=_np(np.asarray(mu_k).reshape(DCH, 128).T))


def build_cmix_test():
    es = ExitStack()
    nc = bass.Bass("TRN2", target_bir_lowering=False)
    x_in = nc.dram_tensor("x_in", [NTOK, D], F32, kind="ExternalInput").ap()
    wrT = nc.dram_tensor("wrT", [D, D], F32, kind="ExternalInput").ap()
    wkT = nc.dram_tensor("wkT", [D, D], F32, kind="ExternalInput").ap()
    wvT = nc.dram_tensor("wvT", [D, D], F32, kind="ExternalInput").ap()
    mur = nc.dram_tensor("mur", [128, DCH], F32, kind="ExternalInput").ap()
    muk = nc.dram_tensor("muk", [128, DCH], F32, kind="ExternalInput").ap()
    x_out = nc.dram_tensor("x_out", [NTOK, D], F32, kind="ExternalOutput").ap()
    ctx = Ctx(nc, es)
    ctx.ps_pool()
    emit_idents(ctx)
    ctx.open_stage()
    emit_cmix_weights(ctx, wrT, wkT, wvT, mur, muk, "cm")
    emit_cmix(ctx, x_in, x_out, "cm")
    ctx.close_stage()
    emit_drain_outputs(ctx)
    es.close()
    return nc





def build_add_test():
    es = ExitStack()
    nc = bass.Bass("TRN2", target_bir_lowering=False)
    a_in = nc.dram_tensor("a_in", [NTOK, D], F32, kind="ExternalInput").ap()
    b_in = nc.dram_tensor("b_in", [NTOK, D], F32, kind="ExternalInput").ap()
    x_out = nc.dram_tensor("x_out", [NTOK, D], F32, kind="ExternalOutput").ap()
    ctx = Ctx(nc, es)
    at = [es.enter_context(nc.sbuf_tensor(f"adda{p}", [128, D], F32)) for p in range(2)]
    bt = [es.enter_context(nc.sbuf_tensor(f"addb{p}", [128, D], F32)) for p in range(2)]
    ot = [es.enter_context(nc.sbuf_tensor(f"addo{p}", [128, D], F32)) for p in range(2)]
    with nc.Block() as blk:
        for s in range(NTOK // 128):
            p = s % 2

            def dma_in(sync, s=s, p=p):
                ctx.wait_cp(sync, ("add", "sum", s - 2))
                ctx.inc(f"Dld{p}", sync.dma_start(
                    out=at[p][:, :], in_=a_in[s * 128:(s + 1) * 128, :]), 16)
                ctx.inc(f"Dld{p}", sync.dma_start(
                    out=bt[p][:, :], in_=b_in[s * 128:(s + 1) * 128, :]), 16)
                ctx.mark(("add", "ld", s), f"Dld{p}")
            blk.sync(dma_in)

            def addv(v, s=s, p=p):
                ctx.wait_cp(v, ("add", "ld", s))
                ctx.wait_cp(v, ("add", "st", s - 2))
                ctx.inc("Ddv", v.tensor_tensor(ot[p][:, :], at[p][:, :],
                                               bt[p][:, :], ALU.add))
                ctx.mark(("add", "sum", s), "Ddv")
            blk.vector(addv)

            def store(gp, s=s, p=p):
                ctx.wait_cp(gp, ("add", "sum", s))
                ctx.inc(f"Dst{p}", gp.dma_start(
                    out=x_out[s * 128:(s + 1) * 128, :], in_=ot[p][:, :]), 16)
                ctx.mark(("add", "st", s), f"Dst{p}")
            blk.gpsimd(store)
    with nc.Block() as blk:
        def fin(gp):
            ctx.wait(gp, "Dst0")
            ctx.wait(gp, "Dst1")
        blk.gpsimd(fin)
    es.close()
    return nc



_PROGRAM_CACHE = {}


def _prog(name, builder):
    if name not in _PROGRAM_CACHE:
        _PROGRAM_CACHE[name] = builder()
    return _PROGRAM_CACHE[name]


def _run(name, builder, in_maps, out_key):
    nc = _prog(name, builder)
    res = run_bass_kernel_spmd(nc, in_maps, core_ids=list(range(NCORES)))
    return [r[out_key] for r in res.results]


def _rope_tables():
    half = 16
    inv = 1.0 / 10000.0 ** (np.arange(half) / half)
    ang = np.arange(T)[:, None] * inv[None, :]
    rcos = np.cos(ang).astype(np.float32).reshape(16, 128, 16).transpose(1, 0, 2).reshape(128, 256)
    rsin = np.sin(ang).astype(np.float32).reshape(16, 128, 16).transpose(1, 0, 2).reshape(128, 256)
    return np.ascontiguousarray(rcos), np.ascontiguousarray(rsin)


pass





def emit_bf_to_fm(ctx, blk, x_dram_bf, row0, ntok, fm_buf, fm_unit, fm_col0, tag):
    nsub = ntok // 128
    ident = ctx.bufs["ident_bf"]
    psf, psw, psb = ctx.ps_pool()
    xt = [ctx.ssb(f"bffx{p}", (128, D), BF16) for p in range(2)]
    for s in range(nsub):
        p = s % 2
        r0 = row0 + s * 128

        def dma_in(sync, s=s, p=p, r0=r0):
            ctx.wait_cp(sync, (tag, "xtfree", s - 2))
            ctx.inc(f"Fld{p}", sync.dma_start(out=xt[p][:, :],
                                              in_=x_dram_bf[r0:r0 + 128, :]), 16)
            ctx.mark((tag, "xload", s), f"Fld{p}")
        blk.sync(dma_in)

        def pe_t(te, s=s, p=p):
            ctx.wait_cp(te, (tag, "xload", s))
            ctx.wait_cp(te, (tag, "tpfree", s - 2))
            last = None
            for c in range(DCH):
                last = te.transpose(psb[p][:, c * 128:(c + 1) * 128],
                                    xt[p][:, c * 128:(c + 1) * 128], ident[:, :])
            ctx.inc("Fpe", last)
            ctx.mark((tag, "tdone", s), "Fpe")
            ctx.mark((tag, "xtfree", s), "Fpe")
        blk.tensor(pe_t)

        def evac(v, s=s, p=p):
            ctx.wait_cp(v, (tag, "tdone", s))
            dst = fmv(fm_buf, fm_unit, fm_col0 + s * 128, fm_col0 + (s + 1) * 128)
            srcv = psb[p][:, :].rearrange("q (c u) -> q c u", c=DCH)
            ctx.inc("Fev", v.tensor_copy(dst, srcv))
            ctx.mark((tag, "tpfree", s), "Fev")
        blk.vector(evac)
    ctx.mark((tag, "done"), "Fev")


def emit_lncast(ctx, x_in, xnb_out, tag, q_cb=None):
    nc = ctx.nc
    xt = [ctx.ssb(f"lcx{p}", (128, D), F32) for p in range(2)]
    st_ = [ctx.ssb(f"lcs{p}", (128, 12), F32) for p in range(2)]
    mv = [ctx.ssb(f"lcm{p}", (128, 2), F32) for p in range(2)]
    rs = [ctx.ssb(f"lcr{p}", (128, 1), F32) for p in range(2)]
    nm = [ctx.ssb(f"lcn{p}", (128, 1), F32) for p in range(2)]
    xn = [ctx.ssb(f"lco{p}", (128, D), BF16) for p in range(2)]
    with nc.Block() as blk:
        for s in range(NTOK // 128):
            p = s % 2
            r0 = s * 128

            def dma_in(sync, s=s, p=p, r0=r0):
                ctx.wait_cp(sync, (tag, "normd", s - 2))
                ctx.inc(f"Gld{p}", sync.dma_start(out=xt[p][:, :],
                                                  in_=x_in[r0:r0 + 128, :]), 16)
                ctx.mark((tag, "xload", s), f"Gld{p}")
            blk.sync(dma_in)

            def dve(v, s=s, p=p):
                ctx.wait_cp(v, (tag, "xload", s))
                ctx.inc("Gdv", v.bn_stats(st_[p][:, 0:6], xt[p][:, 0:512]))
                ctx.inc("Gdv", v.bn_stats(st_[p][:, 6:12], xt[p][:, 512:1024]))
                ctx.wait(v, "Gdv")
                ctx.inc("Gdv", v.bn_aggr(mv[p][:, :], st_[p][:, :]))
                ctx.mark((tag, "aggr", s), "Gdv")
            blk.vector(dve)

            def act1(sc, s=s, p=p):
                ctx.wait_cp(sc, (tag, "aggr", s))
                ctx.inc("Gac", sc.activation(rs[p][:, :], mv[p][:, 1:2], AF.Ln,
                                             bias=ctx.bufs["eps"][:, 0:1]))
                ctx.wait(sc, "Gac")
                ctx.inc("Gac", sc.activation(rs[p][:, :], rs[p][:, :], AF.Exp,
                                             scale=-0.5))
                ctx.mark((tag, "rstd", s), "Gac")
            blk.scalar(act1)

            def dve2(v, s=s, p=p):
                ctx.wait_cp(v, (tag, "rstd", s))
                ctx.inc("Gdv", v.scalar_tensor_tensor(nm[p][:, :], mv[p][:, 0:1],
                                                      -1.0, rs[p][:, :],
                                                      ALU.mult, ALU.mult))
                ctx.wait(v, "Gdv")
                ctx.wait_cp(v, (tag, "stored", s - 2))
                ctx.inc("Gdv", v.tensor_scalar(xn[p][:, :], xt[p][:, :],
                                               rs[p][:, 0:1], nm[p][:, 0:1],
                                               ALU.mult, ALU.add))
                ctx.mark((tag, "normd", s), "Gdv")
            blk.vector(dve2)

            def store(gp, s=s, p=p, r0=r0):
                ctx.wait_cp(gp, (tag, "normd", s))
                ctx.inc(f"Gst{p}", gp.dma_start(
                    out=xnb_out[r0:r0 + 128, :], in_=xn[p][:, :]), 16)
                ctx.mark((tag, "stored", s), f"Gst{p}")
            blk.gpsimd(store)

            if q_cb is not None and s % 16 == 15:
                q_cb(s // 16, blk)


def emit_dma_flush(ctx):
    nc = ctx.nc
    with nc.Block() as blk:
        def fl(sync):
            for name in sorted(ctx.dma_sems):
                ctx.wait(sync, name)
        blk.sync(fl)


def build_fused():
    es = ExitStack()
    nc = bass.Bass("TRN2", target_bir_lowering=False)
    DI = nc.dram_tensor
    x_in = DI("x_in", [NTOK, D], F32, kind="ExternalInput").ap()
    x_out = DI("x_out", [NTOK, D], F32, kind="ExternalOutput").ap()
    xb0 = DI("xb0", [NTOK, D], F32, kind="Internal").ap()
    xb1 = DI("xb1", [NTOK, D], F32, kind="Internal").ap()
    xp = DI("xp", [NTOK, H4], F32, kind="Internal").ap()
    hs = DI("hs", [NTOK, D], F32, kind="Internal").ap()

    wq = DI("wq", [D, D], F32, kind="ExternalInput").ap()
    wqr = DI("wqr", [D, 512], F32, kind="ExternalInput").ap()
    wkr = DI("wkr", [D, 32], F32, kind="ExternalInput").ap()
    wkvd = DI("wkvd", [D, 256], F32, kind="ExternalInput").ap()
    wku = DI("wku", [256, D], F32, kind="ExternalInput").ap()
    wvu = DI("wvu", [256, D], F32, kind="ExternalInput").ap()
    wo_mla = DI("wo_mla", [D, D], F32, kind="ExternalInput").ap()
    lobo = DI("lobo", [1, 16], F32, kind="ExternalInput").ap()
    rcos = DI("rcos", [128, 256], F32, kind="ExternalInput").ap()
    rsin = DI("rsin", [128, 256], F32, kind="ExternalInput").ap()
    wg = DI("wg", [D, 8], F32, kind="ExternalInput").ap()
    w1 = DI("w1", [8, D, HMOE], F32, kind="ExternalInput").ap()
    b1 = DI("b1", [8, HMOE], F32, kind="ExternalInput").ap()
    w2 = DI("w2", [8, HMOE, D], F32, kind="ExternalInput").ap()
    b2 = DI("b2", [8, D], F32, kind="ExternalInput").ap()
    kwT = DI("kwT", [D, 64], F32, kind="ExternalInput").ap()
    vwT = DI("vwT", [D, 64], F32, kind="ExternalInput").ap()
    owT = DI("owT", [64, D], F32, kind="ExternalInput").ap()
    tsg = DI("tsg", [1, D], F32, kind="ExternalInput").ap()
    lnw = DI("lnw", [128, DCH], F32, kind="ExternalInput").ap()
    lnb = DI("lnb", [128, DCH], F32, kind="ExternalInput").ap()
    td = DI("td", [64, 1], F32, kind="ExternalInput").ap()
    tf = DI("tf", [64, 1], F32, kind="ExternalInput").ap()
    wq1 = DI("wq1", [D, D], F32, kind="ExternalInput").ap()
    wk1 = DI("wk1", [D, D], F32, kind="ExternalInput").ap()
    wv1 = DI("wv1", [D, D], F32, kind="ExternalInput").ap()
    wo1 = DI("wo1", [D, D], F32, kind="ExternalInput").ap()
    wrT = DI("wrT", [D, D], F32, kind="ExternalInput").ap()
    wkT = DI("wkT", [D, D], F32, kind="ExternalInput").ap()
    wvT = DI("wvT", [D, D], F32, kind="ExternalInput").ap()
    mur = DI("mur", [128, DCH], F32, kind="ExternalInput").ap()
    muk = DI("muk", [128, DCH], F32, kind="ExternalInput").ap()
    wl_ = DI("wl", [D, H4], F32, kind="ExternalInput").ap()
    ul_ = DI("ul", [D, H4], F32, kind="ExternalInput").ap()
    biasl = DI("biasl", [1, H4], F32, kind="ExternalInput").ap()

    ctx = Ctx(nc, es)
    ctx.ps_pool()
    emit_idents(ctx)

    ctx.open_stage()
    emit_mla(ctx, x_in, xb0, wq, wqr, wkr, wkvd, wku, wvu, wo_mla,
             lobo, rcos, rsin, "ml")
    ctx.close_stage()
    emit_dma_flush(ctx)

    ctx.open_stage()
    emit_moe(ctx, xb0, xb1, wg, w1, b1, w2, b2, "mo1")
    ctx.close_stage()
    emit_dma_flush(ctx)

    ctx.open_stage()
    emit_tmix(ctx, xb1, xb0, kwT, vwT, owT, tsg, lnw, lnb, td, tf, "tx")
    ctx.close_stage()
    emit_dma_flush(ctx)

    ctx.open_stage()
    emit_moe(ctx, xb0, xb1, wg, w1, b1, w2, b2, "mo2")
    ctx.close_stage()
    emit_dma_flush(ctx)

    ctx.open_stage()
    emit_mha(ctx, xb1, xb0, wq1, wk1, wv1, wo1, "mh")
    ctx.close_stage()
    emit_dma_flush(ctx)

    ctx.open_stage()
    emit_cmix_weights(ctx, wrT, wkT, wvT, mur, muk, "cm1")
    emit_cmix(ctx, xb0, xb1, "cm1")
    ctx.close_stage()
    emit_dma_flush(ctx)

    xnb = DI("xnb", [NTOK, D], BF16, kind="Internal").ap()
    agb_q = [DI(f"agbq{q}", [NCORES * 2048, D], BF16, kind="Internal",
                addr_space="Shared").ap() for q in range(4)]
    lstm_x = DI("lstm_x", [NTOK, D], BF16, kind="Internal").ap()
    ag2q = DI("ag2q", [4 * NCORES * 2048, D], F32, kind="Internal",
              addr_space="Shared").ap()
    hsb = DI("hsb", [NTOK, D], F32, kind="Internal").ap()

    def lc_qcb(q, blk):
        def agx(gp, q=q):
            ctx.wait(gp, "Gst0")
            ctx.wait(gp, "Gst1")
            i = gp.collective_compute(
                "AllGather", ALU.bypass,
                replica_groups=[list(range(NCORES))],
                ins=[xnb[q * 2048:(q + 1) * 2048, :].opt()],
                outs=[agb_q[q].opt()])
            ctx.inc("ccs", i, 1)
        blk.gpsimd(agx)

    ctx.open_stage()
    emit_lncast(ctx, xb1, xnb, "lc", q_cb=lc_qcb)
    ctx.close_stage()
    emit_dma_flush(ctx)

    with nc.Block() as blk:
        def cc1(gp):
            ctx.wait(gp, "ccs")
            with gp.register("agr") as r2:
                gp.reg_mov(r2, gp.partition_id())
                gp.reg_mul(r2, r2, 256 * D)
                for q in range(4):
                    ctx.inc("ccd", gp.dma_start(
                        out=bass.AP(lstm_x.tensor, q * 256 * D,
                                    [[1024 * D, 8], [D, 256], [1, D]]),
                        in_=bass.AP(agb_q[q].tensor, r2,
                                    [[2048 * D, 8], [D, 256], [1, D]])), 16)
                ctx.wait(gp, "ccd")
        blk.gpsimd(cc1)

    def lstm_qcb(q, blk):
        def agq(gp, q=q):
            ctx.wait(gp, "Lhs0")
            ctx.wait(gp, "Lhs1")
            i = gp.collective_compute(
                "AllGather", ALU.bypass,
                replica_groups=[list(range(NCORES))],
                ins=[hs[q * 2048:(q + 1) * 2048, :].opt()],
                outs=[ag2q[q * NCORES * 2048:(q + 1) * NCORES * 2048, :].opt()])
            ctx.inc("ccs", i, 1)
        blk.gpsimd(agq)

    ctx.open_stage()
    emit_lstm(ctx, lstm_x, hs, wl_, ul_, biasl, xp, "ls", q_cb=lstm_qcb)
    ctx.close_stage()
    emit_dma_flush(ctx)

    with nc.Block() as blk:
        def cc2(gp):
            ctx.wait(gp, "ccs")
            with gp.register("agr3") as r2, gp.register("agr4") as r3:
                gp.reg_mov(r2, gp.partition_id())
                gp.reg_div(r2, r2, 2)
                gp.reg_mul(r2, r2, NCORES * 2048 * D)
                gp.reg_mov(r3, gp.partition_id())
                gp.reg_mod(r3, r3, 2)
                gp.reg_mul(r3, r3, 4 * 256 * D)
                gp.reg_add(r2, r2, r3)
                for sl in range(4):
                    ctx.inc("ccd", gp.dma_start(
                        out=hsb[sl * T:(sl + 1) * T, :].rearrange(
                            "(c t) d -> c t d", t=256),
                        in_=bass.AP(ag2q.tensor, r2,
                                    [[2048 * D, 8], [D, 256], [1, D]])), 16)
                    if sl < 3:
                        gp.reg_add(r2, r2, 256 * D)
                ctx.wait(gp, "ccd")
        blk.gpsimd(cc2)

    at = [ctx.sb(f"fadda{p}", (128, D), F32) for p in range(2)]
    bt = [ctx.sb(f"faddb{p}", (128, D), F32) for p in range(2)]
    ot = [ctx.sb(f"faddo{p}", (128, D), F32) for p in range(2)]
    with nc.Block() as blk:
        for s in range(NTOK // 128):
            p = s % 2

            def dma_in(sync, s=s, p=p):
                ctx.wait_cp(sync, ("add", "sum", s - 2))
                ctx.inc(f"Dld{p}", sync.dma_start(
                    out=at[p][:, :], in_=xb1[s * 128:(s + 1) * 128, :]), 16)
                ctx.inc(f"Dld{p}", sync.dma_start(
                    out=bt[p][:, :], in_=hsb[s * 128:(s + 1) * 128, :]), 16)
                ctx.mark(("add", "ld", s), f"Dld{p}")
            blk.sync(dma_in)

            def addv(v, s=s, p=p):
                ctx.wait_cp(v, ("add", "ld", s))
                ctx.wait_cp(v, ("add", "st", s - 2))
                ctx.inc("Ddv", v.tensor_tensor(ot[p][:, :], at[p][:, :],
                                               bt[p][:, :], ALU.add))
                ctx.mark(("add", "sum", s), "Ddv")
            blk.vector(addv)

            def store(gp, s=s, p=p):
                ctx.wait_cp(gp, ("add", "sum", s))
                ctx.inc(f"Dst{p}", gp.dma_start(
                    out=xb0[s * 128:(s + 1) * 128, :], in_=ot[p][:, :]), 16)
                ctx.mark(("add", "st", s), f"Dst{p}")
            blk.gpsimd(store)
    emit_dma_flush(ctx)

    ctx.open_stage()
    emit_cmix_weights(ctx, wrT, wkT, wvT, mur, muk, "cm2")
    emit_cmix(ctx, xb0, x_out, "cm2")
    ctx.close_stage()

    emit_dma_flush(ctx)
    es.close()
    return nc


_FUSED = None


def kernel(**kw):
    global _FUSED
    if _FUSED is None:
        _FUSED = build_fused()
    rcos, rsin = _rope_tables()
    common = {
        "wq": _np(kw["L0_attn_a_Wq"]), "wqr": _np(kw["L0_attn_a_Wqr"]),
        "wkr": _np(kw["L0_attn_a_Wkr"]), "wkvd": _np(kw["L0_attn_a_Wkvd"]),
        "wku": _np(kw["L0_attn_a_Wku"]), "wvu": _np(kw["L0_attn_a_Wvu"]),
        "wo_mla": _np(kw["L0_attn_a_Wo"]),
        "lobo": _np(kw["L0_attn_a_lobo_log"]).reshape(1, 16),
        "rcos": rcos, "rsin": rsin,
        "wg": _np(kw["L0_ffn_Wg"]), "w1": _np(kw["L0_ffn_W1"]),
        "b1": _np(kw["L0_ffn_b1"]), "w2": _np(kw["L0_ffn_W2"]),
        "b2": _np(kw["L0_ffn_b2"]),
        "kwT": _np(np.asarray(kw["L0_attn_b_key_w"]).T),
        "vwT": _np(np.asarray(kw["L0_attn_b_value_w"]).T),
        "owT": _np(np.asarray(kw["L0_attn_b_output_w"]).T),
        "tsg": _np(kw["L0_attn_b_time_shift_gates"]).reshape(1, D),
        "lnw": _np(np.asarray(kw["L0_attn_b_ln_w"]).reshape(DCH, 128).T),
        "lnb": _np(np.asarray(kw["L0_attn_b_ln_b"]).reshape(DCH, 128).T),
        "td": _np(kw["L0_attn_b_time_decay"]).reshape(64, 1),
        "tf": _np(kw["L0_attn_b_time_first"]).reshape(64, 1),
        "wq1": _np(kw["L1_attn_a_W_Q"]), "wk1": _np(kw["L1_attn_a_W_K"]),
        "wv1": _np(kw["L1_attn_a_W_V"]), "wo1": _np(kw["L1_attn_a_W_o"]),
        "wrT": _np(np.asarray(kw["L1_ffn_W_r"]).T),
        "wkT": _np(np.asarray(kw["L1_ffn_W_k"]).T),
        "wvT": _np(np.asarray(kw["L1_ffn_W_v"]).T),
        "mur": _np(np.asarray(kw["L1_ffn_mu_r"]).reshape(DCH, 128).T),
        "muk": _np(np.asarray(kw["L1_ffn_mu_k"]).reshape(DCH, 128).T),
        "wl": _np(kw["L1_attn_b_w"]), "ul": _np(kw["L1_attn_b_u"]),
        "biasl": _np(kw["L1_attn_b_bias"]).reshape(1, H4),
    }
    in_maps = []
    for c in range(NCORES):
        xc = np.ascontiguousarray(
            np.asarray(kw["L0_attn_a_x"])[4 * c:4 * c + 4].reshape(NTOK, D),
            dtype=np.float32)
        in_maps.append(dict(x_in=xc, **common))
    res = run_bass_kernel_spmd(_FUSED, in_maps, core_ids=list(range(NCORES)))
    out = np.concatenate([r["x_out"].reshape(4, T, D) for r in res.results],
                         axis=0)
    return out


# version 16
# speedup vs baseline: 1.0549x; 1.0072x over previous
import numpy as np
from contextlib import ExitStack

import concourse.bass as bass
import concourse.mybir as mybir
from concourse.bass_utils import run_bass_kernel_spmd

F32 = mybir.dt.float32
BF16 = mybir.dt.bfloat16
AF = mybir.ActivationFunctionType
ALU = mybir.AluOpType
AX = mybir.AxisListType

NCORES = 8
B, T, D = 32, 2048, 1024
BL = B // NCORES
NTOK = BL * T
EPS = 1e-5
DCH = D // 128


class Ctx:
    def __init__(self, nc, es):
        self.nc = nc
        self.es = es
        self._sems = {}
        self.bufs = {}
        self.cp = {}
        self.dma_sems = set()

    def semh(self, name):
        if name not in self._sems:
            h = self.es.enter_context(self.nc.semaphore(name))
            self._sems[name] = [h, 0]
        return self._sems[name][0]

    def inc(self, name, inst, n=1):
        h = self.semh(name)
        self._sems[name][1] += n
        inst.then_inc(h, n)
        if n >= 16:
            self.dma_sems.add(name)
        return self._sems[name][1]

    def val(self, name):
        self.semh(name)
        return self._sems[name][1]

    def wait(self, eng, name, v=None):
        eng.wait_ge(self.semh(name), self.val(name) if v is None else v)

    def mark(self, key, sem):
        self.cp[key] = (sem, self.val(sem))

    def wait_cp(self, eng, key):
        if key in self.cp:
            sem, v = self.cp[key]
            if v > 0:
                eng.wait_ge(self.semh(sem), v)

    def sb(self, name, shape, dtype):
        if name not in self.bufs:
            self.bufs[name] = self.es.enter_context(
                self.nc.sbuf_tensor(name, list(shape), dtype))
        return self.bufs[name]

    def open_stage(self):
        self.stage_es = ExitStack()
        self.stage_bufs = {}
        self._scopes = []
        self._stage_id = getattr(self, "_stage_id", -1) + 1

    def ssb(self, name, shape, dtype):
        if name not in self.stage_bufs:
            es = self._scopes[-1][0] if self._scopes else self.stage_es
            uname = f"{name}_s{self._stage_id}"
            self.stage_bufs[name] = es.enter_context(
                self.nc.sbuf_tensor(uname, list(shape), dtype))
            if self._scopes:
                self._scopes[-1][1].append(name)
        return self.stage_bufs[name]

    def push_scope(self):
        self._scopes.append((ExitStack(), []))

    def pop_scope(self):
        es, names = self._scopes.pop()
        es.close()
        for n in names:
            del self.stage_bufs[n]

    def close_stage(self):
        self.stage_es.close()
        self.stage_bufs = {}

    def constv(self, val):
        name = f"cst_{abs(hash(('c', float(val)))) % 10**10}"
        if name not in self.bufs:
            buf = self.sb(name, (128, 1), F32)
            with self.nc.Block() as blk:
                def g(gp, buf=buf, val=val):
                    gp.memset(buf[:, :], float(val))
                blk.gpsimd(g)
        return self.bufs[name]

    def ps_pool(self):
        if "PSF0" not in self.bufs:
            for i in range(4):
                self.bufs[f"PSF{i}"] = self.es.enter_context(
                    self.nc.psum_tensor(f"PSF{i}", [128, 512], F32))
            self.bufs["PSW"] = self.es.enter_context(
                self.nc.psum_tensor("PSW", [128, 1024], F32))
            for i in range(2):
                self.bufs[f"PSB{i}"] = self.es.enter_context(
                    self.nc.psum_tensor(f"PSB{i}", [128, 1024], BF16))
        return [self.bufs[f"PSF{i}"] for i in range(4)], self.bufs["PSW"], \
               [self.bufs[f"PSB{i}"] for i in range(2)]


def fmv(buf, unit, c0, c1):
    return buf[:, :].rearrange("p (c u) -> p c u", c=DCH)[:, :, c0:c1]


def emit_idents(ctx):
    nc = ctx.nc
    ib = ctx.sb("ident_bf", (128, 128), BF16)
    if_ = ctx.sb("ident_f32", (128, 128), F32)
    cm = ctx.sb("cmask", (128, 128), F32)
    eps = ctx.sb("eps", (128, 1), F32)
    with nc.Block() as blk:
        def g(gp):
            ctx.inc("sv", gp.memset(eps[:, :], EPS))
            ctx.inc("sv", gp.memset(ib[:, :], 1.0))
            ctx.inc("sv", gp.memset(if_[:, :], 1.0))
            ctx.inc("sv", gp.memset(cm[:, :], 0.0))
            ctx.wait(gp, "sv")
            ctx.inc("sv", gp.affine_select(cm[:, :], cm[:, :], [[-1, 128]],
                                           ALU.is_ge, -1e30, base=0,
                                           channel_multiplier=1))
            ctx.wait(gp, "sv")
            ctx.inc("sv", gp.affine_select(ib[:, :], ib[:, :], [[1, 128]],
                                           ALU.is_equal, 0.0, base=0,
                                           channel_multiplier=-1))
            ctx.inc("sv", gp.affine_select(if_[:, :], if_[:, :], [[1, 128]],
                                           ALU.is_equal, 0.0, base=0,
                                           channel_multiplier=-1))
            ctx.wait(gp, "sv")
        blk.gpsimd(g)



def emit_ln_to_fm(ctx, blk, x_dram, row0, ntok, fm_buf, fm_unit, fm_col0, tag,
                  out_f32=False, sub_range=None):
    nsub = ntok // 128
    dt_out = F32 if out_f32 else BF16
    ident = ctx.bufs["ident_f32" if out_f32 else "ident_bf"]
    psf, psw, psb = ctx.ps_pool()

    xt = [ctx.ssb(f"lnx{p}", (128, D), F32) for p in range(2)]
    st_ = [ctx.ssb(f"lns{p}", (128, 12), F32) for p in range(2)]
    mv = [ctx.ssb(f"lnm{p}", (128, 2), F32) for p in range(2)]
    rs = [ctx.ssb(f"lnr{p}", (128, 1), F32) for p in range(2)]
    nm = [ctx.ssb(f"lnn{p}", (128, 1), F32) for p in range(2)]
    if out_f32:
        xn = [ctx.ssb(f"lnof{p}", (128, D), F32) for p in range(2)]
    else:
        xn = [ctx.ssb(f"lnob{p}", (128, D), BF16) for p in range(2)]

    dv, ac, pe, ev = "Ldv", "Lac", "Lpe", "Lev"

    for s in (range(nsub) if sub_range is None else range(*sub_range)):
        p = s % 2
        ld = f"Lld{p}"
        r0 = row0 + s * 128
        if out_f32:
            tp_tiles = [psf[2 * p][:, :], psf[2 * p + 1][:, :]]
        else:
            tp_tiles = [psb[p][:, :]]

        def dma_in(sync, s=s, p=p, r0=r0, ld=ld):
            ctx.wait_cp(sync, (tag, "xtfree", s - 2))
            ctx.inc(ld, sync.dma_start(out=xt[p][:, :],
                                       in_=x_dram[r0:r0 + 128, :]), 16)
            ctx.mark((tag, "xload", s), ld)
        blk.sync(dma_in)

        def dve(v, s=s, p=p):
            ctx.wait_cp(v, (tag, "xload", s))
            ctx.inc(dv, v.bn_stats(st_[p][:, 0:6], xt[p][:, 0:512]))
            ctx.inc(dv, v.bn_stats(st_[p][:, 6:12], xt[p][:, 512:1024]))
            ctx.wait(v, dv)
            ctx.inc(dv, v.bn_aggr(mv[p][:, :], st_[p][:, :]))
            ctx.mark((tag, "aggr", s), dv)
        blk.vector(dve)

        def act1(sc, s=s, p=p):
            ctx.wait_cp(sc, (tag, "aggr", s))
            ctx.inc(ac, sc.activation(rs[p][:, :], mv[p][:, 1:2], AF.Ln,
                                      bias=ctx.bufs["eps"][:, 0:1]))
            ctx.wait(sc, ac)
            ctx.inc(ac, sc.activation(rs[p][:, :], rs[p][:, :], AF.Exp, scale=-0.5))
            ctx.mark((tag, "rstd", s), ac)
        blk.scalar(act1)

        def dve2(v, s=s, p=p):
            ctx.wait_cp(v, (tag, "rstd", s))
            ctx.inc(dv, v.scalar_tensor_tensor(nm[p][:, :], mv[p][:, 0:1], -1.0,
                                               rs[p][:, :], ALU.mult, ALU.mult))
            ctx.mark((tag, "nm", s), dv)
        blk.vector(dve2)

        def act2(v, s=s, p=p):
            ctx.wait_cp(v, (tag, "nm", s))
            ctx.wait_cp(v, (tag, "xnfree", s - 2))
            ctx.inc(dv, v.tensor_scalar(xn[p][:, :], xt[p][:, :],
                                        rs[p][:, 0:1], nm[p][:, 0:1],
                                        ALU.mult, ALU.add))
            ctx.mark((tag, "norm", s), dv)
            ctx.mark((tag, "xtfree", s), dv)
        blk.vector(act2)

        def pe_t(te, s=s, p=p, tp_tiles=tp_tiles):
            ctx.wait_cp(te, (tag, "norm", s))
            ctx.wait_cp(te, (tag, "tpfree", s - 2))
            last = None
            for c in range(DCH):
                tpt = tp_tiles[c // 4] if out_f32 else tp_tiles[0]
                off = (c % 4) * 128 if out_f32 else c * 128
                last = te.transpose(tpt[:, off:off + 128],
                                    xn[p][:, c * 128:(c + 1) * 128], ident[:, :])
            ctx.inc(pe, last)
            ctx.mark((tag, "tdone", s), pe)
            ctx.mark((tag, "xnfree", s), pe)
        blk.tensor(pe_t)

        def evac(v, s=s, p=p, tp_tiles=tp_tiles):
            ctx.wait_cp(v, (tag, "tdone", s))
            if out_f32:
                for h in range(2):
                    dst = fmv(fm_buf, fm_unit, fm_col0 + s * 128,
                              fm_col0 + (s + 1) * 128)[:, 4 * h:4 * (h + 1), :]
                    src = tp_tiles[h].rearrange("p (c u) -> p c u", c=4)
                    ctx.inc(ev, v.tensor_copy(dst, src))
            else:
                dst = fmv(fm_buf, fm_unit, fm_col0 + s * 128,
                          fm_col0 + (s + 1) * 128)
                src = tp_tiles[0].rearrange("p (c u) -> p c u", c=DCH)
                ctx.inc(ev, v.tensor_copy(dst, src))
            ctx.mark((tag, "tpfree", s), ev)
        blk.vector(evac)

    if sub_range is None or sub_range[1] >= nsub:
        ctx.mark((tag, "done"), ev)



def emit_cmix_weights(ctx, wrT, wkT, wvT, mur, muk, tag):
    nc = ctx.nc
    wr = ctx.ssb(f"{tag}wr", (128, DCH * D), BF16)
    wk = ctx.ssb(f"{tag}wk", (128, DCH * D), BF16)
    wv = ctx.ssb(f"{tag}wv", (128, DCH * D), BF16)
    mr = ctx.ssb(f"{tag}mr", (128, DCH), F32)
    mk = ctx.ssb(f"{tag}mk", (128, DCH), F32)
    omr = ctx.ssb(f"{tag}omr", (128, DCH), F32)
    omk = ctx.ssb(f"{tag}omk", (128, DCH), F32)
    with nc.Block() as blk:
        def wload(gp):
            for (dst, src) in ((wr, wrT), (wk, wkT), (wv, wvT)):
                for c in range(DCH):
                    ctx.inc("wl", gp.dma_start(
                        out=dst[:, c * D:(c + 1) * D],
                        in_=src[c * 128:(c + 1) * 128, :]), 16)
            ctx.inc("wl", gp.dma_start(out=mr[:, :], in_=mur[:, :]), 16)
            ctx.inc("wl", gp.dma_start(out=mk[:, :], in_=muk[:, :]), 16)
            ctx.mark((tag, "wloaded"), "wl")
        blk.gpsimd(wload)

        def prep(v):
            ctx.wait_cp(v, (tag, "wloaded"))
            ctx.inc("sv", v.tensor_scalar(omr[:, :], mr[:, :], -1.0, 1.0,
                                          ALU.mult, ALU.add))
            ctx.inc("sv", v.tensor_scalar(omk[:, :], mk[:, :], -1.0, 1.0,
                                          ALU.mult, ALU.add))
        blk.vector(prep)


def emit_cmix(ctx, x_in, x_out, tag):
    nc = ctx.nc
    HB = 1024
    NU = NTOK // HB
    NSUB = HB // 128
    psf, psw, psb = ctx.ps_pool()

    wr, wk, wv = ctx.stage_bufs[f"{tag}wr"], ctx.stage_bufs[f"{tag}wk"], ctx.stage_bufs[f"{tag}wv"]
    mr, mk = ctx.stage_bufs[f"{tag}mr"], ctx.stage_bufs[f"{tag}mk"]
    omr, omk = ctx.stage_bufs[f"{tag}omr"], ctx.stage_bufs[f"{tag}omk"]

    xnf = ctx.ssb("cmx_xnf", (128, DCH * HB), F32)
    bnd = ctx.ssb("cmx_bnd", (128, DCH), F32)
    xrb = ctx.ssb("cmx_xrb", (128, DCH * HB), BF16)
    xkb = ctx.ssb("cmx_xkb", (128, DCH * HB), BF16)
    kab = ctx.ssb("cmx_kab", (128, DCH * HB), BF16)
    t1 = ctx.ssb("cmx_t1", (128, HB), F32)
    krs = [ctx.ssb(f"cmx_krs{p}", (128, 512), F32) for p in range(2)]
    rtm = [ctx.ssb(f"cmx_rtm{p}", (128, D), F32) for p in range(2)]
    xot = [ctx.ssb(f"cmx_xot{p}", (128, D), F32) for p in range(2)]
    xnt = [ctx.ssb(f"cmx_xnt{p}", (128, D), F32) for p in range(2)]

    pk, aK, pR, aR, evo, sto = "Cpk", "CaK", "CpR", "CaR", "Cev", "Cst"

    for u in range(NU):
        new_seq = (u % (T // HB) == 0)
        row0 = u * HB

        with nc.Block() as blk:
            emit_ln_to_fm(ctx, blk, x_in, row0, HB, xnf, HB, 0,
                          ("cmA", tag, u), out_f32=True)

        with nc.Block() as blk:
            def mix(v, u=u, new_seq=new_seq):
                for c in range(DCH):
                    xc = xnf[:, c * HB:(c + 1) * HB]
                    for (dst, m, om) in ((xrb, mr, omr), (xkb, mk, omk)):
                        dc = dst[:, c * HB:(c + 1) * HB]
                        ctx.inc("sv", v.tensor_scalar(
                            t1[:, 1:HB], xnf[:, c * HB:c * HB + HB - 1],
                            om[:, c:c + 1], None, ALU.mult))
                        if new_seq:
                            i0 = v.memset(t1[:, 0:1], 0.0)
                        else:
                            i0 = v.tensor_tensor(t1[:, 0:1], bnd[:, c:c + 1],
                                                 om[:, c:c + 1], ALU.mult)
                        ctx.inc("sv", i0)
                        ctx.wait(v, "sv")
                        ctx.inc("sv", v.scalar_tensor_tensor(
                            dc, xc, m[:, c:c + 1], t1[:, :], ALU.mult, ALU.add))
                        ctx.wait(v, "sv")
                for c in range(DCH):
                    ctx.inc("sv", v.tensor_copy(bnd[:, c:c + 1],
                                                xnf[:, (c + 1) * HB - 1:(c + 1) * HB]))
            blk.vector(mix)

        with nc.Block() as blk:
            for g in range(2 * DCH):
                do, tb = g // 2, g % 2
                gp_ = g % 2

                def c1_pe(te, do=do, tb=tb, g=g, gp_=gp_):
                    ctx.wait_cp(te, (tag, "kreluD", u, g - 2))
                    last = None
                    for c in range(DCH):
                        last = te.matmul(
                            psf[gp_][:, :],
                            wk[:, c * D + do * 128: c * D + (do + 1) * 128],
                            xkb[:, c * HB + tb * 512: c * HB + (tb + 1) * 512],
                            start=(c == 0), stop=(c == DCH - 1))
                    ctx.inc(pk, last)
                    ctx.mark((tag, "kmm", u, g), pk)
                blk.tensor(c1_pe)

                def c1_act(sc, do=do, tb=tb, g=g, gp_=gp_):
                    ctx.wait_cp(sc, (tag, "kmm", u, g))
                    ctx.inc(aK, sc.activation(krs[gp_][:, :], psf[gp_][:, :], AF.Relu))
                    ctx.mark((tag, "kreluD", u, g), aK)
                    ctx.wait(sc, aK)
                    ctx.inc(aK, sc.activation(
                        kab[:, do * HB + tb * 512: do * HB + (tb + 1) * 512],
                        krs[gp_][:, :], AF.Square))
                blk.scalar(c1_act)

        with nc.Block() as blk:
            for s in range(NSUB):
                p = s % 2

                def c2_dma(sync, s=s, p=p):
                    ctx.wait_cp(sync, (tag, "xofree", u, s - 2))
                    ctx.inc(f"Cldx{p}", sync.dma_start(
                        out=xot[p][:, :],
                        in_=x_in[row0 + s * 128: row0 + (s + 1) * 128, :]), 16)
                    ctx.mark((tag, "xold", u, s), f"Cldx{p}")
                blk.sync(c2_dma)

                def c2_pe(te, s=s, p=p):
                    ctx.wait_cp(te, (tag, "sig", u, s - 1))
                    last = None
                    for c in range(DCH):
                        for half in range(2):
                            last = te.matmul(
                                psw[:, half * 512:(half + 1) * 512],
                                xrb[:, c * HB + s * 128: c * HB + (s + 1) * 128],
                                wr[:, c * D + half * 512: c * D + (half + 1) * 512],
                                start=(c == 0), stop=(c == DCH - 1))
                    ctx.inc(pR, last)
                    ctx.mark((tag, "rmm", u, s), pR)
                    ctx.wait_cp(te, (tag, "omul", u, s - 1))
                    last = None
                    for c in range(DCH):
                        for half in range(2):
                            last = te.matmul(
                                psf[2 + half][:, :],
                                kab[:, c * HB + s * 128: c * HB + (s + 1) * 128],
                                wv[:, c * D + half * 512: c * D + (half + 1) * 512],
                                start=(c == 0), stop=(c == DCH - 1))
                    ctx.inc(pR, last)
                    ctx.mark((tag, "omm", u, s), pR)
                blk.tensor(c2_pe)

                def c2_act(sc, s=s, p=p):
                    ctx.wait_cp(sc, (tag, "rmm", u, s))
                    ctx.wait_cp(sc, (tag, "rtmfree", u, s - 2))
                    ctx.inc(aR, sc.activation(rtm[p][:, :], psw[:, :], AF.Sigmoid))
                    ctx.mark((tag, "sig", u, s), aR)
                blk.scalar(c2_act)

                def c2_ev(v, s=s, p=p, u=u):
                    ctx.wait_cp(v, (tag, "sig", u, s))
                    ctx.wait_cp(v, (tag, "omm", u, s))
                    ctx.wait_cp(v, (tag, "xold", u, s))
                    ctx.wait_cp(v, (tag, "xnfree", u * NSUB + s - 2))
                    for half in range(2):
                        ctx.inc(evo, v.tensor_tensor(
                            xnt[p][:, half * 512:(half + 1) * 512],
                            rtm[p][:, half * 512:(half + 1) * 512],
                            psf[2 + half][:, :], ALU.mult))
                    ctx.mark((tag, "omul", u, s), evo)
                    ctx.mark((tag, "rtmfree", u, s), evo)
                    ctx.wait(v, evo)
                    ctx.inc(evo, v.tensor_tensor(xnt[p][:, :], xnt[p][:, :],
                                                 xot[p][:, :], ALU.add))
                    ctx.mark((tag, "sum", u, s), evo)
                    ctx.mark((tag, "xofree", u, s), evo)
                blk.vector(c2_ev)

                def c2_st(gp, s=s, p=p, u=u):
                    ctx.wait_cp(gp, (tag, "sum", u, s))
                    i = gp.dma_start(
                        out=x_out[row0 + s * 128: row0 + (s + 1) * 128, :],
                        in_=xnt[p][:, :])
                    ctx.inc(f"Cst{p}", i, 16)
                    ctx.mark((tag, "xnfree", u * NSUB + s), f"Cst{p}")
                blk.gpsimd(c2_st)




NH, DH = 16, 64

def emit_mha(ctx, x_in, x_out, wq_d, wk_d, wv_d, wo_d, tag):
    nc = ctx.nc
    psf, psw, psb = ctx.ps_pool()
    ident = ctx.bufs["ident_bf"]
    scale = 1.0 / np.sqrt(DH)

    Kf = ctx.ssb("mhaKf", (128, DCH * T), BF16)
    Vt = ctx.ssb("mhaVt", (128, (T // 128) * D), BF16)
    xnf = ctx.ssb("mhaxnf", (128, DCH * T), BF16)
    wbuf = ctx.ssb("mhawb", (128, DCH * D), BF16)
    wo = ctx.ssb("mhawo", (128, DCH * D), BF16)
    qt = [ctx.ssb("mhaqt", (128, DCH * 128), BF16)] * 2
    srow = [ctx.ssb(f"mhasr{p}", (128, T), F32) for p in range(2)]
    pb = [ctx.ssb(f"mhapb{p}", (128, T), BF16) for p in range(4)]
    ptb = [ctx.ssb(f"mhapt{p}", (128, T), BF16) for p in range(2)]
    yfm = [ctx.ssb("mhayf", (128, DCH * 128), BF16)] * 2
    mx = [ctx.ssb(f"mhamx{p}", (128, 1), F32) for p in range(2)]
    ngm = [ctx.ssb(f"mhangm{p}", (128, 1), F32) for p in range(4)]
    sxp = [ctx.ssb(f"mhasxp{p}", (128, 1), F32) for p in range(4)]
    rcp = [ctx.ssb(f"mharcp{p}", (128, 1), F32) for p in range(4)]
    xnt = [ctx.ssb(f"mhaxnt{p}", (128, D), F32) for p in range(2)]
    xot = xnt

    wl, pq, evq, pvs, sev, msk, dvx, axp, pvm, yev, pop, res, sto = (
        f"M{n}" for n in ("wl", "pq", "evq", "pvs", "sev", "msk", "dvx",
                          "axp", "pvm", "yev", "pop", "res", "sto"))

    def load_w(dst, src_dram):
        with nc.Block() as blk:
            def g(gp):
                for c in range(DCH):
                    ctx.inc(wl, gp.dma_start(
                        out=dst[:, c * D:(c + 1) * D],
                        in_=src_dram[c * 128:(c + 1) * 128, :]), 16)
                ctx.wait(gp, wl)
            blk.gpsimd(g)

    for b in range(BL):
        row0 = b * T

        with nc.Block() as blk:
            emit_ln_to_fm(ctx, blk, x_in, row0, T, xnf, T, 0, (tag, "ln", b))

        load_w(wbuf, wk_d)
        with nc.Block() as blk:
            for tb in range(T // 512):
                for do in range(DCH):
                    gpar = (tb * DCH + do) % 2

                    def kmm(te, tb=tb, do=do, gpar=gpar):
                        ctx.wait_cp(te, (tag, "kev", b, tb * DCH + do - 2))
                        last = None
                        for c in range(DCH):
                            last = te.matmul(
                                psf[gpar][:, :],
                                wbuf[:, c * D + do * 128: c * D + (do + 1) * 128],
                                xnf[:, c * T + tb * 512: c * T + (tb + 1) * 512],
                                start=(c == 0), stop=(c == DCH - 1))
                        ctx.inc(pq, last)
                        ctx.mark((tag, "kmm", b, tb * DCH + do), pq)
                    blk.tensor(kmm)

                    def kev(v, tb=tb, do=do, gpar=gpar):
                        ctx.wait_cp(v, (tag, "kmm", b, tb * DCH + do))
                        ctx.inc(evq, v.tensor_copy(
                            Kf[:, do * T + tb * 512: do * T + (tb + 1) * 512],
                            psf[gpar][:, :]))
                        ctx.mark((tag, "kev", b, tb * DCH + do), evq)
                    blk.vector(kev)

        load_w(wbuf, wv_d)
        with nc.Block() as blk:
            for kt in range(T // 128):
                gpar = kt % 2

                def vmm(te, kt=kt, gpar=gpar):
                    ctx.wait_cp(te, (tag, "vev", b, kt - 2))
                    for half in range(2):
                        last = None
                        for c in range(DCH):
                            last = te.matmul(
                                psf[2 * gpar + half][:, :],
                                xnf[:, c * T + kt * 128: c * T + (kt + 1) * 128],
                                wbuf[:, c * D + half * 512: c * D + (half + 1) * 512],
                                start=(c == 0), stop=(c == DCH - 1))
                        ctx.inc(pq, last)
                    ctx.mark((tag, "vmm", b, kt), pq)
                blk.tensor(vmm)

                def vev(v, kt=kt, gpar=gpar):
                    ctx.wait_cp(v, (tag, "vmm", b, kt))
                    for half in range(2):
                        ctx.inc(evq, v.tensor_copy(
                            Vt[:, kt * D + half * 512: kt * D + (half + 1) * 512],
                            psf[2 * gpar + half][:, :]))
                    ctx.mark((tag, "vev", b, kt), evq)
                blk.vector(vev)

        load_w(wbuf, wq_d)
        load_w(wo, wo_d)
        with nc.Block() as blk:
            for qi in range(T // 128):
                qp = qi % 2
                klen = (qi + 1) * 128
                nkt = qi + 1
                nkb = (klen + 511) // 512
                q0 = row0 + qi * 128

                for rnd in range(2):
                    def qmm(te, qi=qi, rnd=rnd):
                        ctx.wait_cp(te, (tag, "qfree", b, qi - 1))
                        ctx.wait_cp(te, (tag, "qev", b, qi, rnd - 1))
                        if rnd == 0:
                            ctx.wait_cp(te, (tag, "qev", b, qi - 1, 1))
                        for do in range(4 * rnd, 4 * rnd + 4):
                            last = None
                            for c in range(DCH):
                                last = te.matmul(
                                    psf[2][:, (do % 4) * 128:(do % 4) * 128 + 128],
                                    wbuf[:, c * D + do * 128: c * D + (do + 1) * 128],
                                    xnf[:, c * T + qi * 128: c * T + (qi + 1) * 128],
                                    start=(c == 0), stop=(c == DCH - 1))
                            ctx.inc(pq, last)
                        ctx.mark((tag, "qmm", b, qi, rnd), pq)
                    blk.tensor(qmm)

                    def qev(v, qi=qi, qp=qp, rnd=rnd):
                        ctx.wait_cp(v, (tag, "qmm", b, qi, rnd))
                        ctx.inc(evq, v.tensor_scalar(
                            qt[qp][:, rnd * 512:(rnd + 1) * 512], psf[2][:, :],
                            scale, None, ALU.mult))
                        ctx.mark((tag, "qev", b, qi, rnd), evq)
                        if rnd == 1:
                            ctx.mark((tag, "qdone", b, qi), evq)
                    blk.vector(qev)

                def _front(h):
                    g = qi * NH + h
                    hp = h % 4
                    hps = h % 2
                    hp2 = h % 2
                    hp4 = h % 4
                    hr = 64 * (h % 2)
                    hc = (h // 2) * 128

                    for kb in range(nkb):
                        kw = min(512, klen - kb * 512)
                        sb_id = ctx.cp.get((tag, "sblkid", b), 0)
                        ctx.cp[(tag, "sblkid", b)] = sb_id + 1
                        par = sb_id % 2

                        def smm(te, qi=qi, h=h, kb=kb, kw=kw, sb_id=sb_id, par=par):
                            if kb == 0:
                                ctx.wait_cp(te, (tag, "qdone", b, qi))
                            ctx.wait_cp(te, (tag, "sfree", b, sb_id - 2))
                            last = te.matmul(
                                psf[par][:, 0:kw],
                                qt[qp][hr:hr + 64, hc:hc + 128],
                                Kf[hr:hr + 64,
                                   (h // 2) * T + kb * 512:
                                   (h // 2) * T + kb * 512 + kw],
                                start=True, stop=True)
                            ctx.inc(pvs, last)
                            ctx.mark((tag, "smm", b, sb_id), pvs)
                        blk.tensor(smm)

                        def sevac(v, qi=qi, h=h, g=g, kb=kb, kw=kw, sb_id=sb_id,
                                  par=par, nkb=nkb, hp=hp):
                            ctx.wait_cp(v, (tag, "smm", b, sb_id))
                            if kb == 0:
                                ctx.wait_cp(v, (tag, "exp", b, g - 2))
                            if kb == nkb - 1:
                                if kw > 128:
                                    ctx.inc(sev, v.tensor_copy(
                                        srow[hps][:, kb * 512: kb * 512 + kw - 128],
                                        psf[par][:, 0:kw - 128]))
                                ctx.inc(sev, v.tensor_tensor(
                                    srow[hps][:, kb * 512 + kw - 128: kb * 512 + kw],
                                    psf[par][:, kw - 128:kw],
                                    ctx.bufs["cmask"][:, :], ALU.add))
                            else:
                                ctx.inc(sev, v.tensor_copy(
                                    srow[hps][:, kb * 512: kb * 512 + kw],
                                    psf[par][:, 0:kw]))
                            ctx.mark((tag, "sfree", b, sb_id), sev)
                            if kb == nkb - 1:
                                ctx.mark((tag, "srow", b, g), sev)
                        blk.vector(sevac)

                    def dmax(v, g=g, klen=klen, hp=hp, hps=hps):
                        ctx.wait_cp(v, (tag, "srow", b, g))
                        ctx.wait_cp(v, (tag, "exp", b, g - 4))
                        ctx.inc(dvx, v.reduce_max(ngm[hp][:, :],
                                                  srow[hps][:, 0:klen], AX.X,
                                                  negate=True))
                        ctx.mark((tag, "max", b, g), dvx)
                    blk.vector(dmax)



                def _fexp(h0):
                    def aexp2(sc, h0=h0):
                        for h in (h0, h0 + 1):
                            g = qi * NH + h
                            hp = h % 4
                            hps = h % 2
                            hp4 = h % 4
                            ctx.wait_cp(sc, (tag, "max", b, g))
                            ctx.wait_cp(sc, (tag, "ptmm", b, g - 4))
                            ctx.inc(axp, sc.activation(pb[hp4][:, 0:klen],
                                                       srow[hps][:, 0:klen],
                                                       AF.Exp,
                                                       bias=ngm[hp][:, 0:1],
                                                       accum_out=sxp[hp4][:, 0:1]))
                            ctx.mark((tag, "exp", b, g), axp)
                    blk.scalar(aexp2)

                def _tail(h):
                    g = qi * NH + h
                    hp = h % 4
                    hps = h % 2
                    hp2 = h % 2
                    hp4 = h % 4
                    hr = 64 * (h % 2)
                    hc = (h // 2) * 128

                    def dscl(v, g=g, klen=klen, hp=hp, hp4=hp4):
                        ctx.wait_cp(v, (tag, "exp", b, g))
                        ctx.inc(dvx, v.reciprocal(rcp[hp][:, :], sxp[hp4][:, :]))
                        ctx.wait(v, dvx)
                        ctx.inc(dvx, v.tensor_scalar(pb[hp4][:, 0:klen],
                                                     pb[hp4][:, 0:klen],
                                                     rcp[hp][:, 0:1], None, ALU.mult))
                        ctx.mark((tag, "pscl", b, g), dvx)
                    blk.vector(dscl)

                    for r0 in range(0, nkt, 8):
                        r1 = min(nkt, r0 + 8)

                        def ptr(te, g=g, r0=r0, r1=r1, hp=hp, hp2=hp2, hp4=hp4):
                            ctx.wait_cp(te, (tag, "pscl", b, g))
                            if r0 == 0:
                                ctx.wait_cp(te, (tag, "ptev", b, g - 2, 0))
                                ctx.wait_cp(te, (tag, "ptev", b, g - 2, 8))
                            else:
                                ctx.wait_cp(te, (tag, "ptev", b, g, 0))
                            last = None
                            for kc in range(r0, r1):
                                last = te.transpose(
                                    psb[hp2][:, (kc % 8) * 128:(kc % 8) * 128 + 128],
                                    pb[hp4][:, kc * 128:(kc + 1) * 128], ident[:, :])
                            ctx.inc(pvm, last)
                            ctx.mark((tag, "ptmm", b, g), pvm)
                            ctx.mark((tag, "ptmm2", b, g, r0), pvm)
                        blk.tensor(ptr)

                        def ptev(v, g=g, r0=r0, r1=r1, hp=hp, hp2=hp2):
                            ctx.wait_cp(v, (tag, "ptmm2", b, g, r0))
                            if r0 == 0:
                                ctx.wait_cp(v, (tag, "pv", b, g - 2))
                            w = (r1 - r0) * 128
                            ctx.inc(sev, v.tensor_copy(
                                ptb[hp2][:, r0 * 128: r0 * 128 + w],
                                psb[hp2][:, 0:w]))
                            ctx.mark((tag, "ptev", b, g, r0), sev)
                            if r1 == nkt if False else True:
                                ctx.mark((tag, "ptevd", b, g), sev)
                        blk.vector(ptev)

                    def pvmm(te, g=g, h=h, nkt=nkt, hp=hp, hp2=hp2):
                        ctx.wait_cp(te, (tag, "ptevd", b, g))
                        if h % 2 == 0:
                            ctx.wait_cp(te, (tag, "yevd", b, g - 2))
                        last = None
                        for kc in range(nkt):
                            last = te.matmul(
                                psf[3][64 * (h % 2): 64 * (h % 2) + 64, 0:128],
                                Vt[:, kc * D + 64 * h: kc * D + 64 * h + 64],
                                ptb[hp2][:, kc * 128:(kc + 1) * 128],
                                start=(kc == 0), stop=(kc == nkt - 1))
                        ctx.inc(pvm, last)
                        ctx.mark((tag, "pv", b, g), pvm)
                    blk.tensor(pvmm)

                    if h % 2 == 1:
                        def yevac(v, g=g, qi=qi, h=h, qp=qp):
                            if h == 1:
                                ctx.wait_cp(v, (tag, "oproj", b, qi - 1))
                            ctx.wait_cp(v, (tag, "pv", b, g - 1))
                            ctx.wait_cp(v, (tag, "pv", b, g))
                            ctx.inc(sev, v.tensor_copy(
                                yfm[qp][:, (h // 2) * 128:(h // 2) * 128 + 128],
                                psf[3][:, 0:128]))
                            ctx.mark((tag, "yevd", b, g), sev)
                            ctx.mark((tag, "yevd", b, g - 1), sev)
                            if h == NH - 1:
                                ctx.mark((tag, "ydone", b, qi), sev)
                        blk.vector(yevac)


                for j in range(NH // 4 + 1):
                    if j >= 1:
                        for hh in range(4 * j - 4, 4 * j):
                            _tail(hh)
                    if j < NH // 4:
                        _front(4 * j)
                        _front(4 * j + 1)
                        _fexp(4 * j)
                        _front(4 * j + 2)
                        _front(4 * j + 3)
                        _fexp(4 * j + 2)

                def xold(sync, qi=qi, qp=qp, q0=q0, b=b):
                    ctx.wait_cp(sync, (tag, "xnfree", b * 16 + qi - 2))
                    ctx.inc(f"Mldx{qp}", sync.dma_start(out=xnt[qp][:, :],
                                                in_=x_in[q0:q0 + 128, :]), 16)
                    ctx.mark((tag, "xold", b, qi), f"Mldx{qp}")
                blk.sync(xold)

                def oproj(te, qi=qi, qp=qp):
                    ctx.wait_cp(te, (tag, "ydone", b, qi))
                    ctx.wait_cp(te, (tag, "ofree", b, qi - 1))
                    for half in range(2):
                        last = None
                        for c in range(DCH):
                            last = te.matmul(
                                psw[:, half * 512:(half + 1) * 512],
                                yfm[qp][:, c * 128:(c + 1) * 128],
                                wo[:, c * D + half * 512: c * D + (half + 1) * 512],
                                start=(c == 0), stop=(c == DCH - 1))
                        ctx.inc(pop, last)
                    ctx.mark((tag, "oproj", b, qi), pop)
                    ctx.mark((tag, "qfree", b, qi), pop)
                blk.tensor(oproj)

                def resid(v, qi=qi, qp=qp, b=b):
                    ctx.wait_cp(v, (tag, "oproj", b, qi))
                    ctx.wait_cp(v, (tag, "xold", b, qi))
                    ctx.wait_cp(v, (tag, "xnfree", b * 16 + qi - 2))
                    ctx.inc(res, v.tensor_tensor(xnt[qp][:, :], psw[:, :],
                                                 xnt[qp][:, :], ALU.add))
                    ctx.mark((tag, "resid", b, qi), res)
                    ctx.mark((tag, "ofree", b, qi), res)
                blk.vector(resid)

                def store(gp, qi=qi, qp=qp, q0=q0, b=b):
                    ctx.wait_cp(gp, (tag, "resid", b, qi))
                    ctx.inc(f"Mst{qp}", gp.dma_start(out=x_out[q0:q0 + 128, :],
                                              in_=xnt[qp][:, :]), 16)
                    ctx.mark((tag, "xnfree", b * 16 + qi), f"Mst{qp}")
                blk.gpsimd(store)




HMOE = 2048
HCH = HMOE // 128
MOE_DEBUG = None

def emit_moe(ctx, x_in, x_out, wg_d, w1_d, b1_d, w2_d, b2_d, tag):
    nc = ctx.nc
    psf, psw, psb = ctx.ps_pool()
    ST = 1024
    NST = NTOK // ST

    xnf = [ctx.ssb(f"moex{p}", (128, DCH * ST), BF16) for p in range(2)]
    yac = ctx.ssb("moey", (128, ST * 8), F32)
    hfm = [ctx.ssb(f"moeh{p}", (128, HCH * 512), BF16) for p in range(2)]
    w1b = ctx.ssb("moew1", (128, DCH * HMOE), BF16)
    w2b = ctx.ssb("moew2", (128, HCH * D), BF16)
    wgb = ctx.ssb("moewg", (128, DCH * 8), BF16)
    b1r = ctx.ssb("moeb1", (1, HMOE), BF16)
    b2r = ctx.ssb("moeb2", (1, D), BF16)
    ones = ctx.ssb("moeones", (1, 512), BF16)
    gates = [ctx.ssb(f"moeg{p}", (128, 64), F32) for p in range(2)]
    ge8 = ctx.ssb("moege", (128, 8), F32)
    gm8 = ctx.ssb("moegm", (128, 1), F32)
    gsum = ctx.ssb("moegs", (128, 1), F32)
    grc = ctx.ssb("moegrc", (128, 1), F32)
    gt1 = ctx.ssb("moegt1", (128, 1), F32)
    gt2 = ctx.ssb("moegt2", (128, 1), F32)
    gmk = ctx.ssb("moegmk", (128, 8), F32)
    ge2 = ctx.ssb("moege2", (128, 8), F32)

    wl, pmm, rl, yev, sto, gdv, gac = (f"O{n}" for n in
                                       ("wl", "pmm", "rl", "yev", "sto", "gdv", "gac"))

    with nc.Block() as blk:
        def g0(gp):
            ctx.inc("sv", gp.memset(ones[:, :], 1.0))
            for c in range(DCH):
                ctx.inc(wl, gp.dma_start(out=wgb[:, c * 8:(c + 1) * 8],
                                         in_=wg_d[c * 128:(c + 1) * 128, :]), 16)
            ctx.wait(gp, wl)
        blk.gpsimd(g0)

    def emit_ln_u(u, blk, sub_range=None):
        emit_ln_to_fm(ctx, blk, x_in, u * ST, ST, xnf[u % 2], ST, 0,
                      (tag, "ln", u), sub_range=sub_range)

    def emit_gate_u(u, blk):
        for s in range(8):
            def gmm(te, s=s, u=u):
                ctx.wait_cp(te, (tag, "gex", u, s - 2))
                ctx.wait_cp(te, ((tag, "ln", u), "done"))
                ctx.wait_cp(te, (tag, "relu", u - 1, 7, 1, 14 + (s % 2)))
                last = None
                for c in range(DCH):
                    last = te.matmul(
                        psf[s % 2][:, 0:8],
                        xnf[u % 2][:, c * ST + s * 128: c * ST + s * 128 + 128],
                        wgb[:, c * 8:(c + 1) * 8],
                        start=(c == 0), stop=(c == DCH - 1))
                ctx.inc(pmm, last)
                ctx.mark((tag, "gmm", u, s), pmm)
            blk.tensor(gmm)

            def gsm(v, s=s, u=u):
                ctx.wait_cp(v, (tag, "gmm", u, s))
                ctx.wait_cp(v, (tag, "gex", u, s - 1))
                sl = psf[s % 2][:, 0:8]
                ctx.inc(gdv, v.reduce_max(gm8[:, :], sl, AX.X, negate=True))
                ctx.mark((tag, "gng", u, s), gdv)
            blk.vector(gsm)

            def gex(sc, s=s, u=u):
                ctx.wait_cp(sc, (tag, "gng", u, s))
                ctx.wait_cp(sc, (tag, "gdone", u, s - 1))
                ctx.inc(gac, sc.activation(ge8[:, :], psf[s % 2][:, 0:8],
                                           AF.Exp, bias=gm8[:, 0:1],
                                           accum_out=gsum[:, 0:1]))
                ctx.mark((tag, "gex", u, s), gac)
            blk.scalar(gex)

            def gtop(v, s=s, u=u):
                ctx.wait_cp(v, (tag, "gex", u, s))
                ctx.inc(gdv, v.reciprocal(grc[:, :], gsum[:, :]))
                ctx.inc(gdv, v.reduce_max(gt1[:, :], ge8[:, :], AX.X))
                ctx.wait(v, gdv)
                ctx.inc(gdv, v.tensor_scalar(gmk[:, :], ge8[:, :], gt1[:, 0:1],
                                             None, ALU.is_ge))
                ctx.wait(v, gdv)
                ctx.inc(gdv, v.scalar_tensor_tensor(ge2[:, :], gmk[:, :], -1e30,
                                                    ge8[:, :], ALU.mult, ALU.add))
                ctx.wait(v, gdv)
                ctx.inc(gdv, v.reduce_max(gt2[:, :], ge2[:, :], AX.X))
                ctx.wait(v, gdv)
                ctx.inc(gdv, v.tensor_scalar(gmk[:, :], ge8[:, :], gt2[:, 0:1],
                                             None, ALU.is_ge))
                ctx.wait(v, gdv)
                ctx.inc(gdv, v.tensor_tensor(ge2[:, :], ge8[:, :], gmk[:, :],
                                             ALU.mult))
                ctx.wait(v, gdv)
                ctx.inc(gdv, v.tensor_scalar(gates[u % 2][:, s * 8:(s + 1) * 8],
                                             ge2[:, :], grc[:, 0:1],
                                             None, ALU.mult))
                ctx.mark((tag, "gdone", u, s), gdv)
            blk.vector(gtop)

    with nc.Block() as blk:
        emit_ln_u(0, blk)
    with nc.Block() as blk:
        emit_gate_u(0, blk)
        def w1pre(gp):
            for c in range(DCH):
                ctx.inc("Ow1", gp.dma_start(
                    out=w1b[:, c * HMOE:(c + 1) * HMOE],
                    in_=w1_d[0, c * 128:(c + 1) * 128, :]), 16)
            ctx.inc("Ow1", gp.dma_start(out=b1r[:, :], in_=b1_d[0:1, :]), 16)
            ctx.mark((tag, "w1L", 0, 0), "Ow1")
        blk.gpsimd(w1pre)

    for u in range(NST):
        row0 = u * ST
        with nc.Block() as blk:
            for s in range(8):
                def yinit(sync, s=s, u=u):
                    ctx.wait_cp(sync, (tag, "ystored", u - 1))
                    ctx.inc(sto, sync.dma_start(
                        out=yac[:, s * D:(s + 1) * D],
                        in_=x_in[row0 + s * 128: row0 + (s + 1) * 128, :]), 16)
                    if s == 7:
                        ctx.mark((tag, "yinit", u), sto)
                blk.sync(yinit)

            n_e = 0 if MOE_DEBUG in ("gate", "ln", "noln") else (1 if MOE_DEBUG == "ne1" else 8)
            def w1load_fn(gp, e, u, ue, uw):
                    ctx.wait_cp(gp, (tag, "w1free", uw, (e - 1) % 8))
                    for c in range(DCH):
                        ctx.inc("Ow1", gp.dma_start(
                            out=w1b[:, c * HMOE:(c + 1) * HMOE],
                            in_=w1_d[e, c * 128:(c + 1) * 128, :]), 16)
                    ctx.inc("Ow1", gp.dma_start(out=b1r[:, :], in_=b1_d[e:e + 1, :]), 16)
                    ctx.mark((tag, "w1L", ue, e), "Ow1")

            for e in range(n_e):
                if e > 0 or (u == 0 and n_e != 8):
                    blk.gpsimd(lambda gp, e=e, u=u: w1load_fn(gp, e, u, u, u))

                for half in range(2):
                    hp = half
                    for ho in range(HCH):
                        par = ho % 2

                        def w1mm(te, e=e, half=half, ho=ho, par=par, hp=hp, u=u):
                            ctx.wait_cp(te, (tag, "w1L", u, e))
                            if e == 0 and half == 0 and ho == 0:
                                ctx.wait_cp(te, ((tag, "ln", u), "done"))
                            if ho < 2:
                                pe_, ph_ = (e, half - 1) if half > 0 else (e - 1, 1)
                                ctx.wait_cp(te, (tag, "relu", u, pe_, ph_,
                                                 HCH - 2 + ho))
                            else:
                                ctx.wait_cp(te, (tag, "relu", u, e, half, ho - 2))
                            last = None
                            for c in range(DCH):
                                last = te.matmul(
                                    psf[par][:, :],
                                    w1b[:, c * HMOE + ho * 128: c * HMOE + (ho + 1) * 128],
                                    xnf[u % 2][:, c * ST + half * 512: c * ST + (half + 1) * 512],
                                    start=(c == 0), stop=False)
                            last = te.matmul(
                                psf[par][:, :],
                                b1r[0:1, ho * 128:(ho + 1) * 128],
                                ones[0:1, :],
                                start=False, stop=True)
                            ctx.inc(pmm, last)
                            ctx.mark((tag, "w1mm", u, e, half, ho), pmm)
                            if half == 1 and ho == HCH - 1:
                                ctx.mark((tag, "w1free", u, e), pmm)
                        blk.tensor(w1mm)

                        def relu(v, e=e, half=half, ho=ho, par=par, hp=hp, u=u):
                            ctx.wait_cp(v, (tag, "w1mm", u, e, half, ho))
                            ctx.inc(rl, v.tensor_scalar_max(
                                hfm[hp][:, ho * 512:(ho + 1) * 512],
                                psf[par][:, :], 0.0))
                            ctx.mark((tag, "relu", u, e, half, ho), rl)
                        blk.vector(relu)

                    if half == 0:
                        def w2load(gp, e=e, u=u):
                            ctx.wait_cp(gp, (tag, "w2free", u, e - 1))
                            for hc in range(HCH):
                                ctx.inc("Ow2", gp.dma_start(
                                    out=w2b[:, hc * D:(hc + 1) * D],
                                    in_=w2_d[e, hc * 128:(hc + 1) * 128, :]), 16)
                            ctx.inc("Ow2", gp.dma_start(out=b2r[:, :],
                                                     in_=b2_d[e:e + 1, :]), 16)
                            ctx.mark((tag, "w2L", u, e), "Ow2")
                        blk.gpsimd(w2load)

                    for tl in (range(0) if MOE_DEBUG == "w1" else range(4)):
                        s = half * 4 + tl

                        def w2mm(te, e=e, half=half, tl=tl, s=s, hp=hp, u=u):
                            ctx.wait_cp(te, (tag, "relu", u, e, half, HCH - 1))
                            ctx.wait_cp(te, (tag, "w2L", u, e))
                            ctx.wait_cp(te, (tag, "yev", u, e, s - 2))
                            wt = ([psw[:, 0:512], psw[:, 512:1024]] if s % 2 == 0
                                  else [psf[2][:, :], psf[3][:, :]])
                            for hc in range(HCH):
                                for dh in range(2):
                                    te.matmul(
                                        wt[dh],
                                        hfm[hp][:, hc * 512 + tl * 128:
                                                 hc * 512 + (tl + 1) * 128],
                                        w2b[:, hc * D + dh * 512: hc * D + (dh + 1) * 512],
                                        start=(hc == 0), stop=False)
                            last = None
                            for dh in range(2):
                                last = te.matmul(
                                    wt[dh],
                                    ones[0:1, 0:128],
                                    b2r[0:1, dh * 512:(dh + 1) * 512],
                                    start=False, stop=True)
                            ctx.inc(pmm, last)
                            ctx.mark((tag, "w2mm", u, e, s), pmm)
                            if s == 7:
                                ctx.mark((tag, "w2free", u, e), pmm)
                        blk.tensor(w2mm)

                        def w2ev(v, e=e, s=s, u=u):
                            ctx.wait_cp(v, (tag, "w2mm", u, e, s))
                            if e == 0:
                                ctx.wait_cp(v, (tag, "yinit", u))
                                ctx.wait_cp(v, (tag, "gdone", u, 7))
                            gcol = gates[u % 2][:, s * 8 + e: s * 8 + e + 1]
                            if s % 2 == 0:
                                ctx.inc(yev, v.scalar_tensor_tensor(
                                    yac[:, s * D:(s + 1) * D], psw[:, :], gcol,
                                    yac[:, s * D:(s + 1) * D], ALU.mult, ALU.add))
                            else:
                                for dh in range(2):
                                    c0 = s * D + dh * 512
                                    ctx.inc(yev, v.scalar_tensor_tensor(
                                        yac[:, c0:c0 + 512], psf[2 + dh][:, :],
                                        gcol, yac[:, c0:c0 + 512],
                                        ALU.mult, ALU.add))
                            ctx.mark((tag, "yev", u, e, s), yev)
                        blk.vector(w2ev)

                if u + 1 < NST and n_e == 8:
                    emit_ln_u(u + 1, blk, sub_range=(e, e + 1))
                if e == 7 and n_e == 8:
                    un = u + 1 if u + 1 < NST else None
                    if un is not None:
                        blk.gpsimd(lambda gp, u=u, un=un:
                                   w1load_fn(gp, 0, u, un, u))

            if u + 1 < NST and n_e != 8:
                emit_ln_u(u + 1, blk)

            if u + 1 < NST:
                emit_gate_u(u + 1, blk)

            for s in range(8):
                def yst(sync, s=s, u=u):
                    if n_e == 0:
                        ctx.wait_cp(sync, (tag, "yinit", u))
                    else:
                        ctx.wait_cp(sync, (tag, "yev", u, n_e - 1, s))
                    i = sync.dma_start(out=x_out[row0 + s * 128: row0 + (s + 1) * 128, :],
                                       in_=yac[:, s * D:(s + 1) * D])
                    ctx.inc("Yst", i, 16)
                    if s == 7:
                        ctx.mark((tag, "ystored", u), "Yst")
                blk.sync(yst)


def moe_host_weights(Wg, W1, b1, W2, b2, prefix):
    return {
        f"{prefix}wg": _np(Wg),
        f"{prefix}w1": _np(W1),
        f"{prefix}b1": _np(b1),
        f"{prefix}w2": _np(W2),
        f"{prefix}b2": _np(b2),
    }


def build_moe_test():
    es = ExitStack()
    nc = bass.Bass("TRN2", target_bir_lowering=False)
    x_in = nc.dram_tensor("x_in", [NTOK, D], F32, kind="ExternalInput").ap()
    wg = nc.dram_tensor("wg", [D, 8], F32, kind="ExternalInput").ap()
    w1 = nc.dram_tensor("w1", [8, D, HMOE], F32, kind="ExternalInput").ap()
    b1 = nc.dram_tensor("b1", [8, HMOE], F32, kind="ExternalInput").ap()
    w2 = nc.dram_tensor("w2", [8, HMOE, D], F32, kind="ExternalInput").ap()
    b2 = nc.dram_tensor("b2", [8, D], F32, kind="ExternalInput").ap()
    x_out = nc.dram_tensor("x_out", [NTOK, D], F32, kind="ExternalOutput").ap()
    ctx = Ctx(nc, es)
    ctx.ps_pool()
    emit_idents(ctx)
    ctx.open_stage()
    emit_moe(ctx, x_in, x_out, wg, w1, b1, w2, b2, "mo")
    ctx.close_stage()
    emit_drain_outputs(ctx)
    es.close()
    return nc



def emit_tmix(ctx, x_in, x_out, kwT_d, vwT_d, owT_d, tsg_d, lnw_d, lnb_d,
              td_d, tf_d, tag):
    nc = ctx.nc
    psf, psw, psb = ctx.ps_pool()
    ident = ctx.bufs["ident_bf"]
    identf = ctx.bufs["ident_f32"]
    N = 64

    G = ctx.ssb("txG", (128, D), F32)
    OMG = ctx.ssb("txOMG", (128, D), F32)
    kTf = ctx.ssb("txkTf", (128, DCH * N), F32)
    vTf = ctx.ssb("txvTf", (128, DCH * N), F32)
    kTs = ctx.ssb("txkTs", (128, DCH * N), BF16)
    vTs = ctx.ssb("txvTs", (128, DCH * N), BF16)
    kTb = ctx.ssb("txkTb", (128, DCH * N), BF16)
    vTb = ctx.ssb("txvTb", (128, DCH * N), BF16)
    lnw = ctx.ssb("txlnw", (128, DCH), F32)
    lnbb = ctx.ssb("txlnb", (128, DCH), BF16)
    oT = ctx.ssb("txoT", (64, D), BF16)
    tsg = ctx.ssb("txtsg", (1, D), F32)
    grow = ctx.ssb("txgrow", (1, D), F32)
    onesf = ctx.ssb("txonesf", (1, 128), F32)
    tdc = ctx.ssb("txtd", (64, 1), F32)
    tfc = ctx.ssb("txtf", (64, 1), F32)
    wcol = ctx.ssb("txw", (64, 1), F32)
    netf = ctx.ssb("txnetf", (64, 1), F32)
    kbias = ctx.ssb("txkb", (64, 1), F32)
    vbias = ctx.ssb("txvb", (64, 1), F32)
    kbr = ctx.ssb("txkbr", (1, 128), F32)
    xt = [ctx.ssb(f"txxt{p}", (128, D), F32) for p in range(2)]
    xr = [ctx.ssb(f"txxr{p}", (128, D), F32) for p in range(2)]
    aa = [ctx.ssb(f"txaa{p}", (128, D), F32) for p in range(2)]
    ar = [ctx.ssb(f"txar{p}", (128, D), F32) for p in range(2)]
    xs = [ctx.ssb(f"txxs{p}", (128, D), F32) for p in range(2)]
    sta = [ctx.ssb(f"txsa{p}", (128, 12), F32) for p in range(2)]
    str_ = [ctx.ssb(f"txsr{p}", (128, 12), F32) for p in range(2)]
    sts = [ctx.ssb(f"txss{p}", (128, 12), F32) for p in range(2)]
    mva = [ctx.ssb(f"txma{p}", (128, 2), F32) for p in range(2)]
    mvr = [ctx.ssb(f"txmr{p}", (128, 2), F32) for p in range(2)]
    mvs = [ctx.ssb(f"txms{p}", (128, 2), F32) for p in range(2)]
    rsa = [ctx.ssb(f"txra{p}", (128, 1), F32) for p in range(2)]
    rsr = [ctx.ssb(f"txrr{p}", (128, 1), F32) for p in range(2)]
    rss = [ctx.ssb(f"txrz{p}", (128, 1), F32) for p in range(2)]
    nma = [ctx.ssb(f"txna{p}", (128, 1), F32) for p in range(2)]
    nmr = [ctx.ssb(f"txnr{p}", (128, 1), F32) for p in range(2)]
    nms = [ctx.ssb(f"txnz{p}", (128, 1), F32) for p in range(2)]
    t1m = [ctx.ssb(f"txt1{p}", (128, D), F32) for p in range(2)]
    xn2 = [ctx.ssb(f"txn2{p}", (128, D), BF16) for p in range(2)]
    xn2f = ctx.ssb("txn2f", (128, DCH * 512), BF16)
    kbuf = ctx.ssb("txkv", (64, 512), F32)
    vbuf = ctx.ssb("txvv", (64, 512), F32)
    ekb = ctx.ssb("txek", (64, 512), F32)
    WKV = ctx.ssb("txwkv", (64, T), F32)
    states = ctx.ssb("txsts2", (64, T), F32)
    stb = ctx.ssb("txstb", (64, T), BF16)
    xot = [ctx.ssb(f"txxo{p}", (128, D), F32) for p in range(2)]
    xnt = [ctx.ssb(f"txxn{p}", (128, D), F32) for p in range(2)]

    dv, ac, pe, ev = "Tdv", "Tac", "Tpe", "Tev"

    with nc.Block() as blk:
        def wload(gp):
            for (dst, srcd) in ((kTf, kwT_d), (vTf, vwT_d)):
                for c in range(DCH):
                    ctx.inc("Twl", gp.dma_start(
                        out=dst[:, c * N:(c + 1) * N],
                        in_=srcd[c * 128:(c + 1) * 128, :]), 16)
            for (dst, srcd) in ((kTb, kwT_d), (vTb, vwT_d)):
                for c in range(DCH):
                    ctx.inc("Twl", gp.dma_start(
                        out=dst[:, c * N:(c + 1) * N],
                        in_=srcd[c * 128:(c + 1) * 128, :]), 16)
            ctx.inc("Twl", gp.dma_start(out=oT[:, :], in_=owT_d[:, :]), 16)
            ctx.inc("Twl", gp.dma_start(out=tsg[:, :], in_=tsg_d[:, :]), 16)
            ctx.inc("Twl", gp.dma_start(out=lnw[:, :], in_=lnw_d[:, :]), 16)
            ctx.inc("Twl", gp.dma_start(out=lnbb[:, :], in_=lnb_d[:, :]), 16)
            ctx.inc("Twl", gp.dma_start(out=tdc[:, :], in_=td_d[:, :]), 16)
            ctx.inc("Twl", gp.dma_start(out=tfc[:, :], in_=tf_d[:, :]), 16)
            ctx.wait(gp, "Twl")
            ctx.inc("sv", gp.memset(onesf[:, :], 1.0))
            ctx.wait(gp, "sv")
        blk.gpsimd(wload)

    with nc.Block() as blk:
        def setup_act(sc):
            ctx.inc(ac, sc.activation(grow[:, :], tsg[:, :], AF.Sigmoid))
            ctx.inc(ac, sc.activation(wcol[:, :], tdc[:, :], AF.Exp))
            ctx.inc(ac, sc.activation(netf[:, :], tfc[:, :], AF.Exp))
            ctx.mark((tag, "sact"), ac)
        blk.scalar(setup_act)

        def setup_dv(v):
            ctx.wait_cp(v, (tag, "sact"))
            ctx.inc(dv, v.tensor_scalar(netf[:, :], netf[:, :], -1.0, None,
                                        ALU.mult))
            for c in range(DCH):
                ctx.inc(dv, v.tensor_scalar(kTs[:, c * N:(c + 1) * N],
                                            kTf[:, c * N:(c + 1) * N],
                                            lnw[:, c:c + 1], None, ALU.mult))
                ctx.inc(dv, v.tensor_scalar(vTs[:, c * N:(c + 1) * N],
                                            vTf[:, c * N:(c + 1) * N],
                                            lnw[:, c:c + 1], None, ALU.mult))
            ctx.mark((tag, "sdv"), dv)
        blk.vector(setup_dv)

        def setup_pe(te):
            ctx.wait_cp(te, (tag, "sdv"))
            for half in range(2):
                last = te.matmul(psw[:, half * 512:(half + 1) * 512],
                                 onesf[0:1, :],
                                 grow[0:1, half * 512:(half + 1) * 512],
                                 start=True, stop=True)
                ctx.inc(pe, last)
            for c in range(DCH):
                te.matmul(psf[0][0:1, 0:N], lnbb[:, c:c + 1],
                          kTb[:, c * N:(c + 1) * N],
                          start=(c == 0), stop=(c == DCH - 1))
            for c in range(DCH):
                last = te.matmul(psf[0][0:1, N:2 * N], lnbb[:, c:c + 1],
                                 vTb[:, c * N:(c + 1) * N],
                                 start=(c == 0), stop=(c == DCH - 1))
            ctx.inc(pe, last)
            ctx.mark((tag, "spe"), pe)
        blk.tensor(setup_pe)

        def setup_ev(v):
            ctx.wait_cp(v, (tag, "spe"))
            ctx.inc(ev, v.tensor_copy(G[:, :], psw[:, :]))
            ctx.inc(ev, v.tensor_copy(kbr[:, :], psf[0][0:1, 0:128]))
            ctx.wait(v, ev)
            ctx.inc(ev, v.tensor_scalar(OMG[:, :], G[:, :], -1.0, 1.0,
                                        ALU.mult, ALU.add))
            ctx.mark((tag, "sev"), ev)
        blk.vector(setup_ev)

        def setup_pe2(te):
            ctx.wait_cp(te, (tag, "sev"))
            ctx.inc(pe, te.transpose(psf[1][:, 0:1], kbr[:, :], identf[0:1, 0:1]))
            ctx.mark((tag, "spe2"), pe)
        blk.tensor(setup_pe2)

        def setup_ev2(v):
            ctx.wait_cp(v, (tag, "spe2"))
            ctx.inc(ev, v.tensor_copy(kbias[:, :], psf[1][0:N, 0:1]))
            ctx.inc(ev, v.tensor_copy(vbias[:, :], psf[1][N:2 * N, 0:1]))
        blk.vector(setup_ev2)

    for b in range(BL):
        row0 = b * T
        with nc.Block() as blk:
            for grp in range(4):
                for sl in range(4):
                    s = grp * 4 + sl
                    p = s % 2

                    def dma_in(sync, s=s, p=p):
                        ctx.wait_cp(sync, (tag, "xtfree", b, s - 2))
                        r0 = row0 + s * 128
                        rr = row0 + ((s * 128 + 1024) % T)
                        ctx.inc(f"Tld{p}", sync.dma_start(
                            out=xt[p][:, :], in_=x_in[r0:r0 + 128, :]), 16)
                        ctx.inc(f"Tld{p}", sync.dma_start(
                            out=xr[p][:, :], in_=x_in[rr:rr + 128, :]), 16)
                        ctx.mark((tag, "xload", b, s), f"Tld{p}")
                    blk.sync(dma_in)

                    def bn_ar(v, s=s, p=p):
                        ctx.wait_cp(v, (tag, "xload", b, s))
                        for (src, stt, mvv, i) in ((xt[p], sta[p], mva[p], 0),
                                                   (xr[p], str_[p], mvr[p], 1)):
                            ctx.inc(dv, v.bn_stats(stt[:, 0:6], src[:, 0:512]))
                            ctx.inc(dv, v.bn_stats(stt[:, 6:12], src[:, 512:1024]))
                            ctx.wait(v, dv)
                            ctx.inc(dv, v.bn_aggr(mvv[:, :], stt[:, :]))
                            ctx.mark((tag, "agg", b, s, i), dv)
                    blk.vector(bn_ar)

                    def rstds(sc, s=s, p=p):
                        for (mvv, rsv, i) in ((mva[p], rsa[p], 0),
                                              (mvr[p], rsr[p], 1)):
                            ctx.wait_cp(sc, (tag, "agg", b, s, i))
                            ctx.inc(ac, sc.activation(rsv[:, :], mvv[:, 1:2],
                                                      AF.Ln, bias=ctx.bufs["eps"][:, 0:1]))
                            ctx.wait(sc, ac)
                            ctx.inc(ac, sc.activation(rsv[:, :], rsv[:, :],
                                                      AF.Exp, scale=-0.5))
                            ctx.mark((tag, "rstd", b, s, i), ac)
                    blk.scalar(rstds)

                    def nm_dv(v, s=s, p=p):
                        for (mvv, rsv, nmv, i) in ((mva[p], rsa[p], nma[p], 0),
                                                   (mvr[p], rsr[p], nmr[p], 1)):
                            ctx.wait_cp(v, (tag, "rstd", b, s, i))
                            ctx.inc(dv, v.scalar_tensor_tensor(
                                nmv[:, :], mvv[:, 0:1], -1.0, rsv[:, :],
                                ALU.mult, ALU.mult))
                        ctx.mark((tag, "nm", b, s), dv)
                    blk.vector(nm_dv)

                    def norms(sc, s=s, p=p):
                        ctx.wait_cp(sc, (tag, "nm", b, s))
                        ctx.wait_cp(sc, (tag, "aafree", b, s - 2))
                        ctx.inc(ac, sc.activation(aa[p][:, :], xt[p][:, :],
                                                  AF.Identity, bias=nma[p][:, 0:1],
                                                  scale=rsa[p][:, 0:1]))
                        ctx.inc(ac, sc.activation(ar[p][:, :], xr[p][:, :],
                                                  AF.Identity, bias=nmr[p][:, 0:1],
                                                  scale=rsr[p][:, 0:1]))
                        ctx.mark((tag, "norm", b, s), ac)
                        ctx.mark((tag, "xtfree", b, s), ac)
                    blk.scalar(norms)

                    def mix(v, s=s, p=p):
                        ctx.wait_cp(v, (tag, "norm", b, s))
                        ctx.wait_cp(v, (tag, "xsfree", b, s - 2))
                        ctx.inc(dv, v.tensor_tensor(t1m[p][:, :], ar[p][:, :],
                                                    G[:, :], ALU.mult))
                        ctx.inc(dv, v.tensor_tensor(xs[p][:, :], aa[p][:, :],
                                                    OMG[:, :], ALU.mult))
                        ctx.wait(v, dv)
                        ctx.inc(dv, v.tensor_tensor(xs[p][:, :], xs[p][:, :],
                                                    t1m[p][:, :], ALU.add))
                        ctx.mark((tag, "mix", b, s), dv)
                        ctx.mark((tag, "aafree", b, s), dv)
                        ctx.wait(v, dv)
                        ctx.inc(dv, v.bn_stats(sts[p][:, 0:6], xs[p][:, 0:512]))
                        ctx.inc(dv, v.bn_stats(sts[p][:, 6:12], xs[p][:, 512:1024]))
                        ctx.wait(v, dv)
                        ctx.inc(dv, v.bn_aggr(mvs[p][:, :], sts[p][:, :]))
                        ctx.mark((tag, "agg2", b, s), dv)
                    blk.vector(mix)

                    def rstd2(sc, s=s, p=p):
                        ctx.wait_cp(sc, (tag, "agg2", b, s))
                        ctx.inc(ac, sc.activation(rss[p][:, :], mvs[p][:, 1:2],
                                                  AF.Ln, bias=ctx.bufs["eps"][:, 0:1]))
                        ctx.wait(sc, ac)
                        ctx.inc(ac, sc.activation(rss[p][:, :], rss[p][:, :],
                                                  AF.Exp, scale=-0.5))
                        ctx.mark((tag, "rstd2", b, s), ac)
                    blk.scalar(rstd2)

                    def nm2_dv(v, s=s, p=p):
                        ctx.wait_cp(v, (tag, "rstd2", b, s))
                        ctx.inc(dv, v.scalar_tensor_tensor(
                            nms[p][:, :], mvs[p][:, 0:1], -1.0, rss[p][:, :],
                            ALU.mult, ALU.mult))
                        ctx.mark((tag, "nm2", b, s), dv)
                    blk.vector(nm2_dv)

                    def norm2(sc, s=s, p=p):
                        ctx.wait_cp(sc, (tag, "nm2", b, s))
                        ctx.wait_cp(sc, (tag, "xn2free", b, s - 2))
                        ctx.inc(ac, sc.activation(xn2[p][:, :], xs[p][:, :],
                                                  AF.Identity, bias=nms[p][:, 0:1],
                                                  scale=rss[p][:, 0:1]))
                        ctx.mark((tag, "norm2", b, s), ac)
                        ctx.mark((tag, "xsfree", b, s), ac)
                    blk.scalar(norm2)

                    def tpose(te, s=s, p=p):
                        ctx.wait_cp(te, (tag, "norm2", b, s))
                        ctx.wait_cp(te, (tag, "tpfree", b, s - 2))
                        last = None
                        for c in range(DCH):
                            last = te.transpose(psb[p][:, c * 128:(c + 1) * 128],
                                                xn2[p][:, c * 128:(c + 1) * 128],
                                                ident[:, :])
                        ctx.inc(pe, last)
                        ctx.mark((tag, "tp", b, s), pe)
                        ctx.mark((tag, "xn2free", b, s), pe)
                    blk.tensor(tpose)

                    def tevac(v, s=s, p=p, sl=sl, grp=grp):
                        ctx.wait_cp(v, (tag, "tp", b, s))
                        if sl == 0:
                            ctx.wait_cp(v, (tag, "kvmm", b, grp - 1))
                        dst = xn2f[:, :].rearrange("q (c u) -> q c u", c=DCH)[
                            :, :, sl * 128:(sl + 1) * 128]
                        src = psb[p][:, :].rearrange("q (c u) -> q c u", c=DCH)
                        ctx.inc(ev, v.tensor_copy(dst, src))
                        ctx.mark((tag, "tpfree", b, s), ev)
                        ctx.mark((tag, "xn2f", b, s), ev)
                    blk.vector(tevac)

                def kvmm(te, grp=grp, b=b):
                    ctx.wait_cp(te, (tag, "xn2f", b, grp * 4 + 3))
                    ctx.wait_cp(te, (tag, "kvev", b, grp - 1))
                    for (dstps, wT) in ((psf[0], kTs), (psf[1], vTs)):
                        last = None
                        for c in range(DCH):
                            last = te.matmul(
                                dstps[0:N, :], wT[:, c * N:(c + 1) * N],
                                xn2f[:, c * 512:(c + 1) * 512],
                                start=(c == 0), stop=(c == DCH - 1))
                        ctx.inc(pe, last)
                    ctx.mark((tag, "kvmm", b, grp), pe)
                blk.tensor(kvmm)

                def kvev(v, grp=grp, b=b):
                    ctx.wait_cp(v, (tag, "kvmm", b, grp))
                    ctx.inc(dv, v.tensor_scalar(kbuf[:, :], psf[0][0:N, :],
                                                kbias[:, 0:1], None, ALU.add))
                    ctx.inc(dv, v.tensor_scalar(vbuf[:, :], psf[1][0:N, :],
                                                vbias[:, 0:1], None, ALU.add))
                    ctx.mark((tag, "kvev", b, grp), dv)
                blk.vector(kvev)

                def ekact(sc, grp=grp, b=b):
                    ctx.wait_cp(sc, (tag, "kvev", b, grp))
                    ctx.wait_cp(sc, (tag, "wkvd", b, grp - 1))
                    ctx.inc(ac, sc.activation(ekb[:, :], kbuf[:, :], AF.Exp,
                                              scale=netf[:, 0:1]))
                    ctx.mark((tag, "ek", b, grp), ac)
                blk.scalar(ekact)

                def wkvmul(v, grp=grp, b=b):
                    ctx.wait_cp(v, (tag, "ek", b, grp))
                    ctx.inc(dv, v.tensor_tensor(WKV[:, grp * 512:(grp + 1) * 512],
                                                ekb[:, :], vbuf[:, :], ALU.mult))
                    ctx.mark((tag, "wkv", b, grp), dv)
                    ctx.mark((tag, "wkvd", b, grp), dv)
                blk.vector(wkvmul)

            def scan(v, b=b):
                ctx.wait_cp(v, (tag, "wkv", b, 3))
                w_b = bass.AP(wcol, 0, [[1, N], [0, T]])
                ctx.inc(dv, v.tensor_tensor_scan(states[:, :], w_b, WKV[:, :],
                                                 0.0, ALU.mult, ALU.add))
                ctx.wait(v, dv)
                ctx.inc(dv, v.tensor_copy(stb[:, :], states[:, :]))
                ctx.mark((tag, "scan", b), dv)
            blk.vector(scan)

            for s in range(16):
                p = s % 2

                def xold(sync, s=s, p=p):
                    ctx.wait_cp(sync, (tag, "xofree", b, s - 2))
                    ctx.inc(f"Tlo{p}", sync.dma_start(
                        out=xot[p][:, :],
                        in_=x_in[row0 + s * 128: row0 + (s + 1) * 128, :]), 16)
                    ctx.mark((tag, "xold", b, s), f"Tlo{p}")
                blk.sync(xold)

                def oproj(te, s=s, p=p):
                    ctx.wait_cp(te, (tag, "scan", b))
                    ctx.wait_cp(te, (tag, "resid", b, s - 1))
                    last = None
                    for half in range(2):
                        last = te.matmul(psw[:, half * 512:(half + 1) * 512],
                                         stb[:, s * 128:(s + 1) * 128],
                                         oT[:, half * 512:(half + 1) * 512],
                                         start=True, stop=True)
                        ctx.inc(pe, last)
                    ctx.mark((tag, "omm", b, s), pe)
                blk.tensor(oproj)

                def resid(v, s=s, p=p, b=b):
                    ctx.wait_cp(v, (tag, "omm", b, s))
                    ctx.wait_cp(v, (tag, "xold", b, s))
                    ctx.wait_cp(v, (tag, "xnfree", b * 16 + s - 2))
                    ctx.inc(dv, v.tensor_tensor(xnt[p][:, :], psw[:, :],
                                                xot[p][:, :], ALU.add))
                    ctx.mark((tag, "resid", b, s), dv)
                    ctx.mark((tag, "xofree", b, s), dv)
                blk.vector(resid)

                def store(gp, s=s, p=p, b=b):
                    ctx.wait_cp(gp, (tag, "resid", b, s))
                    ctx.inc(f"Tst{p}", gp.dma_start(
                        out=x_out[row0 + s * 128: row0 + (s + 1) * 128, :],
                        in_=xnt[p][:, :]), 16)
                    ctx.mark((tag, "xnfree", b * 16 + s), f"Tst{p}")
                blk.gpsimd(store)


def build_tmix_test():
    es = ExitStack()
    nc = bass.Bass("TRN2", target_bir_lowering=False)
    x_in = nc.dram_tensor("x_in", [NTOK, D], F32, kind="ExternalInput").ap()
    kwT = nc.dram_tensor("kwT", [D, 64], F32, kind="ExternalInput").ap()
    vwT = nc.dram_tensor("vwT", [D, 64], F32, kind="ExternalInput").ap()
    owT = nc.dram_tensor("owT", [64, D], F32, kind="ExternalInput").ap()
    tsg = nc.dram_tensor("tsg", [1, D], F32, kind="ExternalInput").ap()
    lnw = nc.dram_tensor("lnw", [128, DCH], F32, kind="ExternalInput").ap()
    lnb = nc.dram_tensor("lnb", [128, DCH], F32, kind="ExternalInput").ap()
    td = nc.dram_tensor("td", [64, 1], F32, kind="ExternalInput").ap()
    tf = nc.dram_tensor("tf", [64, 1], F32, kind="ExternalInput").ap()
    x_out = nc.dram_tensor("x_out", [NTOK, D], F32, kind="ExternalOutput").ap()
    ctx = Ctx(nc, es)
    ctx.ps_pool()
    emit_idents(ctx)
    ctx.open_stage()
    emit_tmix(ctx, x_in, x_out, kwT, vwT, owT, tsg, lnw, lnb, td, tf, "tx")
    ctx.close_stage()
    emit_drain_outputs(ctx)
    es.close()
    return nc



MSCALE = 1.0 / np.sqrt(96.0)

def emit_mla(ctx, x_in, x_out, wq_d, wqr_d, wkr_d, wkvd_d, wku_d, wvu_d, wo_d,
             lobo_d, rcos_d, rsin_d, tag):
    nc = ctx.nc
    psf, psw, psb = ctx.ps_pool()
    ident = ctx.bufs["ident_bf"]

    Kc = ctx.ssb("mlaKc", (128, DCH * T), BF16)
    Krf = ctx.ssb("mlaKr", (128, T), BF16)
    Vt = ctx.ssb("mlaVt", (128, (T // 128) * D), BF16)
    xnf = ctx.ssb("mlaxnf", (128, DCH * T), BF16)
    latf = ctx.ssb("mlalat", (128, 2 * 512), BF16)
    wbuf = ctx.ssb("mlawb", (128, DCH * D), BF16)
    wo2 = ctx.ssb("mlawo2", (128, DCH * D), BF16)
    wqrb = ctx.ssb("mlawqr", (128, DCH * 512), BF16)
    wkrb = ctx.ssb("mlawkr", (128, DCH * 32), BF16)
    wkvb = ctx.ssb("mlawkv", (128, DCH * 256), BF16)
    wkub = ctx.ssb("mlawku", (128, 2 * D), BF16)
    wvub = ctx.ssb("mlawvu", (128, 2 * D), BF16)
    rcos = ctx.ssb("mlacos", (128, 256), F32)
    rsin = ctx.ssb("mlasin", (128, 256), F32)
    lobo = ctx.ssb("mlalobo", (1, 16), F32)
    Cb = ctx.ssb("mlaCb", (128, 16), F32)
    onesf = ctx.ssb("mlaones", (1, 128), F32)
    crow = ctx.ssb("mlacrow", (1, 16), F32)
    qt = [ctx.ssb("mlaqt", (128, DCH * 128), BF16)] * 2
    qrf = [ctx.ssb("mlaqrf", (128, 512), BF16)] * 2
    qrt = ctx.ssb("mlaqrt", (128, 512), BF16)
    ta_ = ctx.ssb("mlata", (128, 256), F32)
    tb_ = ctx.ssb("mlatb", (128, 256), F32)
    krt = ctx.ssb("mlakrt", (128, 32), BF16)
    srow = [ctx.ssb("mlasr", (128, T), F32)] * 2
    pb = [ctx.ssb(f"mlapb{p}", (128, T), BF16) for p in range(2)]
    ptb = [ctx.ssb(f"mlapt{p}", (128, T), BF16) for p in range(2)]
    yfm = [ctx.ssb("mlayf", (128, DCH * 128), BF16)] * 2
    mx = [ctx.ssb(f"mlamx{p}", (128, 1), F32) for p in range(2)]
    ngm = [ctx.ssb(f"mlangm{p}", (128, 1), F32) for p in range(2)]
    sxp = [ctx.ssb(f"mlasxp{p}", (128, 1), F32) for p in range(2)]
    rcp = [ctx.ssb(f"mlarcp{p}", (128, 1), F32) for p in range(2)]
    xnt = [ctx.ssb("mlaxnt", (128, D), F32)] * 2
    xot = xnt

    wl, pq, evq, pvs, sev, dvx, axp, pvm, pop, res = (
        f"A{n}" for n in ("wl", "pq", "evq", "pvs", "sev", "dvx",
                          "axp", "pvm", "pop", "res"))

    with nc.Block() as blk:
        def g0(gp):
            for c in range(DCH):
                ctx.inc(wl, gp.dma_start(out=wqrb[:, c * 512:(c + 1) * 512],
                                         in_=wqr_d[c * 128:(c + 1) * 128, :]), 16)
                ctx.inc(wl, gp.dma_start(out=wkrb[:, c * 32:(c + 1) * 32],
                                         in_=wkr_d[c * 128:(c + 1) * 128, :]), 16)
                ctx.inc(wl, gp.dma_start(out=wkvb[:, c * 256:(c + 1) * 256],
                                         in_=wkvd_d[c * 128:(c + 1) * 128, :]), 16)
            for l2 in range(2):
                ctx.inc(wl, gp.dma_start(out=wkub[:, l2 * D:(l2 + 1) * D],
                                         in_=wku_d[l2 * 128:(l2 + 1) * 128, :]), 16)
                ctx.inc(wl, gp.dma_start(out=wvub[:, l2 * D:(l2 + 1) * D],
                                         in_=wvu_d[l2 * 128:(l2 + 1) * 128, :]), 16)
            ctx.inc(wl, gp.dma_start(out=rcos[:, :], in_=rcos_d[:, :]), 16)
            ctx.inc(wl, gp.dma_start(out=rsin[:, :], in_=rsin_d[:, :]), 16)
            ctx.inc(wl, gp.dma_start(out=lobo[:, :], in_=lobo_d[:, :]), 16)
            ctx.wait(gp, wl)
            ctx.inc("sv", gp.memset(onesf[:, :], 1.0))
            ctx.wait(gp, "sv")
        blk.gpsimd(g0)

    with nc.Block() as blk:
        def cexp(sc):
            ctx.inc(axp, sc.activation(crow[:, :], lobo[:, :], AF.Exp))
            ctx.mark((tag, "cexp"), axp)
        blk.scalar(cexp)

        def cbc(te):
            ctx.wait_cp(te, (tag, "cexp"))
            ctx.inc(pq, te.matmul(psf[0][:, 0:16], onesf[0:1, :], crow[0:1, :],
                                  start=True, stop=True))
            ctx.mark((tag, "cbc"), pq)
        blk.tensor(cbc)

        def cev(v):
            ctx.wait_cp(v, (tag, "cbc"))
            ctx.inc(sev, v.tensor_copy(Cb[:, :], psf[0][:, 0:16]))
        blk.vector(cev)

    def load_w(dst, src_dram, width):
        with nc.Block() as blk:
            def g(gp):
                for c in range(DCH):
                    ctx.inc(wl, gp.dma_start(
                        out=dst[:, c * width:(c + 1) * width],
                        in_=src_dram[c * 128:(c + 1) * 128, :]), 16)
                ctx.wait(gp, wl)
            blk.gpsimd(g)

    for b in range(BL):
        row0 = b * T

        with nc.Block() as blk:
            emit_ln_to_fm(ctx, blk, x_in, row0, T, xnf, T, 0, (tag, "ln", b))

        with nc.Block() as blk:
            for tb in range(T // 512):
                def latmm(te, tb=tb):
                    ctx.wait_cp(te, (tag, "latev", b, tb - 1))
                    ctx.wait_cp(te, (tag, "krrope", b, tb * 4 - 1))
                    for l2 in range(2):
                        last = None
                        for c in range(DCH):
                            last = te.matmul(
                                psf[l2][:, :],
                                wkvb[:, c * 256 + l2 * 128: c * 256 + (l2 + 1) * 128],
                                xnf[:, c * T + tb * 512: c * T + (tb + 1) * 512],
                                start=(c == 0), stop=(c == DCH - 1))
                        ctx.inc(pq, last)
                    ctx.mark((tag, "latmm", b, tb), pq)
                blk.tensor(latmm)

                def latev(v, tb=tb):
                    ctx.wait_cp(v, (tag, "latmm", b, tb))
                    ctx.wait_cp(v, (tag, "kvdone", b, tb - 1))
                    for l2 in range(2):
                        ctx.inc(evq, v.tensor_copy(latf[:, l2 * 512:(l2 + 1) * 512],
                                                   psf[l2][:, :]))
                    ctx.mark((tag, "latev", b, tb), evq)
                blk.vector(latev)

                for do in range(DCH):
                    def kcmm(te, tb=tb, do=do):
                        ctx.wait_cp(te, (tag, "latev", b, tb))
                        ctx.wait_cp(te, (tag, "kcev", b, tb, do - 2))
                        last = None
                        for l2 in range(2):
                            last = te.matmul(
                                psf[2 + do % 2][:, :],
                                wkub[:, l2 * D + do * 128: l2 * D + (do + 1) * 128],
                                latf[:, l2 * 512:(l2 + 1) * 512],
                                start=(l2 == 0), stop=(l2 == 1))
                        ctx.inc(pq, last)
                        ctx.mark((tag, "kcmm", b, tb, do), pq)
                    blk.tensor(kcmm)

                    def kcev(v, tb=tb, do=do):
                        ctx.wait_cp(v, (tag, "kcmm", b, tb, do))
                        ctx.inc(evq, v.tensor_copy(
                            Kc[:, do * T + tb * 512: do * T + (tb + 1) * 512],
                            psf[2 + do % 2][:, :]))
                        ctx.mark((tag, "kcev", b, tb, do), evq)
                    blk.vector(kcev)

                for tl in range(4):
                    kt = tb * 4 + tl

                    def vmm(te, tb=tb, tl=tl, kt=kt):
                        ctx.wait_cp(te, (tag, "kcev", b, tb, DCH - 1))
                        ctx.wait_cp(te, (tag, "vev", b, kt - 1))
                        for half in range(2):
                            last = None
                            for l2 in range(2):
                                last = te.matmul(
                                    psf[2 * (tl % 2) + half][:, :] if False else
                                    psf[2 + half][:, :],
                                    latf[:, l2 * 512 + tl * 128: l2 * 512 + (tl + 1) * 128],
                                    wvub[:, l2 * D + half * 512: l2 * D + (half + 1) * 512],
                                    start=(l2 == 0), stop=(l2 == 1))
                            ctx.inc(pq, last)
                        ctx.mark((tag, "vmm", b, kt), pq)
                    blk.tensor(vmm)

                    def vev(v, kt=kt):
                        ctx.wait_cp(v, (tag, "vmm", b, kt))
                        for half in range(2):
                            ctx.inc(evq, v.tensor_copy(
                                Vt[:, kt * D + half * 512: kt * D + (half + 1) * 512],
                                psf[2 + half][:, :]))
                        ctx.mark((tag, "vev", b, kt), evq)
                    blk.vector(vev)

                for tl in range(4):
                    kt = tb * 4 + tl

                    def krmm(te, tb=tb, tl=tl, kt=kt):
                        ctx.wait_cp(te, (tag, "krrope", b, kt - 1))
                        last = None
                        for c in range(DCH):
                            last = te.matmul(
                                psf[0][:, 0:32],
                                xnf[:, c * T + kt * 128: c * T + (kt + 1) * 128],
                                wkrb[:, c * 32:(c + 1) * 32],
                                start=(c == 0), stop=(c == DCH - 1))
                        ctx.inc(pq, last)
                        ctx.mark((tag, "krmm", b, kt), pq)
                    blk.tensor(krmm)

                    def krrope(v, kt=kt):
                        ctx.wait_cp(v, (tag, "krmm", b, kt))
                        ctx.wait_cp(v, (tag, "krtp", b, kt - 1))
                        cs = rcos[:, (kt % 16) * 16:(kt % 16) * 16 + 16]
                        sn = rsin[:, (kt % 16) * 16:(kt % 16) * 16 + 16]
                        q1 = psf[0][:, 0:16]
                        q2 = psf[0][:, 16:32]
                        ctx.inc(dvx, v.tensor_tensor(ta_[:, 0:16], q1, cs, ALU.mult))
                        ctx.inc(dvx, v.tensor_tensor(tb_[:, 0:16], q2, sn, ALU.mult))
                        ctx.wait(v, dvx)
                        ctx.inc(dvx, v.tensor_tensor(krt[:, 0:16], ta_[:, 0:16],
                                                     tb_[:, 0:16], ALU.subtract))
                        ctx.wait(v, dvx)
                        ctx.inc(dvx, v.tensor_tensor(ta_[:, 0:16], q2, cs, ALU.mult))
                        ctx.inc(dvx, v.tensor_tensor(tb_[:, 0:16], q1, sn, ALU.mult))
                        ctx.wait(v, dvx)
                        ctx.inc(dvx, v.tensor_tensor(krt[:, 16:32], ta_[:, 0:16],
                                                     tb_[:, 0:16], ALU.add))
                        ctx.mark((tag, "krrope", b, kt), dvx)
                    blk.vector(krrope)

                    def krtp(te, kt=kt):
                        ctx.wait_cp(te, (tag, "krrope", b, kt))
                        ctx.wait_cp(te, (tag, "krev", b, kt - 1))
                        ctx.inc(pvm, te.transpose(psb[0][:, 0:32] if False else
                                                  psb[0][0:32, 0:128],
                                                  krt[:, 0:32], ident[:, :]))
                        ctx.mark((tag, "krtp", b, kt), pvm)
                    blk.tensor(krtp)

                    def krev(v, kt=kt):
                        ctx.wait_cp(v, (tag, "krtp", b, kt))
                        for a in range(4):
                            ctx.inc(evq, v.tensor_copy(
                                Krf[32 * a:32 * a + 32, kt * 128:(kt + 1) * 128],
                                psb[0][0:32, 0:128]))
                        ctx.mark((tag, "krev", b, kt), evq)
                    blk.vector(krev)

            ctx.mark((tag, "kvdone", b, 100), evq)

        if b == 0:
            load_w(wbuf, wq_d, D)
            load_w(wo2, wo_d, D)
        with nc.Block() as blk:
            for qi in range(T // 128):
                qp = qi % 2
                klen = (qi + 1) * 128
                nkt = qi + 1
                nkb = (klen + 511) // 512
                q0 = row0 + qi * 128

                for rnd in range(2):
                    def qmm(te, qi=qi, rnd=rnd):
                        ctx.wait_cp(te, (tag, "qfree", b, qi - 1))
                        ctx.wait_cp(te, (tag, "qev", b, qi, rnd - 1))
                        if rnd == 0:
                            ctx.wait_cp(te, (tag, "qev", b, qi - 1, 1))
                        for do in range(4 * rnd, 4 * rnd + 4):
                            last = None
                            for c in range(DCH):
                                last = te.matmul(
                                    psf[2][:, (do % 4) * 128:(do % 4) * 128 + 128],
                                    wbuf[:, c * D + do * 128: c * D + (do + 1) * 128],
                                    xnf[:, c * T + qi * 128: c * T + (qi + 1) * 128],
                                    start=(c == 0), stop=(c == DCH - 1))
                            ctx.inc(pq, last)
                        ctx.mark((tag, "qmm", b, qi, rnd), pq)
                    blk.tensor(qmm)

                    def qev(v, qi=qi, qp=qp, rnd=rnd):
                        ctx.wait_cp(v, (tag, "qmm", b, qi, rnd))
                        ctx.inc(evq, v.tensor_scalar(
                            qt[qp][:, rnd * 512:(rnd + 1) * 512], psf[2][:, :],
                            MSCALE, None, ALU.mult))
                        ctx.mark((tag, "qev", b, qi, rnd), evq)
                        if rnd == 1:
                            ctx.mark((tag, "qdone", b, qi), evq)
                    blk.vector(qev)

                def qrmm(te, qi=qi):
                    ctx.wait_cp(te, (tag, "qrrope", b, qi - 1))
                    ctx.wait_cp(te, (tag, "yevd", b, qi * NH - 2))
                    last = None
                    for c in range(DCH):
                        last = te.matmul(
                            psf[3][:, :],
                            xnf[:, c * T + qi * 128: c * T + (qi + 1) * 128],
                            wqrb[:, c * 512:(c + 1) * 512],
                            start=(c == 0), stop=(c == DCH - 1))
                    ctx.inc(pq, last)
                    ctx.mark((tag, "qrmm", b, qi), pq)
                blk.tensor(qrmm)

                def qrrope(v, qi=qi):
                    ctx.wait_cp(v, (tag, "qrmm", b, qi))
                    ctx.wait_cp(v, (tag, "qrtp", b, qi - 1))
                    cs = bass.AP(rcos, (qi % 16) * 16, [[256, 128], [0, 16], [1, 16]])
                    sn = bass.AP(rsin, (qi % 16) * 16, [[256, 128], [0, 16], [1, 16]])
                    q1 = psf[3][:, :].rearrange("p (h i) -> p h i", i=32)[:, :, 0:16]
                    q2 = psf[3][:, :].rearrange("p (h i) -> p h i", i=32)[:, :, 16:32]
                    o1 = qrt[:, :].rearrange("p (h i) -> p h i", i=32)[:, :, 0:16]
                    o2 = qrt[:, :].rearrange("p (h i) -> p h i", i=32)[:, :, 16:32]
                    tav = ta_[:, :].rearrange("p (h i) -> p h i", i=16)
                    tbv = tb_[:, :].rearrange("p (h i) -> p h i", i=16)
                    ctx.inc(dvx, v.tensor_tensor(tav, q1, cs, ALU.mult))
                    ctx.inc(dvx, v.tensor_tensor(tbv, q2, sn, ALU.mult))
                    ctx.wait(v, dvx)
                    ctx.inc(dvx, v.tensor_tensor(o1, tav, tbv, ALU.subtract))
                    ctx.wait(v, dvx)
                    ctx.inc(dvx, v.tensor_tensor(tav, q2, cs, ALU.mult))
                    ctx.inc(dvx, v.tensor_tensor(tbv, q1, sn, ALU.mult))
                    ctx.wait(v, dvx)
                    ctx.inc(dvx, v.tensor_tensor(o2, tav, tbv, ALU.add))
                    ctx.mark((tag, "qrrope", b, qi), dvx)
                blk.vector(qrrope)

                def qrtp(te, qi=qi):
                    ctx.wait_cp(te, (tag, "qrrope", b, qi))
                    ctx.wait_cp(te, (tag, "qrev", b, qi - 1))
                    last = None
                    for j in range(4):
                        last = te.transpose(psb[0][:, j * 128:(j + 1) * 128],
                                            qrt[:, j * 128:(j + 1) * 128],
                                            ident[:, :])
                    ctx.inc(pvm, last)
                    ctx.mark((tag, "qrtp", b, qi), pvm)
                blk.tensor(qrtp)

                def qrev(v, qi=qi, qp=qp):
                    ctx.wait_cp(v, (tag, "qrtp", b, qi))
                    ctx.wait_cp(v, (tag, "qrffree", b, qi - 1))
                    ctx.inc(evq, v.tensor_scalar(qrf[qp][:, :], psb[0][:, 0:512],
                                                 MSCALE, None, ALU.mult))
                    ctx.mark((tag, "qrev", b, qi), evq)
                blk.vector(qrev)

                def _front(h):
                    g = qi * NH + h
                    hp = h % 2
                    hr = 64 * (h % 2)
                    hc = (h // 2) * 128
                    ja = h % 4

                    for kb in range(nkb):
                        kw = min(512, klen - kb * 512)
                        sb_id = ctx.cp.get((tag, "sblkid", b), 0)
                        ctx.cp[(tag, "sblkid", b)] = sb_id + 1
                        par = sb_id % 2

                        def smm(te, qi=qi, h=h, kb=kb, kw=kw, sb_id=sb_id,
                                par=par, hr=hr, hc=hc, ja=ja, qp=qp):
                            if kb == 0 and h == 0:
                                ctx.wait_cp(te, (tag, "qdone", b, qi))
                                ctx.wait_cp(te, (tag, "qrev", b, qi))
                            ctx.wait_cp(te, (tag, "sfree", b, sb_id - 2))
                            te.matmul(
                                psf[par][:, 0:kw],
                                qt[qp][hr:hr + 64, hc:hc + 128],
                                Kc[hr:hr + 64,
                                   (h // 2) * T + kb * 512:
                                   (h // 2) * T + kb * 512 + kw],
                                start=True, stop=False)
                            last = te.matmul(
                                psf[par][:, 0:kw],
                                qrf[qp][32 * ja:32 * ja + 32,
                                        (h // 4) * 128:(h // 4) * 128 + 128],
                                Krf[32 * ja:32 * ja + 32, kb * 512: kb * 512 + kw],
                                start=False, stop=True,
                                tile_position=(32 * ja, 0))
                            ctx.inc(pvs, last)
                            ctx.mark((tag, "smm", b, sb_id), pvs)
                        blk.tensor(smm)

                        def sevac(v, g=g, kb=kb, kw=kw, sb_id=sb_id, par=par,
                                  nkb=nkb, hp=hp):
                            ctx.wait_cp(v, (tag, "smm", b, sb_id))
                            if kb == 0:
                                ctx.wait_cp(v, (tag, "exp", b, g - 1))
                            if kb == nkb - 1:
                                if kw > 128:
                                    ctx.inc(sev, v.tensor_copy(
                                        srow[hp][:, kb * 512: kb * 512 + kw - 128],
                                        psf[par][:, 0:kw - 128]))
                                ctx.inc(sev, v.tensor_tensor(
                                    srow[hp][:, kb * 512 + kw - 128: kb * 512 + kw],
                                    psf[par][:, kw - 128:kw],
                                    ctx.bufs["cmask"][:, :], ALU.add))
                            else:
                                ctx.inc(sev, v.tensor_copy(
                                    srow[hp][:, kb * 512: kb * 512 + kw],
                                    psf[par][:, 0:kw]))
                            ctx.mark((tag, "sfree", b, sb_id), sev)
                            if kb == nkb - 1:
                                ctx.mark((tag, "srow", b, g), sev)
                        blk.vector(sevac)

                    def dmax(v, g=g, klen=klen, hp=hp):
                        ctx.wait_cp(v, (tag, "srow", b, g))
                        ctx.wait_cp(v, (tag, "exp", b, g - 2))
                        ctx.inc(dvx, v.reduce_max(ngm[hp][:, :],
                                                  srow[hp][:, 0:klen], AX.X,
                                                  negate=True))
                        ctx.mark((tag, "max", b, g), dvx)
                    blk.vector(dmax)

                    def aexp(sc, g=g, klen=klen, hp=hp):
                        ctx.wait_cp(sc, (tag, "max", b, g))
                        ctx.wait_cp(sc, (tag, "ptmm", b, g - 2))
                        ctx.inc(axp, sc.activation(pb[hp][:, 0:klen],
                                                   srow[hp][:, 0:klen],
                                                   AF.Exp, bias=ngm[hp][:, 0:1],
                                                   accum_out=sxp[hp][:, 0:1]))
                        ctx.mark((tag, "exp", b, g), axp)
                    blk.scalar(aexp)


                def _tail(h):
                    g = qi * NH + h
                    hp = h % 2
                    hr = 64 * (h % 2)
                    hc = (h // 2) * 128
                    ja = h % 4

                    def dscl(v, g=g, h=h, klen=klen, hp=hp):
                        ctx.wait_cp(v, (tag, "exp", b, g))
                        ctx.inc(dvx, v.tensor_scalar(rcp[hp][:, :], sxp[hp][:, :],
                                                     Cb[:, h:h + 1], None, ALU.add))
                        ctx.wait(v, dvx)
                        ctx.inc(dvx, v.reciprocal(rcp[hp][:, :], rcp[hp][:, :]))
                        ctx.wait(v, dvx)
                        ctx.inc(dvx, v.tensor_scalar(pb[hp][:, 0:klen],
                                                     pb[hp][:, 0:klen],
                                                     rcp[hp][:, 0:1], None, ALU.mult))
                        ctx.mark((tag, "pscl", b, g), dvx)
                    blk.vector(dscl)

                    for r0 in range(0, nkt, 8):
                        r1 = min(nkt, r0 + 8)

                        def ptr(te, g=g, r0=r0, r1=r1, hp=hp):
                            ctx.wait_cp(te, (tag, "pscl", b, g))
                            if r0 == 0:
                                ctx.wait_cp(te, (tag, "ptev", b, g - 2, 0))
                                ctx.wait_cp(te, (tag, "ptev", b, g - 2, 8))
                            else:
                                ctx.wait_cp(te, (tag, "ptev", b, g, 0))
                            last = None
                            for kc in range(r0, r1):
                                last = te.transpose(
                                    psb[hp][:, (kc % 8) * 128:(kc % 8) * 128 + 128],
                                    pb[hp][:, kc * 128:(kc + 1) * 128], ident[:, :])
                            ctx.inc(pvm, last)
                            ctx.mark((tag, "ptmm", b, g), pvm)
                            ctx.mark((tag, "ptmm2", b, g, r0), pvm)
                        blk.tensor(ptr)

                        def ptev(v, g=g, r0=r0, r1=r1, hp=hp, nkt=nkt):
                            ctx.wait_cp(v, (tag, "ptmm2", b, g, r0))
                            if r0 == 0:
                                ctx.wait_cp(v, (tag, "pv", b, g - 2))
                            w = (r1 - r0) * 128
                            ctx.inc(sev, v.tensor_copy(
                                ptb[hp][:, r0 * 128: r0 * 128 + w],
                                psb[hp][:, 0:w]))
                            ctx.mark((tag, "ptev", b, g, r0), sev)
                            ctx.mark((tag, "ptevd", b, g), sev)
                        blk.vector(ptev)

                    def pvmm(te, g=g, h=h, nkt=nkt, hp=hp):
                        ctx.wait_cp(te, (tag, "ptevd", b, g))
                        if h % 2 == 0:
                            ctx.wait_cp(te, (tag, "yevd", b, g - 2))
                        last = None
                        for kc in range(nkt):
                            last = te.matmul(
                                psf[3][64 * (h % 2): 64 * (h % 2) + 64, 0:128],
                                Vt[:, kc * D + 64 * h: kc * D + 64 * h + 64],
                                ptb[hp][:, kc * 128:(kc + 1) * 128],
                                start=(kc == 0), stop=(kc == nkt - 1))
                        ctx.inc(pvm, last)
                        ctx.mark((tag, "pv", b, g), pvm)
                    blk.tensor(pvmm)

                    if h % 2 == 1:
                        def yevac(v, g=g, qi=qi, h=h, qp=qp):
                            if h == 1:
                                ctx.wait_cp(v, (tag, "oproj", b, qi - 1))
                            ctx.wait_cp(v, (tag, "pv", b, g - 1))
                            ctx.wait_cp(v, (tag, "pv", b, g))
                            ctx.inc(sev, v.tensor_copy(
                                yfm[qp][:, (h // 2) * 128:(h // 2) * 128 + 128],
                                psf[3][:, 0:128]))
                            ctx.mark((tag, "yevd", b, g), sev)
                            ctx.mark((tag, "yevd", b, g - 1), sev)
                            if h == NH - 1:
                                ctx.mark((tag, "ydone", b, qi), sev)
                                ctx.mark((tag, "qrffree", b, qi), sev)
                        blk.vector(yevac)


                for h in range(NH + 1):
                    if h < NH:
                        _front(h)
                    if h >= 1:
                        _tail(h - 1)

                def xold(sync, qi=qi, qp=qp, q0=q0, b=b):
                    ctx.wait_cp(sync, (tag, "xnfree", b * 16 + qi - 1))
                    ctx.inc(f"Aldx{qp}", sync.dma_start(out=xnt[qp][:, :],
                                                        in_=x_in[q0:q0 + 128, :]), 16)
                    ctx.mark((tag, "xold", b, qi), f"Aldx{qp}")
                blk.sync(xold)

                def oproj(te, qi=qi, qp=qp):
                    ctx.wait_cp(te, (tag, "ydone", b, qi))
                    ctx.wait_cp(te, (tag, "ofree", b, qi - 1))
                    for half in range(2):
                        last = None
                        for c in range(DCH):
                            last = te.matmul(
                                psw[:, half * 512:(half + 1) * 512],
                                yfm[qp][:, c * 128:(c + 1) * 128],
                                wo2[:, c * D + half * 512:
                                        c * D + (half + 1) * 512],
                                start=(c == 0), stop=(c == DCH - 1))
                        ctx.inc(pop, last)
                    ctx.mark((tag, "oproj", b, qi), pop)
                    ctx.mark((tag, "qfree", b, qi), pop)
                blk.tensor(oproj)

                def resid(v, qi=qi, qp=qp, b=b):
                    ctx.wait_cp(v, (tag, "oproj", b, qi))
                    ctx.wait_cp(v, (tag, "xold", b, qi))
                    ctx.wait_cp(v, (tag, "xnfree", b * 16 + qi - 1))
                    ctx.inc(res, v.tensor_tensor(xnt[qp][:, :], psw[:, :],
                                                 xnt[qp][:, :], ALU.add))
                    ctx.mark((tag, "resid", b, qi), res)
                    ctx.mark((tag, "ofree", b, qi), res)
                blk.vector(resid)

                def store(gp, qi=qi, qp=qp, q0=q0, b=b):
                    ctx.wait_cp(gp, (tag, "resid", b, qi))
                    ctx.inc(f"Ast{qp}", gp.dma_start(out=x_out[q0:q0 + 128, :],
                                                     in_=xnt[qp][:, :]), 16)
                    ctx.mark((tag, "xnfree", b * 16 + qi), f"Ast{qp}")
                blk.gpsimd(store)


def build_mla_test():
    es = ExitStack()
    nc = bass.Bass("TRN2", target_bir_lowering=False)
    x_in = nc.dram_tensor("x_in", [NTOK, D], F32, kind="ExternalInput").ap()
    wq = nc.dram_tensor("wq", [D, D], F32, kind="ExternalInput").ap()
    wqr = nc.dram_tensor("wqr", [D, 512], F32, kind="ExternalInput").ap()
    wkr = nc.dram_tensor("wkr", [D, 32], F32, kind="ExternalInput").ap()
    wkvd = nc.dram_tensor("wkvd", [D, 256], F32, kind="ExternalInput").ap()
    wku = nc.dram_tensor("wku", [256, D], F32, kind="ExternalInput").ap()
    wvu = nc.dram_tensor("wvu", [256, D], F32, kind="ExternalInput").ap()
    wo = nc.dram_tensor("wo", [D, D], F32, kind="ExternalInput").ap()
    lobo = nc.dram_tensor("lobo", [1, 16], F32, kind="ExternalInput").ap()
    rcos = nc.dram_tensor("rcos", [128, 256], F32, kind="ExternalInput").ap()
    rsin = nc.dram_tensor("rsin", [128, 256], F32, kind="ExternalInput").ap()
    x_out = nc.dram_tensor("x_out", [NTOK, D], F32, kind="ExternalOutput").ap()
    ctx = Ctx(nc, es)
    ctx.ps_pool()
    emit_idents(ctx)
    ctx.open_stage()
    emit_mla(ctx, x_in, x_out, wq, wqr, wkr, wkvd, wku, wvu, wo,
             lobo, rcos, rsin, "ml")
    ctx.close_stage()
    emit_drain_outputs(ctx)
    es.close()
    return nc



NSTEP = 32
TSH = NTOK // NSTEP
H4 = 4096


def emit_lstm(ctx, x_in, hs_out, w_d, u_d, bias_d, xp_dram, tag, q_cb=None):
    nc = ctx.nc
    psf, psw, psb = ctx.ps_pool()
    ident = ctx.bufs["ident_bf"]

    ub = ctx.ssb("lsub", (128, DCH * H4), BF16)
    biasr = ctx.ssb("lsbias", (1, H4), BF16)
    onesb = ctx.ssb("lsones", (1, 128), BF16)

    dv, ac, pe, ev = "Ldv2", "Lac2", "Lpe2", "Lev2"

    with nc.Block() as blk:
        def uload(gp):
            for c in range(DCH):
                ctx.inc("Lwu", gp.dma_start(
                    out=ub[:, c * H4:(c + 1) * H4],
                    in_=u_d[c * 128:(c + 1) * 128, :]), 16)
            ctx.inc("Lwu", gp.dma_start(out=biasr[:, :], in_=bias_d[:, :]), 16)
            ctx.wait(gp, "Lwu")
            ctx.inc("sv", gp.memset(onesb[:, :], 1.0))
            ctx.wait(gp, "sv")
        blk.gpsimd(uload)

    ctx.push_scope()
    xnf = ctx.ssb("lsxnf", (128, DCH * 512), BF16)
    wch = ctx.ssb("lswc", (128, DCH * 512), BF16)
    xpev = [ctx.ssb(f"lsxe{p}", (128, 512), F32) for p in range(2)]

    for g in range(NTOK // 512):
        r0 = g * 512
        with nc.Block() as blk:
            emit_bf_to_fm(ctx, blk, x_in, r0, 512, xnf, 512, 0,
                          (tag, "ln", g))
        with nc.Block() as blk:
            for nb in range(8):
                def wload(gp, g=g, nb=nb):
                    ctx.wait_cp(gp, (tag, "xpmm", g, nb - 1, 3))
                    for c in range(DCH):
                        ctx.inc("Lww", gp.dma_start(
                            out=wch[:, c * 512:(c + 1) * 512],
                            in_=w_d[c * 128:(c + 1) * 128,
                                    nb * 512:(nb + 1) * 512]), 16)
                    ctx.mark((tag, "wld", g, nb), "Lww")
                blk.gpsimd(wload)

                for rb in range(4):
                    par = (nb * 4 + rb) % 2

                    def xpmm(te, g=g, nb=nb, rb=rb, par=par):
                        ctx.wait_cp(te, (tag, "wld", g, nb))
                        ctx.wait_cp(te, (tag, "xpev", g * 32 + nb * 4 + rb - 2))
                        for c in range(DCH):
                            te.matmul(
                                psf[par][:, :],
                                xnf[:, c * 512 + rb * 128: c * 512 + (rb + 1) * 128],
                                wch[:, c * 512:(c + 1) * 512],
                                start=(c == 0), stop=False)
                        last = te.matmul(
                            psf[par][:, :], onesb[0:1, 0:128],
                            biasr[0:1, nb * 512:(nb + 1) * 512],
                            start=False, stop=True)
                        ctx.inc(pe, last)
                        ctx.mark((tag, "xpmm", g, nb, rb), pe)
                    blk.tensor(xpmm)

                    def xpevac(v, g=g, nb=nb, rb=rb, par=par):
                        ctx.wait_cp(v, (tag, "xpmm", g, nb, rb))
                        ctx.wait_cp(v, (tag, "xpst", g * 32 + nb * 4 + rb - 2))
                        ctx.inc(ev, v.tensor_copy(xpev[par][:, :], psf[par][:, :]))
                        ctx.mark((tag, "xpev", g * 32 + nb * 4 + rb), ev)
                    blk.vector(xpevac)

                    def xpstore(gp, g=g, nb=nb, rb=rb, par=par):
                        ctx.wait_cp(gp, (tag, "xpev", g * 32 + nb * 4 + rb))
                        ctx.inc(f"Lxs{par}", gp.dma_start(
                            out=xp_dram[g * 512 + rb * 128:
                                        g * 512 + (rb + 1) * 128,
                                        nb * 512:(nb + 1) * 512],
                            in_=xpev[par][:, :]), 16)
                        ctx.mark((tag, "xpst", g * 32 + nb * 4 + rb), f"Lxs{par}")
                    blk.gpsimd(xpstore)
    ctx.pop_scope()

    gact = [ctx.ssb(f"lsga{t}", (128, H4), F32) for t in range(2)]
    c_ = [ctx.ssb(f"lsc{t}", (128, D), F32) for t in range(2)]
    hfm = ctx.ssb("lshfm", (128, DCH * TSH), BF16)
    ht = [ctx.ssb(f"lsht{t}", (128, D), F32) for t in range(2)]
    htb = [ctx.ssb(f"lshtb{t}", (128, D), BF16) for t in range(2)]
    ctmp = [ctx.ssb("lsct", (128, D), F32)] * 2
    ctmp2 = [ctx.ssb("lsc2", (128, D), F32)] * 2
    ttc = [ctx.ssb(f"lsttc{t}", (128, D), F32) for t in range(2)]
    gtmp = [ctx.ssb(f"lsgt{p}", (128, 512), F32) for p in range(8)]
    xpt = [ctx.ssb(f"lsxp{p}", (128, 512), F32) for p in range(8)]
    with nc.Block() as blk:
        def init2(gp):
            ctx.inc("sv", gp.memset(c_[0][:, :], 0.0))
            ctx.inc("sv", gp.memset(c_[1][:, :], 0.0))
            ctx.inc("sv", gp.memset(hfm[:, :], 0.0))
            ctx.wait(gp, "sv")
        blk.gpsimd(init2)

    with nc.Block() as blk:
        for i in range(NSTEP):
            for tt in range(2):
                r0 = i * TSH + tt * 128
                for nb in range(8):
                    par = nb % 2

                    def xpl(sync, i=i, tt=tt, nb=nb, par=par, r0=r0):
                        gi = (i * 2 + tt) * 8 + nb
                        if gi == 0:
                            ctx.wait(sync, "Lxs0")
                            ctx.wait(sync, "Lxs1")
                        pi, pt = (i, tt - 1) if tt == 1 else (i - 1, 1)
                        ctx.wait_cp(sync, (tag, "gac", pi, pt, nb))
                        ctx.inc(f"Lxl{nb}", sync.dma_start(
                            out=xpt[nb][:, :],
                            in_=xp_dram[r0:r0 + 128, nb * 512:(nb + 1) * 512]), 16)
                        ctx.mark((tag, "xpl", i, tt, nb), f"Lxl{nb}")
                    blk.sync(xpl)

                    if i > 0:
                        def gmm(te, i=i, tt=tt, nb=nb, par=par):
                            gi = (i * 2 + tt) * 8 + nb
                            ctx.wait_cp(te, (tag, "psfree", gi - 2))
                            ctx.wait_cp(te, (tag, "hfm", i - 1, tt, "done"))
                            last = None
                            for c in range(DCH):
                                last = te.matmul(
                                    psf[par][:, :],
                                    hfm[:, c * TSH + tt * 128: c * TSH + (tt + 1) * 128],
                                    ub[:, c * H4 + nb * 512: c * H4 + (nb + 1) * 512],
                                    start=(c == 0), stop=(c == DCH - 1))
                            ctx.inc(pe, last)
                            ctx.mark((tag, "gmm", i, tt, nb), pe)
                        blk.tensor(gmm)

                        def gadd(v, i=i, tt=tt, nb=nb, par=par):
                            gi = (i * 2 + tt) * 8 + nb
                            ctx.wait_cp(v, (tag, "gmm", i, tt, nb))
                            ctx.wait_cp(v, (tag, "xpl", i, tt, nb))
                            pi, pt = (i, tt - 1) if tt == 1 else (i - 1, 1)
                            ctx.wait_cp(v, (tag, "gac", pi, pt, nb))
                            ctx.inc(dv, v.tensor_tensor(gtmp[nb][:, :],
                                                        psf[par][:, :],
                                                        xpt[nb][:, :], ALU.add))
                            ctx.mark((tag, "gadd", i, tt, nb), dv)
                            ctx.mark((tag, "psfree", gi), dv)
                        blk.vector(gadd)

                def gactf_all(sc, i=i, tt=tt):
                    ctx.wait_cp(sc, (tag, "gactfree", i, tt))
                    for nb in range(8):
                        gi = (i * 2 + tt) * 8 + nb
                        par = nb % 2
                        fn = AF.Sigmoid if nb < 6 else AF.Tanh
                        if i > 0:
                            ctx.wait_cp(sc, (tag, "gadd", i, tt, nb))
                            srcap = gtmp[nb][:, :]
                        else:
                            ctx.wait_cp(sc, (tag, "xpl", i, tt, nb))
                            srcap = xpt[nb][:, :]
                        ctx.inc(ac, sc.activation(
                            gact[tt][:, nb * 512:(nb + 1) * 512], srcap, fn))
                        ctx.mark((tag, "gac", i, tt, nb), ac)
                        ctx.mark((tag, "gtfree", gi), ac)
                        if i == 0:
                            ctx.mark((tag, "xptfree", gi), ac)
                blk.scalar(gactf_all)

                def chm(v, i=i, tt=tt):
                    ctx.wait_cp(v, (tag, "gac", i, tt, 7))
                    ga = gact[tt]
                    ctx.inc(dv, v.tensor_tensor(ctmp[tt][:, :], ga[:, 0:1024],
                                                ga[:, 3072:4096], ALU.mult))
                    ctx.inc(dv, v.tensor_tensor(ctmp2[tt][:, :], ga[:, 1024:2048],
                                                c_[tt][:, :], ALU.mult))
                    ctx.wait(v, dv)
                    ctx.inc(dv, v.tensor_tensor(c_[tt][:, :], ctmp[tt][:, :],
                                                ctmp2[tt][:, :], ALU.add))
                    ctx.mark((tag, "cnew", i, tt), dv)
                blk.vector(chm)

                def ctanh(sc, i=i, tt=tt):
                    ctx.wait_cp(sc, (tag, "cnew", i, tt))
                    ctx.inc(ac, sc.activation(ttc[tt][:, :], c_[tt][:, :], AF.Tanh))
                    ctx.mark((tag, "tanh", i, tt), ac)
                blk.scalar(ctanh)

                def hmul(v, i=i, tt=tt):
                    ctx.wait_cp(v, (tag, "tanh", i, tt))
                    ctx.wait_cp(v, (tag, "hst", i - 1, tt))
                    ctx.inc(dv, v.tensor_tensor(ht[tt][:, :],
                                                gact[tt][:, 2048:3072],
                                                ttc[tt][:, :], ALU.mult))
                    ctx.wait(v, dv)
                    ctx.inc(dv, v.tensor_copy(htb[tt][:, :], ht[tt][:, :]))
                    ctx.mark((tag, "hnew", i, tt), dv)
                    ctx.mark((tag, "gactfree", i + 1, tt), dv)
                blk.vector(hmul)

                def hstore(gp, i=i, tt=tt, r0=r0):
                    ctx.wait_cp(gp, (tag, "hnew", i, tt))
                    ctx.inc(f"Lhs{tt}", gp.dma_start(
                        out=hs_out[r0:r0 + 128, :], in_=ht[tt][:, :]), 16)
                    ctx.mark((tag, "hst", i, tt), f"Lhs{tt}")
                blk.gpsimd(hstore)

                def htp(te, i=i, tt=tt):
                    ctx.wait_cp(te, (tag, "hnew", i, tt))
                    ctx.wait_cp(te, (tag, "hev", i - 1, tt))
                    last = None
                    for c in range(DCH):
                        last = te.transpose(psb[tt][:, c * 128:(c + 1) * 128],
                                            htb[tt][:, c * 128:(c + 1) * 128],
                                            ident[:, :])
                    ctx.inc(pe, last)
                    ctx.mark((tag, "htp", i, tt), pe)
                blk.tensor(htp)

                def hev(v, i=i, tt=tt):
                    ctx.wait_cp(v, (tag, "htp", i, tt))
                    dst = hfm[:, :].rearrange("p (c u) -> p c u", c=DCH)[
                        :, :, tt * 128:(tt + 1) * 128]
                    srcv = psb[tt][:, :].rearrange("p (c u) -> p c u", c=DCH)
                    ctx.inc(ev, v.tensor_copy(dst, srcv))
                    ctx.mark((tag, "hev", i, tt), ev)
                    ctx.mark((tag, "hfm", i, tt, "done"), ev)
                blk.vector(hev)

            if q_cb is not None and i % 4 == 3:
                q_cb(i // 4, blk)


def build_lstm_test():
    es = ExitStack()
    nc = bass.Bass("TRN2", target_bir_lowering=False)
    x_in = nc.dram_tensor("x_in", [NTOK, D], BF16, kind="ExternalInput").ap()
    w = nc.dram_tensor("w", [D, H4], F32, kind="ExternalInput").ap()
    u = nc.dram_tensor("u", [D, H4], F32, kind="ExternalInput").ap()
    bias = nc.dram_tensor("bias", [1, H4], F32, kind="ExternalInput").ap()
    hs = nc.dram_tensor("hs", [NTOK, D], F32, kind="ExternalOutput").ap()
    xp = nc.dram_tensor("xp", [NTOK, H4], F32, kind="Internal").ap()
    ctx = Ctx(nc, es)
    ctx.ps_pool()
    emit_idents(ctx)
    ctx.open_stage()
    emit_lstm(ctx, x_in, hs, w, u, bias, xp, "ls")
    ctx.close_stage()
    with nc.Block() as blk:
        def fin(gp):
            for name in ("Lhs0", "Lhs1"):
                if name in ctx._sems:
                    ctx.wait(gp, name)
        blk.gpsimd(fin)
    es.close()
    return nc


def build_mha_test():
    es = ExitStack()
    nc = bass.Bass("TRN2", target_bir_lowering=False)
    x_in = nc.dram_tensor("x_in", [NTOK, D], F32, kind="ExternalInput").ap()
    wq = nc.dram_tensor("wq", [D, D], F32, kind="ExternalInput").ap()
    wk = nc.dram_tensor("wk", [D, D], F32, kind="ExternalInput").ap()
    wv = nc.dram_tensor("wv", [D, D], F32, kind="ExternalInput").ap()
    wo = nc.dram_tensor("wo", [D, D], F32, kind="ExternalInput").ap()
    x_out = nc.dram_tensor("x_out", [NTOK, D], F32, kind="ExternalOutput").ap()
    ctx = Ctx(nc, es)
    ctx.ps_pool()
    emit_idents(ctx)
    ctx.open_stage()
    emit_mha(ctx, x_in, x_out, wq, wk, wv, wo, "mh")
    ctx.close_stage()
    emit_drain_outputs(ctx)
    es.close()
    return nc


def emit_drain_outputs(ctx):
    nc = ctx.nc
    with nc.Block() as blk:
        def fin(gp):
            for name in list(ctx._sems):
                if ("st" in name and name[0] in "CMOTA") or name.endswith("sto"):
                    ctx.wait(gp, name)
        blk.gpsimd(fin)



def _np(x):
    return np.ascontiguousarray(np.asarray(x, dtype=np.float32))


def cmix_host_weights(W_r, W_k, W_v, mu_r, mu_k):
    W_r, W_k, W_v = np.asarray(W_r), np.asarray(W_k), np.asarray(W_v)
    return dict(
        wrT=_np(W_r.T), wkT=_np(W_k.T), wvT=_np(W_v.T),
        mur=_np(np.asarray(mu_r).reshape(DCH, 128).T),
        muk=_np(np.asarray(mu_k).reshape(DCH, 128).T))


def build_cmix_test():
    es = ExitStack()
    nc = bass.Bass("TRN2", target_bir_lowering=False)
    x_in = nc.dram_tensor("x_in", [NTOK, D], F32, kind="ExternalInput").ap()
    wrT = nc.dram_tensor("wrT", [D, D], F32, kind="ExternalInput").ap()
    wkT = nc.dram_tensor("wkT", [D, D], F32, kind="ExternalInput").ap()
    wvT = nc.dram_tensor("wvT", [D, D], F32, kind="ExternalInput").ap()
    mur = nc.dram_tensor("mur", [128, DCH], F32, kind="ExternalInput").ap()
    muk = nc.dram_tensor("muk", [128, DCH], F32, kind="ExternalInput").ap()
    x_out = nc.dram_tensor("x_out", [NTOK, D], F32, kind="ExternalOutput").ap()
    ctx = Ctx(nc, es)
    ctx.ps_pool()
    emit_idents(ctx)
    ctx.open_stage()
    emit_cmix_weights(ctx, wrT, wkT, wvT, mur, muk, "cm")
    emit_cmix(ctx, x_in, x_out, "cm")
    ctx.close_stage()
    emit_drain_outputs(ctx)
    es.close()
    return nc





def build_add_test():
    es = ExitStack()
    nc = bass.Bass("TRN2", target_bir_lowering=False)
    a_in = nc.dram_tensor("a_in", [NTOK, D], F32, kind="ExternalInput").ap()
    b_in = nc.dram_tensor("b_in", [NTOK, D], F32, kind="ExternalInput").ap()
    x_out = nc.dram_tensor("x_out", [NTOK, D], F32, kind="ExternalOutput").ap()
    ctx = Ctx(nc, es)
    at = [es.enter_context(nc.sbuf_tensor(f"adda{p}", [128, D], F32)) for p in range(2)]
    bt = [es.enter_context(nc.sbuf_tensor(f"addb{p}", [128, D], F32)) for p in range(2)]
    ot = [es.enter_context(nc.sbuf_tensor(f"addo{p}", [128, D], F32)) for p in range(2)]
    with nc.Block() as blk:
        for s in range(NTOK // 128):
            p = s % 2

            def dma_in(sync, s=s, p=p):
                ctx.wait_cp(sync, ("add", "sum", s - 2))
                ctx.inc(f"Dld{p}", sync.dma_start(
                    out=at[p][:, :], in_=a_in[s * 128:(s + 1) * 128, :]), 16)
                ctx.inc(f"Dld{p}", sync.dma_start(
                    out=bt[p][:, :], in_=b_in[s * 128:(s + 1) * 128, :]), 16)
                ctx.mark(("add", "ld", s), f"Dld{p}")
            blk.sync(dma_in)

            def addv(v, s=s, p=p):
                ctx.wait_cp(v, ("add", "ld", s))
                ctx.wait_cp(v, ("add", "st", s - 2))
                ctx.inc("Ddv", v.tensor_tensor(ot[p][:, :], at[p][:, :],
                                               bt[p][:, :], ALU.add))
                ctx.mark(("add", "sum", s), "Ddv")
            blk.vector(addv)

            def store(gp, s=s, p=p):
                ctx.wait_cp(gp, ("add", "sum", s))
                ctx.inc(f"Dst{p}", gp.dma_start(
                    out=x_out[s * 128:(s + 1) * 128, :], in_=ot[p][:, :]), 16)
                ctx.mark(("add", "st", s), f"Dst{p}")
            blk.gpsimd(store)
    with nc.Block() as blk:
        def fin(gp):
            ctx.wait(gp, "Dst0")
            ctx.wait(gp, "Dst1")
        blk.gpsimd(fin)
    es.close()
    return nc



_PROGRAM_CACHE = {}


def _prog(name, builder):
    if name not in _PROGRAM_CACHE:
        _PROGRAM_CACHE[name] = builder()
    return _PROGRAM_CACHE[name]


def _run(name, builder, in_maps, out_key):
    nc = _prog(name, builder)
    res = run_bass_kernel_spmd(nc, in_maps, core_ids=list(range(NCORES)))
    return [r[out_key] for r in res.results]


def _rope_tables():
    half = 16
    inv = 1.0 / 10000.0 ** (np.arange(half) / half)
    ang = np.arange(T)[:, None] * inv[None, :]
    rcos = np.cos(ang).astype(np.float32).reshape(16, 128, 16).transpose(1, 0, 2).reshape(128, 256)
    rsin = np.sin(ang).astype(np.float32).reshape(16, 128, 16).transpose(1, 0, 2).reshape(128, 256)
    return np.ascontiguousarray(rcos), np.ascontiguousarray(rsin)


pass





def emit_bf_to_fm(ctx, blk, x_dram_bf, row0, ntok, fm_buf, fm_unit, fm_col0, tag):
    nsub = ntok // 128
    ident = ctx.bufs["ident_bf"]
    psf, psw, psb = ctx.ps_pool()
    xt = [ctx.ssb(f"bffx{p}", (128, D), BF16) for p in range(2)]
    for s in range(nsub):
        p = s % 2
        r0 = row0 + s * 128

        def dma_in(sync, s=s, p=p, r0=r0):
            ctx.wait_cp(sync, (tag, "xtfree", s - 2))
            ctx.inc(f"Fld{p}", sync.dma_start(out=xt[p][:, :],
                                              in_=x_dram_bf[r0:r0 + 128, :]), 16)
            ctx.mark((tag, "xload", s), f"Fld{p}")
        blk.sync(dma_in)

        def pe_t(te, s=s, p=p):
            ctx.wait_cp(te, (tag, "xload", s))
            ctx.wait_cp(te, (tag, "tpfree", s - 2))
            last = None
            for c in range(DCH):
                last = te.transpose(psb[p][:, c * 128:(c + 1) * 128],
                                    xt[p][:, c * 128:(c + 1) * 128], ident[:, :])
            ctx.inc("Fpe", last)
            ctx.mark((tag, "tdone", s), "Fpe")
            ctx.mark((tag, "xtfree", s), "Fpe")
        blk.tensor(pe_t)

        def evac(v, s=s, p=p):
            ctx.wait_cp(v, (tag, "tdone", s))
            dst = fmv(fm_buf, fm_unit, fm_col0 + s * 128, fm_col0 + (s + 1) * 128)
            srcv = psb[p][:, :].rearrange("q (c u) -> q c u", c=DCH)
            ctx.inc("Fev", v.tensor_copy(dst, srcv))
            ctx.mark((tag, "tpfree", s), "Fev")
        blk.vector(evac)
    ctx.mark((tag, "done"), "Fev")


def emit_lncast(ctx, x_in, xnb_out, tag, q_cb=None):
    nc = ctx.nc
    xt = [ctx.ssb(f"lcx{p}", (128, D), F32) for p in range(2)]
    st_ = [ctx.ssb(f"lcs{p}", (128, 12), F32) for p in range(2)]
    mv = [ctx.ssb(f"lcm{p}", (128, 2), F32) for p in range(2)]
    rs = [ctx.ssb(f"lcr{p}", (128, 1), F32) for p in range(2)]
    nm = [ctx.ssb(f"lcn{p}", (128, 1), F32) for p in range(2)]
    xn = [ctx.ssb(f"lco{p}", (128, D), BF16) for p in range(2)]
    with nc.Block() as blk:
        for s in range(NTOK // 128):
            p = s % 2
            r0 = s * 128

            def dma_in(sync, s=s, p=p, r0=r0):
                ctx.wait_cp(sync, (tag, "normd", s - 2))
                ctx.inc(f"Gld{p}", sync.dma_start(out=xt[p][:, :],
                                                  in_=x_in[r0:r0 + 128, :]), 16)
                ctx.mark((tag, "xload", s), f"Gld{p}")
            blk.sync(dma_in)

            def dve(v, s=s, p=p):
                ctx.wait_cp(v, (tag, "xload", s))
                ctx.inc("Gdv", v.bn_stats(st_[p][:, 0:6], xt[p][:, 0:512]))
                ctx.inc("Gdv", v.bn_stats(st_[p][:, 6:12], xt[p][:, 512:1024]))
                ctx.wait(v, "Gdv")
                ctx.inc("Gdv", v.bn_aggr(mv[p][:, :], st_[p][:, :]))
                ctx.mark((tag, "aggr", s), "Gdv")
            blk.vector(dve)

            def act1(sc, s=s, p=p):
                ctx.wait_cp(sc, (tag, "aggr", s))
                ctx.inc("Gac", sc.activation(rs[p][:, :], mv[p][:, 1:2], AF.Ln,
                                             bias=ctx.bufs["eps"][:, 0:1]))
                ctx.wait(sc, "Gac")
                ctx.inc("Gac", sc.activation(rs[p][:, :], rs[p][:, :], AF.Exp,
                                             scale=-0.5))
                ctx.mark((tag, "rstd", s), "Gac")
            blk.scalar(act1)

            def dve2(v, s=s, p=p):
                ctx.wait_cp(v, (tag, "rstd", s))
                ctx.inc("Gdv", v.scalar_tensor_tensor(nm[p][:, :], mv[p][:, 0:1],
                                                      -1.0, rs[p][:, :],
                                                      ALU.mult, ALU.mult))
                ctx.wait(v, "Gdv")
                ctx.wait_cp(v, (tag, "stored", s - 2))
                ctx.inc("Gdv", v.tensor_scalar(xn[p][:, :], xt[p][:, :],
                                               rs[p][:, 0:1], nm[p][:, 0:1],
                                               ALU.mult, ALU.add))
                ctx.mark((tag, "normd", s), "Gdv")
            blk.vector(dve2)

            def store(gp, s=s, p=p, r0=r0):
                ctx.wait_cp(gp, (tag, "normd", s))
                ctx.inc(f"Gst{p}", gp.dma_start(
                    out=xnb_out[r0:r0 + 128, :], in_=xn[p][:, :]), 16)
                ctx.mark((tag, "stored", s), f"Gst{p}")
            blk.gpsimd(store)

            if q_cb is not None and s % 16 == 15:
                q_cb(s // 16, blk)


def emit_dma_flush(ctx):
    nc = ctx.nc
    with nc.Block() as blk:
        def fl(sync):
            for name in sorted(ctx.dma_sems):
                ctx.wait(sync, name)
        blk.sync(fl)


def build_fused():
    es = ExitStack()
    nc = bass.Bass("TRN2", target_bir_lowering=False)
    DI = nc.dram_tensor
    x_in = DI("x_in", [NTOK, D], F32, kind="ExternalInput").ap()
    x_out = DI("x_out", [NTOK, D], F32, kind="ExternalOutput").ap()
    xb0 = DI("xb0", [NTOK, D], F32, kind="Internal").ap()
    xb1 = DI("xb1", [NTOK, D], F32, kind="Internal").ap()
    xp = DI("xp", [NTOK, H4], F32, kind="Internal").ap()
    hs = DI("hs", [NTOK, D], F32, kind="Internal").ap()

    wq = DI("wq", [D, D], F32, kind="ExternalInput").ap()
    wqr = DI("wqr", [D, 512], F32, kind="ExternalInput").ap()
    wkr = DI("wkr", [D, 32], F32, kind="ExternalInput").ap()
    wkvd = DI("wkvd", [D, 256], F32, kind="ExternalInput").ap()
    wku = DI("wku", [256, D], F32, kind="ExternalInput").ap()
    wvu = DI("wvu", [256, D], F32, kind="ExternalInput").ap()
    wo_mla = DI("wo_mla", [D, D], F32, kind="ExternalInput").ap()
    lobo = DI("lobo", [1, 16], F32, kind="ExternalInput").ap()
    rcos = DI("rcos", [128, 256], F32, kind="ExternalInput").ap()
    rsin = DI("rsin", [128, 256], F32, kind="ExternalInput").ap()
    wg = DI("wg", [D, 8], F32, kind="ExternalInput").ap()
    w1 = DI("w1", [8, D, HMOE], F32, kind="ExternalInput").ap()
    b1 = DI("b1", [8, HMOE], F32, kind="ExternalInput").ap()
    w2 = DI("w2", [8, HMOE, D], F32, kind="ExternalInput").ap()
    b2 = DI("b2", [8, D], F32, kind="ExternalInput").ap()
    kwT = DI("kwT", [D, 64], F32, kind="ExternalInput").ap()
    vwT = DI("vwT", [D, 64], F32, kind="ExternalInput").ap()
    owT = DI("owT", [64, D], F32, kind="ExternalInput").ap()
    tsg = DI("tsg", [1, D], F32, kind="ExternalInput").ap()
    lnw = DI("lnw", [128, DCH], F32, kind="ExternalInput").ap()
    lnb = DI("lnb", [128, DCH], F32, kind="ExternalInput").ap()
    td = DI("td", [64, 1], F32, kind="ExternalInput").ap()
    tf = DI("tf", [64, 1], F32, kind="ExternalInput").ap()
    wq1 = DI("wq1", [D, D], F32, kind="ExternalInput").ap()
    wk1 = DI("wk1", [D, D], F32, kind="ExternalInput").ap()
    wv1 = DI("wv1", [D, D], F32, kind="ExternalInput").ap()
    wo1 = DI("wo1", [D, D], F32, kind="ExternalInput").ap()
    wrT = DI("wrT", [D, D], F32, kind="ExternalInput").ap()
    wkT = DI("wkT", [D, D], F32, kind="ExternalInput").ap()
    wvT = DI("wvT", [D, D], F32, kind="ExternalInput").ap()
    mur = DI("mur", [128, DCH], F32, kind="ExternalInput").ap()
    muk = DI("muk", [128, DCH], F32, kind="ExternalInput").ap()
    wl_ = DI("wl", [D, H4], F32, kind="ExternalInput").ap()
    ul_ = DI("ul", [D, H4], F32, kind="ExternalInput").ap()
    biasl = DI("biasl", [1, H4], F32, kind="ExternalInput").ap()

    ctx = Ctx(nc, es)
    ctx.ps_pool()
    emit_idents(ctx)

    ctx.open_stage()
    emit_mla(ctx, x_in, xb0, wq, wqr, wkr, wkvd, wku, wvu, wo_mla,
             lobo, rcos, rsin, "ml")
    ctx.close_stage()
    emit_dma_flush(ctx)

    ctx.open_stage()
    emit_moe(ctx, xb0, xb1, wg, w1, b1, w2, b2, "mo1")
    ctx.close_stage()
    emit_dma_flush(ctx)

    ctx.open_stage()
    emit_tmix(ctx, xb1, xb0, kwT, vwT, owT, tsg, lnw, lnb, td, tf, "tx")
    ctx.close_stage()
    emit_dma_flush(ctx)

    ctx.open_stage()
    emit_moe(ctx, xb0, xb1, wg, w1, b1, w2, b2, "mo2")
    ctx.close_stage()
    emit_dma_flush(ctx)

    ctx.open_stage()
    emit_mha(ctx, xb1, xb0, wq1, wk1, wv1, wo1, "mh")
    ctx.close_stage()
    emit_dma_flush(ctx)

    ctx.open_stage()
    emit_cmix_weights(ctx, wrT, wkT, wvT, mur, muk, "cm1")
    emit_cmix(ctx, xb0, xb1, "cm1")
    ctx.close_stage()
    emit_dma_flush(ctx)

    xnb = DI("xnb", [NTOK, D], BF16, kind="Internal").ap()
    agb_q = [DI(f"agbq{q}", [NCORES * 2048, D], BF16, kind="Internal",
                addr_space="Shared").ap() for q in range(4)]
    lstm_x = DI("lstm_x", [NTOK, D], BF16, kind="Internal").ap()
    ag2q = DI("ag2q", [8 * NCORES * 1024, D], F32, kind="Internal",
              addr_space="Shared").ap()
    hsb = DI("hsb", [NTOK, D], F32, kind="Internal").ap()

    def lc_qcb(q, blk):
        def agx(gp, q=q):
            ctx.wait(gp, "Gst0")
            ctx.wait(gp, "Gst1")
            i = gp.collective_compute(
                "AllGather", ALU.bypass,
                replica_groups=[list(range(NCORES))],
                ins=[xnb[q * 2048:(q + 1) * 2048, :].opt()],
                outs=[agb_q[q].opt()])
            ctx.inc("ccs", i, 1)
        blk.gpsimd(agx)

    ctx.open_stage()
    emit_lncast(ctx, xb1, xnb, "lc", q_cb=lc_qcb)
    ctx.close_stage()
    emit_dma_flush(ctx)

    with nc.Block() as blk:
        def cc1(gp):
            ctx.wait(gp, "ccs")
            with gp.register("agr") as r2:
                gp.reg_mov(r2, gp.partition_id())
                gp.reg_mul(r2, r2, 256 * D)
                for q in range(4):
                    ctx.inc("ccd", gp.dma_start(
                        out=bass.AP(lstm_x.tensor, q * 256 * D,
                                    [[1024 * D, 8], [D, 256], [1, D]]),
                        in_=bass.AP(agb_q[q].tensor, r2,
                                    [[2048 * D, 8], [D, 256], [1, D]])), 16)
                ctx.wait(gp, "ccd")
        blk.gpsimd(cc1)

    def lstm_qcb(q, blk):
        def agq(gp, q=q):
            ctx.wait(gp, "Lhs0")
            ctx.wait(gp, "Lhs1")
            i = gp.collective_compute(
                "AllGather", ALU.bypass,
                replica_groups=[list(range(NCORES))],
                ins=[hs[q * 1024:(q + 1) * 1024, :].opt()],
                outs=[ag2q[q * NCORES * 1024:(q + 1) * NCORES * 1024, :].opt()])
            ctx.inc("ccs", i, 1)
        blk.gpsimd(agq)

    ctx.open_stage()
    emit_lstm(ctx, lstm_x, hs, wl_, ul_, biasl, xp, "ls", q_cb=lstm_qcb)
    ctx.close_stage()
    emit_dma_flush(ctx)

    with nc.Block() as blk:
        def cc2(gp):
            ctx.wait(gp, "ccs")
            with gp.register("agr3") as r2:
                gp.reg_mov(r2, gp.partition_id())
                gp.reg_mul(r2, r2, NCORES * 1024 * D)
                for sl in range(4):
                    ctx.inc("ccd", gp.dma_start(
                        out=hsb[sl * T:(sl + 1) * T, :].rearrange(
                            "(c t) d -> c t d", t=256),
                        in_=bass.AP(ag2q.tensor, r2,
                                    [[1024 * D, 8], [D, 256], [1, D]])), 16)
                    if sl < 3:
                        gp.reg_add(r2, r2, 256 * D)
                ctx.wait(gp, "ccd")
        blk.gpsimd(cc2)

    at = [ctx.sb(f"fadda{p}", (128, D), F32) for p in range(2)]
    bt = [ctx.sb(f"faddb{p}", (128, D), F32) for p in range(2)]
    ot = [ctx.sb(f"faddo{p}", (128, D), F32) for p in range(2)]
    with nc.Block() as blk:
        for s in range(NTOK // 128):
            p = s % 2

            def dma_in(sync, s=s, p=p):
                ctx.wait_cp(sync, ("add", "sum", s - 2))
                ctx.inc(f"Dld{p}", sync.dma_start(
                    out=at[p][:, :], in_=xb1[s * 128:(s + 1) * 128, :]), 16)
                ctx.inc(f"Dld{p}", sync.dma_start(
                    out=bt[p][:, :], in_=hsb[s * 128:(s + 1) * 128, :]), 16)
                ctx.mark(("add", "ld", s), f"Dld{p}")
            blk.sync(dma_in)

            def addv(v, s=s, p=p):
                ctx.wait_cp(v, ("add", "ld", s))
                ctx.wait_cp(v, ("add", "st", s - 2))
                ctx.inc("Ddv", v.tensor_tensor(ot[p][:, :], at[p][:, :],
                                               bt[p][:, :], ALU.add))
                ctx.mark(("add", "sum", s), "Ddv")
            blk.vector(addv)

            def store(gp, s=s, p=p):
                ctx.wait_cp(gp, ("add", "sum", s))
                ctx.inc(f"Dst{p}", gp.dma_start(
                    out=xb0[s * 128:(s + 1) * 128, :], in_=ot[p][:, :]), 16)
                ctx.mark(("add", "st", s), f"Dst{p}")
            blk.gpsimd(store)
    emit_dma_flush(ctx)

    ctx.open_stage()
    emit_cmix_weights(ctx, wrT, wkT, wvT, mur, muk, "cm2")
    emit_cmix(ctx, xb0, x_out, "cm2")
    ctx.close_stage()

    emit_dma_flush(ctx)
    es.close()
    return nc


_FUSED = None


def kernel(**kw):
    global _FUSED
    if _FUSED is None:
        _FUSED = build_fused()
    rcos, rsin = _rope_tables()
    common = {
        "wq": _np(kw["L0_attn_a_Wq"]), "wqr": _np(kw["L0_attn_a_Wqr"]),
        "wkr": _np(kw["L0_attn_a_Wkr"]), "wkvd": _np(kw["L0_attn_a_Wkvd"]),
        "wku": _np(kw["L0_attn_a_Wku"]), "wvu": _np(kw["L0_attn_a_Wvu"]),
        "wo_mla": _np(kw["L0_attn_a_Wo"]),
        "lobo": _np(kw["L0_attn_a_lobo_log"]).reshape(1, 16),
        "rcos": rcos, "rsin": rsin,
        "wg": _np(kw["L0_ffn_Wg"]), "w1": _np(kw["L0_ffn_W1"]),
        "b1": _np(kw["L0_ffn_b1"]), "w2": _np(kw["L0_ffn_W2"]),
        "b2": _np(kw["L0_ffn_b2"]),
        "kwT": _np(np.asarray(kw["L0_attn_b_key_w"]).T),
        "vwT": _np(np.asarray(kw["L0_attn_b_value_w"]).T),
        "owT": _np(np.asarray(kw["L0_attn_b_output_w"]).T),
        "tsg": _np(kw["L0_attn_b_time_shift_gates"]).reshape(1, D),
        "lnw": _np(np.asarray(kw["L0_attn_b_ln_w"]).reshape(DCH, 128).T),
        "lnb": _np(np.asarray(kw["L0_attn_b_ln_b"]).reshape(DCH, 128).T),
        "td": _np(kw["L0_attn_b_time_decay"]).reshape(64, 1),
        "tf": _np(kw["L0_attn_b_time_first"]).reshape(64, 1),
        "wq1": _np(kw["L1_attn_a_W_Q"]), "wk1": _np(kw["L1_attn_a_W_K"]),
        "wv1": _np(kw["L1_attn_a_W_V"]), "wo1": _np(kw["L1_attn_a_W_o"]),
        "wrT": _np(np.asarray(kw["L1_ffn_W_r"]).T),
        "wkT": _np(np.asarray(kw["L1_ffn_W_k"]).T),
        "wvT": _np(np.asarray(kw["L1_ffn_W_v"]).T),
        "mur": _np(np.asarray(kw["L1_ffn_mu_r"]).reshape(DCH, 128).T),
        "muk": _np(np.asarray(kw["L1_ffn_mu_k"]).reshape(DCH, 128).T),
        "wl": _np(kw["L1_attn_b_w"]), "ul": _np(kw["L1_attn_b_u"]),
        "biasl": _np(kw["L1_attn_b_bias"]).reshape(1, H4),
    }
    in_maps = []
    for c in range(NCORES):
        xc = np.ascontiguousarray(
            np.asarray(kw["L0_attn_a_x"])[4 * c:4 * c + 4].reshape(NTOK, D),
            dtype=np.float32)
        in_maps.append(dict(x_in=xc, **common))
    res = run_bass_kernel_spmd(_FUSED, in_maps, core_ids=list(range(NCORES)))
    out = np.concatenate([r["x_out"].reshape(4, T, D) for r in res.results],
                         axis=0)
    return out
